# version 01
# baseline (speedup 1.0000x reference)
import numpy as np
import ml_dtypes
import concourse.bass as bass
import concourse.mybir as mybir
from concourse.bass_utils import run_bass_kernel_spmd

F32 = mybir.dt.float32
BF16 = mybir.dt.bfloat16
AF = mybir.ActivationFunctionType
ALU = mybir.AluOpType

NCORE = 8
S = 8192
D = 4096
TC = S // NCORE
NP = D // 128
TH = 512
EPS = 1e-5

PIDX = ((np.arange(D) % 128) * 32 + (np.arange(D) // 128)).astype(np.int64)
BF = ml_dtypes.bfloat16


class Sem:
    __slots__ = ("h", "_n", "name")
    def __init__(self, h, name):
        self.h = h; self._n = 0; self.name = name
    @property
    def n(self):
        return self._n
    def bump(self, k):
        self._n += k
    def absval(self, v):
        return v


class SemView:
    __slots__ = ("s", "base", "name")
    def __init__(self, s, name):
        self.s = s; self.base = s.n; self.name = name
    @property
    def h(self):
        return self.s.h
    @property
    def n(self):
        return self.s.n - self.base
    def bump(self, k):
        self.s.bump(k)
    def absval(self, v):
        return self.base + v


class Ctx:
    ENGINES = ("sync", "tensor", "vector", "scalar", "gpsimd")

    def __init__(self, nc):
        self.nc = nc
        self.q = {e: [] for e in self.ENGINES}
        self.sbuf_off = 16512
        self.sbuf_limit = nc.SBUF_PARTITION_SIZE_BYTES
        self.peak = 0

    def sem_global(self, name):
        return Sem(self.nc.alloc_semaphore(name), name)

    def begin(self, level):
        if not hasattr(self, "_pools"):
            self._pools = {"phase": [[], []], "half": [[], []]}
            self._dpools = {"phase": [[], []], "half": [[], []]}
            self._gen = {"phase": 0, "half": 0}
        g = self._gen[level] % 2
        self._cur = self._pools[level][g]
        self._dcur = self._dpools[level][g]
        self._gen[level] += 1
        self._pos = 0
        self._dpos = 0

    def _draw(self, pool, pos, name):
        if pos < len(pool):
            s = pool[pos]
        else:
            s = Sem(self.nc.alloc_semaphore(name), name)
            pool.append(s)
        return SemView(s, name)

    def sem(self, name):
        if not hasattr(self, "_cur") or self._cur is None:
            return self.sem_global(name)
        v = self._draw(self._cur, self._pos, name)
        self._pos += 1
        return v

    def dsem(self, name):
        if not hasattr(self, "_dcur") or self._dcur is None:
            return self.sem_global(name)
        v = self._draw(self._dcur, self._dpos, name)
        self._dpos += 1
        return v

    def sb(self, name, cols, dtype):
        nbytes = cols * (4 if dtype == F32 else 2)
        off = (self.sbuf_off + 63) & ~63
        assert off + nbytes <= self.sbuf_limit, (name, off, nbytes)
        t = self.nc.alloc_sbuf_tensor_at(name, [128, cols], dtype, offset=off)
        self.sbuf_off = off + nbytes
        self.peak = max(self.peak, self.sbuf_off)
        return t

    def sb_at(self, name, cols, dtype, off):
        nbytes = cols * (4 if dtype == F32 else 2)
        assert off + nbytes <= self.sbuf_limit, (name, off, nbytes)
        return self.nc.alloc_sbuf_tensor_at(name, [128, cols], dtype, offset=off)

    def mark(self):
        return self.sbuf_off

    def release(self, mark):
        self.sbuf_off = mark

    def on(self, eng, fn, wait=(), inc=None, incs=()):
        allincs = list(incs) + ([inc] if inc is not None else [])
        for s, k in allincs:
            s.bump(k)
        w = [(s.h, s.absval(v)) for (s, v) in wait]
        def run(e, w=w, fn=fn, allincs=allincs):
            for h, v in w:
                e.wait_ge(h, v)
            r = fn(e)
            for s, k in allincs:
                r = r.then_inc(s.h, k)
        self.q[eng].append(run)

    def finalize(self):
        nc = self.nc
        with nc.Block() as block:
            @block.sync
            def _(e, q=self.q["sync"]):
                for f in q: f(e)
            @block.tensor
            def _(e, q=self.q["tensor"]):
                for f in q: f(e)
            @block.vector
            def _(e, q=self.q["vector"]):
                for f in q: f(e)
            @block.scalar
            def _(e, q=self.q["scalar"]):
                for f in q: f(e)
            @block.gpsimd
            def _(e, q=self.q["gpsimd"]):
                for f in q: f(e)


def w_only(ctx, eng, *pairs):
    w = [(s.h, s.absval(v)) for (s, v) in pairs]
    def run(e, w=w):
        for h, v in w:
            e.wait_ge(h, v)
    ctx.q[eng].append(run)


def host_consts():
    rbc = np.zeros((8, 128), dtype=np.float32)
    for e in range(8):
        rbc[e, e * 16:(e + 1) * 16] = 2.0
    return {
        "c_eyef": np.eye(128, dtype=np.float32),
        "c_eyeb": np.eye(128, dtype=BF),
        "c_onesb": np.ones((128, 1), dtype=BF),
        "c_ones1f": np.ones((1, 128), dtype=np.float32),
        "c_rbc": rbc,
    }


def host_inputs_p0(inp):
    wq = np.ascontiguousarray(inp["L0_attn_Wq"].transpose(2, 0, 1)[:, PIDX, :])
    wk = np.ascontiguousarray(inp["L0_attn_Wk"].transpose(2, 0, 1)[:, PIDX, :])
    wv = np.ascontiguousarray(inp["L0_attn_Wv"].transpose(2, 0, 1)[:, PIDX, :])
    return {"p0_wq": wq, "p0_wk": wk, "p0_wv": wv,
            "p0_bq": np.ascontiguousarray(inp["L0_attn_bq"]),
            "p0_bk": np.ascontiguousarray(inp["L0_attn_bk"]),
            "p0_bv": np.ascontiguousarray(inp["L0_attn_bv"])}


def host_inputs_p1(inp):
    baseT = np.ascontiguousarray(inp["L0_ffn_base_W"].T[PIDX][:, PIDX])
    wrT = np.ascontiguousarray(inp["L0_ffn_router_W"].T[PIDX])
    laA = np.ascontiguousarray(inp["L0_ffn_lora_A"].transpose(1, 0, 2).reshape(D, 128)[PIDX])
    laB = np.ascontiguousarray(inp["L0_ffn_lora_B"].reshape(128, D)[:, PIDX])
    return {"p1_baseT": baseT, "p1_wrT": wrT, "p1_laA": laA, "p1_laB": laB}


def shard_x(x):
    x2 = np.ascontiguousarray(x.reshape(S, D)[:, PIDX].T)
    return [np.ascontiguousarray(x2[:, c * TC:(c + 1) * TC]) for c in range(NCORE)]


def unshard_out(parts):
    full = np.concatenate(parts, axis=1)
    out = np.empty((S, D), dtype=np.float32)
    out[:, PIDX] = full.T
    return out.reshape(1, S, D)


class MK:
    pass


def bankf(mk, b, n=512):
    return mk.psumf[:, b * 512:b * 512 + n]


def declare_common(ctx, mk):
    nc = ctx.nc
    mk.x_in = nc.dram_tensor("x_in", [D, TC], F32, kind="ExternalInput")
    mk.y_out = nc.dram_tensor("y_out", [D, TC], F32, kind="ExternalOutput")
    mk.c_eyef = nc.dram_tensor("c_eyef", [128, 128], F32, kind="ExternalInput")
    mk.c_eyeb = nc.dram_tensor("c_eyeb", [128, 128], BF16, kind="ExternalInput")
    mk.c_onesb = nc.dram_tensor("c_onesb", [128, 1], BF16, kind="ExternalInput")
    mk.c_ones1f = nc.dram_tensor("c_ones1f", [1, 128], F32, kind="ExternalInput")
    mk.c_rbc = nc.dram_tensor("c_rbc", [8, 128], F32, kind="ExternalInput")

    mk.stream = ctx.sb("stream", NP * TC, F32)
    mk.xbh = ctx.sb("xbh", NP * TH, BF16)
    mk.eyef = ctx.sb("eyef", 128, F32)
    mk.eyeb = ctx.sb("eyeb", 128, BF16)
    mk.onesb = ctx.sb("onesb", 1, BF16)
    mk.ones1f = ctx.sb("ones1f", 128, F32)
    mk.rbc = ctx.sb("rbc", 128, F32)
    mk.epsc = ctx.sb("epsc", 1, F32)
    mk.psumf = nc.alloc_psum_tensor("psumf", [128, 6 * 512], F32)
    mk.psumb = nc.alloc_psum_tensor("psumb", [128, 2048], BF16)
    mk.sInit = ctx.sem("sInit")


def emit_init_loads(ctx, mk):
    s = mk.sInit
    for src, dst in ((mk.c_eyef, mk.eyef[:]), (mk.c_eyeb, mk.eyeb[:]),
                     (mk.c_onesb, mk.onesb[:]), (mk.c_ones1f, mk.ones1f[0:1, :]),
                     (mk.c_rbc, mk.rbc[0:8, :])):
        ctx.on("sync", lambda e, src=src, dst=dst: e.dma_start(out=dst, in_=src[:, :]),
               inc=(s, 16))
    xr = mk.x_in.rearrange("(a p) t -> p a t", p=128)
    so = mk.stream[:].rearrange("p (a t) -> p a t", a=NP)
    ctx.on("sync", lambda e, xr=xr, so=so: e.dma_start(out=so, in_=xr), inc=(s, 16))
    mk.sConst = ctx.sem("sConst")
    ctx.on("vector", lambda e: e.memset(mk.epsc[:], EPS), inc=(mk.sConst, 1))
    mk.init_ready = [(s, 96)]


def emit_cast_xbh(ctx, mk, hf, waits):
    sC = ctx.sem(f"cast_h{hf}_{len(ctx.q['vector'])}")
    for p in range(NP):
        ctx.on("vector",
               lambda e, p=p: e.tensor_copy(
                   mk.xbh[:, p * TH:(p + 1) * TH],
                   mk.stream[:, p * TC + hf * TH:p * TC + hf * TH + TH]),
               wait=waits if p == 0 else (), inc=(sC, 1))
    return sC


def emit_store_out(ctx, mk, waits):
    sSt = ctx.sem("sSt")
    yr = mk.y_out.rearrange("(a p) t -> p a t", p=128)
    so = mk.stream[:].rearrange("p (a t) -> p a t", a=NP)
    ctx.on("sync", lambda e, yr=yr, so=so: e.dma_start(out=yr, in_=so),
           wait=waits, inc=(sSt, 16))
    w_only(ctx, "sync", (sSt, 16))


def emit_ln_half(ctx, mk, tag, hf, s_add, add_target, extra_first_wait=(), alias_off=None,
                 flat=False, alias_all=False):
    mark = ctx.mark()
    ybf = mk.xbh
    if alias_all and alias_off is not None:
        sq = ctx.sb_at(f"{tag}_sq", 2 * TH, BF16, alias_off)
        rab = ctx.sb_at(f"{tag}_rab", 2 * TH, F32, alias_off + 2 * TH * 2)
        absb = ctx.sb_at(f"{tag}_ab", 2 * TH, F32, alias_off + 2 * TH * 2 + 2 * TH * 4)
    elif alias_off is not None:
        sq = ctx.sb(f"{tag}_sq", 2 * TH, BF16)
        rab = ctx.sb(f"{tag}_rab", 2 * TH, F32)
        absb = ctx.sb_at(f"{tag}_ab", 2 * TH, F32, alias_off)
    else:
        sq = ctx.sb(f"{tag}_sq", 2 * TH, BF16)
        rab = ctx.sb(f"{tag}_rab", 2 * TH, F32)
        absb = ctx.sb(f"{tag}_ab", 2 * TH, F32)

    sCast = ctx.sem(f"{tag}_cast")
    sSq = ctx.sem(f"{tag}_sq")
    sMM = ctx.sem(f"{tag}_mm")
    sSum = ctx.sem(f"{tag}_sum")
    sStat = ctx.sem(f"{tag}_stat")
    sBc = ctx.sem(f"{tag}_bc")
    sBcE = ctx.sem(f"{tag}_bce")
    sNorm = ctx.sem(f"{tag}_norm")

    row_a = rab[0:1, 0:TH]
    row_b = rab[0:1, TH:2 * TH]
    psum6 = mk.psumf[0:1, 5 * 512:5 * 512 + TH]
    psum5 = mk.psumf[0:1, 4 * 512:4 * 512 + TH]

    for p in range(NP):
        st = mk.stream[:, p * TC + hf * TH:p * TC + hf * TH + TH]
        tw = [(s_add, add_target)] if flat else [(s_add, add_target + p)]
        ctx.on("vector",
               lambda e, p=p, st=st: e.tensor_copy(ybf[:, p * TH:(p + 1) * TH], st),
               wait=(tw + (list(extra_first_wait) if p == 0 else [])),
               inc=(sCast, 1))
        ctx.on("scalar",
               lambda e, p=p, st=st: e.activation(
                   out=sq[:, (p % 2) * TH:(p % 2) * TH + TH], in_=st, func=AF.Square),
               wait=tw + ([(sMM, p - 1)] if p >= 2 else []) +
                    (list(extra_first_wait) if p == 0 else []),
               inc=(sSq, 1))
        ctx.on("tensor",
               lambda e, p=p: e.matmul(
                   psum6, mk.onesb[:, 0:1], ybf[:, p * TH:(p + 1) * TH],
                   start=(p == 0), stop=(p == NP - 1)),
               wait=[(sCast, p + 1)])
        ctx.on("tensor",
               lambda e, p=p: e.matmul(
                   psum5, mk.onesb[:, 0:1], sq[:, (p % 2) * TH:(p % 2) * TH + TH],
                   start=(p == 0), stop=(p == NP - 1)),
               wait=[(sSq, p + 1)],
               inc=(sMM, 1) if p < NP - 1 else (sSum, 1))

    c = 1.0 / D
    ctx.on("vector", lambda e: e.tensor_scalar_mul(row_b, psum6, c),
           wait=[(sSum, 1)], inc=(sStat, 1))
    ctx.on("vector", lambda e: e.tensor_scalar_mul(row_a, psum5, c),
           wait=[(sStat, 1)], inc=(sStat, 1))
    ctx.on("vector", lambda e: e.tensor_tensor(psum6, row_b, row_b, ALU.mult),
           wait=[(sStat, 2)], inc=(sStat, 1))
    ctx.on("vector", lambda e: e.tensor_tensor(row_a, row_a, psum6, ALU.subtract),
           wait=[(sStat, 3)], inc=(sStat, 1))
    ctx.on("scalar", lambda e: e.activation(out=row_a, in_=row_a, func=AF.Ln,
                                            bias=mk.epsc[0:1, 0:1]),
           wait=[(sStat, 4), (mk.sConst, 1)], inc=(sStat, 1))
    ctx.on("scalar", lambda e: e.activation(out=row_a, in_=row_a, func=AF.Exp, scale=-0.5),
           wait=[(sStat, 5)], inc=(sStat, 1))
    ctx.on("vector", lambda e: e.tensor_tensor(row_b, row_b, row_a, ALU.mult),
           wait=[(sStat, 6)], inc=(sStat, 1))
    ctx.on("vector", lambda e: e.tensor_scalar_mul(row_b, row_b, -1.0),
           wait=[(sStat, 7)], inc=(sStat, 1))

    ctx.on("tensor", lambda e: e.matmul(bankf(mk, 5), mk.ones1f[0:1, :], row_a,
                                        start=True, stop=True),
           wait=[(sStat, 8)], inc=(sBc, 1))
    ctx.on("tensor", lambda e: e.matmul(bankf(mk, 4), mk.ones1f[0:1, :], row_b,
                                        start=True, stop=True),
           wait=[(sStat, 8)], inc=(sBc, 1))
    ctx.on("vector", lambda e: e.tensor_copy(absb[:, 0:TH], bankf(mk, 5)),
           wait=[(sBc, 2)], inc=(sBcE, 1))
    ctx.on("vector", lambda e: e.tensor_copy(absb[:, TH:2 * TH], bankf(mk, 4)),
           wait=[(sBcE, 1)], inc=(sBcE, 1))

    sNm = ctx.sem(f"{tag}_nm")
    for p in range(NP):
        st = mk.stream[:, p * TC + hf * TH:p * TC + hf * TH + TH]
        ctx.on("vector",
               lambda e, st=st: e.tensor_tensor(st, st, absb[:, 0:TH], ALU.mult),
               wait=[(sBcE, 2)] if p == 0 else (), inc=(sNm, 1))
        ctx.on("vector",
               lambda e, st=st: e.tensor_tensor(st, st, absb[:, TH:2 * TH], ALU.add),
               wait=[(sNm, p + 1)], inc=(sNorm, 1))
    ctx.release(mark)
    return sNorm, sSum


def declare_p0(ctx, mk):
    nc = ctx.nc
    mk.p0_wq = nc.dram_tensor("p0_wq", [32, D, 128], F32, kind="ExternalInput")
    mk.p0_wk = nc.dram_tensor("p0_wk", [8, D, 128], F32, kind="ExternalInput")
    mk.p0_wv = nc.dram_tensor("p0_wv", [8, D, 128], F32, kind="ExternalInput")
    mk.p0_bq = nc.dram_tensor("p0_bq", [128, 32], F32, kind="ExternalInput")
    mk.p0_bk = nc.dram_tensor("p0_bk", [128, 8], F32, kind="ExternalInput")
    mk.p0_bv = nc.dram_tensor("p0_bv", [128, 8], F32, kind="ExternalInput")
    mk.p0_qd = nc.dram_tensor("p0_qd", [8, TH, 128], BF16)
    mk.p0_kd = nc.dram_tensor("p0_kd", [8, TH, 128], BF16)


def emit_p0(ctx, mk, s_ready):
    ctx.begin("phase")
    mark = ctx.mark()
    wb_base = (ctx.sbuf_off + 63) & ~63
    wbuf = [ctx.sb("p0_wb0", D, BF16), ctx.sb("p0_wb1", D, BF16)]
    qc = [ctx.sb("p0_qc0", TH, BF16), ctx.sb("p0_qc1", TH, BF16)]
    vc = ctx.sb("p0_vc", 8 * TH, BF16)
    tev = [ctx.sb("p0_tev0", 128, BF16), ctx.sb("p0_tev1", 128, BF16)]
    qt_alias = wb_base
    qt = [ctx.sb_at("p0_qt0", 16 * 128, BF16, wb_base),
          ctx.sb_at("p0_qt1", 16 * 128, BF16, wb_base + 4096)]
    kt = [ctx.sb_at("p0_kt0", 16 * 128, BF16, wb_base + 8192),
          ctx.sb_at("p0_kt1", 16 * 128, BF16, wb_base + 12288)]
    vcat = [ctx.sb("p0_vcat0", 9 * 56, BF16), ctx.sb("p0_vcat1", 9 * 56, BF16)]
    ubuf = [ctx.sb("p0_u0", 512, BF16), ctx.sb("p0_u1", 512, BF16)]
    zinv = ctx.sb("p0_zinv", 56, F32)
    tmpa = ctx.sb("p0_tmpa", 56, F32)
    bq = ctx.sb("p0_bq", 32, F32)
    bk = ctx.sb("p0_bk", 8, F32)
    bv = ctx.sb("p0_bv", 8, F32)
    bqc = ctx.sb("p0_bqc", 8, F32)
    ln_base = ctx.mark()

    SCL = 1.0 / float(np.sqrt(128.0))
    NB = (TH + 55) // 56
    NCH = TH // 16

    sLdB = ctx.dsem("p0_sLdB")
    ctx.on("scalar", lambda e: e.dma_start(out=bq[:, 0:32], in_=mk.p0_bq[:, :]), inc=(sLdB, 16))
    ctx.on("scalar", lambda e: e.dma_start(out=bk[:, 0:8], in_=mk.p0_bk[:, :]), inc=(sLdB, 16))
    ctx.on("scalar", lambda e: e.dma_start(out=bv[:, 0:8], in_=mk.p0_bv[:, :]), inc=(sLdB, 16))
    sBqc = ctx.sem("p0_sBqc")
    ctx.on("vector", lambda e: e.tensor_tensor(bqc[:, 0:8], bq[:, 0:32:4], bq[:, 1:32:4], ALU.add),
           wait=[(sLdB, 48)], inc=(sBqc, 1))
    ctx.on("vector", lambda e: e.tensor_tensor(tmpa[:, 0:8], bq[:, 2:32:4], bq[:, 3:32:4], ALU.add),
           wait=[(sBqc, 1)], inc=(sBqc, 1))
    ctx.on("vector", lambda e: e.tensor_tensor(bqc[:, 0:8], bqc[:, 0:8], tmpa[:, 0:8], ALU.add),
           wait=[(sBqc, 2)], inc=(sBqc, 1))

    sW = [ctx.dsem("p0_sW0"), ctx.dsem("p0_sW1")]
    sWF = [ctx.sem("p0_sWF0"), ctx.sem("p0_sWF1")]
    sQD = [ctx.dsem("p0_sQD0"), ctx.dsem("p0_sQD1")]
    sRB = [ctx.dsem("p0_sRB0"), ctx.dsem("p0_sRB1")]
    norm_sems = []
    prev_gate = list(s_ready)
    prev_norm = None
    prev_sSum = None

    for hf in range(2):
        ctx.begin("half")
        sXC = emit_cast_xbh(ctx, mk, hf, prev_gate)
        wsrcs = ([("q", h) for h in range(32)] + [("k", h) for h in range(8)]
                 + [("v", h) for h in range(8)])
        for li, (kind, h) in enumerate(wsrcs):
            gi = hf * 48 + li
            par = gi % 2
            dram = {"q": mk.p0_wq, "k": mk.p0_wk, "v": mk.p0_wv}[kind]
            w = [(sWF[par], gi // 2)] if gi >= 2 else []
            if li == 0 and prev_norm is not None:
                w = w + [prev_norm, (prev_sSum, 1)]
            ctx.on("gpsimd",
                   lambda e, dram=dram, h=h, par=par: e.dma_start(
                       out=wbuf[par][:].rearrange("p (a d) -> p a d", a=NP),
                       in_=dram[h].rearrange("(a p) d -> p a d", p=128)),
                   wait=w, inc=(sW[par], 16))

        sAcQ = ctx.sem(f"p0h{hf}_sAcQ")
        sEvQ = ctx.sem(f"p0h{hf}_sEvQ")
        sTrp = ctx.sem(f"p0h{hf}_sTrp")
        sTrE = ctx.sem(f"p0h{hf}_sTrE")
        sSc = ctx.sem(f"p0h{hf}_sSc")
        sU = ctx.sem(f"p0h{hf}_sU")
        sUC = ctx.sem(f"p0h{hf}_sUC")
        sVg = ctx.sem(f"p0h{hf}_sVg")
        sM = ctx.sem(f"p0h{hf}_sM")
        sZi = ctx.sem(f"p0h{hf}_sZi")
        sAsm = ctx.sem(f"p0h{hf}_sAsm")
        sMu = ctx.sem(f"p0h{hf}_sMu")
        sVin = ctx.sem(f"p0h{hf}_sVin")

        proj = ([("qc", hk) for hk in range(8)] + [("kc", hk) for hk in range(8)]
                + [("vc", hk) for hk in range(8)])
        for gidx, (kind, hk) in enumerate(proj):
            bank = gidx % 2
            if kind == "qc":
                lis = [4 * hk + r for r in range(4)]
            elif kind == "kc":
                lis = [32 + hk]
            else:
                lis = [40 + hk]
            mms = [(li, a) for li in lis for a in range(NP)]
            for mi, (li, a) in enumerate(mms):
                gi = hf * 48 + li
                par = gi % 2
                w = []
                if a == 0:
                    w.append((sW[par], 16 * (gi // 2 + 1)))
                if mi == 0:
                    w.append((sXC, NP))
                    if gidx >= 2:
                        w.append((sEvQ, gidx - 1))
                last = (mi == len(mms) - 1)
                if last:
                    inc = (sAcQ, 1)
                elif a == NP - 1:
                    inc = (sWF[par], 1)
                else:
                    inc = None
                ctx.on("tensor",
                       lambda e, a=a, par=par, bank=bank, mi=mi, last=last:
                           e.matmul(bankf(mk, bank), wbuf[par][:, a * 128:(a + 1) * 128],
                                    mk.xbh[:, a * TH:(a + 1) * TH],
                                    start=(mi == 0), stop=last),
                       wait=w, inc=inc)
            par_last = (hf * 48 + lis[-1]) % 2
            ctx.on("vector", lambda e: e.engine_nop(),
                   wait=[(sAcQ, gidx + 1)], inc=(sWF[par_last], 1))
            if kind == "vc":
                ctx.on("scalar",
                       lambda e, hk=hk, bank=bank: e.activation(
                           out=vc[:, hk * TH:(hk + 1) * TH], in_=bankf(mk, bank),
                           func=AF.Identity, bias=bv[:, hk:hk + 1]),
                       wait=[(sAcQ, gidx + 1), (sBqc, 3)], inc=(sEvQ, 1))
            else:
                bias = bqc[:, hk:hk + 1] if kind == "qc" else bk[:, hk:hk + 1]
                dstt = qc[gidx % 2]
                ctx.on("scalar",
                       lambda e, dstt=dstt, bank=bank, bias=bias: e.activation(
                           out=dstt[:], in_=bankf(mk, bank),
                           func=AF.Identity, bias=bias),
                       wait=[(sAcQ, gidx + 1), (sBqc, 3)] +
                            ([(sTrp, 4 * (gidx - 1))] if gidx >= 2 else []),
                       inc=(sEvQ, 1))
                dram = mk.p0_qd if kind == "qc" else mk.p0_kd
                for tc in range(4):
                    kk = 4 * gidx + tc
                    pslice = mk.psumb[:, (kk % 2) * 1024:(kk % 2) * 1024 + 128]
                    w = [(sEvQ, gidx + 1)] if tc == 0 else []
                    if kk >= 2:
                        w.append((sTrE, kk - 1))
                    ctx.on("tensor",
                           lambda e, dstt=dstt, tc=tc, pslice=pslice: e.transpose(
                               pslice, dstt[:, tc * 128:(tc + 1) * 128], mk.eyeb[:]),
                           wait=w, inc=(sTrp, 1))
                    w = [(sTrp, kk + 1)]
                    if kk >= 2:
                        w.append((sQD[kk % 2], 16 * (hf * 32 + kk // 2)))
                    ctx.on("vector",
                           lambda e, kk=kk, pslice=pslice: e.tensor_copy(
                               tev[kk % 2][:], pslice),
                           wait=w, inc=(sTrE, 1))
                    ctx.on("scalar",
                           lambda e, dram=dram, hk=hk, tc=tc, kk=kk: e.dma_start(
                               out=dram[hk, tc * 128:(tc + 1) * 128, :], in_=tev[kk % 2][:]),
                           wait=[(sTrE, kk + 1)], inc=(sQD[kk % 2], 16))

        for b in range(2):
            ctx.on("vector", lambda e, b=b: e.memset(vcat[b][:, 8:504:9], 1.0),
                   inc=(sVin, 1))

        for c in range(NCH):
            w = [(sQD[0], 16 * 32 * (hf + 1)), (sQD[1], 16 * 32 * (hf + 1))]
            if c == 0:
                w.append((sAcQ, 24))
            if c == 0 and prev_norm is not None:
                w.append(prev_norm)
            if c >= 2:
                w.append((sSc, 4 * (c - 1)))
            ctx.on("sync",
                   lambda e, c=c: e.dma_start(
                       out=qt[c % 2][0:8, 0:2048],
                       in_=mk.p0_qd[:, c * 16:(c + 1) * 16, :].rearrange("h t d -> h (t d)")),
                   wait=w, inc=(sRB[c % 2], 16))
            ctx.on("sync",
                   lambda e, c=c: e.dma_start(
                       out=kt[c % 2][0:8, 0:2048],
                       in_=mk.p0_kd[:, c * 16:(c + 1) * 16, :].rearrange("h t d -> h (t d)")),
                   inc=(sRB[c % 2], 16))

        for sb in range(TH // 4):
            bank = 2 + sb % 2
            mb = (sb * 4) // 56
            t0 = mb * 56
            B = min(56, TH - t0)
            bankM = 4 + mb % 2
            if sb * 4 == t0:
                for hk in range(8):
                    w = [(sVin, 2)]
                    if hk == 0:
                        w.append((sEvQ, 24))
                        w.append((sBqc, 3))
                        if mb >= 2:
                            w.append((sM, mb - 1))
                    ctx.on("vector",
                           lambda e, hk=hk, t0=t0, B=B, mb=mb: e.tensor_copy(
                               vcat[mb % 2][:, hk:9 * B:9],
                               vc[:, hk * TH + t0:hk * TH + t0 + B]),
                           wait=w, inc=(sVg, 1))
            for i in range(4):
                t = sb * 4 + i
                c = t // 16; tl = t % 16
                w = []
                if i == 0:
                    w.append((sRB[c % 2], 32 * (hf * NCH // 2 + c // 2 + 1)))
                    if sb >= 2:
                        w.append((sU, sb - 1))
                ctx.on("tensor",
                       lambda e, bank=bank, i=i, c=c, tl=tl: e.matmul(
                           bankf(mk, bank)[:, i * 128:(i + 1) * 128],
                           kt[c % 2][0:8, tl * 128:(tl + 1) * 128],
                           qt[c % 2][0:8, tl * 128:(tl + 1) * 128],
                           start=True, stop=True),
                       wait=w, inc=(sSc, 1) if i == 3 else None)
            ctx.on("scalar",
                   lambda e, sb=sb, bank=bank: e.activation(
                       out=ubuf[sb % 2][:], in_=bankf(mk, bank), func=AF.Exp, scale=SCL),
                   wait=[(sSc, sb + 1)] + ([(sUC, sb - 1)] if sb >= 2 else []),
                   inc=(sU, 1))
            for i in range(4):
                t = sb * 4 + i
                j = t - t0
                w = []
                if i == 0:
                    w.append((sU, sb + 1))
                if j == 0:
                    w.append((sVg, 8 * (mb + 1)))
                    if mb >= 2:
                        w.append((sAsm, 32 * (mb - 1)))
                    if mb == 0 and prev_norm is not None:
                        w.append(prev_norm)
                lastmb = (j == B - 1)
                ctx.on("tensor",
                       lambda e, bankM=bankM, j=j, sb=sb, i=i, mb=mb: e.matmul(
                           bankf(mk, bankM)[:, j * 9:(j + 1) * 9],
                           ubuf[sb % 2][:, i * 128:(i + 1) * 128],
                           vcat[mb % 2][:, j * 9:(j + 1) * 9],
                           start=True, stop=True),
                       wait=w,
                       inc=(sM, 1) if lastmb else ((sUC, 1) if i == 3 else None))
            if (sb * 4 + 4) == t0 + B:
                ctx.on("vector", lambda e, mb=mb: e.engine_nop(),
                       wait=[(sM, mb + 1)], inc=(sUC, 1))
                ctx.on("vector",
                       lambda e, bankM=bankM, B=B: e.reciprocal(
                           zinv[:, 0:B], bankf(mk, bankM)[:, 8:9 * B:9]),
                       wait=[(sM, mb + 1)], inc=(sZi, 1))
                for hk in range(8):
                    w = [(sZi, mb + 1)] if hk == 0 else [(sAsm, 32 * mb + 4 * hk)]
                    if hk == 0 and mb == 0:
                        w.append((sBqc, 3))
                    ctx.on("vector",
                           lambda e, bankM=bankM, hk=hk, B=B: e.tensor_tensor(
                               tmpa[:, 0:B], bankf(mk, bankM)[:, hk:9 * B:9], zinv[:, 0:B],
                               ALU.mult),
                           wait=w, inc=(sMu, 1))
                    nmul = sMu.n
                    for r in range(4):
                        h = 4 * hk + r
                        st = mk.stream[:, h * TC + hf * TH + t0:h * TC + hf * TH + t0 + B]
                        ctx.on("vector",
                               lambda e, st=st, B=B: e.tensor_tensor(
                                   st, st, tmpa[:, 0:B], ALU.add),
                               wait=[(sMu, nmul)], inc=(sAsm, 1))

        ctx.release(ln_base)
        sNorm, sSum = emit_ln_half(ctx, mk, f"p0ln{hf}", hf, sAsm, 32 * NB,
                                   extra_first_wait=[(sAcQ, 24)], flat=True,
                                   alias_off=qt_alias, alias_all=True)
        prev_gate = [(sAcQ, 24), (sAsm, 32 * NB), (sSum, 1)]
        prev_norm = (sNorm, NP)
        prev_sSum = sSum
        norm_sems.append((sNorm, NP))
    ctx.release(mark)
    return norm_sems


def declare_p1(ctx, mk):
    nc = ctx.nc
    mk.p1_baseT = nc.dram_tensor("p1_baseT", [D, D], F32, kind="ExternalInput")
    mk.p1_wrT = nc.dram_tensor("p1_wrT", [D, 8], F32, kind="ExternalInput")
    mk.p1_laA = nc.dram_tensor("p1_laA", [D, 128], F32, kind="ExternalInput")
    mk.p1_laB = nc.dram_tensor("p1_laB", [128, D], F32, kind="ExternalInput")


def emit_p1(ctx, mk, s_ready, trunc=None):
    import os
    ctx.begin("phase")
    """L0_ffn + residual + LN (both halves). s_ready: stream fully valid.
    Returns list of per-half sNorm sems [(sem, NP), ...] (all tiles normalized)."""
    mark = ctx.mark()
    wb = [ctx.sb("p1_wb0", D, BF16), ctx.sb("p1_wb1", D, BF16)]
    laAb = ctx.sb("p1_laAb", NP * 128, BF16)
    laBb = ctx.sb("p1_laBb", D, BF16)
    wrt = ctx.sb("p1_wrt", NP * 8, F32)
    logit = ctx.sb("p1_logit", 4 * 8, F32)
    wds = ctx.sb("p1_wds", 4 * 8, F32)
    ab_alias = (ctx.sbuf_off + 63) & ~63
    wdT = ctx.sb("p1_wdT", TH, F32)
    wfull = ctx.sb("p1_wfull", TH, F32)
    midw = ctx.sb("p1_midw", TH, BF16)
    srow = ctx.sb("p1_srow", 16, F32)
    ln_base = ctx.mark()

    sLdS = ctx.dsem("p1_sLdS")
    sLdG = ctx.dsem("p1_sLdG")
    ctx.on("scalar", lambda e: e.dma_start(
        out=wrt[:].rearrange("p (a e) -> p a e", a=NP),
        in_=mk.p1_wrT.rearrange("(a p) e -> p a e", p=128)),
        wait=list(s_ready), inc=(sLdS, 16))
    ctx.on("gpsimd", lambda e: e.dma_start(
        out=laAb[:].rearrange("p (a r) -> p a r", a=NP),
        in_=mk.p1_laA.rearrange("(a p) r -> p a r", p=128)),
        wait=list(s_ready), inc=(sLdG, 16))
    ctx.on("gpsimd", lambda e: e.dma_start(out=laBb[:], in_=mk.p1_laB[:, :]), inc=(sLdG, 16))

    sW = [ctx.dsem("p1_sW0"), ctx.dsem("p1_sW1")]
    sWF = [ctx.sem("p1_sWF0"), ctx.sem("p1_sWF1")]
    norm_sems = []
    prev_sAcc = None
    prev_sSum = None

    for hf in range(2):
        ctx.begin("half")
        if prev_sAcc is None:
            cast_wait = list(s_ready)
        else:
            cast_wait = [(prev_sAcc, NP), (prev_sSum, 1)]
        sXC = emit_cast_xbh(ctx, mk, hf, cast_wait)
        if trunc == "h1cast" and hf == 1:
            norm_sems.append((sXC, NP))
            break
        if trunc is None or (hf == 0 and trunc != "logits"):
            for o in range(NP):
                par = o % 2
                gi = hf * NP + o
                ctx.on("gpsimd",
                       lambda e, o=o, par=par: e.dma_start(
                           out=wb[par][:].rearrange("p (a o) -> p a o", a=NP),
                           in_=mk.p1_baseT[:, o * 128:(o + 1) * 128].rearrange(
                               "(a p) o -> p a o", p=128)),
                       wait=[(sWF[par], gi // 2)] if gi >= 2 else (),
                       inc=(sW[par], 16))

        sLg = ctx.sem(f"p1h{hf}_sLg")
        sLgE = ctx.sem(f"p1h{hf}_sLgE")
        for tt in range(4):
            bank = 2 + tt % 2
            dst = mk.psumf[0:128, bank * 512:bank * 512 + 8]
            if trunc == "h1mm" and hf == 1 and tt > 0:
                continue
            for a in range(NP):
                w = []
                if tt == 0 and a == 0:
                    w = list(s_ready) + [(sLdS, 16)]
                elif a == 0 and tt >= 2:
                    w = [(sLgE, tt - 1)]
                ctx.on("tensor",
                       lambda e, tt=tt, a=a, dst=dst, hf=hf: e.matmul(
                           dst,
                           mk.stream[:, a * TC + hf * TH + tt * 128:
                                     a * TC + hf * TH + (tt + 1) * 128],
                           wrt[:, a * 8:(a + 1) * 8],
                           start=(a == 0), stop=(a == NP - 1)),
                       wait=w, inc=(sLg, 1) if a == NP - 1 else None)
            if not (trunc == "h1mm" and hf == 1):
                ctx.on("vector",
                       lambda e, tt=tt, dst=dst: e.tensor_copy(logit[:, tt * 8:(tt + 1) * 8], dst),
                       wait=[(sLg, tt + 1)], inc=(sLgE, 1))
        if trunc == "h1mm" and hf == 1:
            norm_sems.append((sLg, 1))
            break
        if trunc == "logits":
            norm_sems.append((sLgE, 4))
            break
        if trunc == "h1logits" and hf == 1:
            norm_sems.append((sLgE, 4))
            break
        if trunc == "h0full" and hf == 0:
            pass

        BIGC = 1.0e30
        sT2 = ctx.sem(f"p1h{hf}_sT2")
        sTop = ctx.sem(f"p1h{hf}_sTop")
        for tt in range(4):
            L = logit[:, tt * 8:(tt + 1) * 8]
            W8 = wds[:, tt * 8:(tt + 1) * 8]
            m1 = srow[:, 0:1]; m2 = srow[:, 1:2]; nm1 = srow[:, 2:3]
            e2 = srow[:, 3:4]; dinv = srow[:, 4:5]; t8 = srow[:, 8:16]
            k = sT2.n
            ctx.on("vector", lambda e, L=L, m1=m1: e.reduce_max(m1, L, axis=mybir.AxisListType.X),
                   wait=[(sLgE, tt + 1)] + ([(sT2, k)] if k else []), inc=(sT2, 1))
            ctx.on("vector", lambda e, L=L, m1=m1, t8=t8: e.tensor_scalar(
                t8, L, m1[:, 0:1], BIGC, ALU.is_equal, ALU.mult),
                   wait=[(sT2, k + 1)], inc=(sT2, 1))
            ctx.on("vector", lambda e, L=L, t8=t8: e.tensor_tensor(t8, L, t8, ALU.subtract),
                   wait=[(sT2, k + 2)], inc=(sT2, 1))
            ctx.on("vector", lambda e, m2=m2, t8=t8: e.reduce_max(m2, t8, axis=mybir.AxisListType.X),
                   wait=[(sT2, k + 3)], inc=(sT2, 1))
            ctx.on("vector", lambda e, m1=m1, nm1=nm1: e.tensor_scalar_mul(nm1, m1, -1.0),
                   wait=[(sT2, k + 4)], inc=(sT2, 1))
            ctx.on("scalar", lambda e, m2=m2, e2=e2, nm1=nm1: e.activation(
                out=e2, in_=m2, func=AF.Exp, bias=nm1[:, 0:1]),
                   wait=[(sT2, k + 5)], inc=(sT2, 1))
            ctx.on("vector", lambda e, e2=e2: e.tensor_scalar_add(e2, e2, 1.0),
                   wait=[(sT2, k + 6)], inc=(sT2, 1))
            ctx.on("vector", lambda e, e2=e2, dinv=dinv: e.reciprocal(dinv, e2),
                   wait=[(sT2, k + 7)], inc=(sT2, 1))
            ctx.on("scalar", lambda e, L=L, W8=W8, nm1=nm1: e.activation(
                out=W8, in_=L, func=AF.Exp, bias=nm1[:, 0:1]),
                   wait=[(sT2, k + 8)], inc=(sT2, 1))
            ctx.on("vector", lambda e, L=L, t8=t8, m2=m2: e.tensor_scalar(
                t8, L, m2[:, 0:1], None, ALU.is_ge),
                   wait=[(sT2, k + 9)], inc=(sT2, 1))
            ctx.on("vector", lambda e, W8=W8, t8=t8: e.tensor_tensor(W8, W8, t8, ALU.mult),
                   wait=[(sT2, k + 10)], inc=(sT2, 1))
            ctx.on("vector", lambda e, W8=W8, dinv=dinv: e.tensor_scalar_mul(W8, W8, dinv[:, 0:1]),
                   wait=[(sT2, k + 11)], inc=(sTop, 1))

        sTrp = ctx.sem(f"p1h{hf}_sTrp")
        sTrpE = ctx.sem(f"p1h{hf}_sTrpE")
        tp = mk.psumf[0:8, 2 * 512:2 * 512 + 128]
        for tt in range(4):
            w = [(sTop, tt + 1)]
            if tt == 0:
                w.append((sLgE, 4))
            else:
                w.append((sTrpE, tt))
            ctx.on("tensor",
                   lambda e, tt=tt: e.transpose(tp, wds[:, tt * 8:(tt + 1) * 8], mk.eyef[:]),
                   wait=w, inc=(sTrp, 1))
            ctx.on("vector",
                   lambda e, tt=tt: e.tensor_copy(wdT[0:8, tt * 128:(tt + 1) * 128], tp),
                   wait=[(sTrp, tt + 1)], inc=(sTrpE, 1))
        sWfl = ctx.sem(f"p1h{hf}_sWfl")
        sWflE = ctx.sem(f"p1h{hf}_sWflE")
        ctx.on("tensor",
               lambda e: e.matmul(bankf(mk, 3), mk.rbc[0:8, :], wdT[0:8, 0:TH],
                                  start=True, stop=True),
               wait=[(sTrpE, 4)], inc=(sWfl, 1))
        ctx.on("vector",
               lambda e: e.tensor_copy(wfull[:], bankf(mk, 3)),
               wait=[(sWfl, 1)], inc=(sWflE, 1))

        sMid = ctx.sem(f"p1h{hf}_sMid")
        sMidw = ctx.sem(f"p1h{hf}_sMidw")
        for a in range(NP):
            ctx.on("tensor",
                   lambda e, a=a: e.matmul(
                       bankf(mk, 4), laAb[:, a * 128:(a + 1) * 128],
                       mk.xbh[:, a * TH:(a + 1) * TH],
                       start=(a == 0), stop=(a == NP - 1)),
                   wait=[(sLdG, 32), (sXC, NP)] if a == 0 else (),
                   inc=(sMid, 1) if a == NP - 1 else None)
        ctx.on("vector",
               lambda e: e.tensor_tensor(midw[:], bankf(mk, 4), wfull[:], ALU.mult),
               wait=[(sMid, 1), (sWflE, 1)], inc=(sMidw, 1))

        sAcc = ctx.sem(f"p1h{hf}_sAcc")
        sAdd = ctx.sem(f"p1h{hf}_sAdd")
        for o in range(NP):
            par = o % 2
            bank = o % 2
            gi = hf * NP + o
            w = [(sW[par], 16 * (gi // 2 + 1))]
            if o == 0:
                w.append((sMidw, 1))
            if o >= 2:
                w.append((sAdd, o - 1))
            for a in range(NP):
                ctx.on("tensor",
                       lambda e, a=a, par=par, bank=bank: e.matmul(
                           bankf(mk, bank), wb[par][:, a * 128:(a + 1) * 128],
                           mk.xbh[:, a * TH:(a + 1) * TH],
                           start=(a == 0), stop=False),
                       wait=w if a == 0 else ())
            ctx.on("tensor",
                   lambda e, o=o, bank=bank: e.matmul(
                       bankf(mk, bank), laBb[:, o * 128:(o + 1) * 128], midw[:],
                       start=False, stop=True),
                   inc=(sAcc, 1))
            ctx.on("vector",
                   lambda e, o=o, bank=bank, hf=hf: e.tensor_tensor(
                       mk.stream[:, o * TC + hf * TH:o * TC + hf * TH + TH],
                       mk.stream[:, o * TC + hf * TH:o * TC + hf * TH + TH],
                       bankf(mk, bank), ALU.add),
                   wait=[(sAcc, o + 1)], inc=(sAdd, 1))
            ctx.on("vector", lambda e, o=o: e.engine_nop(), inc=(sWF[par], 1))

        prev_sAcc = sAcc
        ctx.release(ln_base)
        sNorm, prev_sSum = emit_ln_half(ctx, mk, f"p1ln{hf}", hf, sAdd, 1,
                                        extra_first_wait=[(sAcc, NP)],
                                        alias_off=ab_alias)
        norm_sems.append((sNorm, NP))
        if trunc == "h0full":
            break
    ctx.release(mark)
    return norm_sems


def host_inputs_p2(inp):
    def percol(a):
        return np.ascontiguousarray(np.asarray(a).reshape(4, 4, 128).transpose(2, 0, 1)
                                    .reshape(128, 16))
    winT = np.ascontiguousarray(inp["L1_attn_Win"].T[PIDX])
    w1T = np.ascontiguousarray(inp["L1_attn_W1"].transpose(0, 2, 1))
    w2T = np.ascontiguousarray(inp["L1_attn_W2"].transpose(0, 2, 1))
    woutT = np.ascontiguousarray(inp["L1_attn_Wout"][PIDX].T)
    return {
        "p2_winT": winT, "p2_w1T": w1T, "p2_w2T": w2T, "p2_woutT": woutT,
        "p2_bin": np.ascontiguousarray(inp["L1_attn_bin"].reshape(4, 128).T),
        "p2_b1": np.ascontiguousarray(inp["L1_attn_b1"].reshape(4, 16, 128)
                                      .transpose(2, 0, 1).reshape(128, 64)),
        "p2_b2": percol(inp["L1_attn_b2"]),
        "p2_ln1w": percol(inp["L1_attn_ln1_w"]), "p2_ln1b": percol(inp["L1_attn_ln1_b"]),
        "p2_ln2w": percol(inp["L1_attn_ln2_w"]), "p2_ln2b": percol(inp["L1_attn_ln2_b"]),
        "p2_phzre": percol(inp["L1_attn_phz_re"]), "p2_phzim": percol(inp["L1_attn_phz_im"]),
        "p2_lcre": percol(inp["L1_attn_lc_re"]), "p2_lcim": percol(inp["L1_attn_lc_im"]),
        "p2_scl": percol(inp["L1_attn_sc_lin"]),
        "p2_lnlw": np.ascontiguousarray(inp["L1_attn_lnl_w"].reshape(4, 128).T),
        "p2_lnlb": np.ascontiguousarray(inp["L1_attn_lnl_b"].reshape(4, 128).T),
        "p2_bout": np.ascontiguousarray(inp["L1_attn_bout"][PIDX].reshape(32, 128).T),
    }


def declare_p2(ctx, mk):
    nc = ctx.nc
    mk.p2_winT = nc.dram_tensor("p2_winT", [D, 512], F32, kind="ExternalInput")
    mk.p2_w1T = nc.dram_tensor("p2_w1T", [4, 512, 2048], F32, kind="ExternalInput")
    mk.p2_w2T = nc.dram_tensor("p2_w2T", [4, 2048, 512], F32, kind="ExternalInput")
    mk.p2_woutT = nc.dram_tensor("p2_woutT", [512, D], F32, kind="ExternalInput")
    for nm, sh in (("bin", [128, 4]), ("b1", [128, 64]), ("b2", [128, 16]),
                   ("ln1w", [128, 16]), ("ln1b", [128, 16]), ("ln2w", [128, 16]),
                   ("ln2b", [128, 16]), ("phzre", [128, 16]), ("phzim", [128, 16]),
                   ("lcre", [128, 16]), ("lcim", [128, 16]), ("scl", [128, 16]),
                   ("lnlw", [128, 4]), ("lnlb", [128, 4]), ("bout", [128, 32])):
        setattr(mk, "p2_" + nm, nc.dram_tensor("p2_" + nm, sh, F32, kind="ExternalInput"))
    mk.p2_mask = nc.dram_tensor("p2_mask", [1, 8], F32, kind="ExternalInput")
    mk.p2_xsp = nc.dram_tensor("p2_xsp", [128, NP * TC], F32)
    mk.p2_cin = nc.dram_tensor("p2_cin", [128, 8], F32)
    mk.p2_cout = nc.dram_tensor("p2_cout", [1024, 8], F32)


def emit_ln_inner(ctx, mk, S, h, hbf, out, rows, absb, wcol, bcol, sq,
                  silu=False, first_wait=()):
    sC, sMM, sEv, sSt, sBc, sN = (S["c"], S["mm"], S["ev"], S["st"], S["bc"], S["n"])
    kc, kmm, kev, kst, kbc = sC.n, sMM.n, sEv.n, sSt.n, sBc.n
    sum5 = mk.psumf[0:1, 5 * 512:5 * 512 + TH]
    ssq4 = mk.psumf[0:1, 4 * 512:4 * 512 + TH]
    row_a = rows[0:1, 0:TC]
    row_b = rows[0:1, TC:2 * TC]
    ctx.on("vector", lambda e: e.tensor_copy(hbf[:], h[:]),
           wait=list(first_wait) + [(sBc, kbc)], inc=(sC, 1))
    ctx.on("scalar", lambda e: e.activation(out=sq[:], in_=h[:], func=AF.Square),
           wait=list(first_wait), inc=(sC, 1))
    for g, (srcb, bank) in enumerate(((hbf, sum5), (sq, ssq4))):
        for hf in range(2):
            for dt in range(4):
                w = []
                if dt == 0:
                    if g == 0 and hf == 0:
                        w = [(sC, kc + 2), (sBc, kbc)]
                    else:
                        w = [(sEv, sEv.n)]
                ctx.on("tensor",
                       lambda e, srcb=srcb, bank=bank, dt=dt, hf=hf: e.matmul(
                           bank, mk.onesb[:, 0:1],
                           srcb[:, dt * TC + hf * TH:dt * TC + hf * TH + TH],
                           start=(dt == 0), stop=(dt == 3)),
                       wait=w, inc=(sMM, 1) if dt == 3 else None)
            dstrow = row_a if g == 0 else row_b
            ctx.on("vector",
                   lambda e, dstrow=dstrow, hf=hf, bank=bank: e.tensor_copy(
                       dstrow[:, hf * TH:hf * TH + TH], bank),
                   wait=[(sMM, sMM.n)], inc=(sEv, 1))
    c = 1.0 / 512.0
    ctx.on("vector", lambda e: e.tensor_scalar_mul(row_a, row_a, c),
           wait=[(sEv, kev + 4)], inc=(sSt, 1))
    ctx.on("vector", lambda e: e.tensor_scalar_mul(row_b, row_b, c),
           wait=[(sSt, sSt.n)], inc=(sSt, 1))
    ctx.on("vector", lambda e: e.tensor_tensor(absb[0:1, 0:TC],
                                               row_a, row_a, ALU.mult),
           wait=[(sSt, sSt.n)], inc=(sSt, 1))
    ctx.on("vector", lambda e: e.tensor_tensor(row_b, row_b, absb[0:1, 0:TC],
                                               ALU.subtract),
           wait=[(sSt, sSt.n)], inc=(sSt, 1))
    ctx.on("scalar", lambda e: e.activation(out=row_b, in_=row_b, func=AF.Ln,
                                            bias=mk.epsc[0:1, 0:1]),
           wait=[(sSt, sSt.n), (mk.sConst, 1)], inc=(sSt, 1))
    ctx.on("scalar", lambda e: e.activation(out=row_b, in_=row_b, func=AF.Exp, scale=-0.5),
           wait=[(sSt, sSt.n)], inc=(sSt, 1))
    ctx.on("vector", lambda e: e.tensor_tensor(row_a, row_a, row_b, ALU.mult),
           wait=[(sSt, sSt.n)], inc=(sSt, 1))
    ctx.on("vector", lambda e: e.tensor_scalar_mul(row_a, row_a, -1.0),
           wait=[(sSt, sSt.n)], inc=(sSt, 1))
    for g, (srcr, bank) in enumerate(((row_b, 5), (row_a, 4))):
        for hf in range(2):
            w = [(sSt, kst + 8)] if (g == 0 and hf == 0) else []
            if hf == 1:
                w.append((sEv, sEv.n))
            ctx.on("tensor",
                   lambda e, srcr=srcr, bank=bank, hf=hf: e.matmul(
                       mk.psumf[:, bank * 512:bank * 512 + TH], mk.ones1f[0:1, :],
                       srcr[:, hf * TH:hf * TH + TH], start=True, stop=True),
                   wait=w, inc=(sBc, 1))
            off = (0 if g == 0 else TC) + hf * TH
            ctx.on("vector",
                   lambda e, off=off, bank=bank: e.tensor_copy(
                       absb[:, off:off + TH],
                       mk.psumf[:, bank * 512:bank * 512 + TH]),
                   wait=[(sBc, sBc.n)], inc=(sEv, 1))
    for dt in range(4):
        ctx.on("vector",
               lambda e, dt=dt: e.tensor_tensor(
                   out[:, dt * TC:(dt + 1) * TC], h[:, dt * TC:(dt + 1) * TC],
                   absb[:, 0:TC], ALU.mult),
               wait=[(sEv, kev + 8)] if dt == 0 else [(sN, sN.n)], inc=(sN, 1))
        ctx.on("vector",
               lambda e, dt=dt: e.tensor_tensor(
                   out[:, dt * TC:(dt + 1) * TC], out[:, dt * TC:(dt + 1) * TC],
                   absb[:, TC:2 * TC], ALU.add),
               wait=[(sN, sN.n)], inc=(sN, 1))
        ctx.on("vector",
               lambda e, dt=dt: e.tensor_scalar(
                   out[:, dt * TC:(dt + 1) * TC], out[:, dt * TC:(dt + 1) * TC],
                   wcol[:, dt:dt + 1], bcol[:, dt:dt + 1], ALU.mult, ALU.add),
               wait=[(sN, sN.n)], inc=(sN, 1))
        if silu:
            ctx.on("scalar",
                   lambda e, dt=dt: e.activation(
                       out=out[:, dt * TC:(dt + 1) * TC],
                       in_=out[:, dt * TC:(dt + 1) * TC], func=AF.Silu),
                   wait=[(sN, sN.n)], inc=(sN, 1))
    return


def emit_p2(ctx, mk, s_ready):
    ctx.begin("phase")
    ctx.begin("half")
    RB = 16512
    Ec = ctx.sb_at("p2_Ec", 4 * TC, F32, RB + 0)
    Es = ctx.sb_at("p2_Es", 4 * TC, F32, RB + 16 * 1024)
    br = ctx.sb_at("p2_br", 4 * TC, F32, RB + 32 * 1024)
    bi = ctx.sb_at("p2_bi", 4 * TC, F32, RB + 48 * 1024)
    hso = ctx.sb_at("p2_h", 4 * TC, F32, RB + 64 * 1024)
    ybuf = ctx.sb_at("p2_y", 4 * TC, F32, RB + 80 * 1024)
    scr = ctx.sb_at("p2_scr", TC, F32, RB + 96 * 1024)
    Ein = ctx.sb_at("p2_Ein", 2 * 512, F32, RB + 100 * 1024)
    cols = ctx.sb_at("p2_cols", 256, F32, RB + 104 * 1024)
    rows = ctx.sb_at("p2_rows", 2 * TC, F32, RB + 106 * 1024)
    absb = ctx.sb_at("p2_absb", 2 * TC, F32, RB + 114 * 1024)
    gath = ctx.sb_at("p2_gath", 64, F32, RB + 122 * 1024)
    cinb = ctx.sb_at("p2_cinb", 8, F32, RB + 122 * 1024 + 512)
    ybf = ctx.sb_at("p2_ybf", 4 * TC, BF16, RB + 128 * 1024)
    h1b = ctx.sb_at("p2_h1b", 16 * TH, BF16, RB + 136 * 1024)
    sqb = ctx.sb_at("p2_sqb", 4 * TC, BF16, RB + 152 * 1024)
    W1REG = RB + 162 * 1024
    w1b = ctx.sb_at("p2_w1b", 4 * 2048, BF16, W1REG)
    w2b = ctx.sb_at("p2_w2b", 16 * 512, BF16, W1REG + 16 * 1024)
    woutb = ctx.sb_at("p2_woutb", 4 * D, BF16, W1REG)
    winb = ctx.sb_at("p2_winb", 32 * 512, BF16, W1REG)
    prm = ctx.sb_at("p2_prm", 288, F32, RB + 204 * 1024)
    maskb = ctx.sb_at("p2_maskb", 8, F32, RB + 206 * 1024)
    wsel = ctx.sb_at("p2_wsel", 8, F32, RB + 206 * 1024 + 64)
    wbuf8 = ctx.sb_at("p2_wbuf8", 64, F32, RB + 206 * 1024 + 128)
    ln_off = RB + 194 * 1024

    PRM = {}
    off = 0
    for nm, wdt in (("bin", 4), ("b1", 64), ("b2", 16), ("ln1w", 16), ("ln1b", 16),
                    ("ln2w", 16), ("ln2b", 16), ("phzre", 16), ("phzim", 16),
                    ("lcre", 16), ("lcim", 16), ("scl", 16), ("lnlw", 4), ("lnlb", 4),
                    ("bout", 32)):
        PRM[nm] = (off, wdt)
        off += wdt
    assert off <= 288

    sLdP = ctx.dsem("p2_sLdP")
    sWinL = ctx.dsem("p2_sWinL")
    sSp = ctx.dsem("p2_sSp")
    sWL = [ctx.dsem("p2_sWL0"), ctx.dsem("p2_sWL1")]
    sWoL = ctx.dsem("p2_sWoL")
    sCC = ctx.dsem("p2_sCC")
    sXC = ctx.sem("p2_sXC")
    sWinA = ctx.sem("p2_sWinA")
    sWinE = ctx.sem("p2_sWinE")
    sCol = ctx.sem("p2_sCol")
    sTab = ctx.sem("p2_sTab")
    sFFA = ctx.sem("p2_sFFA")
    sFFE = ctx.sem("p2_sFFE")
    sFFE2 = ctx.sem("p2_sFFE2")
    sWoE = ctx.sem("p2_sWoE")
    sFFH = ctx.sem("p2_sFFH")
    sWoA = ctx.sem("p2_sWoA")
    sHFB = ctx.sem("p2_sHFB")
    sRel = [ctx.dsem("p2_sRel0"), ctx.dsem("p2_sRel1")]
    sFin = ctx.sem("p2_sFin")
    SLN = {"c": ctx.sem("p2_lnc"), "mm": ctx.sem("p2_lnmm"), "ev": ctx.sem("p2_lnev"),
           "st": ctx.sem("p2_lnst"), "bc": ctx.sem("p2_lnbc"), "n": ctx.sem("p2_lnn")}

    for nm in PRM:
        o, wdt = PRM[nm]
        dram = getattr(mk, "p2_" + nm)
        ctx.on("scalar", lambda e, dram=dram, o=o, wdt=wdt: e.dma_start(
            out=prm[:, o:o + wdt], in_=dram[:, :]),
            wait=list(s_ready) if o == 0 else [], inc=(sLdP, 16))
    ctx.on("scalar", lambda e: e.dma_start(out=maskb[0:1, 0:8], in_=mk.p2_mask[:, :]),
           inc=(sLdP, 16))
    ctx.on("gpsimd", lambda e: e.dma_start(
        out=winb[:].rearrange("p (a d) -> p a d", a=NP),
        in_=mk.p2_winT.rearrange("(a p) d -> p a d", p=128)),
        wait=list(s_ready), inc=(sWinL, 16))

    prev = None
    cast_sems = []
    for hf in range(2):
        cast_wait = list(s_ready) if hf == 0 else [(prev, 4)]
        sXCh = emit_cast_xbh(ctx, mk, hf, cast_wait)
        cast_sems.append(sXCh)
        for dt in range(4):
            idx = hf * 4 + dt
            bank = idx % 4
            for a in range(NP):
                w = []
                if a == 0:
                    w = [(sXCh, NP), (sWinL, 16)]
                    if idx >= 4:
                        w.append((sWinE, idx - 3))
                ctx.on("tensor",
                       lambda e, a=a, dt=dt, hf=hf, bank=bank: e.matmul(
                           bankf(mk, bank), winb[:, a * 512 + dt * 128:a * 512 + (dt + 1) * 128],
                           mk.xbh[:, a * TH:(a + 1) * TH],
                           start=(a == 0), stop=(a == NP - 1)),
                       wait=w, inc=(sWinA, 1) if a == NP - 1 else None)
            ctx.on("scalar",
                   lambda e, dt=dt, hf=hf, bank=bank: e.activation(
                       out=hso[:, dt * TC + hf * TH:dt * TC + hf * TH + TH],
                       in_=bankf(mk, bank), func=AF.Identity,
                       bias=prm[:, PRM["bin"][0] + dt:PRM["bin"][0] + dt + 1]),
                   wait=[(sWinA, idx + 1), (sLdP, 16 * 16), (sSp, 16)],
                   inc=(sWinE, 1))
        prev = sWinA
    ctx.on("sync", lambda e: e.dma_start(out=mk.p2_xsp[:, :], in_=mk.stream[:]),
           wait=[(cast_sems[1], NP)], inc=(sSp, 16))
    ctx.on("tensor", lambda e: e.matmul(mk.psumf[:, 3 * 512:3 * 512 + 8],
                                        mk.ones1f[0:1, :], maskb[0:1, 0:8],
                                        start=True, stop=True),
           wait=[(sLdP, 16 * 16), (sWinE, 8)], inc=(sWinA, 1))
    ctx.on("vector", lambda e: e.tensor_copy(maskb[:], mk.psumf[:, 3 * 512:3 * 512 + 8]),
           wait=[(sWinA, sWinA.n)], inc=(sTab, 1))

    def emit_wload(l):
        par = 0
        w = [(sWinA, 8)] if l == 0 else [(sFFH, 8 * l)]
        ctx.on("gpsimd",
               lambda e, l=l: e.dma_start(
                   out=w1b[:].rearrange("p (a f) -> p a f", a=4),
                   in_=mk.p2_w1T[l].rearrange("(a p) f -> p a f", p=128)),
               wait=w, inc=(sWL[0], 16))
        ctx.on("gpsimd",
               lambda e, l=l: e.dma_start(
                   out=w2b[:].rearrange("p (a d) -> p a d", a=16),
                   in_=mk.p2_w2T[l].rearrange("(a p) d -> p a d", p=128)),
               inc=(sWL[1], 16))
    emit_wload(0)

    import os as _os
    NL = int(_os.environ.get("P2_NLAYERS", "4"))
    for l in range(NL):
        P = lambda nm, i: prm[:, PRM[nm][0] + l * 4 + i:PRM[nm][0] + l * 4 + i + 1]
        fw = [(sWinE, 8), (sSp, 16)] if l == 0 else [(sFFH, 8 * l)]
        emit_ln_inner(ctx, mk, SLN, hso, ybf, ybuf, rows, absb,
                      prm[:, PRM["ln1w"][0] + l * 4:PRM["ln1w"][0] + (l + 1) * 4],
                      prm[:, PRM["ln1b"][0] + l * 4:PRM["ln1b"][0] + (l + 1) * 4],
                      sqb, silu=True, first_wait=fw + [(sLdP, 16 * 16)])

        kt = sTab.n
        c_er = cols[:, 0:4]; c_ei = cols[:, 4:8]; c_rho = cols[:, 8:12]
        c_r2 = cols[:, 12:16]; c_ri = cols[:, 16:20]
        c_Ec = cols[:, 20:24]; c_Es = cols[:, 24:28]
        c_R = cols[:, 28:32]
        c_t1 = cols[:, 32:36]; c_t2 = cols[:, 36:40]
        c_wr = cols[:, 40:44]; c_wi = cols[:, 44:48]
        c_nEs = cols[:, 48:52]
        c_Qc = cols[:, 64:96]; c_Qs = cols[:, 96:128]
        ctx.on("vector", lambda e, l=l: e.tensor_tensor(c_r2,
                                                        prm[:, PRM["phzre"][0] + l * 4:PRM["phzre"][0] + l * 4 + 4],
                                                        prm[:, PRM["phzre"][0] + l * 4:PRM["phzre"][0] + l * 4 + 4], ALU.mult),
               wait=[(sTab, kt), (sLdP, 16 * 16)] + ([(sSp, 16)] if l == 0 else []),
               inc=(sTab, 1))
        ctx.on("vector", lambda e, l=l: e.tensor_tensor(c_t1, prm[:, PRM["phzim"][0] + l * 4:PRM["phzim"][0] + l * 4 + 4],
                                                        prm[:, PRM["phzim"][0] + l * 4:PRM["phzim"][0] + l * 4 + 4], ALU.mult),
               wait=[(sTab, kt + 1)], inc=(sTab, 1))
        ctx.on("vector", lambda e: e.tensor_tensor(c_r2, c_r2, c_t1, ALU.add),
               wait=[(sTab, kt + 2)], inc=(sTab, 1))
        ctx.on("scalar", lambda e: e.activation(out=c_ri, in_=c_r2, func=AF.Ln),
               wait=[(sTab, kt + 3)], inc=(sTab, 1))
        ctx.on("scalar", lambda e: e.activation(out=c_ri, in_=c_ri, func=AF.Exp, scale=-0.5),
               wait=[(sTab, kt + 4)], inc=(sTab, 1))
        ctx.on("vector", lambda e, l=l: e.tensor_tensor(c_er, prm[:, PRM["phzre"][0] + l * 4:PRM["phzre"][0] + l * 4 + 4], c_ri, ALU.mult),
               wait=[(sTab, kt + 5)], inc=(sTab, 1))
        ctx.on("vector", lambda e, l=l: e.tensor_tensor(c_ei, prm[:, PRM["phzim"][0] + l * 4:PRM["phzim"][0] + l * 4 + 4], c_ri, ALU.mult),
               wait=[(sTab, kt + 6)], inc=(sTab, 1))
        ctx.on("vector", lambda e: e.tensor_tensor(c_t1, c_r2, c_ri, ALU.mult),
               wait=[(sTab, kt + 7)], inc=(sTab, 1))
        ctx.on("scalar", lambda e: e.activation(out=c_rho, in_=c_t1, func=AF.Tanh),
               wait=[(sTab, kt + 8)], inc=(sTab, 1))
        Ci = Ein[:, 0:512]; Si = Ein[:, 512:1024]
        ctx.on("vector", lambda e: e.memset(Ci[:, 0:512:128], 1.0),
               wait=[(sTab, kt + 9)], inc=(sTab, 1))
        ctx.on("vector", lambda e: e.memset(Si[:, 0:512:128], 0.0),
               wait=[(sTab, sTab.n)], inc=(sTab, 1))
        ctx.on("vector", lambda e: e.tensor_scalar_mul(c_nEs, c_ei, -1.0),
               wait=[(sTab, sTab.n)], inc=(sTab, 1))
        for step_i, m in enumerate([1, 2, 4, 8, 16, 32, 64]):
            for dt in range(4):
                base = dt * 128
                pass
            break
        em_r = cols[:, 52:56]; em_i = cols[:, 56:60]; tmpc = cols[:, 60:64]
        ctx.on("vector", lambda e: e.tensor_copy(em_r, c_er), wait=[(sTab, sTab.n)], inc=(sTab, 1))
        ctx.on("vector", lambda e: e.tensor_copy(em_i, c_nEs), wait=[(sTab, sTab.n)], inc=(sTab, 1))
        for m in [1, 2, 4, 8, 16, 32, 64]:
            for dt in range(4):
                b0 = dt * 128
                n = m
                ctx.on("vector", lambda e, b0=b0, m=m, n=n, dt=dt: e.tensor_scalar_mul(
                    scr[:, 0:n], Ci[:, b0:b0 + n], em_r[:, dt:dt + 1]),
                    wait=[(sTab, sTab.n)], inc=(sTab, 1))
                ctx.on("vector", lambda e, b0=b0, m=m, n=n, dt=dt: e.scalar_tensor_tensor(
                    Ci[:, b0 + m:b0 + m + n], Si[:, b0:b0 + n], em_i[:, dt:dt + 1],
                    scr[:, 0:n], ALU.mult, ALU.subtract),
                    wait=[(sTab, sTab.n)], inc=(sTab, 1))
                ctx.on("vector", lambda e, b0=b0, m=m, n=n: e.tensor_scalar_mul(
                    Ci[:, b0 + m:b0 + m + n], Ci[:, b0 + m:b0 + m + n], -1.0),
                    wait=[(sTab, sTab.n)], inc=(sTab, 1))
                ctx.on("vector", lambda e, b0=b0, m=m, n=n, dt=dt: e.tensor_scalar_mul(
                    scr[:, 0:n], Ci[:, b0:b0 + n], em_i[:, dt:dt + 1]),
                    wait=[(sTab, sTab.n)], inc=(sTab, 1))
                ctx.on("vector", lambda e, b0=b0, m=m, n=n, dt=dt: e.scalar_tensor_tensor(
                    Si[:, b0 + m:b0 + m + n], Si[:, b0:b0 + n], em_r[:, dt:dt + 1],
                    scr[:, 0:n], ALU.mult, ALU.add),
                    wait=[(sTab, sTab.n)], inc=(sTab, 1))
            if m < 64:
                ctx.on("vector", lambda e: e.tensor_tensor(tmpc, em_r, em_r, ALU.mult),
                       wait=[(sTab, sTab.n)], inc=(sTab, 1))
                ctx.on("vector", lambda e: e.tensor_tensor(c_t2, em_i, em_i, ALU.mult),
                       wait=[(sTab, sTab.n)], inc=(sTab, 1))
                ctx.on("vector", lambda e: e.tensor_tensor(c_t1, em_r, em_i, ALU.mult),
                       wait=[(sTab, sTab.n)], inc=(sTab, 1))
                ctx.on("vector", lambda e: e.tensor_tensor(em_r, tmpc, c_t2, ALU.subtract),
                       wait=[(sTab, sTab.n)], inc=(sTab, 1))
                ctx.on("vector", lambda e: e.tensor_scalar_mul(em_i, c_t1, 2.0),
                       wait=[(sTab, sTab.n)], inc=(sTab, 1))
        ctx.on("vector", lambda e: e.tensor_tensor(tmpc, em_r, em_r, ALU.mult),
               wait=[(sTab, sTab.n)], inc=(sTab, 1))
        ctx.on("vector", lambda e: e.tensor_tensor(c_t2, em_i, em_i, ALU.mult),
               wait=[(sTab, sTab.n)], inc=(sTab, 1))
        ctx.on("vector", lambda e: e.tensor_tensor(c_t1, em_r, em_i, ALU.mult),
               wait=[(sTab, sTab.n)], inc=(sTab, 1))
        ctx.on("vector", lambda e: e.tensor_tensor(em_r, tmpc, c_t2, ALU.subtract),
               wait=[(sTab, sTab.n)], inc=(sTab, 1))
        ctx.on("vector", lambda e: e.tensor_scalar_mul(em_i, c_t1, 2.0),
               wait=[(sTab, sTab.n)], inc=(sTab, 1))
        for dt in range(4):
            ctx.on("vector", lambda e, dt=dt: e.memset(c_Qc[:, dt * 8:dt * 8 + 1], 1.0),
                   wait=[(sTab, sTab.n)], inc=(sTab, 1))
            ctx.on("vector", lambda e, dt=dt: e.memset(c_Qs[:, dt * 8:dt * 8 + 1], 0.0),
                   wait=[(sTab, sTab.n)], inc=(sTab, 1))
            for q in range(7):
                i0 = dt * 8 + q
                ctx.on("vector", lambda e, i0=i0, dt=dt: e.tensor_scalar_mul(
                    c_t1[:, 0:1], c_Qc[:, i0:i0 + 1], em_r[:, dt:dt + 1]),
                    wait=[(sTab, sTab.n)], inc=(sTab, 1))
                ctx.on("vector", lambda e, i0=i0, dt=dt: e.scalar_tensor_tensor(
                    c_Qc[:, i0 + 1:i0 + 2], c_Qs[:, i0:i0 + 1], em_i[:, dt:dt + 1],
                    c_t1[:, 0:1], ALU.mult, ALU.subtract),
                    wait=[(sTab, sTab.n)], inc=(sTab, 1))
                ctx.on("vector", lambda e, i0=i0: e.tensor_scalar_mul(
                    c_Qc[:, i0 + 1:i0 + 2], c_Qc[:, i0 + 1:i0 + 2], -1.0),
                    wait=[(sTab, sTab.n)], inc=(sTab, 1))
                ctx.on("vector", lambda e, i0=i0, dt=dt: e.tensor_scalar_mul(
                    c_t1[:, 0:1], c_Qc[:, i0:i0 + 1], em_i[:, dt:dt + 1]),
                    wait=[(sTab, sTab.n)], inc=(sTab, 1))
                ctx.on("vector", lambda e, i0=i0, dt=dt: e.scalar_tensor_tensor(
                    c_Qs[:, i0 + 1:i0 + 2], c_Qs[:, i0:i0 + 1], em_r[:, dt:dt + 1],
                    c_t1[:, 0:1], ALU.mult, ALU.add),
                    wait=[(sTab, sTab.n)], inc=(sTab, 1))
        for dt in range(4):
            qc = c_Qc[:, dt * 8:dt * 8 + 8].rearrange("p (q o) -> p q o", o=1).broadcast_to([128, 8, 128])
            qs = c_Qs[:, dt * 8:dt * 8 + 8].rearrange("p (q o) -> p q o", o=1).broadcast_to([128, 8, 128])
            ci3 = Ci[:, dt * 128:(dt + 1) * 128].rearrange("p (o s) -> p o s", o=1).broadcast_to([128, 8, 128])
            si3 = Si[:, dt * 128:(dt + 1) * 128].rearrange("p (o s) -> p o s", o=1).broadcast_to([128, 8, 128])
            ecv = Ec[:, dt * TC:(dt + 1) * TC].rearrange("p (q s) -> p q s", q=8)
            esv = Es[:, dt * TC:(dt + 1) * TC].rearrange("p (q s) -> p q s", q=8)
            scv = scr[:].rearrange("p (q s) -> p q s", q=8)
            ctx.on("vector", lambda e, ecv=ecv, qc=qc, ci3=ci3: e.tensor_tensor(ecv, qc, ci3, ALU.mult),
                   wait=[(sTab, sTab.n)], inc=(sTab, 1))
            ctx.on("vector", lambda e, scv=scv, qs=qs, si3=si3: e.tensor_tensor(scv, qs, si3, ALU.mult),
                   wait=[(sTab, sTab.n)], inc=(sTab, 1))
            ctx.on("vector", lambda e, ecv=ecv, scv=scv, dt=dt: e.tensor_tensor(
                Ec[:, dt * TC:(dt + 1) * TC], Ec[:, dt * TC:(dt + 1) * TC], scr[:], ALU.subtract),
                   wait=[(sTab, sTab.n)], inc=(sTab, 1))
            ctx.on("vector", lambda e, esv=esv, qc=qc, si3=si3: e.tensor_tensor(esv, qc, si3, ALU.mult),
                   wait=[(sTab, sTab.n)], inc=(sTab, 1))
            ctx.on("vector", lambda e, scv=scv, qs=qs, ci3=ci3: e.tensor_tensor(scv, qs, ci3, ALU.mult),
                   wait=[(sTab, sTab.n)], inc=(sTab, 1))
            ctx.on("vector", lambda e, dt=dt: e.tensor_tensor(
                Es[:, dt * TC:(dt + 1) * TC], Es[:, dt * TC:(dt + 1) * TC], scr[:], ALU.add),
                   wait=[(sTab, sTab.n)], inc=(sTab, 1))
        ctx.on("vector", lambda e: e.tensor_copy(c_Ec, c_er), wait=[(sTab, sTab.n)], inc=(sTab, 1))
        ctx.on("vector", lambda e: e.tensor_copy(c_Es, c_ei), wait=[(sTab, sTab.n)], inc=(sTab, 1))
        ctx.on("vector", lambda e: e.tensor_copy(c_R, c_rho), wait=[(sTab, sTab.n)], inc=(sTab, 1))
        for _ in range(10):
            ctx.on("vector", lambda e: e.tensor_tensor(c_t1, c_Ec, c_Ec, ALU.mult),
                   wait=[(sTab, sTab.n)], inc=(sTab, 1))
            ctx.on("vector", lambda e: e.tensor_tensor(c_t2, c_Es, c_Es, ALU.mult),
                   wait=[(sTab, sTab.n)], inc=(sTab, 1))
            ctx.on("vector", lambda e: e.tensor_tensor(tmpc, c_Ec, c_Es, ALU.mult),
                   wait=[(sTab, sTab.n)], inc=(sTab, 1))
            ctx.on("vector", lambda e: e.tensor_tensor(c_Ec, c_t1, c_t2, ALU.subtract),
                   wait=[(sTab, sTab.n)], inc=(sTab, 1))
            ctx.on("vector", lambda e: e.tensor_scalar_mul(c_Es, tmpc, 2.0),
                   wait=[(sTab, sTab.n)], inc=(sTab, 1))
            ctx.on("vector", lambda e: e.tensor_tensor(c_R, c_R, c_R, ALU.mult),
                   wait=[(sTab, sTab.n)], inc=(sTab, 1))
        ctx.on("vector", lambda e: e.tensor_scalar_mul(c_nEs, c_Es, -1.0),
               wait=[(sTab, sTab.n)], inc=(sTab, 1))

        kb = sTab.n
        for dt in range(4):
            ctx.on("vector", lambda e, dt=dt: e.tensor_tensor(
                br[:, dt * TC:(dt + 1) * TC], Ec[:, dt * TC:(dt + 1) * TC],
                ybuf[:, dt * TC:(dt + 1) * TC], ALU.mult),
                   wait=[(sTab, sTab.n), (SLN["n"], SLN["n"].n)], inc=(sTab, 1))
            ctx.on("vector", lambda e, dt=dt: e.tensor_tensor(
                bi[:, dt * TC:(dt + 1) * TC], Es[:, dt * TC:(dt + 1) * TC],
                ybuf[:, dt * TC:(dt + 1) * TC], ALU.mult),
                   wait=[(sTab, sTab.n)], inc=(sTab, 1))
            rhob = c_rho[:, dt:dt + 1].broadcast_to([128, TC])
            ctx.on("vector", lambda e, dt=dt, rhob=rhob: e.tensor_tensor_scan(
                scr[:], rhob, br[:, dt * TC:(dt + 1) * TC], 0.0, ALU.mult, ALU.add),
                   wait=[(sTab, sTab.n)], inc=(sTab, 1))
            ctx.on("vector", lambda e, dt=dt: e.tensor_copy(
                cinb[:, dt * 2:dt * 2 + 1], scr[:, TC - 1:TC]),
                   wait=[(sTab, sTab.n)], inc=(sTab, 1))
            ctx.on("vector", lambda e, dt=dt, rhob=rhob: e.tensor_tensor_scan(
                scr[:], rhob, bi[:, dt * TC:(dt + 1) * TC], 0.0, ALU.mult, ALU.add),
                   wait=[(sTab, sTab.n)], inc=(sTab, 1))
            ctx.on("vector", lambda e, dt=dt: e.tensor_copy(
                cinb[:, dt * 2 + 1:dt * 2 + 2], scr[:, TC - 1:TC]),
                   wait=[(sTab, sTab.n)], inc=(sTab, 1))
        kcc = sCC.n
        ctx.on("gpsimd", lambda e: e.dma_start(out=mk.p2_cin[:, :], in_=cinb[:]),
               wait=[(sTab, sTab.n)], inc=(sCC, 16))
        ctx.on("gpsimd", lambda e: e.collective_compute(
            "AllGather", ALU.bypass, replica_groups=[list(range(8))],
            ins=[mk.p2_cin[:, :]], outs=[mk.p2_cout[:, :]]),
            wait=[(sCC, kcc + 16)], inc=(sCol, 1))
        ctx.on("gpsimd", lambda e: e.dma_start(
            out=gath[:].rearrange("p (r c) -> p r c", r=8),
            in_=mk.p2_cout.rearrange("(r p) c -> p r c", p=128)),
            wait=[(sCol, sCol.n)], inc=(sCC, 16))
        kw = sTab.n
        lcr = prm[:, PRM["lcre"][0] + l * 4:PRM["lcre"][0] + l * 4 + 4]
        lci = prm[:, PRM["lcim"][0] + l * 4:PRM["lcim"][0] + l * 4 + 4]
        ctx.on("vector", lambda e, lcr=lcr: e.tensor_tensor(c_wr, c_er, lcr, ALU.mult),
               wait=[(sTab, sTab.n), (sCC, kcc + 32)], inc=(sTab, 1))
        ctx.on("vector", lambda e, lci=lci: e.scalar_tensor_tensor(
            c_wr, lci, -1.0, c_wr, ALU.mult, ALU.add) if False else e.tensor_tensor(
            c_t1, c_ei, lci, ALU.mult),
               wait=[(sTab, sTab.n)], inc=(sTab, 1))
        ctx.on("vector", lambda e: e.tensor_tensor(c_wr, c_wr, c_t1, ALU.subtract),
               wait=[(sTab, sTab.n)], inc=(sTab, 1))
        ctx.on("vector", lambda e, lci=lci: e.tensor_tensor(c_wi, c_er, lci, ALU.mult),
               wait=[(sTab, sTab.n)], inc=(sTab, 1))
        ctx.on("vector", lambda e, lcr=lcr: e.tensor_tensor(c_t1, c_ei, lcr, ALU.mult),
               wait=[(sTab, sTab.n)], inc=(sTab, 1))
        ctx.on("vector", lambda e: e.tensor_tensor(c_wi, c_wi, c_t1, ALU.add),
               wait=[(sTab, sTab.n)], inc=(sTab, 1))
        ctx.on("vector", lambda e: e.tensor_copy(wbuf8[:, 0:8:2], c_wr),
               wait=[(sTab, sTab.n)], inc=(sTab, 1))
        ctx.on("vector", lambda e: e.tensor_copy(wbuf8[:, 1:8:2], c_wi),
               wait=[(sTab, sTab.n)], inc=(sTab, 1))
        for k in range(1, 8):
            gl = gath[:, (k - 1) * 8:(k - 1) * 8 + 8]
            for dt in range(4):
                ctx.on("vector", lambda e, gl=gl, dt=dt: e.scalar_tensor_tensor(
                    c_t1[:, dt:dt + 1], c_wr[:, dt:dt + 1], c_R[:, dt:dt + 1],
                    gl[:, dt * 2:dt * 2 + 1], ALU.mult, ALU.add),
                       wait=[(sTab, sTab.n)], inc=(sTab, 1))
                ctx.on("vector", lambda e, gl=gl, dt=dt: e.scalar_tensor_tensor(
                    c_t2[:, dt:dt + 1], c_wi[:, dt:dt + 1], c_R[:, dt:dt + 1],
                    gl[:, dt * 2 + 1:dt * 2 + 2], ALU.mult, ALU.add),
                       wait=[(sTab, sTab.n)], inc=(sTab, 1))
            ctx.on("vector", lambda e: e.tensor_tensor(c_wr, c_Ec, c_t1, ALU.mult),
                   wait=[(sTab, sTab.n)], inc=(sTab, 1))
            ctx.on("vector", lambda e: e.tensor_tensor(tmpc, c_Es, c_t2, ALU.mult),
                   wait=[(sTab, sTab.n)], inc=(sTab, 1))
            ctx.on("vector", lambda e: e.tensor_tensor(c_wr, c_wr, tmpc, ALU.subtract),
                   wait=[(sTab, sTab.n)], inc=(sTab, 1))
            ctx.on("vector", lambda e: e.tensor_tensor(c_wi, c_Ec, c_t2, ALU.mult),
                   wait=[(sTab, sTab.n)], inc=(sTab, 1))
            ctx.on("vector", lambda e: e.tensor_tensor(tmpc, c_Es, c_t1, ALU.mult),
                   wait=[(sTab, sTab.n)], inc=(sTab, 1))
            ctx.on("vector", lambda e: e.tensor_tensor(c_wi, c_wi, tmpc, ALU.add),
                   wait=[(sTab, sTab.n)], inc=(sTab, 1))
            ctx.on("vector", lambda e, k=k: e.tensor_copy(wbuf8[:, k * 8:k * 8 + 8:2], c_wr),
                   wait=[(sTab, sTab.n)], inc=(sTab, 1))
            ctx.on("vector", lambda e, k=k: e.tensor_copy(wbuf8[:, k * 8 + 1:k * 8 + 8:2], c_wi),
                   wait=[(sTab, sTab.n)], inc=(sTab, 1))
        ctx.on("vector", lambda e: e.memset(wsel[:], 0.0),
               wait=[(sTab, sTab.n)], inc=(sTab, 1))
        for k in range(8):
            ctx.on("vector", lambda e, k=k: e.scalar_tensor_tensor(
                wsel[:], wbuf8[:, k * 8:(k + 1) * 8], maskb[:, k:k + 1], wsel[:],
                ALU.mult, ALU.add),
                   wait=[(sTab, sTab.n)], inc=(sTab, 1))
        for dt in range(4):
            rhob = c_rho[:, dt:dt + 1].broadcast_to([128, TC])
            ctx.on("vector", lambda e, dt=dt, rhob=rhob: e.tensor_tensor_scan(
                br[:, dt * TC:(dt + 1) * TC], rhob, br[:, dt * TC:(dt + 1) * TC],
                wsel[:, dt * 2:dt * 2 + 1], ALU.mult, ALU.add),
                   wait=[(sTab, sTab.n)], inc=(sTab, 1))
            ctx.on("vector", lambda e, dt=dt, rhob=rhob: e.tensor_tensor_scan(
                bi[:, dt * TC:(dt + 1) * TC], rhob, bi[:, dt * TC:(dt + 1) * TC],
                wsel[:, dt * 2 + 1:dt * 2 + 2], ALU.mult, ALU.add),
                   wait=[(sTab, sTab.n)], inc=(sTab, 1))
            ctx.on("vector", lambda e, dt=dt: e.tensor_tensor(
                scr[:], Ec[:, dt * TC:(dt + 1) * TC], br[:, dt * TC:(dt + 1) * TC], ALU.mult),
                   wait=[(sTab, sTab.n)], inc=(sTab, 1))
            ctx.on("vector", lambda e, dt=dt: e.tensor_tensor(
                br[:, dt * TC:(dt + 1) * TC], Es[:, dt * TC:(dt + 1) * TC],
                bi[:, dt * TC:(dt + 1) * TC], ALU.mult),
                   wait=[(sTab, sTab.n)], inc=(sTab, 1))
            ctx.on("vector", lambda e, dt=dt: e.tensor_tensor(
                scr[:], scr[:], br[:, dt * TC:(dt + 1) * TC], ALU.add),
                   wait=[(sTab, sTab.n)], inc=(sTab, 1))
            ctx.on("vector", lambda e, dt=dt, l=l: e.scalar_tensor_tensor(
                hso[:, dt * TC:(dt + 1) * TC], scr[:],
                prm[:, PRM["scl"][0] + l * 4 + dt:PRM["scl"][0] + l * 4 + dt + 1],
                hso[:, dt * TC:(dt + 1) * TC], ALU.mult, ALU.add),
                   wait=[(sTab, sTab.n)], inc=(sTab, 1))
        emit_ln_inner(ctx, mk, SLN, hso, ybf, ybuf, rows, absb,
                      prm[:, PRM["ln2w"][0] + l * 4:PRM["ln2w"][0] + (l + 1) * 4],
                      prm[:, PRM["ln2b"][0] + l * 4:PRM["ln2b"][0] + (l + 1) * 4],
                      sqb, silu=False, first_wait=[(sTab, sTab.n)])
        ctx.on("vector", lambda e: e.tensor_copy(ybf[:], ybuf[:]),
               wait=[(SLN["n"], SLN["n"].n)], inc=(sTab, 1))
        ktc = sTab.n
        kffa = sFFA.n; kffe = sFFE.n; kffh = sFFH.n
        for hf in range(2):
            hbase = kffa + hf * 20
            for ft in range(16):
                gi = hf * 16 + ft
                bank = gi % 2
                for a in range(4):
                    w = []
                    if a == 0:
                        w = [(sTab, ktc), (sWL[0], 16 * (l + 1))]
                        if gi >= 2:
                            w.append((sFFE, kffe + gi - 1))
                    ctx.on("tensor",
                           lambda e, a=a, ft=ft, hf=hf, bank=bank: e.matmul(
                               bankf(mk, bank), w1b[:, a * 2048 + ft * 128:a * 2048 + (ft + 1) * 128],
                               ybf[:, a * TC + hf * TH:a * TC + hf * TH + TH],
                               start=(a == 0), stop=(a == 3)),
                           wait=w, inc=(sFFA, 1) if a == 3 else None)
                ctx.on("scalar",
                       lambda e, ft=ft, hf=hf, bank=bank, l=l: e.activation(
                           out=h1b[:, ft * TH:(ft + 1) * TH], in_=bankf(mk, bank),
                           func=AF.Silu, bias=prm[:, PRM["b1"][0] + l * 16 + ft:PRM["b1"][0] + l * 16 + ft + 1]),
                       wait=[(sFFA, hbase + ft + 1)], inc=(sFFE, 1))
            for dt in range(4):
                gi = hf * 4 + dt
                bank = 2 + gi % 2
                for f in range(16):
                    w = []
                    if f == 0:
                        w = [(sFFE, kffe + hf * 16 + 16), (sWL[1], 16 * (l + 1))]
                        if gi >= 2:
                            w.append((sFFH, kffh + gi - 1))
                        if l == 0 and gi == 1:
                            w.append((sTab, 1))
                    ctx.on("tensor",
                           lambda e, f=f, dt=dt, bank=bank: e.matmul(
                               bankf(mk, bank), w2b[:, f * 512 + dt * 128:f * 512 + (dt + 1) * 128],
                               h1b[:, f * TH:(f + 1) * TH],
                               start=(f == 0), stop=(f == 15)),
                           wait=w, inc=(sFFA, 1) if f == 15 else None)
                ctx.on("scalar",
                       lambda e, dt=dt, bank=bank, l=l: e.activation(
                           out=scr[:, 0:TH], in_=bankf(mk, bank), func=AF.Identity,
                           bias=prm[:, PRM["b2"][0] + l * 4 + dt:PRM["b2"][0] + l * 4 + dt + 1]),
                       wait=[(sFFA, sFFA.n), (sFFH, sFFH.n)], inc=(sFFE2, 1))
                ctx.on("vector",
                       lambda e, dt=dt, hf=hf: e.tensor_tensor(
                           hso[:, dt * TC + hf * TH:dt * TC + hf * TH + TH],
                           hso[:, dt * TC + hf * TH:dt * TC + hf * TH + TH],
                           scr[:, 0:TH], ALU.add),
                       wait=[(sFFE2, sFFE2.n)], inc=(sFFH, 1))
        if l < NL - 1:
            emit_wload(l + 1)

    emit_ln_inner(ctx, mk, SLN, hso, ybf, ybuf, rows, absb,
                  prm[:, PRM["lnlw"][0]:PRM["lnlw"][0] + 4],
                  prm[:, PRM["lnlb"][0]:PRM["lnlb"][0] + 4], sqb, silu=False,
                  first_wait=[(sFFH, sFFH.n)])
    ctx.on("vector", lambda e: e.tensor_copy(ybf[:], ybuf[:]),
           wait=[(SLN["n"], SLN["n"].n)], inc=(sHFB, 1))
    ctx.on("gpsimd", lambda e: e.dma_start(
        out=woutb[:].rearrange("p (a o) -> p a o", a=4),
        in_=mk.p2_woutT.rearrange("(a p) o -> p a o", p=128)),
        wait=[(sFFH, sFFH.n)], inc=(sWoL, 16))
    wo_ev = ctx.sb_at("p2_woev", 2 * TC, F32, RB + 136 * 1024)
    wo_rl = ctx.sb_at("p2_worl", 2 * TC, F32, RB + 144 * 1024)
    for o in range(NP):
        kf = o % 2
        w = [(sSp, 16), (sHFB, 1)] if o == 0 else []
        if o >= 2:
            w.append((sFin, o - 1))
        ctx.on("sync",
               lambda e, o=o, kf=kf: e.dma_start(
                   out=wo_rl[:, kf * TC:(kf + 1) * TC],
                   in_=mk.p2_xsp[:, o * TC:(o + 1) * TC]),
               wait=w, inc=(sRel[kf], 16))
        for th in range(2):
            idx = o * 2 + th
            bank = idx % 4
            for a in range(4):
                w2 = []
                if a == 0:
                    w2 = [(sHFB, 1), (sWoL, 16)]
                    if idx >= 4:
                        w2.append((sWoE, idx - 3))
                ctx.on("tensor",
                       lambda e, a=a, o=o, th=th, bank=bank: e.matmul(
                           bankf(mk, bank),
                           woutb[:, a * D + o * 128:a * D + (o + 1) * 128],
                           ybf[:, a * TC + th * TH:a * TC + th * TH + TH],
                           start=(a == 0), stop=(a == 3)),
                       wait=w2, inc=(sWoA, 1) if a == 3 else None)
            ctx.on("scalar",
                   lambda e, o=o, th=th, bank=bank: e.activation(
                       out=wo_ev[:, (o % 2) * TC + th * TH:(o % 2) * TC + th * TH + TH],
                       in_=bankf(mk, bank), func=AF.Identity,
                       bias=prm[:, PRM["bout"][0] + o:PRM["bout"][0] + o + 1]),
                   wait=[(sWoA, idx + 1), (sHFB, 1)] +
                        ([(sFin, o - 1)] if o >= 2 else []),
                   inc=(sWoE, 1))
        ctx.on("vector",
               lambda e, o=o, kf=kf: e.tensor_tensor(
                   mk.stream[:, o * TC:(o + 1) * TC],
                   wo_ev[:, kf * TC:(kf + 1) * TC],
                   wo_rl[:, kf * TC:(kf + 1) * TC], ALU.add),
               wait=[(sWoE, 2 * o + 2), (sRel[kf], 16 * (o // 2 + 1))], inc=(sFin, 1))

    norm_sems = []
    prev = None
    for hf in range(2):
        ctx.begin("half")
        extra = [] if prev is None else [(prev, 1)]
        sNorm, sSum = emit_ln_half(ctx, mk, f"p2ln{hf}", hf, sFin, NP,
                                   extra_first_wait=extra, alias_off=ln_off,
                                   alias_all=True, flat=True)
        norm_sems.append((sNorm, NP))
        prev = sSum
    return norm_sems


def host_inputs_p3(inp):
    wgT = np.ascontiguousarray(inp["L1_ffn_Wg"].T[PIDX])
    wuT = np.ascontiguousarray(inp["L1_ffn_Wu"].T[PIDX])
    wdT3 = np.ascontiguousarray(inp["L1_ffn_Wd"].T[:, PIDX])
    ag = np.ascontiguousarray(inp["L1_ffn_ag"].reshape(86, 128).T)
    au = np.ascontiguousarray(inp["L1_ffn_au"].reshape(86, 128).T)
    ad = np.ascontiguousarray(inp["L1_ffn_ad"][PIDX].reshape(32, 128).T)
    return {"p3_wgT": wgT, "p3_wuT": wuT, "p3_wdT": wdT3,
            "p3_ag": ag, "p3_au": au, "p3_ad": ad}


def declare_p3(ctx, mk):
    nc = ctx.nc
    I3 = 11008
    mk.p3_wgT = nc.dram_tensor("p3_wgT", [D, I3], F32, kind="ExternalInput")
    mk.p3_wuT = nc.dram_tensor("p3_wuT", [D, I3], F32, kind="ExternalInput")
    mk.p3_wdT = nc.dram_tensor("p3_wdT", [I3, D], F32, kind="ExternalInput")
    mk.p3_ag = nc.dram_tensor("p3_ag", [128, 86], F32, kind="ExternalInput")
    mk.p3_au = nc.dram_tensor("p3_au", [128, 86], F32, kind="ExternalInput")
    mk.p3_ad = nc.dram_tensor("p3_ad", [128, 32], F32, kind="ExternalInput")
    mk.p3_xsp = nc.dram_tensor("p3_xsp", [128, NP * TC], F32)
    mk.p3_dacc = nc.dram_tensor("p3_dacc", [NP, 128, TC], F32)


def emit_p3(ctx, mk, s_ready):
    ctx.begin("phase")
    ctx.begin("half")
    I3 = 11008
    CH_TILES = [22, 22, 21, 21]
    CH_I0 = [0, 22, 44, 65]

    RB = 16512
    xbf = ctx.sb_at("p3_xbf", NP * TC, BF16, RB + 96 * 1024)
    hbuf = ctx.sb_at("p3_h", 22 * TC, BF16, RB)
    wg = [ctx.sb_at("p3_wg0", D, BF16, RB + 45056),
          ctx.sb_at("p3_wg1", D, BF16, RB + 45056 + 8192)]
    wu = [ctx.sb_at("p3_wu0", D, BF16, RB + 61440),
          ctx.sb_at("p3_wu1", D, BF16, RB + 61440 + 8192)]
    wd = [ctx.sb_at("p3_wd0", 22 * 128, BF16, RB + 77824),
          ctx.sb_at("p3_wd1", 22 * 128, BF16, RB + 77824 + 5632)]
    gs = [ctx.sb_at("p3_gs0", TH, F32, RB + 89088),
          ctx.sb_at("p3_gs1", TH, F32, RB + 89088 + 2048)]
    us = [ctx.sb_at("p3_us0", TH, F32, RB + 93184),
          ctx.sb_at("p3_us1", TH, F32, RB + 93184 + 2048)]
    dsg = [ctx.sb_at("p3_dsg0", TH, F32, RB + 162 * 1024),
           ctx.sb_at("p3_dsg1", TH, F32, RB + 162 * 1024 + 2048)]
    XB = RB + 128 * 1024
    fin = [ctx.sb_at("p3_fin0", 2 * TC, F32, XB),
           ctx.sb_at("p3_fin1", 2 * TC, F32, XB + 8192)]
    dst_ = [ctx.sb_at("p3_dst0", TC, F32, XB + 16384),
            ctx.sb_at("p3_dst1", TC, F32, XB + 16384 + 4096)]
    agt = ctx.sb_at("p3_agt", 86, F32, RB + 166 * 1024)
    aut = ctx.sb_at("p3_aut", 86, F32, RB + 166 * 1024 + 512)
    adt = ctx.sb_at("p3_adt", 32, F32, RB + 166 * 1024 + 1024)
    ln_off = RB + 176 * 1024

    sLd = ctx.dsem("p3_sLd")
    sSp = ctx.dsem("p3_sSp")
    sXC = ctx.sem("p3_sXC")
    sWg = [ctx.dsem("p3_sWg0"), ctx.dsem("p3_sWg1")]
    sWgF = [ctx.sem("p3_sWgF0"), ctx.sem("p3_sWgF1")]
    sWu = [ctx.dsem("p3_sWu0"), ctx.dsem("p3_sWu1")]
    sWuF = [ctx.sem("p3_sWuF0"), ctx.sem("p3_sWuF1")]
    sWd = [ctx.dsem("p3_sWd0"), ctx.dsem("p3_sWd1")]
    sWdF = [ctx.sem("p3_sWdF0"), ctx.sem("p3_sWdF1")]
    sGA = ctx.sem("p3_sGA")
    sGE = ctx.sem("p3_sGE")
    sH = ctx.sem("p3_sH")
    sDA = ctx.sem("p3_sDA")
    sDE = ctx.sem("p3_sDE")
    sDD = [ctx.dsem("p3_sDD0"), ctx.dsem("p3_sDD1")]
    sFL = [ctx.dsem("p3_sFL0"), ctx.dsem("p3_sFL1")]
    sFin = ctx.sem("p3_sFin")
    sFA = ctx.sem("p3_sFA")
    sFS = ctx.sem("p3_sFS")

    ctx.on("scalar", lambda e: e.dma_start(out=agt[:, 0:86], in_=mk.p3_ag[:, :]),
           wait=list(s_ready), inc=(sLd, 16))
    ctx.on("scalar", lambda e: e.dma_start(out=aut[:, 0:86], in_=mk.p3_au[:, :]), inc=(sLd, 16))
    ctx.on("scalar", lambda e: e.dma_start(out=adt[:, 0:32], in_=mk.p3_ad[:, :]), inc=(sLd, 16))
    ctx.on("sync", lambda e: e.dma_start(out=mk.p3_xsp[:, :], in_=mk.stream[:]),
           wait=list(s_ready), inc=(sSp, 16))
    for p in range(NP - 1, -1, -1):
        ctx.on("vector",
               lambda e, p=p: e.tensor_copy(xbf[:, p * TC:(p + 1) * TC],
                                            mk.stream[:, p * TC:(p + 1) * TC]),
               wait=[(sSp, 16)] if p == NP - 1 else [(sXC, NP - 1 - p)],
               inc=(sXC, 1))

    def emit_dn_load(ld):
        if ld >= 4 * NP:
            return
        ic2 = ld // NP
        o2 = ld % NP
        nt2 = CH_TILES[ic2]
        i0 = CH_I0[ic2] * 128
        ni = nt2 * 128
        par = ld % 2
        w = [(sWdF[par], ld // 2)] if ld >= 2 else [(sXC, NP)]
        ctx.on("gpsimd",
               lambda e, i0=i0, ni=ni, o2=o2, par=par, nt2=nt2: e.dma_start(
                   out=wd[par][:, 0:nt2 * 128].rearrange("p (a o) -> p a o", a=nt2),
                   in_=mk.p3_wdT[i0:i0 + ni, o2 * 128:(o2 + 1) * 128].rearrange(
                       "(a p) o -> p a o", p=128)),
               wait=w, inc=(sWd[par], 16))

    nlg = 0; nld = 0; ng = 0; nd = 0
    emit_dn_load(0)
    emit_dn_load(1)
    for ic in range(4):
        nt = CH_TILES[ic]
        for j in range(nt):
            it = CH_I0[ic] + j
            lg = nlg + j
            par = lg % 2
            w = [(sXC, NP)] if lg < 2 else [(sWgF[par], lg // 2)]
            ctx.on("gpsimd",
                   lambda e, it=it, par=par: e.dma_start(
                       out=wg[par][:].rearrange("p (a i) -> p a i", a=NP),
                       in_=mk.p3_wgT[:, it * 128:(it + 1) * 128].rearrange(
                           "(a p) i -> p a i", p=128)),
                   wait=w, inc=(sWg[par], 16))
            w = [(sWuF[par], lg // 2)] if lg >= 2 else [(sXC, NP)]
            ctx.on("gpsimd",
                   lambda e, it=it, par=par: e.dma_start(
                       out=wu[par][:].rearrange("p (a i) -> p a i", a=NP),
                       in_=mk.p3_wuT[:, it * 128:(it + 1) * 128].rearrange(
                           "(a p) i -> p a i", p=128)),
                   wait=w, inc=(sWu[par], 16))
        for j in range(nt):
            it = CH_I0[ic] + j
            parw = nlg % 2
            for th in range(2):
                for a in range(NP):
                    w = []
                    if a == 0:
                        w = [(sWg[parw], 16 * (nlg // 2 + 1))]
                        if ng >= 2:
                            w.append((sGE, 2 * (ng - 1)))
                    ctx.on("tensor",
                           lambda e, a=a, th=th, parw=parw: e.matmul(
                               bankf(mk, th), wg[parw][:, a * 128:(a + 1) * 128],
                               xbf[:, a * TC + th * TH:a * TC + th * TH + TH],
                               start=(a == 0), stop=(a == NP - 1)),
                           wait=w, inc=(sGA, 1) if a == NP - 1 else None)
                for a in range(NP):
                    w = []
                    if a == 0:
                        w = [(sWu[parw], 16 * (nlg // 2 + 1))]
                        if ng >= 2:
                            w.append((sGE, 2 * (ng - 1)))
                    ctx.on("tensor",
                           lambda e, a=a, th=th, parw=parw: e.matmul(
                               bankf(mk, 2 + th), wu[parw][:, a * 128:(a + 1) * 128],
                               xbf[:, a * TC + th * TH:a * TC + th * TH + TH],
                               start=(a == 0), stop=(a == NP - 1)),
                           wait=w, inc=(sGA, 1) if a == NP - 1 else None)
                kge = ng % 2
                ctx.on("scalar",
                       lambda e, th=th, it=it, kge=kge: e.activation(
                           out=gs[kge][:], in_=bankf(mk, th), func=AF.Silu,
                           scale=agt[:, it:it + 1]),
                       wait=[(sGA, 2 * ng + 2), (sLd, 48)] +
                            ([(sH, ng - 1)] if ng >= 2 else []),
                       inc=(sGE, 1))
                ctx.on("scalar",
                       lambda e, th=th, it=it, kge=kge: e.activation(
                           out=us[kge][:], in_=bankf(mk, 2 + th), func=AF.Identity,
                           scale=aut[:, it:it + 1]),
                       inc=(sGE, 1))
                wh = [(sGE, 2 * ng + 2)]
                if j == 0 and th == 0 and ic >= 1:
                    wh.append((sDA, 64 * ic))
                ctx.on("vector",
                       lambda e, j=j, th=th, kge=kge: e.tensor_tensor(
                           hbuf[:, j * TC + th * TH:j * TC + th * TH + TH],
                           gs[kge][:], us[kge][:], ALU.mult),
                       wait=wh, inc=(sH, 1))
                ng += 1
            ctx.on("vector", lambda e: e.engine_nop(), wait=[(sGA, 2 * ng)],
                   inc=(sWgF[parw], 1))
            ctx.on("vector", lambda e: e.engine_nop(), inc=(sWuF[parw], 1))
            nlg += 1
        for o in range(NP):
            parw = nld % 2
            for th in range(2):
                bankd = 4 + nd % 2
                for j in range(nt):
                    w = []
                    if j == 0:
                        w = [(sWd[parw], 16 * (nld // 2 + 1)), (sH, 2 * (CH_I0[ic] + nt))]
                        if nd >= 2:
                            w.append((sDE, nd - 1))
                    ctx.on("tensor",
                           lambda e, j=j, th=th, parw=parw, bankd=bankd, nt=nt: e.matmul(
                               bankf(mk, bankd), wd[parw][:, j * 128:(j + 1) * 128],
                               hbuf[:, j * TC + th * TH:j * TC + th * TH + TH],
                               start=(j == 0), stop=(j == nt - 1)),
                           wait=w, inc=(sDA, 1) if j == nt - 1 else None)
                kd = nd % 2
                ctx.on("vector",
                       lambda e, bankd=bankd, kd=kd: e.tensor_copy(dsg[kd][:], bankf(mk, bankd)),
                       wait=[(sDA, nd + 1)] + ([(sDD[kd], 16 * (nd // 2))] if nd >= 2 else []),
                       inc=(sDE, 1))
                op = ALU.bypass if ic == 0 else ALU.add
                ctx.on("gpsimd",
                       lambda e, o=o, th=th, kd=kd, op=op: e.dma_start(
                           out=mk.p3_dacc[o, :, th * TH:th * TH + TH], in_=dsg[kd][:],
                           accum_op=op),
                       wait=[(sDE, nd + 1)] + ([(sDD[kd], 16 * (nd // 2))] if nd >= 2 else []),
                       inc=(sDD[kd], 16))
                nd += 1
            ctx.on("vector", lambda e, parw=parw: e.engine_nop(),
                   wait=[(sDA, nd)], inc=(sWdF[parw], 1))
            emit_dn_load(nld + 2)
            nld += 1

    for o in range(NP):
        kf = o % 2
        w = [(sDD[0], 16 * 128), (sDD[1], 16 * 128)] if o == 0 else []
        if o >= 2:
            w.append((sFin, o - 1))
        ctx.on("sync",
               lambda e, o=o, kf=kf: e.dma_start(
                   out=fin[kf][:, 0:TC], in_=mk.p3_dacc[o, :, :]),
               wait=w, inc=(sFL[kf], 16))
        ctx.on("sync",
               lambda e, o=o, kf=kf: e.dma_start(
                   out=fin[kf][:, TC:2 * TC], in_=mk.p3_xsp[:, o * TC:(o + 1) * TC]),
               inc=(sFL[kf], 16))
        nf = sFL[kf].n
        ctx.on("scalar",
               lambda e, o=o, kf=kf: e.activation(
                   out=dst_[kf][:], in_=fin[kf][:, 0:TC], func=AF.Identity,
                   scale=adt[:, o:o + 1]),
               wait=[(sFL[kf], nf)] + ([(sFin, o - 1)] if o >= 2 else []),
               inc=(sFS, 1))
        ctx.on("vector",
               lambda e, o=o, kf=kf: e.tensor_tensor(
                   mk.stream[:, o * TC:(o + 1) * TC], dst_[kf][:],
                   fin[kf][:, TC:2 * TC], ALU.add),
               wait=[(sFS, o + 1)], inc=(sFin, 1))

    norm_sems = []
    prev = None
    for hf in range(2):
        ctx.begin("half")
        extra = [] if prev is None else [(prev, 1)]
        sNorm, sSum = emit_ln_half(ctx, mk, f"p3ln{hf}", hf, sFin, NP,
                                   extra_first_wait=extra, alias_off=ln_off, alias_all=True,
                                   flat=True)
        norm_sems.append((sNorm, NP))
        prev = sSum
    return norm_sems


def build_kernel(phases=("p1",)):
    nc = bass.Bass("TRN2", target_bir_lowering=False, debug=False)
    ctx = Ctx(nc)
    mk = MK()
    declare_common(ctx, mk)
    if "p0" in phases:
        declare_p0(ctx, mk)
    if "p1" in phases:
        declare_p1(ctx, mk)
    if "p2" in phases:
        declare_p2(ctx, mk)
    if "p3" in phases:
        declare_p3(ctx, mk)
    emit_init_loads(ctx, mk)
    ready = mk.init_ready
    if "p0" in phases:
        norm = emit_p0(ctx, mk, ready)
        ready = norm
    if "p1" in phases:
        import os as _os
        norm = emit_p1(ctx, mk, ready, trunc=_os.environ.get("P1_TRUNC"))
        ready = norm
    if "p2" in phases:
        norm = emit_p2(ctx, mk, ready)
        ready = norm
    if "p3" in phases:
        norm = emit_p3(ctx, mk, ready)
        ready = norm
    emit_store_out(ctx, mk, list(ready))
    ctx.finalize()
    return nc


def kernel(**inputs):
    inp = {k: np.asarray(v) for k, v in inputs.items()}
    x = np.asarray(inp["L0_attn_x"], np.float32)
    shards = shard_x(x)
    ins = dict(host_consts(), **host_inputs_p0(inp), **host_inputs_p1(inp),
               **host_inputs_p2(inp), **host_inputs_p3(inp))
    nc = build_kernel(("p0", "p1", "p2", "p3"))
    in_maps = []
    for c in range(NCORE):
        m = np.zeros((1, 8), np.float32)
        m[0, c] = 1.0
        in_maps.append(dict(ins, x_in=shards[c], p2_mask=m))
    res = run_bass_kernel_spmd(nc, in_maps, core_ids=list(range(NCORE)))
    parts = [np.asarray(res.results[c]["y_out"]) for c in range(NCORE)]
    return unshard_out(parts).astype(np.float32)


# version 02
# speedup vs baseline: 1.0321x; 1.0321x over previous
import numpy as np
import ml_dtypes
import concourse.bass as bass
import concourse.mybir as mybir
from concourse.bass_utils import run_bass_kernel_spmd

F32 = mybir.dt.float32
BF16 = mybir.dt.bfloat16
AF = mybir.ActivationFunctionType
ALU = mybir.AluOpType

NCORE = 8
S = 8192
D = 4096
TC = S // NCORE
NP = D // 128
TH = 512
EPS = 1e-5

PIDX = ((np.arange(D) % 128) * 32 + (np.arange(D) // 128)).astype(np.int64)
BF = ml_dtypes.bfloat16


class Sem:
    __slots__ = ("h", "_n", "name")
    def __init__(self, h, name):
        self.h = h; self._n = 0; self.name = name
    @property
    def n(self):
        return self._n
    def bump(self, k):
        self._n += k
    def absval(self, v):
        return v


class SemView:
    __slots__ = ("s", "base", "name")
    def __init__(self, s, name):
        self.s = s; self.base = s.n; self.name = name
    @property
    def h(self):
        return self.s.h
    @property
    def n(self):
        return self.s.n - self.base
    def bump(self, k):
        self.s.bump(k)
    def absval(self, v):
        return self.base + v


class Ctx:
    ENGINES = ("sync", "tensor", "vector", "scalar", "gpsimd")

    def __init__(self, nc):
        self.nc = nc
        self.q = {e: [] for e in self.ENGINES}
        self.sbuf_off = 16512
        self.sbuf_limit = nc.SBUF_PARTITION_SIZE_BYTES
        self.peak = 0

    def sem_global(self, name):
        return Sem(self.nc.alloc_semaphore(name), name)

    def begin(self, level):
        if not hasattr(self, "_pools"):
            self._pools = {"phase": [[], []], "half": [[], []]}
            self._dpools = {"phase": [[], []], "half": [[], []]}
            self._gen = {"phase": 0, "half": 0}
        g = self._gen[level] % 2
        self._cur = self._pools[level][g]
        self._dcur = self._dpools[level][g]
        self._gen[level] += 1
        self._pos = 0
        self._dpos = 0

    def _draw(self, pool, pos, name):
        if pos < len(pool):
            s = pool[pos]
        else:
            s = Sem(self.nc.alloc_semaphore(name), name)
            pool.append(s)
        return SemView(s, name)

    def sem(self, name):
        if not hasattr(self, "_cur") or self._cur is None:
            return self.sem_global(name)
        v = self._draw(self._cur, self._pos, name)
        self._pos += 1
        return v

    def dsem(self, name):
        if not hasattr(self, "_dcur") or self._dcur is None:
            return self.sem_global(name)
        v = self._draw(self._dcur, self._dpos, name)
        self._dpos += 1
        return v

    def sb(self, name, cols, dtype):
        nbytes = cols * (4 if dtype == F32 else 2)
        off = (self.sbuf_off + 63) & ~63
        assert off + nbytes <= self.sbuf_limit, (name, off, nbytes)
        t = self.nc.alloc_sbuf_tensor_at(name, [128, cols], dtype, offset=off)
        self.sbuf_off = off + nbytes
        self.peak = max(self.peak, self.sbuf_off)
        return t

    def sb_at(self, name, cols, dtype, off):
        nbytes = cols * (4 if dtype == F32 else 2)
        assert off + nbytes <= self.sbuf_limit, (name, off, nbytes)
        return self.nc.alloc_sbuf_tensor_at(name, [128, cols], dtype, offset=off)

    def mark(self):
        return self.sbuf_off

    def release(self, mark):
        self.sbuf_off = mark

    def on(self, eng, fn, wait=(), inc=None, incs=()):
        allincs = list(incs) + ([inc] if inc is not None else [])
        for s, k in allincs:
            s.bump(k)
        w = [(s.h, s.absval(v)) for (s, v) in wait]
        def run(e, w=w, fn=fn, allincs=allincs):
            for h, v in w:
                e.wait_ge(h, v)
            r = fn(e)
            for s, k in allincs:
                r = r.then_inc(s.h, k)
        self.q[eng].append(run)

    def finalize(self):
        nc = self.nc
        with nc.Block() as block:
            @block.sync
            def _(e, q=self.q["sync"]):
                for f in q: f(e)
            @block.tensor
            def _(e, q=self.q["tensor"]):
                for f in q: f(e)
            @block.vector
            def _(e, q=self.q["vector"]):
                for f in q: f(e)
            @block.scalar
            def _(e, q=self.q["scalar"]):
                for f in q: f(e)
            @block.gpsimd
            def _(e, q=self.q["gpsimd"]):
                for f in q: f(e)


def w_only(ctx, eng, *pairs):
    w = [(s.h, s.absval(v)) for (s, v) in pairs]
    def run(e, w=w):
        for h, v in w:
            e.wait_ge(h, v)
    ctx.q[eng].append(run)


def host_consts():
    rbc = np.zeros((8, 128), dtype=np.float32)
    for e in range(8):
        rbc[e, e * 16:(e + 1) * 16] = 2.0
    return {
        "c_eyef": np.eye(128, dtype=np.float32),
        "c_eyeb": np.eye(128, dtype=BF),
        "c_onesb": np.ones((128, 1), dtype=BF),
        "c_ones1f": np.ones((1, 128), dtype=np.float32),
        "c_rbc": rbc,
    }


def host_inputs_p0(inp):
    wq = np.ascontiguousarray(inp["L0_attn_Wq"].transpose(2, 0, 1)[:, PIDX, :])
    wk = np.ascontiguousarray(inp["L0_attn_Wk"].transpose(2, 0, 1)[:, PIDX, :])
    wv = np.ascontiguousarray(inp["L0_attn_Wv"].transpose(2, 0, 1)[:, PIDX, :])
    return {"p0_wq": wq, "p0_wk": wk, "p0_wv": wv,
            "p0_bq": np.ascontiguousarray(inp["L0_attn_bq"]),
            "p0_bk": np.ascontiguousarray(inp["L0_attn_bk"]),
            "p0_bv": np.ascontiguousarray(inp["L0_attn_bv"])}


def host_inputs_p1(inp):
    baseT = np.ascontiguousarray(inp["L0_ffn_base_W"].T[PIDX][:, PIDX])
    wrT = np.ascontiguousarray(inp["L0_ffn_router_W"].T[PIDX])
    laA = np.ascontiguousarray(inp["L0_ffn_lora_A"].transpose(1, 0, 2).reshape(D, 128)[PIDX])
    laB = np.ascontiguousarray(inp["L0_ffn_lora_B"].reshape(128, D)[:, PIDX])
    return {"p1_baseT": baseT, "p1_wrT": wrT, "p1_laA": laA, "p1_laB": laB}


def shard_x(x):
    x2 = np.ascontiguousarray(x.reshape(S, D)[:, PIDX].T)
    return [np.ascontiguousarray(x2[:, c * TC:(c + 1) * TC]) for c in range(NCORE)]


def unshard_out(parts):
    full = np.concatenate(parts, axis=1)
    out = np.empty((S, D), dtype=np.float32)
    out[:, PIDX] = full.T
    return out.reshape(1, S, D)


class MK:
    pass


def bankf(mk, b, n=512):
    return mk.psumf[:, b * 512:b * 512 + n]


def declare_common(ctx, mk):
    nc = ctx.nc
    mk.x_in = nc.dram_tensor("x_in", [D, TC], F32, kind="ExternalInput")
    mk.y_out = nc.dram_tensor("y_out", [D, TC], F32, kind="ExternalOutput")
    mk.c_eyef = nc.dram_tensor("c_eyef", [128, 128], F32, kind="ExternalInput")
    mk.c_eyeb = nc.dram_tensor("c_eyeb", [128, 128], BF16, kind="ExternalInput")
    mk.c_onesb = nc.dram_tensor("c_onesb", [128, 1], BF16, kind="ExternalInput")
    mk.c_ones1f = nc.dram_tensor("c_ones1f", [1, 128], F32, kind="ExternalInput")
    mk.c_rbc = nc.dram_tensor("c_rbc", [8, 128], F32, kind="ExternalInput")

    mk.stream = ctx.sb("stream", NP * TC, F32)
    mk.xbh = ctx.sb("xbh", NP * TH, BF16)
    mk.eyef = ctx.sb("eyef", 128, F32)
    mk.eyeb = ctx.sb("eyeb", 128, BF16)
    mk.onesb = ctx.sb("onesb", 1, BF16)
    mk.ones1f = ctx.sb("ones1f", 128, F32)
    mk.rbc = ctx.sb("rbc", 128, F32)
    mk.epsc = ctx.sb("epsc", 1, F32)
    mk.psumf = nc.alloc_psum_tensor("psumf", [128, 6 * 512], F32)
    mk.psumb = nc.alloc_psum_tensor("psumb", [128, 2048], BF16)
    mk.sInit = ctx.sem("sInit")


def emit_init_loads(ctx, mk):
    s = mk.sInit
    for src, dst in ((mk.c_eyef, mk.eyef[:]), (mk.c_eyeb, mk.eyeb[:]),
                     (mk.c_onesb, mk.onesb[:]), (mk.c_ones1f, mk.ones1f[0:1, :]),
                     (mk.c_rbc, mk.rbc[0:8, :])):
        ctx.on("sync", lambda e, src=src, dst=dst: e.dma_start(out=dst, in_=src[:, :]),
               inc=(s, 16))
    xr = mk.x_in.rearrange("(a p) t -> p a t", p=128)
    so = mk.stream[:].rearrange("p (a t) -> p a t", a=NP)
    ctx.on("sync", lambda e, xr=xr, so=so: e.dma_start(out=so, in_=xr), inc=(s, 16))
    mk.sConst = ctx.sem("sConst")
    ctx.on("vector", lambda e: e.memset(mk.epsc[:], EPS), inc=(mk.sConst, 1))
    mk.init_ready = [(s, 96)]


def emit_cast_xbh(ctx, mk, hf, waits):
    sC = ctx.sem(f"cast_h{hf}_{len(ctx.q['vector'])}")
    for p in range(NP):
        ctx.on("vector",
               lambda e, p=p: e.tensor_copy(
                   mk.xbh[:, p * TH:(p + 1) * TH],
                   mk.stream[:, p * TC + hf * TH:p * TC + hf * TH + TH]),
               wait=waits if p == 0 else (), inc=(sC, 1))
    return sC


def emit_store_out(ctx, mk, waits):
    sSt = ctx.sem("sSt")
    yr = mk.y_out.rearrange("(a p) t -> p a t", p=128)
    so = mk.stream[:].rearrange("p (a t) -> p a t", a=NP)
    ctx.on("sync", lambda e, yr=yr, so=so: e.dma_start(out=yr, in_=so),
           wait=waits, inc=(sSt, 16))
    w_only(ctx, "sync", (sSt, 16))


def emit_ln_half(ctx, mk, tag, hf, s_add, add_target, extra_first_wait=(), alias_off=None,
                 flat=False, alias_all=False):
    mark = ctx.mark()
    ybf = mk.xbh
    if alias_all and alias_off is not None:
        sq = ctx.sb_at(f"{tag}_sq", 2 * TH, BF16, alias_off)
        rab = ctx.sb_at(f"{tag}_rab", 2 * TH, F32, alias_off + 2 * TH * 2)
        absb = ctx.sb_at(f"{tag}_ab", 2 * TH, F32, alias_off + 2 * TH * 2 + 2 * TH * 4)
    elif alias_off is not None:
        sq = ctx.sb(f"{tag}_sq", 2 * TH, BF16)
        rab = ctx.sb(f"{tag}_rab", 2 * TH, F32)
        absb = ctx.sb_at(f"{tag}_ab", 2 * TH, F32, alias_off)
    else:
        sq = ctx.sb(f"{tag}_sq", 2 * TH, BF16)
        rab = ctx.sb(f"{tag}_rab", 2 * TH, F32)
        absb = ctx.sb(f"{tag}_ab", 2 * TH, F32)

    sCast = ctx.sem(f"{tag}_cast")
    sSq = ctx.sem(f"{tag}_sq")
    sMM = ctx.sem(f"{tag}_mm")
    sSum = ctx.sem(f"{tag}_sum")
    sStat = ctx.sem(f"{tag}_stat")
    sBc = ctx.sem(f"{tag}_bc")
    sBcE = ctx.sem(f"{tag}_bce")
    sNorm = ctx.sem(f"{tag}_norm")

    row_a = rab[0:1, 0:TH]
    row_b = rab[0:1, TH:2 * TH]
    psum6 = mk.psumf[0:1, 5 * 512:5 * 512 + TH]
    psum5 = mk.psumf[0:1, 4 * 512:4 * 512 + TH]

    for p in range(NP):
        st = mk.stream[:, p * TC + hf * TH:p * TC + hf * TH + TH]
        tw = [(s_add, add_target)] if flat else [(s_add, add_target + p)]
        ctx.on("vector",
               lambda e, p=p, st=st: e.tensor_copy(ybf[:, p * TH:(p + 1) * TH], st),
               wait=(tw + (list(extra_first_wait) if p == 0 else [])),
               inc=(sCast, 1))
        ctx.on("scalar",
               lambda e, p=p, st=st: e.activation(
                   out=sq[:, (p % 2) * TH:(p % 2) * TH + TH], in_=st, func=AF.Square),
               wait=tw + ([(sMM, p - 1)] if p >= 2 else []) +
                    (list(extra_first_wait) if p == 0 else []),
               inc=(sSq, 1))
        ctx.on("tensor",
               lambda e, p=p: e.matmul(
                   psum6, mk.onesb[:, 0:1], ybf[:, p * TH:(p + 1) * TH],
                   start=(p == 0), stop=(p == NP - 1)),
               wait=[(sCast, p + 1)])
        ctx.on("tensor",
               lambda e, p=p: e.matmul(
                   psum5, mk.onesb[:, 0:1], sq[:, (p % 2) * TH:(p % 2) * TH + TH],
                   start=(p == 0), stop=(p == NP - 1)),
               wait=[(sSq, p + 1)],
               inc=(sMM, 1) if p < NP - 1 else (sSum, 1))

    c = 1.0 / D
    ctx.on("vector", lambda e: e.tensor_scalar_mul(row_b, psum6, c),
           wait=[(sSum, 1)], inc=(sStat, 1))
    ctx.on("vector", lambda e: e.tensor_scalar_mul(row_a, psum5, c),
           wait=[(sStat, 1)], inc=(sStat, 1))
    ctx.on("vector", lambda e: e.tensor_tensor(psum6, row_b, row_b, ALU.mult),
           wait=[(sStat, 2)], inc=(sStat, 1))
    ctx.on("vector", lambda e: e.tensor_tensor(row_a, row_a, psum6, ALU.subtract),
           wait=[(sStat, 3)], inc=(sStat, 1))
    ctx.on("scalar", lambda e: e.activation(out=row_a, in_=row_a, func=AF.Ln,
                                            bias=mk.epsc[0:1, 0:1]),
           wait=[(sStat, 4), (mk.sConst, 1)], inc=(sStat, 1))
    ctx.on("scalar", lambda e: e.activation(out=row_a, in_=row_a, func=AF.Exp, scale=-0.5),
           wait=[(sStat, 5)], inc=(sStat, 1))
    ctx.on("vector", lambda e: e.tensor_tensor(row_b, row_b, row_a, ALU.mult),
           wait=[(sStat, 6)], inc=(sStat, 1))
    ctx.on("vector", lambda e: e.tensor_scalar_mul(row_b, row_b, -1.0),
           wait=[(sStat, 7)], inc=(sStat, 1))

    ctx.on("tensor", lambda e: e.matmul(bankf(mk, 5), mk.ones1f[0:1, :], row_a,
                                        start=True, stop=True),
           wait=[(sStat, 8)], inc=(sBc, 1))
    ctx.on("tensor", lambda e: e.matmul(bankf(mk, 4), mk.ones1f[0:1, :], row_b,
                                        start=True, stop=True),
           wait=[(sStat, 8)], inc=(sBc, 1))
    ctx.on("vector", lambda e: e.tensor_copy(absb[:, 0:TH], bankf(mk, 5)),
           wait=[(sBc, 2)], inc=(sBcE, 1))
    ctx.on("vector", lambda e: e.tensor_copy(absb[:, TH:2 * TH], bankf(mk, 4)),
           wait=[(sBcE, 1)], inc=(sBcE, 1))

    sNm = ctx.sem(f"{tag}_nm")
    for p in range(NP):
        st = mk.stream[:, p * TC + hf * TH:p * TC + hf * TH + TH]
        ctx.on("vector",
               lambda e, st=st: e.tensor_tensor(st, st, absb[:, 0:TH], ALU.mult),
               wait=[(sBcE, 2)] if p == 0 else (), inc=(sNm, 1))
        ctx.on("vector",
               lambda e, st=st: e.tensor_tensor(st, st, absb[:, TH:2 * TH], ALU.add),
               wait=[(sNm, p + 1)], inc=(sNorm, 1))
    ctx.release(mark)
    return sNorm, sSum


def declare_p0(ctx, mk):
    nc = ctx.nc
    mk.p0_wq = nc.dram_tensor("p0_wq", [32, D, 128], F32, kind="ExternalInput")
    mk.p0_wk = nc.dram_tensor("p0_wk", [8, D, 128], F32, kind="ExternalInput")
    mk.p0_wv = nc.dram_tensor("p0_wv", [8, D, 128], F32, kind="ExternalInput")
    mk.p0_bq = nc.dram_tensor("p0_bq", [128, 32], F32, kind="ExternalInput")
    mk.p0_bk = nc.dram_tensor("p0_bk", [128, 8], F32, kind="ExternalInput")
    mk.p0_bv = nc.dram_tensor("p0_bv", [128, 8], F32, kind="ExternalInput")
    mk.p0_qd = nc.dram_tensor("p0_qd", [8, TH, 128], BF16)
    mk.p0_kd = nc.dram_tensor("p0_kd", [8, TH, 128], BF16)


def emit_p0(ctx, mk, s_ready):
    ctx.begin("phase")
    mark = ctx.mark()
    wb_base = (ctx.sbuf_off + 63) & ~63
    wbuf = [ctx.sb("p0_wb0", D, BF16), ctx.sb("p0_wb1", D, BF16)]
    qc = [ctx.sb("p0_qc0", TH, BF16), ctx.sb("p0_qc1", TH, BF16)]
    vc = ctx.sb("p0_vc", 8 * TH, BF16)
    tev = [ctx.sb("p0_tev0", 128, BF16), ctx.sb("p0_tev1", 128, BF16)]
    qt_alias = wb_base
    qt = [ctx.sb_at("p0_qt0", 16 * 128, BF16, wb_base),
          ctx.sb_at("p0_qt1", 16 * 128, BF16, wb_base + 4096)]
    kt = [ctx.sb_at("p0_kt0", 16 * 128, BF16, wb_base + 8192),
          ctx.sb_at("p0_kt1", 16 * 128, BF16, wb_base + 12288)]
    vcat = [ctx.sb("p0_vcat0", 9 * 56, BF16), ctx.sb("p0_vcat1", 9 * 56, BF16)]
    ubuf = [ctx.sb("p0_u0", 512, BF16), ctx.sb("p0_u1", 512, BF16)]
    zinv = ctx.sb("p0_zinv", 56, F32)
    tmpa = ctx.sb("p0_tmpa", 56, F32)
    bq = ctx.sb("p0_bq", 32, F32)
    bk = ctx.sb("p0_bk", 8, F32)
    bv = ctx.sb("p0_bv", 8, F32)
    bqc = ctx.sb("p0_bqc", 8, F32)
    ln_base = ctx.mark()

    SCL = 1.0 / float(np.sqrt(128.0))
    NB = (TH + 55) // 56
    NCH = TH // 16

    sLdB = ctx.dsem("p0_sLdB")
    ctx.on("scalar", lambda e: e.dma_start(out=bq[:, 0:32], in_=mk.p0_bq[:, :]), inc=(sLdB, 16))
    ctx.on("scalar", lambda e: e.dma_start(out=bk[:, 0:8], in_=mk.p0_bk[:, :]), inc=(sLdB, 16))
    ctx.on("scalar", lambda e: e.dma_start(out=bv[:, 0:8], in_=mk.p0_bv[:, :]), inc=(sLdB, 16))
    sBqc = ctx.sem("p0_sBqc")
    ctx.on("vector", lambda e: e.tensor_tensor(bqc[:, 0:8], bq[:, 0:32:4], bq[:, 1:32:4], ALU.add),
           wait=[(sLdB, 48)], inc=(sBqc, 1))
    ctx.on("vector", lambda e: e.tensor_tensor(tmpa[:, 0:8], bq[:, 2:32:4], bq[:, 3:32:4], ALU.add),
           wait=[(sBqc, 1)], inc=(sBqc, 1))
    ctx.on("vector", lambda e: e.tensor_tensor(bqc[:, 0:8], bqc[:, 0:8], tmpa[:, 0:8], ALU.add),
           wait=[(sBqc, 2)], inc=(sBqc, 1))

    sW = [ctx.dsem("p0_sW0"), ctx.dsem("p0_sW1")]
    sWF = [ctx.sem("p0_sWF0"), ctx.sem("p0_sWF1")]
    sQD = [ctx.dsem("p0_sQD0"), ctx.dsem("p0_sQD1")]
    sRB = [ctx.dsem("p0_sRB0"), ctx.dsem("p0_sRB1")]
    norm_sems = []
    prev_gate = list(s_ready)
    prev_norm = None
    prev_sSum = None

    for hf in range(2):
        ctx.begin("half")
        sXC = emit_cast_xbh(ctx, mk, hf, prev_gate)
        wsrcs = ([("q", h) for h in range(32)] + [("k", h) for h in range(8)]
                 + [("v", h) for h in range(8)])
        for li, (kind, h) in enumerate(wsrcs):
            gi = hf * 48 + li
            par = gi % 2
            dram = {"q": mk.p0_wq, "k": mk.p0_wk, "v": mk.p0_wv}[kind]
            w = [(sWF[par], gi // 2)] if gi >= 2 else []
            if li == 0 and prev_norm is not None:
                w = w + [prev_norm, (prev_sSum, 1)]
            ctx.on("gpsimd",
                   lambda e, dram=dram, h=h, par=par: e.dma_start(
                       out=wbuf[par][:].rearrange("p (a d) -> p a d", a=NP),
                       in_=dram[h].rearrange("(a p) d -> p a d", p=128)),
                   wait=w, inc=(sW[par], 16))

        sAcQ = ctx.sem(f"p0h{hf}_sAcQ")
        sEvQ = ctx.sem(f"p0h{hf}_sEvQ")
        sTrp = ctx.sem(f"p0h{hf}_sTrp")
        sTrE = ctx.sem(f"p0h{hf}_sTrE")
        sSc = ctx.sem(f"p0h{hf}_sSc")
        sU = ctx.sem(f"p0h{hf}_sU")
        sUC = ctx.sem(f"p0h{hf}_sUC")
        sVg = ctx.sem(f"p0h{hf}_sVg")
        sM = ctx.sem(f"p0h{hf}_sM")
        sZi = ctx.sem(f"p0h{hf}_sZi")
        sAsm = ctx.sem(f"p0h{hf}_sAsm")
        sMu = ctx.sem(f"p0h{hf}_sMu")
        sVin = ctx.sem(f"p0h{hf}_sVin")

        proj = ([("qc", hk) for hk in range(8)] + [("kc", hk) for hk in range(8)]
                + [("vc", hk) for hk in range(8)])
        for gidx, (kind, hk) in enumerate(proj):
            bank = gidx % 2
            if kind == "qc":
                lis = [4 * hk + r for r in range(4)]
            elif kind == "kc":
                lis = [32 + hk]
            else:
                lis = [40 + hk]
            mms = [(li, a) for li in lis for a in range(NP)]
            for mi, (li, a) in enumerate(mms):
                gi = hf * 48 + li
                par = gi % 2
                w = []
                if a == 0:
                    w.append((sW[par], 16 * (gi // 2 + 1)))
                if mi == 0:
                    w.append((sXC, NP))
                    if gidx >= 2:
                        w.append((sEvQ, gidx - 1))
                last = (mi == len(mms) - 1)
                if last:
                    inc = (sAcQ, 1)
                elif a == NP - 1:
                    inc = (sWF[par], 1)
                else:
                    inc = None
                ctx.on("tensor",
                       lambda e, a=a, par=par, bank=bank, mi=mi, last=last:
                           e.matmul(bankf(mk, bank), wbuf[par][:, a * 128:(a + 1) * 128],
                                    mk.xbh[:, a * TH:(a + 1) * TH],
                                    start=(mi == 0), stop=last),
                       wait=w, inc=inc)
            par_last = (hf * 48 + lis[-1]) % 2
            ctx.on("vector", lambda e: e.engine_nop(),
                   wait=[(sAcQ, gidx + 1)], inc=(sWF[par_last], 1))
            if kind == "vc":
                ctx.on("scalar",
                       lambda e, hk=hk, bank=bank: e.activation(
                           out=vc[:, hk * TH:(hk + 1) * TH], in_=bankf(mk, bank),
                           func=AF.Identity, bias=bv[:, hk:hk + 1]),
                       wait=[(sAcQ, gidx + 1), (sBqc, 3)], inc=(sEvQ, 1))
            else:
                bias = bqc[:, hk:hk + 1] if kind == "qc" else bk[:, hk:hk + 1]
                dstt = qc[gidx % 2]
                ctx.on("scalar",
                       lambda e, dstt=dstt, bank=bank, bias=bias: e.activation(
                           out=dstt[:], in_=bankf(mk, bank),
                           func=AF.Identity, bias=bias),
                       wait=[(sAcQ, gidx + 1), (sBqc, 3)] +
                            ([(sTrp, 4 * (gidx - 1))] if gidx >= 2 else []),
                       inc=(sEvQ, 1))
                dram = mk.p0_qd if kind == "qc" else mk.p0_kd
                for tc in range(4):
                    kk = 4 * gidx + tc
                    pslice = mk.psumb[:, (kk % 2) * 1024:(kk % 2) * 1024 + 128]
                    w = [(sEvQ, gidx + 1)] if tc == 0 else []
                    if kk >= 2:
                        w.append((sTrE, kk - 1))
                    ctx.on("tensor",
                           lambda e, dstt=dstt, tc=tc, pslice=pslice: e.transpose(
                               pslice, dstt[:, tc * 128:(tc + 1) * 128], mk.eyeb[:]),
                           wait=w, inc=(sTrp, 1))
                    w = [(sTrp, kk + 1)]
                    if kk >= 2:
                        w.append((sQD[kk % 2], 16 * (hf * 32 + kk // 2)))
                    ctx.on("vector",
                           lambda e, kk=kk, pslice=pslice: e.tensor_copy(
                               tev[kk % 2][:], pslice),
                           wait=w, inc=(sTrE, 1))
                    ctx.on("scalar",
                           lambda e, dram=dram, hk=hk, tc=tc, kk=kk: e.dma_start(
                               out=dram[hk, tc * 128:(tc + 1) * 128, :], in_=tev[kk % 2][:]),
                           wait=[(sTrE, kk + 1)], inc=(sQD[kk % 2], 16))

        for b in range(2):
            ctx.on("vector", lambda e, b=b: e.memset(vcat[b][:, 8:504:9], 1.0),
                   inc=(sVin, 1))

        for c in range(NCH):
            w = [(sQD[0], 16 * 32 * (hf + 1)), (sQD[1], 16 * 32 * (hf + 1))]
            if c == 0:
                w.append((sAcQ, 24))
            if c == 0 and prev_norm is not None:
                w.append(prev_norm)
            if c >= 2:
                w.append((sSc, 4 * (c - 1)))
            ctx.on("sync",
                   lambda e, c=c: e.dma_start(
                       out=qt[c % 2][0:8, 0:2048],
                       in_=mk.p0_qd[:, c * 16:(c + 1) * 16, :].rearrange("h t d -> h (t d)")),
                   wait=w, inc=(sRB[c % 2], 16))
            ctx.on("sync",
                   lambda e, c=c: e.dma_start(
                       out=kt[c % 2][0:8, 0:2048],
                       in_=mk.p0_kd[:, c * 16:(c + 1) * 16, :].rearrange("h t d -> h (t d)")),
                   inc=(sRB[c % 2], 16))

        for sb in range(TH // 4):
            bank = 2 + sb % 2
            mb = (sb * 4) // 56
            t0 = mb * 56
            B = min(56, TH - t0)
            bankM = 4 + mb % 2
            if sb * 4 == t0:
                for hk in range(8):
                    w = [(sVin, 2)]
                    if hk == 0:
                        w.append((sEvQ, 24))
                        w.append((sBqc, 3))
                        if mb >= 2:
                            w.append((sM, mb - 1))
                    ctx.on("vector",
                           lambda e, hk=hk, t0=t0, B=B, mb=mb: e.tensor_copy(
                               vcat[mb % 2][:, hk:9 * B:9],
                               vc[:, hk * TH + t0:hk * TH + t0 + B]),
                           wait=w, inc=(sVg, 1))
            for i in range(4):
                t = sb * 4 + i
                c = t // 16; tl = t % 16
                w = []
                if i == 0:
                    w.append((sRB[c % 2], 32 * (hf * NCH // 2 + c // 2 + 1)))
                    if sb >= 2:
                        w.append((sU, sb - 1))
                ctx.on("tensor",
                       lambda e, bank=bank, i=i, c=c, tl=tl: e.matmul(
                           bankf(mk, bank)[:, i * 128:(i + 1) * 128],
                           kt[c % 2][0:8, tl * 128:(tl + 1) * 128],
                           qt[c % 2][0:8, tl * 128:(tl + 1) * 128],
                           start=True, stop=True),
                       wait=w, inc=(sSc, 1) if i == 3 else None)
            ctx.on("scalar",
                   lambda e, sb=sb, bank=bank: e.activation(
                       out=ubuf[sb % 2][:], in_=bankf(mk, bank), func=AF.Exp, scale=SCL),
                   wait=[(sSc, sb + 1)] + ([(sUC, sb - 1)] if sb >= 2 else []),
                   inc=(sU, 1))
            for i in range(4):
                t = sb * 4 + i
                j = t - t0
                w = []
                if i == 0:
                    w.append((sU, sb + 1))
                if j == 0:
                    w.append((sVg, 8 * (mb + 1)))
                    if mb >= 2:
                        w.append((sAsm, 32 * (mb - 1)))
                    if mb == 0 and prev_norm is not None:
                        w.append(prev_norm)
                lastmb = (j == B - 1)
                ctx.on("tensor",
                       lambda e, bankM=bankM, j=j, sb=sb, i=i, mb=mb: e.matmul(
                           bankf(mk, bankM)[:, j * 9:(j + 1) * 9],
                           ubuf[sb % 2][:, i * 128:(i + 1) * 128],
                           vcat[mb % 2][:, j * 9:(j + 1) * 9],
                           start=True, stop=True),
                       wait=w,
                       inc=(sM, 1) if lastmb else ((sUC, 1) if i == 3 else None))
            if (sb * 4 + 4) == t0 + B:
                ctx.on("vector", lambda e, mb=mb: e.engine_nop(),
                       wait=[(sM, mb + 1)], inc=(sUC, 1))
                ctx.on("vector",
                       lambda e, bankM=bankM, B=B: e.reciprocal(
                           zinv[:, 0:B], bankf(mk, bankM)[:, 8:9 * B:9]),
                       wait=[(sM, mb + 1)], inc=(sZi, 1))
                for hk in range(8):
                    w = [(sZi, mb + 1)] if hk == 0 else [(sAsm, 32 * mb + 4 * hk)]
                    if hk == 0 and mb == 0:
                        w.append((sBqc, 3))
                    ctx.on("vector",
                           lambda e, bankM=bankM, hk=hk, B=B: e.tensor_tensor(
                               tmpa[:, 0:B], bankf(mk, bankM)[:, hk:9 * B:9], zinv[:, 0:B],
                               ALU.mult),
                           wait=w, inc=(sMu, 1))
                    nmul = sMu.n
                    for r in range(4):
                        h = 4 * hk + r
                        st = mk.stream[:, h * TC + hf * TH + t0:h * TC + hf * TH + t0 + B]
                        ctx.on("vector",
                               lambda e, st=st, B=B: e.tensor_tensor(
                                   st, st, tmpa[:, 0:B], ALU.add),
                               wait=[(sMu, nmul)], inc=(sAsm, 1))

        ctx.release(ln_base)
        sNorm, sSum = emit_ln_half(ctx, mk, f"p0ln{hf}", hf, sAsm, 32 * NB,
                                   extra_first_wait=[(sAcQ, 24)], flat=True,
                                   alias_off=qt_alias, alias_all=True)
        prev_gate = [(sAcQ, 24), (sAsm, 32 * NB), (sSum, 1)]
        prev_norm = (sNorm, NP)
        prev_sSum = sSum
        norm_sems.append((sNorm, NP))
    ctx.release(mark)
    return norm_sems


def declare_p1(ctx, mk):
    nc = ctx.nc
    mk.p1_baseT = nc.dram_tensor("p1_baseT", [D, D], F32, kind="ExternalInput")
    mk.p1_wrT = nc.dram_tensor("p1_wrT", [D, 8], F32, kind="ExternalInput")
    mk.p1_laA = nc.dram_tensor("p1_laA", [D, 128], F32, kind="ExternalInput")
    mk.p1_laB = nc.dram_tensor("p1_laB", [128, D], F32, kind="ExternalInput")


def emit_p1(ctx, mk, s_ready, trunc=None):
    import os
    ctx.begin("phase")
    """L0_ffn + residual + LN (both halves). s_ready: stream fully valid.
    Returns list of per-half sNorm sems [(sem, NP), ...] (all tiles normalized)."""
    mark = ctx.mark()
    wb = [ctx.sb("p1_wb0", D, BF16), ctx.sb("p1_wb1", D, BF16)]
    laAb = ctx.sb("p1_laAb", NP * 128, BF16)
    laBb = ctx.sb("p1_laBb", D, BF16)
    wrt = ctx.sb("p1_wrt", NP * 8, F32)
    logit = ctx.sb("p1_logit", 4 * 8, F32)
    wds = ctx.sb("p1_wds", 4 * 8, F32)
    ab_alias = (ctx.sbuf_off + 63) & ~63
    wdT = ctx.sb("p1_wdT", TH, F32)
    wfull = ctx.sb("p1_wfull", TH, F32)
    midw = ctx.sb("p1_midw", TH, BF16)
    srow = ctx.sb("p1_srow", 16, F32)
    ln_base = ctx.mark()

    sLdS = ctx.dsem("p1_sLdS")
    sLdG = ctx.dsem("p1_sLdG")
    ctx.on("scalar", lambda e: e.dma_start(
        out=wrt[:].rearrange("p (a e) -> p a e", a=NP),
        in_=mk.p1_wrT.rearrange("(a p) e -> p a e", p=128)),
        wait=list(s_ready), inc=(sLdS, 16))
    ctx.on("gpsimd", lambda e: e.dma_start(
        out=laAb[:].rearrange("p (a r) -> p a r", a=NP),
        in_=mk.p1_laA.rearrange("(a p) r -> p a r", p=128)),
        wait=list(s_ready), inc=(sLdG, 16))
    ctx.on("gpsimd", lambda e: e.dma_start(out=laBb[:], in_=mk.p1_laB[:, :]), inc=(sLdG, 16))

    sW = [ctx.dsem("p1_sW0"), ctx.dsem("p1_sW1")]
    sWF = [ctx.sem("p1_sWF0"), ctx.sem("p1_sWF1")]
    norm_sems = []
    prev_sAcc = None
    prev_sSum = None

    for hf in range(2):
        ctx.begin("half")
        if prev_sAcc is None:
            cast_wait = list(s_ready)
        else:
            cast_wait = [(prev_sAcc, NP), (prev_sSum, 1)]
        sXC = emit_cast_xbh(ctx, mk, hf, cast_wait)
        if trunc == "h1cast" and hf == 1:
            norm_sems.append((sXC, NP))
            break
        if trunc is None or (hf == 0 and trunc != "logits"):
            for o in range(NP):
                par = o % 2
                gi = hf * NP + o
                ctx.on("gpsimd",
                       lambda e, o=o, par=par: e.dma_start(
                           out=wb[par][:].rearrange("p (a o) -> p a o", a=NP),
                           in_=mk.p1_baseT[:, o * 128:(o + 1) * 128].rearrange(
                               "(a p) o -> p a o", p=128)),
                       wait=[(sWF[par], gi // 2)] if gi >= 2 else (),
                       inc=(sW[par], 16))

        sLg = ctx.sem(f"p1h{hf}_sLg")
        sLgE = ctx.sem(f"p1h{hf}_sLgE")
        for tt in range(4):
            bank = 2 + tt % 2
            dst = mk.psumf[0:128, bank * 512:bank * 512 + 8]
            if trunc == "h1mm" and hf == 1 and tt > 0:
                continue
            for a in range(NP):
                w = []
                if tt == 0 and a == 0:
                    w = list(s_ready) + [(sLdS, 16)]
                elif a == 0 and tt >= 2:
                    w = [(sLgE, tt - 1)]
                ctx.on("tensor",
                       lambda e, tt=tt, a=a, dst=dst, hf=hf: e.matmul(
                           dst,
                           mk.stream[:, a * TC + hf * TH + tt * 128:
                                     a * TC + hf * TH + (tt + 1) * 128],
                           wrt[:, a * 8:(a + 1) * 8],
                           start=(a == 0), stop=(a == NP - 1)),
                       wait=w, inc=(sLg, 1) if a == NP - 1 else None)
            if not (trunc == "h1mm" and hf == 1):
                ctx.on("vector",
                       lambda e, tt=tt, dst=dst: e.tensor_copy(logit[:, tt * 8:(tt + 1) * 8], dst),
                       wait=[(sLg, tt + 1)], inc=(sLgE, 1))
        if trunc == "h1mm" and hf == 1:
            norm_sems.append((sLg, 1))
            break
        if trunc == "logits":
            norm_sems.append((sLgE, 4))
            break
        if trunc == "h1logits" and hf == 1:
            norm_sems.append((sLgE, 4))
            break
        if trunc == "h0full" and hf == 0:
            pass

        BIGC = 1.0e30
        sT2 = ctx.sem(f"p1h{hf}_sT2")
        sTop = ctx.sem(f"p1h{hf}_sTop")
        for tt in range(4):
            L = logit[:, tt * 8:(tt + 1) * 8]
            W8 = wds[:, tt * 8:(tt + 1) * 8]
            m1 = srow[:, 0:1]; m2 = srow[:, 1:2]; nm1 = srow[:, 2:3]
            e2 = srow[:, 3:4]; dinv = srow[:, 4:5]; t8 = srow[:, 8:16]
            k = sT2.n
            ctx.on("vector", lambda e, L=L, m1=m1: e.reduce_max(m1, L, axis=mybir.AxisListType.X),
                   wait=[(sLgE, tt + 1)] + ([(sT2, k)] if k else []), inc=(sT2, 1))
            ctx.on("vector", lambda e, L=L, m1=m1, t8=t8: e.tensor_scalar(
                t8, L, m1[:, 0:1], BIGC, ALU.is_equal, ALU.mult),
                   wait=[(sT2, k + 1)], inc=(sT2, 1))
            ctx.on("vector", lambda e, L=L, t8=t8: e.tensor_tensor(t8, L, t8, ALU.subtract),
                   wait=[(sT2, k + 2)], inc=(sT2, 1))
            ctx.on("vector", lambda e, m2=m2, t8=t8: e.reduce_max(m2, t8, axis=mybir.AxisListType.X),
                   wait=[(sT2, k + 3)], inc=(sT2, 1))
            ctx.on("vector", lambda e, m1=m1, nm1=nm1: e.tensor_scalar_mul(nm1, m1, -1.0),
                   wait=[(sT2, k + 4)], inc=(sT2, 1))
            ctx.on("scalar", lambda e, m2=m2, e2=e2, nm1=nm1: e.activation(
                out=e2, in_=m2, func=AF.Exp, bias=nm1[:, 0:1]),
                   wait=[(sT2, k + 5)], inc=(sT2, 1))
            ctx.on("vector", lambda e, e2=e2: e.tensor_scalar_add(e2, e2, 1.0),
                   wait=[(sT2, k + 6)], inc=(sT2, 1))
            ctx.on("vector", lambda e, e2=e2, dinv=dinv: e.reciprocal(dinv, e2),
                   wait=[(sT2, k + 7)], inc=(sT2, 1))
            ctx.on("scalar", lambda e, L=L, W8=W8, nm1=nm1: e.activation(
                out=W8, in_=L, func=AF.Exp, bias=nm1[:, 0:1]),
                   wait=[(sT2, k + 8)], inc=(sT2, 1))
            ctx.on("vector", lambda e, L=L, t8=t8, m2=m2: e.tensor_scalar(
                t8, L, m2[:, 0:1], None, ALU.is_ge),
                   wait=[(sT2, k + 9)], inc=(sT2, 1))
            ctx.on("vector", lambda e, W8=W8, t8=t8: e.tensor_tensor(W8, W8, t8, ALU.mult),
                   wait=[(sT2, k + 10)], inc=(sT2, 1))
            ctx.on("vector", lambda e, W8=W8, dinv=dinv: e.tensor_scalar_mul(W8, W8, dinv[:, 0:1]),
                   wait=[(sT2, k + 11)], inc=(sTop, 1))

        sTrp = ctx.sem(f"p1h{hf}_sTrp")
        sTrpE = ctx.sem(f"p1h{hf}_sTrpE")
        tp = mk.psumf[0:8, 2 * 512:2 * 512 + 128]
        for tt in range(4):
            w = [(sTop, tt + 1)]
            if tt == 0:
                w.append((sLgE, 4))
            else:
                w.append((sTrpE, tt))
            ctx.on("tensor",
                   lambda e, tt=tt: e.transpose(tp, wds[:, tt * 8:(tt + 1) * 8], mk.eyef[:]),
                   wait=w, inc=(sTrp, 1))
            ctx.on("vector",
                   lambda e, tt=tt: e.tensor_copy(wdT[0:8, tt * 128:(tt + 1) * 128], tp),
                   wait=[(sTrp, tt + 1)], inc=(sTrpE, 1))
        sWfl = ctx.sem(f"p1h{hf}_sWfl")
        sWflE = ctx.sem(f"p1h{hf}_sWflE")
        ctx.on("tensor",
               lambda e: e.matmul(bankf(mk, 3), mk.rbc[0:8, :], wdT[0:8, 0:TH],
                                  start=True, stop=True),
               wait=[(sTrpE, 4)], inc=(sWfl, 1))
        ctx.on("vector",
               lambda e: e.tensor_copy(wfull[:], bankf(mk, 3)),
               wait=[(sWfl, 1)], inc=(sWflE, 1))

        sMid = ctx.sem(f"p1h{hf}_sMid")
        sMidw = ctx.sem(f"p1h{hf}_sMidw")
        for a in range(NP):
            ctx.on("tensor",
                   lambda e, a=a: e.matmul(
                       bankf(mk, 4), laAb[:, a * 128:(a + 1) * 128],
                       mk.xbh[:, a * TH:(a + 1) * TH],
                       start=(a == 0), stop=(a == NP - 1)),
                   wait=[(sLdG, 32), (sXC, NP)] if a == 0 else (),
                   inc=(sMid, 1) if a == NP - 1 else None)
        ctx.on("vector",
               lambda e: e.tensor_tensor(midw[:], bankf(mk, 4), wfull[:], ALU.mult),
               wait=[(sMid, 1), (sWflE, 1)], inc=(sMidw, 1))

        sAcc = ctx.sem(f"p1h{hf}_sAcc")
        sAdd = ctx.sem(f"p1h{hf}_sAdd")
        for o in range(NP):
            par = o % 2
            bank = o % 2
            gi = hf * NP + o
            w = [(sW[par], 16 * (gi // 2 + 1))]
            if o >= 2:
                w.append((sAdd, o - 1))
            for a in range(NP):
                ctx.on("tensor",
                       lambda e, a=a, par=par, bank=bank: e.matmul(
                           bankf(mk, bank), wb[par][:, a * 128:(a + 1) * 128],
                           mk.xbh[:, a * TH:(a + 1) * TH],
                           start=(a == 0), stop=False),
                       wait=w if a == 0 else ())
            ctx.on("tensor",
                   lambda e, o=o, bank=bank: e.matmul(
                       bankf(mk, bank), laBb[:, o * 128:(o + 1) * 128], midw[:],
                       start=False, stop=True),
                   wait=[(sMidw, 1)] if o == 0 else [],
                   inc=(sAcc, 1))
            ctx.on("vector",
                   lambda e, o=o, bank=bank, hf=hf: e.tensor_tensor(
                       mk.stream[:, o * TC + hf * TH:o * TC + hf * TH + TH],
                       mk.stream[:, o * TC + hf * TH:o * TC + hf * TH + TH],
                       bankf(mk, bank), ALU.add),
                   wait=[(sAcc, o + 1)], inc=(sAdd, 1))
            ctx.on("vector", lambda e, o=o: e.engine_nop(), inc=(sWF[par], 1))

        prev_sAcc = sAcc
        ctx.release(ln_base)
        sNorm, prev_sSum = emit_ln_half(ctx, mk, f"p1ln{hf}", hf, sAdd, 1,
                                        extra_first_wait=[(sAcc, NP)],
                                        alias_off=ab_alias)
        norm_sems.append((sNorm, NP))
        if trunc == "h0full":
            break
    ctx.release(mark)
    return norm_sems


def host_inputs_p2(inp):
    def percol(a):
        return np.ascontiguousarray(np.asarray(a).reshape(4, 4, 128).transpose(2, 0, 1)
                                    .reshape(128, 16))
    winT = np.ascontiguousarray(inp["L1_attn_Win"].T[PIDX])
    w1T = np.ascontiguousarray(inp["L1_attn_W1"].transpose(0, 2, 1))
    w2T = np.ascontiguousarray(inp["L1_attn_W2"].transpose(0, 2, 1))
    woutT = np.ascontiguousarray(inp["L1_attn_Wout"][PIDX].T)
    return {
        "p2_winT": winT, "p2_w1T": w1T, "p2_w2T": w2T, "p2_woutT": woutT,
        "p2_bin": np.ascontiguousarray(inp["L1_attn_bin"].reshape(4, 128).T),
        "p2_b1": np.ascontiguousarray(inp["L1_attn_b1"].reshape(4, 16, 128)
                                      .transpose(2, 0, 1).reshape(128, 64)),
        "p2_b2": percol(inp["L1_attn_b2"]),
        "p2_ln1w": percol(inp["L1_attn_ln1_w"]), "p2_ln1b": percol(inp["L1_attn_ln1_b"]),
        "p2_ln2w": percol(inp["L1_attn_ln2_w"]), "p2_ln2b": percol(inp["L1_attn_ln2_b"]),
        "p2_phzre": percol(inp["L1_attn_phz_re"]), "p2_phzim": percol(inp["L1_attn_phz_im"]),
        "p2_lcre": percol(inp["L1_attn_lc_re"]), "p2_lcim": percol(inp["L1_attn_lc_im"]),
        "p2_scl": percol(inp["L1_attn_sc_lin"]),
        "p2_lnlw": np.ascontiguousarray(inp["L1_attn_lnl_w"].reshape(4, 128).T),
        "p2_lnlb": np.ascontiguousarray(inp["L1_attn_lnl_b"].reshape(4, 128).T),
        "p2_bout": np.ascontiguousarray(inp["L1_attn_bout"][PIDX].reshape(32, 128).T),
    }


def declare_p2(ctx, mk):
    nc = ctx.nc
    mk.p2_winT = nc.dram_tensor("p2_winT", [D, 512], F32, kind="ExternalInput")
    mk.p2_w1T = nc.dram_tensor("p2_w1T", [4, 512, 2048], F32, kind="ExternalInput")
    mk.p2_w2T = nc.dram_tensor("p2_w2T", [4, 2048, 512], F32, kind="ExternalInput")
    mk.p2_woutT = nc.dram_tensor("p2_woutT", [512, D], F32, kind="ExternalInput")
    for nm, sh in (("bin", [128, 4]), ("b1", [128, 64]), ("b2", [128, 16]),
                   ("ln1w", [128, 16]), ("ln1b", [128, 16]), ("ln2w", [128, 16]),
                   ("ln2b", [128, 16]), ("phzre", [128, 16]), ("phzim", [128, 16]),
                   ("lcre", [128, 16]), ("lcim", [128, 16]), ("scl", [128, 16]),
                   ("lnlw", [128, 4]), ("lnlb", [128, 4]), ("bout", [128, 32])):
        setattr(mk, "p2_" + nm, nc.dram_tensor("p2_" + nm, sh, F32, kind="ExternalInput"))
    mk.p2_mask = nc.dram_tensor("p2_mask", [1, 8], F32, kind="ExternalInput")
    mk.p2_xsp = nc.dram_tensor("p2_xsp", [128, NP * TC], F32)
    mk.p2_cin = nc.dram_tensor("p2_cin", [128, 8], F32)
    mk.p2_cout = nc.dram_tensor("p2_cout", [1024, 8], F32)


def emit_ln_inner(ctx, mk, S, h, hbf, out, rows, absb, wcol, bcol, sq,
                  silu=False, first_wait=()):
    sC, sMM, sEv, sSt, sBc, sN = (S["c"], S["mm"], S["ev"], S["st"], S["bc"], S["n"])
    kc, kmm, kev, kst, kbc = sC.n, sMM.n, sEv.n, sSt.n, sBc.n
    sum5 = mk.psumf[0:1, 5 * 512:5 * 512 + TH]
    ssq4 = mk.psumf[0:1, 4 * 512:4 * 512 + TH]
    row_a = rows[0:1, 0:TC]
    row_b = rows[0:1, TC:2 * TC]
    ctx.on("vector", lambda e: e.tensor_copy(hbf[:], h[:]),
           wait=list(first_wait) + [(sBc, kbc)], inc=(sC, 1))
    ctx.on("scalar", lambda e: e.activation(out=sq[:], in_=h[:], func=AF.Square),
           wait=list(first_wait), inc=(sC, 1))
    for g, (srcb, bank) in enumerate(((hbf, sum5), (sq, ssq4))):
        for hf in range(2):
            for dt in range(4):
                w = []
                if dt == 0:
                    if g == 0 and hf == 0:
                        w = [(sC, kc + 2), (sBc, kbc)]
                    else:
                        w = [(sEv, sEv.n)]
                ctx.on("tensor",
                       lambda e, srcb=srcb, bank=bank, dt=dt, hf=hf: e.matmul(
                           bank, mk.onesb[:, 0:1],
                           srcb[:, dt * TC + hf * TH:dt * TC + hf * TH + TH],
                           start=(dt == 0), stop=(dt == 3)),
                       wait=w, inc=(sMM, 1) if dt == 3 else None)
            dstrow = row_a if g == 0 else row_b
            ctx.on("vector",
                   lambda e, dstrow=dstrow, hf=hf, bank=bank: e.tensor_copy(
                       dstrow[:, hf * TH:hf * TH + TH], bank),
                   wait=[(sMM, sMM.n)], inc=(sEv, 1))
    c = 1.0 / 512.0
    ctx.on("vector", lambda e: e.tensor_scalar_mul(row_a, row_a, c),
           wait=[(sEv, kev + 4)], inc=(sSt, 1))
    ctx.on("vector", lambda e: e.tensor_scalar_mul(row_b, row_b, c),
           wait=[(sSt, sSt.n)], inc=(sSt, 1))
    ctx.on("vector", lambda e: e.tensor_tensor(absb[0:1, 0:TC],
                                               row_a, row_a, ALU.mult),
           wait=[(sSt, sSt.n)], inc=(sSt, 1))
    ctx.on("vector", lambda e: e.tensor_tensor(row_b, row_b, absb[0:1, 0:TC],
                                               ALU.subtract),
           wait=[(sSt, sSt.n)], inc=(sSt, 1))
    ctx.on("scalar", lambda e: e.activation(out=row_b, in_=row_b, func=AF.Ln,
                                            bias=mk.epsc[0:1, 0:1]),
           wait=[(sSt, sSt.n), (mk.sConst, 1)], inc=(sSt, 1))
    ctx.on("scalar", lambda e: e.activation(out=row_b, in_=row_b, func=AF.Exp, scale=-0.5),
           wait=[(sSt, sSt.n)], inc=(sSt, 1))
    ctx.on("vector", lambda e: e.tensor_tensor(row_a, row_a, row_b, ALU.mult),
           wait=[(sSt, sSt.n)], inc=(sSt, 1))
    ctx.on("vector", lambda e: e.tensor_scalar_mul(row_a, row_a, -1.0),
           wait=[(sSt, sSt.n)], inc=(sSt, 1))
    for g, (srcr, bank) in enumerate(((row_b, 5), (row_a, 4))):
        for hf in range(2):
            w = [(sSt, kst + 8)] if (g == 0 and hf == 0) else []
            if hf == 1:
                w.append((sEv, sEv.n))
            ctx.on("tensor",
                   lambda e, srcr=srcr, bank=bank, hf=hf: e.matmul(
                       mk.psumf[:, bank * 512:bank * 512 + TH], mk.ones1f[0:1, :],
                       srcr[:, hf * TH:hf * TH + TH], start=True, stop=True),
                   wait=w, inc=(sBc, 1))
            off = (0 if g == 0 else TC) + hf * TH
            ctx.on("vector",
                   lambda e, off=off, bank=bank: e.tensor_copy(
                       absb[:, off:off + TH],
                       mk.psumf[:, bank * 512:bank * 512 + TH]),
                   wait=[(sBc, sBc.n)], inc=(sEv, 1))
    for dt in range(4):
        ctx.on("vector",
               lambda e, dt=dt: e.tensor_tensor(
                   out[:, dt * TC:(dt + 1) * TC], h[:, dt * TC:(dt + 1) * TC],
                   absb[:, 0:TC], ALU.mult),
               wait=[(sEv, kev + 8)] if dt == 0 else [(sN, sN.n)], inc=(sN, 1))
        ctx.on("vector",
               lambda e, dt=dt: e.tensor_tensor(
                   out[:, dt * TC:(dt + 1) * TC], out[:, dt * TC:(dt + 1) * TC],
                   absb[:, TC:2 * TC], ALU.add),
               wait=[(sN, sN.n)], inc=(sN, 1))
        ctx.on("vector",
               lambda e, dt=dt: e.tensor_scalar(
                   out[:, dt * TC:(dt + 1) * TC], out[:, dt * TC:(dt + 1) * TC],
                   wcol[:, dt:dt + 1], bcol[:, dt:dt + 1], ALU.mult, ALU.add),
               wait=[(sN, sN.n)], inc=(sN, 1))
        if silu:
            ctx.on("scalar",
                   lambda e, dt=dt: e.activation(
                       out=out[:, dt * TC:(dt + 1) * TC],
                       in_=out[:, dt * TC:(dt + 1) * TC], func=AF.Silu),
                   wait=[(sN, sN.n)], inc=(sN, 1))
    return


def emit_p2(ctx, mk, s_ready):
    ctx.begin("phase")
    ctx.begin("half")
    RB = 16512
    Ec = ctx.sb_at("p2_Ec", 4 * TC, F32, RB + 0)
    Es = ctx.sb_at("p2_Es", 4 * TC, F32, RB + 16 * 1024)
    br = ctx.sb_at("p2_br", 4 * TC, F32, RB + 32 * 1024)
    bi = ctx.sb_at("p2_bi", 4 * TC, F32, RB + 48 * 1024)
    hso = ctx.sb_at("p2_h", 4 * TC, F32, RB + 64 * 1024)
    ybuf = ctx.sb_at("p2_y", 4 * TC, F32, RB + 80 * 1024)
    scr = ctx.sb_at("p2_scr", TC, F32, RB + 96 * 1024)
    Ein = ctx.sb_at("p2_Ein", 2 * 512, F32, RB + 100 * 1024)
    cols = ctx.sb_at("p2_cols", 256, F32, RB + 104 * 1024)
    rows = ctx.sb_at("p2_rows", 2 * TC, F32, RB + 106 * 1024)
    absb = ctx.sb_at("p2_absb", 2 * TC, F32, RB + 114 * 1024)
    gath = ctx.sb_at("p2_gath", 64, F32, RB + 122 * 1024)
    cinb = ctx.sb_at("p2_cinb", 8, F32, RB + 122 * 1024 + 512)
    ybf = ctx.sb_at("p2_ybf", 4 * TC, BF16, RB + 128 * 1024)
    h1b = ctx.sb_at("p2_h1b", 16 * TH, BF16, RB + 136 * 1024)
    sqb = ctx.sb_at("p2_sqb", 4 * TC, BF16, RB + 152 * 1024)
    W1REG = RB + 162 * 1024
    w1b = ctx.sb_at("p2_w1b", 4 * 2048, BF16, W1REG)
    w2b = ctx.sb_at("p2_w2b", 16 * 512, BF16, W1REG + 16 * 1024)
    woutb = ctx.sb_at("p2_woutb", 4 * D, BF16, W1REG)
    winb = ctx.sb_at("p2_winb", 32 * 512, BF16, W1REG)
    prm = ctx.sb_at("p2_prm", 288, F32, RB + 204 * 1024)
    maskb = ctx.sb_at("p2_maskb", 8, F32, RB + 206 * 1024)
    wsel = ctx.sb_at("p2_wsel", 8, F32, RB + 206 * 1024 + 64)
    wbuf8 = ctx.sb_at("p2_wbuf8", 64, F32, RB + 206 * 1024 + 128)
    ln_off = RB + 194 * 1024

    PRM = {}
    off = 0
    for nm, wdt in (("bin", 4), ("b1", 64), ("b2", 16), ("ln1w", 16), ("ln1b", 16),
                    ("ln2w", 16), ("ln2b", 16), ("phzre", 16), ("phzim", 16),
                    ("lcre", 16), ("lcim", 16), ("scl", 16), ("lnlw", 4), ("lnlb", 4),
                    ("bout", 32)):
        PRM[nm] = (off, wdt)
        off += wdt
    assert off <= 288

    sLdP = ctx.dsem("p2_sLdP")
    sWinL = ctx.dsem("p2_sWinL")
    sSp = ctx.dsem("p2_sSp")
    sWL = [ctx.dsem("p2_sWL0"), ctx.dsem("p2_sWL1")]
    sWoL = ctx.dsem("p2_sWoL")
    sCC = ctx.dsem("p2_sCC")
    sXC = ctx.sem("p2_sXC")
    sWinA = ctx.sem("p2_sWinA")
    sWinE = ctx.sem("p2_sWinE")
    sCol = ctx.sem("p2_sCol")
    sTab = ctx.sem("p2_sTab")
    sFFA = ctx.sem("p2_sFFA")
    sFFE = ctx.sem("p2_sFFE")
    sFFE2 = ctx.sem("p2_sFFE2")
    sWoE = ctx.sem("p2_sWoE")
    sFFH = ctx.sem("p2_sFFH")
    sWoA = ctx.sem("p2_sWoA")
    sHFB = ctx.sem("p2_sHFB")
    sRel = [ctx.dsem("p2_sRel0"), ctx.dsem("p2_sRel1")]
    sFin = ctx.sem("p2_sFin")
    SLN = {"c": ctx.sem("p2_lnc"), "mm": ctx.sem("p2_lnmm"), "ev": ctx.sem("p2_lnev"),
           "st": ctx.sem("p2_lnst"), "bc": ctx.sem("p2_lnbc"), "n": ctx.sem("p2_lnn")}

    for nm in PRM:
        o, wdt = PRM[nm]
        dram = getattr(mk, "p2_" + nm)
        ctx.on("scalar", lambda e, dram=dram, o=o, wdt=wdt: e.dma_start(
            out=prm[:, o:o + wdt], in_=dram[:, :]),
            wait=list(s_ready) if o == 0 else [], inc=(sLdP, 16))
    ctx.on("scalar", lambda e: e.dma_start(out=maskb[0:1, 0:8], in_=mk.p2_mask[:, :]),
           inc=(sLdP, 16))
    ctx.on("gpsimd", lambda e: e.dma_start(
        out=winb[:].rearrange("p (a d) -> p a d", a=NP),
        in_=mk.p2_winT.rearrange("(a p) d -> p a d", p=128)),
        wait=list(s_ready), inc=(sWinL, 16))

    prev = None
    cast_sems = []
    for hf in range(2):
        cast_wait = list(s_ready) if hf == 0 else [(prev, 4)]
        sXCh = emit_cast_xbh(ctx, mk, hf, cast_wait)
        cast_sems.append(sXCh)
        for dt in range(4):
            idx = hf * 4 + dt
            bank = idx % 4
            for a in range(NP):
                w = []
                if a == 0:
                    w = [(sXCh, NP), (sWinL, 16)]
                    if idx >= 4:
                        w.append((sWinE, idx - 3))
                ctx.on("tensor",
                       lambda e, a=a, dt=dt, hf=hf, bank=bank: e.matmul(
                           bankf(mk, bank), winb[:, a * 512 + dt * 128:a * 512 + (dt + 1) * 128],
                           mk.xbh[:, a * TH:(a + 1) * TH],
                           start=(a == 0), stop=(a == NP - 1)),
                       wait=w, inc=(sWinA, 1) if a == NP - 1 else None)
            ctx.on("scalar",
                   lambda e, dt=dt, hf=hf, bank=bank: e.activation(
                       out=hso[:, dt * TC + hf * TH:dt * TC + hf * TH + TH],
                       in_=bankf(mk, bank), func=AF.Identity,
                       bias=prm[:, PRM["bin"][0] + dt:PRM["bin"][0] + dt + 1]),
                   wait=[(sWinA, idx + 1), (sLdP, 16 * 16), (sSp, 16)],
                   inc=(sWinE, 1))
        prev = sWinA
    ctx.on("sync", lambda e: e.dma_start(out=mk.p2_xsp[:, :], in_=mk.stream[:]),
           wait=[(cast_sems[1], NP)], inc=(sSp, 16))
    ctx.on("tensor", lambda e: e.matmul(mk.psumf[:, 3 * 512:3 * 512 + 8],
                                        mk.ones1f[0:1, :], maskb[0:1, 0:8],
                                        start=True, stop=True),
           wait=[(sLdP, 16 * 16), (sWinE, 8)], inc=(sWinA, 1))
    ctx.on("vector", lambda e: e.tensor_copy(maskb[:], mk.psumf[:, 3 * 512:3 * 512 + 8]),
           wait=[(sWinA, sWinA.n)], inc=(sTab, 1))

    def emit_wload(l):
        par = 0
        w = [(sWinA, 8)] if l == 0 else [(sFFH, 8 * l)]
        ctx.on("gpsimd",
               lambda e, l=l: e.dma_start(
                   out=w1b[:].rearrange("p (a f) -> p a f", a=4),
                   in_=mk.p2_w1T[l].rearrange("(a p) f -> p a f", p=128)),
               wait=w, inc=(sWL[0], 16))
        ctx.on("gpsimd",
               lambda e, l=l: e.dma_start(
                   out=w2b[:].rearrange("p (a d) -> p a d", a=16),
                   in_=mk.p2_w2T[l].rearrange("(a p) d -> p a d", p=128)),
               inc=(sWL[1], 16))
    emit_wload(0)

    import os as _os
    NL = int(_os.environ.get("P2_NLAYERS", "4"))
    for l in range(NL):
        P = lambda nm, i: prm[:, PRM[nm][0] + l * 4 + i:PRM[nm][0] + l * 4 + i + 1]
        kt = sTab.n
        c_er = cols[:, 0:4]; c_ei = cols[:, 4:8]; c_rho = cols[:, 8:12]
        c_r2 = cols[:, 12:16]; c_ri = cols[:, 16:20]
        c_Ec = cols[:, 20:24]; c_Es = cols[:, 24:28]
        c_R = cols[:, 28:32]
        c_t1 = cols[:, 32:36]; c_t2 = cols[:, 36:40]
        c_wr = cols[:, 40:44]; c_wi = cols[:, 44:48]
        c_nEs = cols[:, 48:52]
        c_Qc = cols[:, 64:96]; c_Qs = cols[:, 96:128]
        ctx.on("vector", lambda e, l=l: e.tensor_tensor(c_r2,
                                                        prm[:, PRM["phzre"][0] + l * 4:PRM["phzre"][0] + l * 4 + 4],
                                                        prm[:, PRM["phzre"][0] + l * 4:PRM["phzre"][0] + l * 4 + 4], ALU.mult),
               wait=[(sTab, kt), (sLdP, 16 * 16)] + ([(sSp, 16)] if l == 0 else []),
               inc=(sTab, 1))
        ctx.on("vector", lambda e, l=l: e.tensor_tensor(c_t1, prm[:, PRM["phzim"][0] + l * 4:PRM["phzim"][0] + l * 4 + 4],
                                                        prm[:, PRM["phzim"][0] + l * 4:PRM["phzim"][0] + l * 4 + 4], ALU.mult),
               wait=[(sTab, kt + 1)], inc=(sTab, 1))
        ctx.on("vector", lambda e: e.tensor_tensor(c_r2, c_r2, c_t1, ALU.add),
               wait=[(sTab, kt + 2)], inc=(sTab, 1))
        ctx.on("scalar", lambda e: e.activation(out=c_ri, in_=c_r2, func=AF.Ln),
               wait=[(sTab, kt + 3)], inc=(sTab, 1))
        ctx.on("scalar", lambda e: e.activation(out=c_ri, in_=c_ri, func=AF.Exp, scale=-0.5),
               wait=[(sTab, kt + 4)], inc=(sTab, 1))
        ctx.on("vector", lambda e, l=l: e.tensor_tensor(c_er, prm[:, PRM["phzre"][0] + l * 4:PRM["phzre"][0] + l * 4 + 4], c_ri, ALU.mult),
               wait=[(sTab, kt + 5)], inc=(sTab, 1))
        ctx.on("vector", lambda e, l=l: e.tensor_tensor(c_ei, prm[:, PRM["phzim"][0] + l * 4:PRM["phzim"][0] + l * 4 + 4], c_ri, ALU.mult),
               wait=[(sTab, kt + 6)], inc=(sTab, 1))
        ctx.on("vector", lambda e: e.tensor_tensor(c_t1, c_r2, c_ri, ALU.mult),
               wait=[(sTab, kt + 7)], inc=(sTab, 1))
        ctx.on("scalar", lambda e: e.activation(out=c_rho, in_=c_t1, func=AF.Tanh),
               wait=[(sTab, kt + 8)], inc=(sTab, 1))
        Ci = Ein[:, 0:512]; Si = Ein[:, 512:1024]
        ctx.on("vector", lambda e: e.memset(Ci[:, 0:512:128], 1.0),
               wait=[(sTab, kt + 9)], inc=(sTab, 1))
        ctx.on("vector", lambda e: e.memset(Si[:, 0:512:128], 0.0),
               wait=[(sTab, sTab.n)], inc=(sTab, 1))
        ctx.on("vector", lambda e: e.tensor_scalar_mul(c_nEs, c_ei, -1.0),
               wait=[(sTab, sTab.n)], inc=(sTab, 1))
        for step_i, m in enumerate([1, 2, 4, 8, 16, 32, 64]):
            for dt in range(4):
                base = dt * 128
                pass
            break
        em_r = cols[:, 52:56]; em_i = cols[:, 56:60]; tmpc = cols[:, 60:64]
        ctx.on("vector", lambda e: e.tensor_copy(em_r, c_er), wait=[(sTab, sTab.n)], inc=(sTab, 1))
        ctx.on("vector", lambda e: e.tensor_copy(em_i, c_nEs), wait=[(sTab, sTab.n)], inc=(sTab, 1))
        Ci3 = Ci.rearrange("p (d s) -> p d s", d=4)
        Si3 = Si.rearrange("p (d s) -> p d s", d=4)
        for m in [1, 2, 4, 8, 16, 32, 64]:
            n = m
            erb = em_r.rearrange("p (d o) -> p d o", o=1).broadcast_to([128, 4, n])
            eib = em_i.rearrange("p (d o) -> p d o", o=1).broadcast_to([128, 4, n])
            s1 = scr[:, 0:4 * n].rearrange("p (d s) -> p d s", d=4)
            s2 = scr[:, 512:512 + 4 * n].rearrange("p (d s) -> p d s", d=4)
            ctx.on("vector", lambda e, s1=s1, n=n, erb=erb: e.tensor_tensor(
                s1, Ci3[:, :, 0:n], erb, ALU.mult),
                wait=[(sTab, sTab.n)], inc=(sTab, 1))
            ctx.on("vector", lambda e, s2=s2, n=n, eib=eib: e.tensor_tensor(
                s2, Si3[:, :, 0:n], eib, ALU.mult),
                wait=[(sTab, sTab.n)], inc=(sTab, 1))
            ctx.on("vector", lambda e, s1=s1, s2=s2, m=m, n=n: e.tensor_tensor(
                Ci3[:, :, m:m + n], s1, s2, ALU.subtract),
                wait=[(sTab, sTab.n)], inc=(sTab, 1))
            ctx.on("vector", lambda e, s1=s1, n=n, eib=eib: e.tensor_tensor(
                s1, Ci3[:, :, 0:n], eib, ALU.mult),
                wait=[(sTab, sTab.n)], inc=(sTab, 1))
            ctx.on("vector", lambda e, s2=s2, n=n, erb=erb: e.tensor_tensor(
                s2, Si3[:, :, 0:n], erb, ALU.mult),
                wait=[(sTab, sTab.n)], inc=(sTab, 1))
            ctx.on("vector", lambda e, s1=s1, s2=s2, m=m, n=n: e.tensor_tensor(
                Si3[:, :, m:m + n], s1, s2, ALU.add),
                wait=[(sTab, sTab.n)], inc=(sTab, 1))
            if m < 64:
                ctx.on("vector", lambda e: e.tensor_tensor(tmpc, em_r, em_r, ALU.mult),
                       wait=[(sTab, sTab.n)], inc=(sTab, 1))
                ctx.on("vector", lambda e: e.tensor_tensor(c_t2, em_i, em_i, ALU.mult),
                       wait=[(sTab, sTab.n)], inc=(sTab, 1))
                ctx.on("vector", lambda e: e.tensor_tensor(c_t1, em_r, em_i, ALU.mult),
                       wait=[(sTab, sTab.n)], inc=(sTab, 1))
                ctx.on("vector", lambda e: e.tensor_tensor(em_r, tmpc, c_t2, ALU.subtract),
                       wait=[(sTab, sTab.n)], inc=(sTab, 1))
                ctx.on("vector", lambda e: e.tensor_scalar_mul(em_i, c_t1, 2.0),
                       wait=[(sTab, sTab.n)], inc=(sTab, 1))
        ctx.on("vector", lambda e: e.tensor_tensor(tmpc, em_r, em_r, ALU.mult),
               wait=[(sTab, sTab.n)], inc=(sTab, 1))
        ctx.on("vector", lambda e: e.tensor_tensor(c_t2, em_i, em_i, ALU.mult),
               wait=[(sTab, sTab.n)], inc=(sTab, 1))
        ctx.on("vector", lambda e: e.tensor_tensor(c_t1, em_r, em_i, ALU.mult),
               wait=[(sTab, sTab.n)], inc=(sTab, 1))
        ctx.on("vector", lambda e: e.tensor_tensor(em_r, tmpc, c_t2, ALU.subtract),
               wait=[(sTab, sTab.n)], inc=(sTab, 1))
        ctx.on("vector", lambda e: e.tensor_scalar_mul(em_i, c_t1, 2.0),
               wait=[(sTab, sTab.n)], inc=(sTab, 1))
        Qc = lambda q: cols[:, 64 + q:96:8]
        Qs = lambda q: cols[:, 96 + q:128:8]
        ctx.on("vector", lambda e: e.memset(cols[:, 64:96:8], 1.0),
               wait=[(sTab, sTab.n)], inc=(sTab, 1))
        ctx.on("vector", lambda e: e.memset(cols[:, 96:128:8], 0.0),
               wait=[(sTab, sTab.n)], inc=(sTab, 1))
        for q in range(7):
            ctx.on("vector", lambda e, q=q: e.tensor_tensor(c_t1, Qc(q), em_r, ALU.mult),
                   wait=[(sTab, sTab.n)], inc=(sTab, 1))
            ctx.on("vector", lambda e, q=q: e.tensor_tensor(c_t2, Qs(q), em_i, ALU.mult),
                   wait=[(sTab, sTab.n)], inc=(sTab, 1))
            ctx.on("vector", lambda e, q=q: e.tensor_tensor(Qc(q + 1), c_t1, c_t2, ALU.subtract),
                   wait=[(sTab, sTab.n)], inc=(sTab, 1))
            ctx.on("vector", lambda e, q=q: e.tensor_tensor(c_t1, Qc(q), em_i, ALU.mult),
                   wait=[(sTab, sTab.n)], inc=(sTab, 1))
            ctx.on("vector", lambda e, q=q: e.tensor_tensor(c_t2, Qs(q), em_r, ALU.mult),
                   wait=[(sTab, sTab.n)], inc=(sTab, 1))
            ctx.on("vector", lambda e, q=q: e.tensor_tensor(Qs(q + 1), c_t1, c_t2, ALU.add),
                   wait=[(sTab, sTab.n)], inc=(sTab, 1))
        for dt in range(4):
            qc = c_Qc[:, dt * 8:dt * 8 + 8].rearrange("p (q o) -> p q o", o=1).broadcast_to([128, 8, 128])
            qs = c_Qs[:, dt * 8:dt * 8 + 8].rearrange("p (q o) -> p q o", o=1).broadcast_to([128, 8, 128])
            ci3 = Ci[:, dt * 128:(dt + 1) * 128].rearrange("p (o s) -> p o s", o=1).broadcast_to([128, 8, 128])
            si3 = Si[:, dt * 128:(dt + 1) * 128].rearrange("p (o s) -> p o s", o=1).broadcast_to([128, 8, 128])
            ecv = Ec[:, dt * TC:(dt + 1) * TC].rearrange("p (q s) -> p q s", q=8)
            esv = Es[:, dt * TC:(dt + 1) * TC].rearrange("p (q s) -> p q s", q=8)
            scv = scr[:].rearrange("p (q s) -> p q s", q=8)
            ctx.on("vector", lambda e, ecv=ecv, qc=qc, ci3=ci3: e.tensor_tensor(ecv, qc, ci3, ALU.mult),
                   wait=[(sTab, sTab.n)], inc=(sTab, 1))
            ctx.on("vector", lambda e, scv=scv, qs=qs, si3=si3: e.tensor_tensor(scv, qs, si3, ALU.mult),
                   wait=[(sTab, sTab.n)], inc=(sTab, 1))
            ctx.on("vector", lambda e, ecv=ecv, scv=scv, dt=dt: e.tensor_tensor(
                Ec[:, dt * TC:(dt + 1) * TC], Ec[:, dt * TC:(dt + 1) * TC], scr[:], ALU.subtract),
                   wait=[(sTab, sTab.n)], inc=(sTab, 1))
            ctx.on("vector", lambda e, esv=esv, qc=qc, si3=si3: e.tensor_tensor(esv, qc, si3, ALU.mult),
                   wait=[(sTab, sTab.n)], inc=(sTab, 1))
            ctx.on("vector", lambda e, scv=scv, qs=qs, ci3=ci3: e.tensor_tensor(scv, qs, ci3, ALU.mult),
                   wait=[(sTab, sTab.n)], inc=(sTab, 1))
            ctx.on("vector", lambda e, dt=dt: e.tensor_tensor(
                Es[:, dt * TC:(dt + 1) * TC], Es[:, dt * TC:(dt + 1) * TC], scr[:], ALU.add),
                   wait=[(sTab, sTab.n)], inc=(sTab, 1))
        for _ in range(3):
            ctx.on("vector", lambda e: e.tensor_tensor(c_t1, em_r, em_r, ALU.mult),
                   wait=[(sTab, sTab.n)], inc=(sTab, 1))
            ctx.on("vector", lambda e: e.tensor_tensor(c_t2, em_i, em_i, ALU.mult),
                   wait=[(sTab, sTab.n)], inc=(sTab, 1))
            ctx.on("vector", lambda e: e.tensor_tensor(tmpc, em_r, em_i, ALU.mult),
                   wait=[(sTab, sTab.n)], inc=(sTab, 1))
            ctx.on("vector", lambda e: e.tensor_tensor(em_r, c_t1, c_t2, ALU.subtract),
                   wait=[(sTab, sTab.n)], inc=(sTab, 1))
            ctx.on("vector", lambda e: e.tensor_scalar_mul(em_i, tmpc, 2.0),
                   wait=[(sTab, sTab.n)], inc=(sTab, 1))
        ctx.on("vector", lambda e: e.tensor_copy(c_Ec, em_r), wait=[(sTab, sTab.n)], inc=(sTab, 1))
        ctx.on("vector", lambda e: e.tensor_copy(c_nEs, em_i), wait=[(sTab, sTab.n)], inc=(sTab, 1))
        ctx.on("vector", lambda e: e.tensor_scalar_mul(c_Es, em_i, -1.0),
               wait=[(sTab, sTab.n)], inc=(sTab, 1))
        ctx.on("vector", lambda e: e.tensor_copy(c_R, c_rho), wait=[(sTab, sTab.n)], inc=(sTab, 1))
        for _ in range(10):
            ctx.on("vector", lambda e: e.tensor_tensor(c_R, c_R, c_R, ALU.mult),
                   wait=[(sTab, sTab.n)], inc=(sTab, 1))

        fw = [(sWinE, 8), (sSp, 16)] if l == 0 else [(sFFH, 8 * l)]
        emit_ln_inner(ctx, mk, SLN, hso, ybf, ybuf, rows, absb,
                      prm[:, PRM["ln1w"][0] + l * 4:PRM["ln1w"][0] + (l + 1) * 4],
                      prm[:, PRM["ln1b"][0] + l * 4:PRM["ln1b"][0] + (l + 1) * 4],
                      sqb, silu=True, first_wait=fw + [(sLdP, 16 * 16)])

        kb = sTab.n
        for dt in range(4):
            ctx.on("vector", lambda e, dt=dt: e.tensor_tensor(
                br[:, dt * TC:(dt + 1) * TC], Ec[:, dt * TC:(dt + 1) * TC],
                ybuf[:, dt * TC:(dt + 1) * TC], ALU.mult),
                   wait=[(sTab, sTab.n), (SLN["n"], SLN["n"].n)], inc=(sTab, 1))
            ctx.on("vector", lambda e, dt=dt: e.tensor_tensor(
                bi[:, dt * TC:(dt + 1) * TC], Es[:, dt * TC:(dt + 1) * TC],
                ybuf[:, dt * TC:(dt + 1) * TC], ALU.mult),
                   wait=[(sTab, sTab.n)], inc=(sTab, 1))
            rhob = c_rho[:, dt:dt + 1].broadcast_to([128, TC])
            ctx.on("vector", lambda e, dt=dt, rhob=rhob: e.tensor_tensor_scan(
                scr[:], rhob, br[:, dt * TC:(dt + 1) * TC], 0.0, ALU.mult, ALU.add),
                   wait=[(sTab, sTab.n)], inc=(sTab, 1))
            ctx.on("vector", lambda e, dt=dt: e.tensor_copy(
                cinb[:, dt * 2:dt * 2 + 1], scr[:, TC - 1:TC]),
                   wait=[(sTab, sTab.n)], inc=(sTab, 1))
            ctx.on("vector", lambda e, dt=dt, rhob=rhob: e.tensor_tensor_scan(
                scr[:], rhob, bi[:, dt * TC:(dt + 1) * TC], 0.0, ALU.mult, ALU.add),
                   wait=[(sTab, sTab.n)], inc=(sTab, 1))
            ctx.on("vector", lambda e, dt=dt: e.tensor_copy(
                cinb[:, dt * 2 + 1:dt * 2 + 2], scr[:, TC - 1:TC]),
                   wait=[(sTab, sTab.n)], inc=(sTab, 1))
        kcc = sCC.n
        ctx.on("gpsimd", lambda e: e.dma_start(out=mk.p2_cin[:, :], in_=cinb[:]),
               wait=[(sTab, sTab.n)], inc=(sCC, 16))
        ctx.on("gpsimd", lambda e: e.collective_compute(
            "AllGather", ALU.bypass, replica_groups=[list(range(8))],
            ins=[mk.p2_cin[:, :]], outs=[mk.p2_cout[:, :]]),
            wait=[(sCC, kcc + 16)], inc=(sCol, 1))
        ctx.on("gpsimd", lambda e: e.dma_start(
            out=gath[:].rearrange("p (r c) -> p r c", r=8),
            in_=mk.p2_cout.rearrange("(r p) c -> p r c", p=128)),
            wait=[(sCol, sCol.n)], inc=(sCC, 16))
        kw = sTab.n
        lcr = prm[:, PRM["lcre"][0] + l * 4:PRM["lcre"][0] + l * 4 + 4]
        lci = prm[:, PRM["lcim"][0] + l * 4:PRM["lcim"][0] + l * 4 + 4]
        ctx.on("vector", lambda e, lcr=lcr: e.tensor_tensor(c_wr, c_er, lcr, ALU.mult),
               wait=[(sTab, sTab.n), (sCC, kcc + 32)], inc=(sTab, 1))
        ctx.on("vector", lambda e, lci=lci: e.scalar_tensor_tensor(
            c_wr, lci, -1.0, c_wr, ALU.mult, ALU.add) if False else e.tensor_tensor(
            c_t1, c_ei, lci, ALU.mult),
               wait=[(sTab, sTab.n)], inc=(sTab, 1))
        ctx.on("vector", lambda e: e.tensor_tensor(c_wr, c_wr, c_t1, ALU.subtract),
               wait=[(sTab, sTab.n)], inc=(sTab, 1))
        ctx.on("vector", lambda e, lci=lci: e.tensor_tensor(c_wi, c_er, lci, ALU.mult),
               wait=[(sTab, sTab.n)], inc=(sTab, 1))
        ctx.on("vector", lambda e, lcr=lcr: e.tensor_tensor(c_t1, c_ei, lcr, ALU.mult),
               wait=[(sTab, sTab.n)], inc=(sTab, 1))
        ctx.on("vector", lambda e: e.tensor_tensor(c_wi, c_wi, c_t1, ALU.add),
               wait=[(sTab, sTab.n)], inc=(sTab, 1))
        ctx.on("vector", lambda e: e.tensor_copy(wbuf8[:, 0:8:2], c_wr),
               wait=[(sTab, sTab.n)], inc=(sTab, 1))
        ctx.on("vector", lambda e: e.tensor_copy(wbuf8[:, 1:8:2], c_wi),
               wait=[(sTab, sTab.n)], inc=(sTab, 1))
        for k in range(1, 8):
            gl = gath[:, (k - 1) * 8:(k - 1) * 8 + 8]
            ctx.on("vector", lambda e: e.tensor_tensor(c_t1, c_R, c_wr, ALU.mult),
                   wait=[(sTab, sTab.n)], inc=(sTab, 1))
            ctx.on("vector", lambda e, gl=gl: e.tensor_tensor(c_t1, c_t1, gl[:, 0:8:2], ALU.add),
                   wait=[(sTab, sTab.n)], inc=(sTab, 1))
            ctx.on("vector", lambda e: e.tensor_tensor(c_t2, c_R, c_wi, ALU.mult),
                   wait=[(sTab, sTab.n)], inc=(sTab, 1))
            ctx.on("vector", lambda e, gl=gl: e.tensor_tensor(c_t2, c_t2, gl[:, 1:8:2], ALU.add),
                   wait=[(sTab, sTab.n)], inc=(sTab, 1))
            ctx.on("vector", lambda e: e.tensor_tensor(c_wr, c_Ec, c_t1, ALU.mult),
                   wait=[(sTab, sTab.n)], inc=(sTab, 1))
            ctx.on("vector", lambda e: e.tensor_tensor(tmpc, c_Es, c_t2, ALU.mult),
                   wait=[(sTab, sTab.n)], inc=(sTab, 1))
            ctx.on("vector", lambda e: e.tensor_tensor(c_wr, c_wr, tmpc, ALU.subtract),
                   wait=[(sTab, sTab.n)], inc=(sTab, 1))
            ctx.on("vector", lambda e: e.tensor_tensor(c_wi, c_Ec, c_t2, ALU.mult),
                   wait=[(sTab, sTab.n)], inc=(sTab, 1))
            ctx.on("vector", lambda e: e.tensor_tensor(tmpc, c_Es, c_t1, ALU.mult),
                   wait=[(sTab, sTab.n)], inc=(sTab, 1))
            ctx.on("vector", lambda e: e.tensor_tensor(c_wi, c_wi, tmpc, ALU.add),
                   wait=[(sTab, sTab.n)], inc=(sTab, 1))
            ctx.on("vector", lambda e, k=k: e.tensor_copy(wbuf8[:, k * 8:k * 8 + 8:2], c_wr),
                   wait=[(sTab, sTab.n)], inc=(sTab, 1))
            ctx.on("vector", lambda e, k=k: e.tensor_copy(wbuf8[:, k * 8 + 1:k * 8 + 8:2], c_wi),
                   wait=[(sTab, sTab.n)], inc=(sTab, 1))
        ctx.on("vector", lambda e: e.memset(wsel[:], 0.0),
               wait=[(sTab, sTab.n)], inc=(sTab, 1))
        for k in range(8):
            ctx.on("vector", lambda e, k=k: e.scalar_tensor_tensor(
                wsel[:], wbuf8[:, k * 8:(k + 1) * 8], maskb[:, k:k + 1], wsel[:],
                ALU.mult, ALU.add),
                   wait=[(sTab, sTab.n)], inc=(sTab, 1))
        for dt in range(4):
            rhob = c_rho[:, dt:dt + 1].broadcast_to([128, TC])
            ctx.on("vector", lambda e, dt=dt, rhob=rhob: e.tensor_tensor_scan(
                br[:, dt * TC:(dt + 1) * TC], rhob, br[:, dt * TC:(dt + 1) * TC],
                wsel[:, dt * 2:dt * 2 + 1], ALU.mult, ALU.add),
                   wait=[(sTab, sTab.n)], inc=(sTab, 1))
            ctx.on("vector", lambda e, dt=dt, rhob=rhob: e.tensor_tensor_scan(
                bi[:, dt * TC:(dt + 1) * TC], rhob, bi[:, dt * TC:(dt + 1) * TC],
                wsel[:, dt * 2 + 1:dt * 2 + 2], ALU.mult, ALU.add),
                   wait=[(sTab, sTab.n)], inc=(sTab, 1))
            ctx.on("vector", lambda e, dt=dt: e.tensor_tensor(
                scr[:], Ec[:, dt * TC:(dt + 1) * TC], br[:, dt * TC:(dt + 1) * TC], ALU.mult),
                   wait=[(sTab, sTab.n)], inc=(sTab, 1))
            ctx.on("vector", lambda e, dt=dt: e.tensor_tensor(
                br[:, dt * TC:(dt + 1) * TC], Es[:, dt * TC:(dt + 1) * TC],
                bi[:, dt * TC:(dt + 1) * TC], ALU.mult),
                   wait=[(sTab, sTab.n)], inc=(sTab, 1))
            ctx.on("vector", lambda e, dt=dt: e.tensor_tensor(
                scr[:], scr[:], br[:, dt * TC:(dt + 1) * TC], ALU.add),
                   wait=[(sTab, sTab.n)], inc=(sTab, 1))
            ctx.on("vector", lambda e, dt=dt, l=l: e.scalar_tensor_tensor(
                hso[:, dt * TC:(dt + 1) * TC], scr[:],
                prm[:, PRM["scl"][0] + l * 4 + dt:PRM["scl"][0] + l * 4 + dt + 1],
                hso[:, dt * TC:(dt + 1) * TC], ALU.mult, ALU.add),
                   wait=[(sTab, sTab.n)], inc=(sTab, 1))
        emit_ln_inner(ctx, mk, SLN, hso, ybf, ybuf, rows, absb,
                      prm[:, PRM["ln2w"][0] + l * 4:PRM["ln2w"][0] + (l + 1) * 4],
                      prm[:, PRM["ln2b"][0] + l * 4:PRM["ln2b"][0] + (l + 1) * 4],
                      sqb, silu=False, first_wait=[(sTab, sTab.n)])
        ctx.on("vector", lambda e: e.tensor_copy(ybf[:], ybuf[:]),
               wait=[(SLN["n"], SLN["n"].n)], inc=(sTab, 1))
        ktc = sTab.n
        kffa = sFFA.n; kffe = sFFE.n; kffh = sFFH.n; kffe2 = sFFE2.n
        for hf in range(2):
            hbase = kffa + hf * 20
            for ft in range(16):
                gi = hf * 16 + ft
                bank = gi % 2
                for a in range(4):
                    w = []
                    if a == 0:
                        w = [(sTab, ktc), (sWL[0], 16 * (l + 1))]
                        if gi >= 2:
                            w.append((sFFE, kffe + gi - 1))
                    ctx.on("tensor",
                           lambda e, a=a, ft=ft, hf=hf, bank=bank: e.matmul(
                               bankf(mk, bank), w1b[:, a * 2048 + ft * 128:a * 2048 + (ft + 1) * 128],
                               ybf[:, a * TC + hf * TH:a * TC + hf * TH + TH],
                               start=(a == 0), stop=(a == 3)),
                           wait=w, inc=(sFFA, 1) if a == 3 else None)
                ctx.on("scalar",
                       lambda e, ft=ft, hf=hf, bank=bank, l=l: e.activation(
                           out=h1b[:, ft * TH:(ft + 1) * TH], in_=bankf(mk, bank),
                           func=AF.Silu, bias=prm[:, PRM["b1"][0] + l * 16 + ft:PRM["b1"][0] + l * 16 + ft + 1]),
                       wait=[(sFFA, hbase + ft + 1)], inc=(sFFE, 1))
            for dt in range(4):
                gi = hf * 4 + dt
                bank = 2 + gi % 2
                for f in range(16):
                    w = []
                    if f == 0:
                        w = [(sFFE, kffe + hf * 16 + 16), (sWL[1], 16 * (l + 1))]
                        if gi >= 2:
                            w.append((sFFE2, kffe2 + gi - 1))
                        if l == 0 and gi == 1:
                            w.append((sTab, 1))
                    ctx.on("tensor",
                           lambda e, f=f, dt=dt, bank=bank: e.matmul(
                               bankf(mk, bank), w2b[:, f * 512 + dt * 128:f * 512 + (dt + 1) * 128],
                               h1b[:, f * TH:(f + 1) * TH],
                               start=(f == 0), stop=(f == 15)),
                           wait=w, inc=(sFFA, 1) if f == 15 else None)
                ctx.on("scalar",
                       lambda e, dt=dt, hf=hf, bank=bank, l=l: e.activation(
                           out=ybuf[:, dt * TC + hf * TH:dt * TC + hf * TH + TH],
                           in_=bankf(mk, bank), func=AF.Identity,
                           bias=prm[:, PRM["b2"][0] + l * 4 + dt:PRM["b2"][0] + l * 4 + dt + 1]),
                       wait=[(sFFA, sFFA.n)], inc=(sFFE2, 1))
                ctx.on("gpsimd",
                       lambda e, dt=dt, hf=hf: e.tensor_tensor(
                           hso[:, dt * TC + hf * TH:dt * TC + hf * TH + TH],
                           hso[:, dt * TC + hf * TH:dt * TC + hf * TH + TH],
                           ybuf[:, dt * TC + hf * TH:dt * TC + hf * TH + TH], ALU.add),
                       wait=[(sFFE2, sFFE2.n)], inc=(sFFH, 1))
        if l < NL - 1:
            emit_wload(l + 1)

    emit_ln_inner(ctx, mk, SLN, hso, ybf, ybuf, rows, absb,
                  prm[:, PRM["lnlw"][0]:PRM["lnlw"][0] + 4],
                  prm[:, PRM["lnlb"][0]:PRM["lnlb"][0] + 4], sqb, silu=False,
                  first_wait=[(sFFH, sFFH.n)])
    ctx.on("vector", lambda e: e.tensor_copy(ybf[:], ybuf[:]),
           wait=[(SLN["n"], SLN["n"].n)], inc=(sHFB, 1))
    ctx.on("gpsimd", lambda e: e.dma_start(
        out=woutb[:].rearrange("p (a o) -> p a o", a=4),
        in_=mk.p2_woutT.rearrange("(a p) o -> p a o", p=128)),
        wait=[(sFFH, sFFH.n)], inc=(sWoL, 16))
    wo_ev = ctx.sb_at("p2_woev", 2 * TC, F32, RB + 136 * 1024)
    wo_rl = ctx.sb_at("p2_worl", 2 * TC, F32, RB + 144 * 1024)
    for o in range(NP):
        kf = o % 2
        w = [(sSp, 16), (sHFB, 1)] if o == 0 else []
        if o >= 2:
            w.append((sFin, o - 1))
        ctx.on("sync",
               lambda e, o=o, kf=kf: e.dma_start(
                   out=wo_rl[:, kf * TC:(kf + 1) * TC],
                   in_=mk.p2_xsp[:, o * TC:(o + 1) * TC]),
               wait=w, inc=(sRel[kf], 16))
        for th in range(2):
            idx = o * 2 + th
            bank = idx % 4
            for a in range(4):
                w2 = []
                if a == 0:
                    w2 = [(sHFB, 1), (sWoL, 16)]
                    if idx >= 4:
                        w2.append((sWoE, idx - 3))
                ctx.on("tensor",
                       lambda e, a=a, o=o, th=th, bank=bank: e.matmul(
                           bankf(mk, bank),
                           woutb[:, a * D + o * 128:a * D + (o + 1) * 128],
                           ybf[:, a * TC + th * TH:a * TC + th * TH + TH],
                           start=(a == 0), stop=(a == 3)),
                       wait=w2, inc=(sWoA, 1) if a == 3 else None)
            ctx.on("scalar",
                   lambda e, o=o, th=th, bank=bank: e.activation(
                       out=wo_ev[:, (o % 2) * TC + th * TH:(o % 2) * TC + th * TH + TH],
                       in_=bankf(mk, bank), func=AF.Identity,
                       bias=prm[:, PRM["bout"][0] + o:PRM["bout"][0] + o + 1]),
                   wait=[(sWoA, idx + 1), (sHFB, 1)] +
                        ([(sFin, o - 1)] if o >= 2 else []),
                   inc=(sWoE, 1))
        ctx.on("vector",
               lambda e, o=o, kf=kf: e.tensor_tensor(
                   mk.stream[:, o * TC:(o + 1) * TC],
                   wo_ev[:, kf * TC:(kf + 1) * TC],
                   wo_rl[:, kf * TC:(kf + 1) * TC], ALU.add),
               wait=[(sWoE, 2 * o + 2), (sRel[kf], 16 * (o // 2 + 1))], inc=(sFin, 1))

    norm_sems = []
    prev = None
    for hf in range(2):
        ctx.begin("half")
        extra = [] if prev is None else [(prev, 1)]
        sNorm, sSum = emit_ln_half(ctx, mk, f"p2ln{hf}", hf, sFin, NP,
                                   extra_first_wait=extra, alias_off=ln_off,
                                   alias_all=True, flat=True)
        norm_sems.append((sNorm, NP))
        prev = sSum
    return norm_sems


def host_inputs_p3(inp):
    wgT = np.ascontiguousarray(inp["L1_ffn_Wg"].T[PIDX])
    wuT = np.ascontiguousarray(inp["L1_ffn_Wu"].T[PIDX])
    wdT3 = np.ascontiguousarray(inp["L1_ffn_Wd"].T[:, PIDX])
    ag = np.ascontiguousarray(inp["L1_ffn_ag"].reshape(86, 128).T)
    au = np.ascontiguousarray(inp["L1_ffn_au"].reshape(86, 128).T)
    ad = np.ascontiguousarray(inp["L1_ffn_ad"][PIDX].reshape(32, 128).T)
    return {"p3_wgT": wgT, "p3_wuT": wuT, "p3_wdT": wdT3,
            "p3_ag": ag, "p3_au": au, "p3_ad": ad}


def declare_p3(ctx, mk):
    nc = ctx.nc
    I3 = 11008
    mk.p3_wgT = nc.dram_tensor("p3_wgT", [D, I3], F32, kind="ExternalInput")
    mk.p3_wuT = nc.dram_tensor("p3_wuT", [D, I3], F32, kind="ExternalInput")
    mk.p3_wdT = nc.dram_tensor("p3_wdT", [I3, D], F32, kind="ExternalInput")
    mk.p3_ag = nc.dram_tensor("p3_ag", [128, 86], F32, kind="ExternalInput")
    mk.p3_au = nc.dram_tensor("p3_au", [128, 86], F32, kind="ExternalInput")
    mk.p3_ad = nc.dram_tensor("p3_ad", [128, 32], F32, kind="ExternalInput")
    mk.p3_xsp = nc.dram_tensor("p3_xsp", [128, NP * TC], F32)
    mk.p3_dacc = nc.dram_tensor("p3_dacc", [NP, 128, TC], F32)


def emit_p3(ctx, mk, s_ready):
    ctx.begin("phase")
    ctx.begin("half")
    I3 = 11008
    CH_TILES = [22, 22, 21, 21]
    CH_I0 = [0, 22, 44, 65]

    RB = 16512
    xbf = ctx.sb_at("p3_xbf", NP * TC, BF16, RB + 96 * 1024)
    hbuf = ctx.sb_at("p3_h", 22 * TC, BF16, RB)
    wg = [ctx.sb_at("p3_wg0", D, BF16, RB + 45056),
          ctx.sb_at("p3_wg1", D, BF16, RB + 45056 + 8192)]
    wu = [ctx.sb_at("p3_wu0", D, BF16, RB + 61440),
          ctx.sb_at("p3_wu1", D, BF16, RB + 61440 + 8192)]
    wd = [ctx.sb_at("p3_wd0", 22 * 128, BF16, RB + 77824),
          ctx.sb_at("p3_wd1", 22 * 128, BF16, RB + 77824 + 5632)]
    gs = [ctx.sb_at("p3_gs0", TH, F32, RB + 89088),
          ctx.sb_at("p3_gs1", TH, F32, RB + 89088 + 2048)]
    us = [ctx.sb_at("p3_us0", TH, F32, RB + 93184),
          ctx.sb_at("p3_us1", TH, F32, RB + 93184 + 2048)]
    dsg = [ctx.sb_at("p3_dsg0", TH, F32, RB + 162 * 1024),
           ctx.sb_at("p3_dsg1", TH, F32, RB + 162 * 1024 + 2048)]
    XB = RB + 128 * 1024
    fin = [ctx.sb_at("p3_fin0", 2 * TC, F32, XB),
           ctx.sb_at("p3_fin1", 2 * TC, F32, XB + 8192)]
    dst_ = [ctx.sb_at("p3_dst0", TC, F32, XB + 16384),
            ctx.sb_at("p3_dst1", TC, F32, XB + 16384 + 4096)]
    agt = ctx.sb_at("p3_agt", 86, F32, RB + 166 * 1024)
    aut = ctx.sb_at("p3_aut", 86, F32, RB + 166 * 1024 + 512)
    adt = ctx.sb_at("p3_adt", 32, F32, RB + 166 * 1024 + 1024)
    ln_off = RB + 176 * 1024

    sLd = ctx.dsem("p3_sLd")
    sSp = ctx.dsem("p3_sSp")
    sXC = ctx.sem("p3_sXC")
    sWg = [ctx.dsem("p3_sWg0"), ctx.dsem("p3_sWg1")]
    sWgF = [ctx.sem("p3_sWgF0"), ctx.sem("p3_sWgF1")]
    sWu = [ctx.dsem("p3_sWu0"), ctx.dsem("p3_sWu1")]
    sWuF = [ctx.sem("p3_sWuF0"), ctx.sem("p3_sWuF1")]
    sWd = [ctx.dsem("p3_sWd0"), ctx.dsem("p3_sWd1")]
    sWdF = [ctx.sem("p3_sWdF0"), ctx.sem("p3_sWdF1")]
    sGA = ctx.sem("p3_sGA")
    sGE = ctx.sem("p3_sGE")
    sH = ctx.sem("p3_sH")
    sDA = ctx.sem("p3_sDA")
    sDE = ctx.sem("p3_sDE")
    sDD = [ctx.dsem("p3_sDD0"), ctx.dsem("p3_sDD1")]
    sFL = [ctx.dsem("p3_sFL0"), ctx.dsem("p3_sFL1")]
    sFin = ctx.sem("p3_sFin")
    sFA = ctx.sem("p3_sFA")
    sFS = ctx.sem("p3_sFS")

    ctx.on("scalar", lambda e: e.dma_start(out=agt[:, 0:86], in_=mk.p3_ag[:, :]),
           wait=list(s_ready), inc=(sLd, 16))
    ctx.on("scalar", lambda e: e.dma_start(out=aut[:, 0:86], in_=mk.p3_au[:, :]), inc=(sLd, 16))
    ctx.on("scalar", lambda e: e.dma_start(out=adt[:, 0:32], in_=mk.p3_ad[:, :]), inc=(sLd, 16))
    ctx.on("sync", lambda e: e.dma_start(out=mk.p3_xsp[:, :], in_=mk.stream[:]),
           wait=list(s_ready), inc=(sSp, 16))
    for p in range(NP - 1, -1, -1):
        ctx.on("vector",
               lambda e, p=p: e.tensor_copy(xbf[:, p * TC:(p + 1) * TC],
                                            mk.stream[:, p * TC:(p + 1) * TC]),
               wait=[(sSp, 16)] if p == NP - 1 else [(sXC, NP - 1 - p)],
               inc=(sXC, 1))

    def emit_dn_load(ld):
        if ld >= 4 * NP:
            return
        ic2 = ld // NP
        o2 = ld % NP
        nt2 = CH_TILES[ic2]
        i0 = CH_I0[ic2] * 128
        ni = nt2 * 128
        par = ld % 2
        w = [(sWdF[par], ld // 2)] if ld >= 2 else [(sXC, NP)]
        ctx.on("gpsimd",
               lambda e, i0=i0, ni=ni, o2=o2, par=par, nt2=nt2: e.dma_start(
                   out=wd[par][:, 0:nt2 * 128].rearrange("p (a o) -> p a o", a=nt2),
                   in_=mk.p3_wdT[i0:i0 + ni, o2 * 128:(o2 + 1) * 128].rearrange(
                       "(a p) o -> p a o", p=128)),
               wait=w, inc=(sWd[par], 16))

    nlg = 0; nld = 0; ng = 0; nd = 0
    emit_dn_load(0)
    emit_dn_load(1)
    for ic in range(4):
        nt = CH_TILES[ic]
        for j in range(nt):
            it = CH_I0[ic] + j
            lg = nlg + j
            par = lg % 2
            w = [(sXC, NP)] if lg < 2 else [(sWgF[par], lg // 2)]
            ctx.on("gpsimd",
                   lambda e, it=it, par=par: e.dma_start(
                       out=wg[par][:].rearrange("p (a i) -> p a i", a=NP),
                       in_=mk.p3_wgT[:, it * 128:(it + 1) * 128].rearrange(
                           "(a p) i -> p a i", p=128)),
                   wait=w, inc=(sWg[par], 16))
            w = [(sWuF[par], lg // 2)] if lg >= 2 else [(sXC, NP)]
            ctx.on("gpsimd",
                   lambda e, it=it, par=par: e.dma_start(
                       out=wu[par][:].rearrange("p (a i) -> p a i", a=NP),
                       in_=mk.p3_wuT[:, it * 128:(it + 1) * 128].rearrange(
                           "(a p) i -> p a i", p=128)),
                   wait=w, inc=(sWu[par], 16))
        for j in range(nt):
            it = CH_I0[ic] + j
            parw = nlg % 2
            for th in range(2):
                for a in range(NP):
                    w = []
                    if a == 0:
                        w = [(sWg[parw], 16 * (nlg // 2 + 1))]
                        if ng >= 2:
                            w.append((sGE, 2 * (ng - 1)))
                    ctx.on("tensor",
                           lambda e, a=a, th=th, parw=parw: e.matmul(
                               bankf(mk, th), wg[parw][:, a * 128:(a + 1) * 128],
                               xbf[:, a * TC + th * TH:a * TC + th * TH + TH],
                               start=(a == 0), stop=(a == NP - 1)),
                           wait=w, inc=(sGA, 1) if a == NP - 1 else None)
                for a in range(NP):
                    w = []
                    if a == 0:
                        w = [(sWu[parw], 16 * (nlg // 2 + 1))]
                        if ng >= 2:
                            w.append((sGE, 2 * (ng - 1)))
                    ctx.on("tensor",
                           lambda e, a=a, th=th, parw=parw: e.matmul(
                               bankf(mk, 2 + th), wu[parw][:, a * 128:(a + 1) * 128],
                               xbf[:, a * TC + th * TH:a * TC + th * TH + TH],
                               start=(a == 0), stop=(a == NP - 1)),
                           wait=w, inc=(sGA, 1) if a == NP - 1 else None)
                kge = ng % 2
                ctx.on("scalar",
                       lambda e, th=th, it=it, kge=kge: e.activation(
                           out=gs[kge][:], in_=bankf(mk, th), func=AF.Silu,
                           scale=agt[:, it:it + 1]),
                       wait=[(sGA, 2 * ng + 2), (sLd, 48)] +
                            ([(sH, ng - 1)] if ng >= 2 else []),
                       inc=(sGE, 1))
                ctx.on("scalar",
                       lambda e, th=th, it=it, kge=kge: e.activation(
                           out=us[kge][:], in_=bankf(mk, 2 + th), func=AF.Identity,
                           scale=aut[:, it:it + 1]),
                       inc=(sGE, 1))
                wh = [(sGE, 2 * ng + 2)]
                if j == 0 and th == 0 and ic >= 1:
                    wh.append((sDA, 64 * ic))
                ctx.on("vector",
                       lambda e, j=j, th=th, kge=kge: e.tensor_tensor(
                           hbuf[:, j * TC + th * TH:j * TC + th * TH + TH],
                           gs[kge][:], us[kge][:], ALU.mult),
                       wait=wh, inc=(sH, 1))
                ng += 1
            ctx.on("vector", lambda e: e.engine_nop(), wait=[(sGA, 2 * ng)],
                   inc=(sWgF[parw], 1))
            ctx.on("vector", lambda e: e.engine_nop(), inc=(sWuF[parw], 1))
            nlg += 1
        for o in range(NP):
            parw = nld % 2
            for th in range(2):
                bankd = 4 + nd % 2
                for j in range(nt):
                    w = []
                    if j == 0:
                        w = [(sWd[parw], 16 * (nld // 2 + 1)), (sH, 2 * (CH_I0[ic] + nt))]
                        if nd >= 2:
                            w.append((sDE, nd - 1))
                    ctx.on("tensor",
                           lambda e, j=j, th=th, parw=parw, bankd=bankd, nt=nt: e.matmul(
                               bankf(mk, bankd), wd[parw][:, j * 128:(j + 1) * 128],
                               hbuf[:, j * TC + th * TH:j * TC + th * TH + TH],
                               start=(j == 0), stop=(j == nt - 1)),
                           wait=w, inc=(sDA, 1) if j == nt - 1 else None)
                kd = nd % 2
                ctx.on("vector",
                       lambda e, bankd=bankd, kd=kd: e.tensor_copy(dsg[kd][:], bankf(mk, bankd)),
                       wait=[(sDA, nd + 1)] + ([(sDD[kd], 16 * (nd // 2))] if nd >= 2 else []),
                       inc=(sDE, 1))
                op = ALU.bypass if ic == 0 else ALU.add
                ctx.on("gpsimd",
                       lambda e, o=o, th=th, kd=kd, op=op: e.dma_start(
                           out=mk.p3_dacc[o, :, th * TH:th * TH + TH], in_=dsg[kd][:],
                           accum_op=op),
                       wait=[(sDE, nd + 1)] + ([(sDD[kd], 16 * (nd // 2))] if nd >= 2 else []),
                       inc=(sDD[kd], 16))
                nd += 1
            ctx.on("vector", lambda e, parw=parw: e.engine_nop(),
                   wait=[(sDA, nd)], inc=(sWdF[parw], 1))
            emit_dn_load(nld + 2)
            nld += 1

    for o in range(NP):
        kf = o % 2
        w = [(sDD[0], 16 * 128), (sDD[1], 16 * 128)] if o == 0 else []
        if o >= 2:
            w.append((sFin, o - 1))
        ctx.on("sync",
               lambda e, o=o, kf=kf: e.dma_start(
                   out=fin[kf][:, 0:TC], in_=mk.p3_dacc[o, :, :]),
               wait=w, inc=(sFL[kf], 16))
        ctx.on("sync",
               lambda e, o=o, kf=kf: e.dma_start(
                   out=fin[kf][:, TC:2 * TC], in_=mk.p3_xsp[:, o * TC:(o + 1) * TC]),
               inc=(sFL[kf], 16))
        nf = sFL[kf].n
        ctx.on("scalar",
               lambda e, o=o, kf=kf: e.activation(
                   out=dst_[kf][:], in_=fin[kf][:, 0:TC], func=AF.Identity,
                   scale=adt[:, o:o + 1]),
               wait=[(sFL[kf], nf)] + ([(sFin, o - 1)] if o >= 2 else []),
               inc=(sFS, 1))
        ctx.on("vector",
               lambda e, o=o, kf=kf: e.tensor_tensor(
                   mk.stream[:, o * TC:(o + 1) * TC], dst_[kf][:],
                   fin[kf][:, TC:2 * TC], ALU.add),
               wait=[(sFS, o + 1)], inc=(sFin, 1))

    norm_sems = []
    prev = None
    for hf in range(2):
        ctx.begin("half")
        extra = [] if prev is None else [(prev, 1)]
        sNorm, sSum = emit_ln_half(ctx, mk, f"p3ln{hf}", hf, sFin, NP,
                                   extra_first_wait=extra, alias_off=ln_off, alias_all=True,
                                   flat=True)
        norm_sems.append((sNorm, NP))
        prev = sSum
    return norm_sems


def build_kernel(phases=("p1",)):
    nc = bass.Bass("TRN2", target_bir_lowering=False, debug=False)
    ctx = Ctx(nc)
    mk = MK()
    declare_common(ctx, mk)
    if "p0" in phases:
        declare_p0(ctx, mk)
    if "p1" in phases:
        declare_p1(ctx, mk)
    if "p2" in phases:
        declare_p2(ctx, mk)
    if "p3" in phases:
        declare_p3(ctx, mk)
    emit_init_loads(ctx, mk)
    ready = mk.init_ready
    if "p0" in phases:
        norm = emit_p0(ctx, mk, ready)
        ready = norm
    if "p1" in phases:
        import os as _os
        norm = emit_p1(ctx, mk, ready, trunc=_os.environ.get("P1_TRUNC"))
        ready = norm
    if "p2" in phases:
        norm = emit_p2(ctx, mk, ready)
        ready = norm
    if "p3" in phases:
        norm = emit_p3(ctx, mk, ready)
        ready = norm
    emit_store_out(ctx, mk, list(ready))
    ctx.finalize()
    return nc


def kernel(**inputs):
    inp = {k: np.asarray(v) for k, v in inputs.items()}
    x = np.asarray(inp["L0_attn_x"], np.float32)
    shards = shard_x(x)
    ins = dict(host_consts(), **host_inputs_p0(inp), **host_inputs_p1(inp),
               **host_inputs_p2(inp), **host_inputs_p3(inp))
    nc = build_kernel(("p0", "p1", "p2", "p3"))
    in_maps = []
    for c in range(NCORE):
        m = np.zeros((1, 8), np.float32)
        m[0, c] = 1.0
        in_maps.append(dict(ins, x_in=shards[c], p2_mask=m))
    res = run_bass_kernel_spmd(nc, in_maps, core_ids=list(range(NCORE)))
    parts = [np.asarray(res.results[c]["y_out"]) for c in range(NCORE)]
    return unshard_out(parts).astype(np.float32)


# version 03
# speedup vs baseline: 1.0350x; 1.0028x over previous
import numpy as np
import ml_dtypes
import concourse.bass as bass
import concourse.mybir as mybir
from concourse.bass_utils import run_bass_kernel_spmd

F32 = mybir.dt.float32
BF16 = mybir.dt.bfloat16
AF = mybir.ActivationFunctionType
ALU = mybir.AluOpType

NCORE = 8
S = 8192
D = 4096
TC = S // NCORE
NP = D // 128
TH = 512
EPS = 1e-5

PIDX = ((np.arange(D) % 128) * 32 + (np.arange(D) // 128)).astype(np.int64)
BF = ml_dtypes.bfloat16


class Sem:
    __slots__ = ("h", "_n", "name")
    def __init__(self, h, name):
        self.h = h; self._n = 0; self.name = name
    @property
    def n(self):
        return self._n
    def bump(self, k):
        self._n += k
    def absval(self, v):
        return v


class SemView:
    __slots__ = ("s", "base", "name")
    def __init__(self, s, name):
        self.s = s; self.base = s.n; self.name = name
    @property
    def h(self):
        return self.s.h
    @property
    def n(self):
        return self.s.n - self.base
    def bump(self, k):
        self.s.bump(k)
    def absval(self, v):
        return self.base + v


class Ctx:
    ENGINES = ("sync", "tensor", "vector", "scalar", "gpsimd")

    def __init__(self, nc):
        self.nc = nc
        self.q = {e: [] for e in self.ENGINES}
        self.sbuf_off = 16512
        self.sbuf_limit = nc.SBUF_PARTITION_SIZE_BYTES
        self.peak = 0

    def sem_global(self, name):
        return Sem(self.nc.alloc_semaphore(name), name)

    def begin(self, level):
        if not hasattr(self, "_pools"):
            self._pools = {"phase": [[], []], "half": [[], []]}
            self._dpools = {"phase": [[], []], "half": [[], []]}
            self._gen = {"phase": 0, "half": 0}
        g = self._gen[level] % 2
        self._cur = self._pools[level][g]
        self._dcur = self._dpools[level][g]
        self._gen[level] += 1
        self._pos = 0
        self._dpos = 0

    def _draw(self, pool, pos, name):
        if pos < len(pool):
            s = pool[pos]
        else:
            s = Sem(self.nc.alloc_semaphore(name), name)
            pool.append(s)
        return SemView(s, name)

    def sem(self, name):
        if not hasattr(self, "_cur") or self._cur is None:
            return self.sem_global(name)
        v = self._draw(self._cur, self._pos, name)
        self._pos += 1
        return v

    def dsem(self, name):
        if not hasattr(self, "_dcur") or self._dcur is None:
            return self.sem_global(name)
        v = self._draw(self._dcur, self._dpos, name)
        self._dpos += 1
        return v

    def sb(self, name, cols, dtype):
        nbytes = cols * (4 if dtype == F32 else 2)
        off = (self.sbuf_off + 63) & ~63
        assert off + nbytes <= self.sbuf_limit, (name, off, nbytes)
        t = self.nc.alloc_sbuf_tensor_at(name, [128, cols], dtype, offset=off)
        self.sbuf_off = off + nbytes
        self.peak = max(self.peak, self.sbuf_off)
        return t

    def sb_at(self, name, cols, dtype, off):
        nbytes = cols * (4 if dtype == F32 else 2)
        assert off + nbytes <= self.sbuf_limit, (name, off, nbytes)
        return self.nc.alloc_sbuf_tensor_at(name, [128, cols], dtype, offset=off)

    def mark(self):
        return self.sbuf_off

    def release(self, mark):
        self.sbuf_off = mark

    def on(self, eng, fn, wait=(), inc=None, incs=()):
        allincs = list(incs) + ([inc] if inc is not None else [])
        for s, k in allincs:
            s.bump(k)
        w = [(s.h, s.absval(v)) for (s, v) in wait]
        def run(e, w=w, fn=fn, allincs=allincs):
            for h, v in w:
                e.wait_ge(h, v)
            r = fn(e)
            for s, k in allincs:
                r = r.then_inc(s.h, k)
        self.q[eng].append(run)

    def finalize(self):
        nc = self.nc
        with nc.Block() as block:
            @block.sync
            def _(e, q=self.q["sync"]):
                for f in q: f(e)
            @block.tensor
            def _(e, q=self.q["tensor"]):
                for f in q: f(e)
            @block.vector
            def _(e, q=self.q["vector"]):
                for f in q: f(e)
            @block.scalar
            def _(e, q=self.q["scalar"]):
                for f in q: f(e)
            @block.gpsimd
            def _(e, q=self.q["gpsimd"]):
                for f in q: f(e)


def w_only(ctx, eng, *pairs):
    w = [(s.h, s.absval(v)) for (s, v) in pairs]
    def run(e, w=w):
        for h, v in w:
            e.wait_ge(h, v)
    ctx.q[eng].append(run)


def host_consts():
    rbc = np.zeros((8, 128), dtype=np.float32)
    for e in range(8):
        rbc[e, e * 16:(e + 1) * 16] = 2.0
    return {
        "c_eyef": np.eye(128, dtype=np.float32),
        "c_eyeb": np.eye(128, dtype=BF),
        "c_onesb": np.ones((128, 1), dtype=BF),
        "c_ones1f": np.ones((1, 128), dtype=np.float32),
        "c_rbc": rbc,
    }


def host_inputs_p0(inp):
    wq = np.ascontiguousarray(inp["L0_attn_Wq"].transpose(2, 0, 1)[:, PIDX, :])
    wk = np.ascontiguousarray(inp["L0_attn_Wk"].transpose(2, 0, 1)[:, PIDX, :])
    wv = np.ascontiguousarray(inp["L0_attn_Wv"].transpose(2, 0, 1)[:, PIDX, :])
    return {"p0_wq": wq, "p0_wk": wk, "p0_wv": wv,
            "p0_bq": np.ascontiguousarray(inp["L0_attn_bq"]),
            "p0_bk": np.ascontiguousarray(inp["L0_attn_bk"]),
            "p0_bv": np.ascontiguousarray(inp["L0_attn_bv"])}


def host_inputs_p1(inp):
    baseT = np.ascontiguousarray(inp["L0_ffn_base_W"].T[PIDX][:, PIDX])
    wrT = np.ascontiguousarray(inp["L0_ffn_router_W"].T[PIDX])
    laA = np.ascontiguousarray(inp["L0_ffn_lora_A"].transpose(1, 0, 2).reshape(D, 128)[PIDX])
    laB = np.ascontiguousarray(inp["L0_ffn_lora_B"].reshape(128, D)[:, PIDX])
    return {"p1_baseT": baseT, "p1_wrT": wrT, "p1_laA": laA, "p1_laB": laB}


def shard_x(x):
    x2 = np.ascontiguousarray(x.reshape(S, D)[:, PIDX].T)
    return [np.ascontiguousarray(x2[:, c * TC:(c + 1) * TC]) for c in range(NCORE)]


def unshard_out(parts):
    full = np.concatenate(parts, axis=1)
    out = np.empty((S, D), dtype=np.float32)
    out[:, PIDX] = full.T
    return out.reshape(1, S, D)


class MK:
    pass


def bankf(mk, b, n=512):
    return mk.psumf[:, b * 512:b * 512 + n]


def declare_common(ctx, mk):
    nc = ctx.nc
    mk.x_in = nc.dram_tensor("x_in", [D, TC], F32, kind="ExternalInput")
    mk.y_out = nc.dram_tensor("y_out", [D, TC], F32, kind="ExternalOutput")
    mk.c_eyef = nc.dram_tensor("c_eyef", [128, 128], F32, kind="ExternalInput")
    mk.c_eyeb = nc.dram_tensor("c_eyeb", [128, 128], BF16, kind="ExternalInput")
    mk.c_onesb = nc.dram_tensor("c_onesb", [128, 1], BF16, kind="ExternalInput")
    mk.c_ones1f = nc.dram_tensor("c_ones1f", [1, 128], F32, kind="ExternalInput")
    mk.c_rbc = nc.dram_tensor("c_rbc", [8, 128], F32, kind="ExternalInput")

    mk.stream = ctx.sb("stream", NP * TC, F32)
    mk.xbh = ctx.sb("xbh", NP * TH, BF16)
    mk.eyef = ctx.sb("eyef", 128, F32)
    mk.eyeb = ctx.sb("eyeb", 128, BF16)
    mk.onesb = ctx.sb("onesb", 1, BF16)
    mk.ones1f = ctx.sb("ones1f", 128, F32)
    mk.rbc = ctx.sb("rbc", 128, F32)
    mk.epsc = ctx.sb("epsc", 1, F32)
    mk.psumf = nc.alloc_psum_tensor("psumf", [128, 6 * 512], F32)
    mk.psumb = nc.alloc_psum_tensor("psumb", [128, 2048], BF16)
    mk.sInit = ctx.sem("sInit")


def emit_init_loads(ctx, mk):
    s = mk.sInit
    for src, dst in ((mk.c_eyef, mk.eyef[:]), (mk.c_eyeb, mk.eyeb[:]),
                     (mk.c_onesb, mk.onesb[:]), (mk.c_ones1f, mk.ones1f[0:1, :]),
                     (mk.c_rbc, mk.rbc[0:8, :])):
        ctx.on("sync", lambda e, src=src, dst=dst: e.dma_start(out=dst, in_=src[:, :]),
               inc=(s, 16))
    sX2 = ctx.sem("sInitX")
    xrA = mk.x_in[0:16 * 128, :].rearrange("(a p) t -> p a t", p=128)
    xrB = mk.x_in[16 * 128:D, :].rearrange("(a p) t -> p a t", p=128)
    soA = mk.stream[:, 0:16 * TC].rearrange("p (a t) -> p a t", a=16)
    soB = mk.stream[:, 16 * TC:NP * TC].rearrange("p (a t) -> p a t", a=16)
    ctx.on("sync", lambda e: e.dma_start(out=soA, in_=xrA), inc=(s, 16))
    ctx.on("scalar", lambda e: e.dma_start(out=soB, in_=xrB), inc=(sX2, 16))
    mk.sConst = ctx.sem("sConst")
    ctx.on("vector", lambda e: e.memset(mk.epsc[:], EPS), inc=(mk.sConst, 1))
    mk.init_ready = [(s, 96), (sX2, 16)]


def emit_cast_xbh(ctx, mk, hf, waits):
    sC = ctx.sem(f"cast_h{hf}_{len(ctx.q['vector'])}")
    for p in range(NP):
        ctx.on("vector",
               lambda e, p=p: e.tensor_copy(
                   mk.xbh[:, p * TH:(p + 1) * TH],
                   mk.stream[:, p * TC + hf * TH:p * TC + hf * TH + TH]),
               wait=waits if p == 0 else (), inc=(sC, 1))
    return sC


def emit_store_out(ctx, mk, waits):
    if len(waits) != 2:
        sSt = ctx.sem("sSt")
        yr = mk.y_out.rearrange("(a p) t -> p a t", p=128)
        so = mk.stream[:].rearrange("p (a t) -> p a t", a=NP)
        ctx.on("sync", lambda e, yr=yr, so=so: e.dma_start(out=yr, in_=so),
               wait=waits, inc=(sSt, 16))
        w_only(ctx, "sync", (sSt, 16))
        return
    sStA = ctx.sem("sStA")
    sStB = ctx.sem("sStB")
    so3 = mk.stream[:].rearrange("p (a t) -> p a t", a=NP)
    for hf in range(2):
        yr3 = mk.y_out[:, hf * TH:hf * TH + TH].rearrange("(a p) t -> p a t", p=128)
        w = [waits[hf]]
        ctx.on("sync",
               lambda e, yr3=yr3, hf=hf: e.dma_start(
                   out=yr3[:, 0:16, :], in_=so3[:, 0:16, hf * TH:hf * TH + TH]),
               wait=w, inc=(sStA, 16))
        ctx.on("gpsimd",
               lambda e, yr3=yr3, hf=hf: e.dma_start(
                   out=yr3[:, 16:NP, :], in_=so3[:, 16:NP, hf * TH:hf * TH + TH]),
               wait=w, inc=(sStB, 16))
    w_only(ctx, "sync", (sStA, 32), (sStB, 32))


def emit_ln_half(ctx, mk, tag, hf, s_add, add_target, extra_first_wait=(), alias_off=None,
                 flat=False, alias_all=False):
    mark = ctx.mark()
    ybf = mk.xbh
    if alias_all and alias_off is not None:
        sq = ctx.sb_at(f"{tag}_sq", 2 * TH, BF16, alias_off)
        rab = ctx.sb_at(f"{tag}_rab", 2 * TH, F32, alias_off + 2 * TH * 2)
        absb = ctx.sb_at(f"{tag}_ab", 2 * TH, F32, alias_off + 2 * TH * 2 + 2 * TH * 4)
    elif alias_off is not None:
        sq = ctx.sb(f"{tag}_sq", 2 * TH, BF16)
        rab = ctx.sb(f"{tag}_rab", 2 * TH, F32)
        absb = ctx.sb_at(f"{tag}_ab", 2 * TH, F32, alias_off)
    else:
        sq = ctx.sb(f"{tag}_sq", 2 * TH, BF16)
        rab = ctx.sb(f"{tag}_rab", 2 * TH, F32)
        absb = ctx.sb(f"{tag}_ab", 2 * TH, F32)

    sCast = ctx.sem(f"{tag}_cast")
    sSq = ctx.sem(f"{tag}_sq")
    sMM = ctx.sem(f"{tag}_mm")
    sSum = ctx.sem(f"{tag}_sum")
    sStat = ctx.sem(f"{tag}_stat")
    sBc = ctx.sem(f"{tag}_bc")
    sBcE = ctx.sem(f"{tag}_bce")
    sNorm = ctx.sem(f"{tag}_norm")

    row_a = rab[0:1, 0:TH]
    row_b = rab[0:1, TH:2 * TH]
    psum6 = mk.psumf[0:1, 5 * 512:5 * 512 + TH]
    psum5 = mk.psumf[0:1, 4 * 512:4 * 512 + TH]

    for p in range(NP):
        st = mk.stream[:, p * TC + hf * TH:p * TC + hf * TH + TH]
        tw = [(s_add, add_target)] if flat else [(s_add, add_target + p)]
        ctx.on("vector",
               lambda e, p=p, st=st: e.tensor_copy(ybf[:, p * TH:(p + 1) * TH], st),
               wait=(tw + (list(extra_first_wait) if p == 0 else [])),
               inc=(sCast, 1))
        ctx.on("scalar",
               lambda e, p=p, st=st: e.activation(
                   out=sq[:, (p % 2) * TH:(p % 2) * TH + TH], in_=st, func=AF.Square),
               wait=tw + ([(sMM, p - 1)] if p >= 2 else []) +
                    (list(extra_first_wait) if p == 0 else []),
               inc=(sSq, 1))
        ctx.on("tensor",
               lambda e, p=p: e.matmul(
                   psum6, mk.onesb[:, 0:1], ybf[:, p * TH:(p + 1) * TH],
                   start=(p == 0), stop=(p == NP - 1)),
               wait=[(sCast, p + 1)])
        ctx.on("tensor",
               lambda e, p=p: e.matmul(
                   psum5, mk.onesb[:, 0:1], sq[:, (p % 2) * TH:(p % 2) * TH + TH],
                   start=(p == 0), stop=(p == NP - 1)),
               wait=[(sSq, p + 1)],
               inc=(sMM, 1) if p < NP - 1 else (sSum, 1))

    c = 1.0 / D
    ctx.on("vector", lambda e: e.tensor_scalar_mul(row_b, psum6, c),
           wait=[(sSum, 1)], inc=(sStat, 1))
    ctx.on("vector", lambda e: e.tensor_scalar_mul(row_a, psum5, c),
           wait=[(sStat, 1)], inc=(sStat, 1))
    ctx.on("vector", lambda e: e.tensor_tensor(psum6, row_b, row_b, ALU.mult),
           wait=[(sStat, 2)], inc=(sStat, 1))
    ctx.on("vector", lambda e: e.tensor_tensor(row_a, row_a, psum6, ALU.subtract),
           wait=[(sStat, 3)], inc=(sStat, 1))
    ctx.on("scalar", lambda e: e.activation(out=row_a, in_=row_a, func=AF.Ln,
                                            bias=mk.epsc[0:1, 0:1]),
           wait=[(sStat, 4), (mk.sConst, 1)], inc=(sStat, 1))
    ctx.on("scalar", lambda e: e.activation(out=row_a, in_=row_a, func=AF.Exp, scale=-0.5),
           wait=[(sStat, 5)], inc=(sStat, 1))
    ctx.on("vector", lambda e: e.tensor_tensor(row_b, row_b, row_a, ALU.mult),
           wait=[(sStat, 6)], inc=(sStat, 1))
    ctx.on("vector", lambda e: e.tensor_scalar_mul(row_b, row_b, -1.0),
           wait=[(sStat, 7)], inc=(sStat, 1))

    ctx.on("tensor", lambda e: e.matmul(bankf(mk, 5), mk.ones1f[0:1, :], row_a,
                                        start=True, stop=True),
           wait=[(sStat, 8)], inc=(sBc, 1))
    ctx.on("tensor", lambda e: e.matmul(bankf(mk, 4), mk.ones1f[0:1, :], row_b,
                                        start=True, stop=True),
           wait=[(sStat, 8)], inc=(sBc, 1))
    ctx.on("vector", lambda e: e.tensor_copy(absb[:, 0:TH], bankf(mk, 5)),
           wait=[(sBc, 2)], inc=(sBcE, 1))
    ctx.on("vector", lambda e: e.tensor_copy(absb[:, TH:2 * TH], bankf(mk, 4)),
           wait=[(sBcE, 1)], inc=(sBcE, 1))

    sNm = ctx.sem(f"{tag}_nm")
    for p in range(NP):
        st = mk.stream[:, p * TC + hf * TH:p * TC + hf * TH + TH]
        ctx.on("vector",
               lambda e, st=st: e.tensor_tensor(st, st, absb[:, 0:TH], ALU.mult),
               wait=[(sBcE, 2)] if p == 0 else (), inc=(sNm, 1))
        ctx.on("vector",
               lambda e, st=st: e.tensor_tensor(st, st, absb[:, TH:2 * TH], ALU.add),
               wait=[(sNm, p + 1)], inc=(sNorm, 1))
    ctx.release(mark)
    return sNorm, sSum


def declare_p0(ctx, mk):
    nc = ctx.nc
    mk.p0_wq = nc.dram_tensor("p0_wq", [32, D, 128], F32, kind="ExternalInput")
    mk.p0_wk = nc.dram_tensor("p0_wk", [8, D, 128], F32, kind="ExternalInput")
    mk.p0_wv = nc.dram_tensor("p0_wv", [8, D, 128], F32, kind="ExternalInput")
    mk.p0_bq = nc.dram_tensor("p0_bq", [128, 32], F32, kind="ExternalInput")
    mk.p0_bk = nc.dram_tensor("p0_bk", [128, 8], F32, kind="ExternalInput")
    mk.p0_bv = nc.dram_tensor("p0_bv", [128, 8], F32, kind="ExternalInput")
    mk.p0_qd = nc.dram_tensor("p0_qd", [8, TH, 128], BF16)
    mk.p0_kd = nc.dram_tensor("p0_kd", [8, TH, 128], BF16)


def emit_p0(ctx, mk, s_ready):
    ctx.begin("phase")
    mark = ctx.mark()
    wbuf = [ctx.sb("p0_wb0", D, BF16), ctx.sb("p0_wb1", D, BF16)]
    qc_base = (ctx.sbuf_off + 63) & ~63
    qc = [ctx.sb("p0_qc0", TH, BF16), ctx.sb("p0_qc1", TH, BF16)]
    vc = ctx.sb("p0_vc", 8 * TH, BF16)
    tev = [ctx.sb("p0_tev0", 128, BF16), ctx.sb("p0_tev1", 128, BF16)]
    qt_alias = (ctx.sbuf_off + 63) & ~63
    qt = [ctx.sb("p0_qt0", 16 * 128, BF16), ctx.sb("p0_qt1", 16 * 128, BF16)]
    kt = [ctx.sb("p0_kt0", 16 * 128, BF16), ctx.sb("p0_kt1", 16 * 128, BF16)]
    vcat = [ctx.sb("p0_vcat0", 9 * 56, BF16), ctx.sb("p0_vcat1", 9 * 56, BF16)]
    ubuf = [ctx.sb_at("p0_u0", 512, BF16, qc_base),
            ctx.sb_at("p0_u1", 512, BF16, qc_base + 1024)]
    zinv = ctx.sb("p0_zinv", 56, F32)
    tmpa = ctx.sb("p0_tmpa", 56, F32)
    bq = ctx.sb("p0_bq", 32, F32)
    bk = ctx.sb("p0_bk", 8, F32)
    bv = ctx.sb("p0_bv", 8, F32)
    bqc = ctx.sb("p0_bqc", 8, F32)
    ln_base = ctx.mark()

    SCL = 1.0 / float(np.sqrt(128.0))
    NB = (TH + 55) // 56
    NCH = TH // 16

    sLdB = ctx.dsem("p0_sLdB")
    ctx.on("scalar", lambda e: e.dma_start(out=bq[:, 0:32], in_=mk.p0_bq[:, :]), inc=(sLdB, 16))
    ctx.on("scalar", lambda e: e.dma_start(out=bk[:, 0:8], in_=mk.p0_bk[:, :]), inc=(sLdB, 16))
    ctx.on("scalar", lambda e: e.dma_start(out=bv[:, 0:8], in_=mk.p0_bv[:, :]), inc=(sLdB, 16))
    sBqc = ctx.sem("p0_sBqc")
    ctx.on("vector", lambda e: e.tensor_tensor(bqc[:, 0:8], bq[:, 0:32:4], bq[:, 1:32:4], ALU.add),
           wait=[(sLdB, 48)], inc=(sBqc, 1))
    ctx.on("vector", lambda e: e.tensor_tensor(tmpa[:, 0:8], bq[:, 2:32:4], bq[:, 3:32:4], ALU.add),
           wait=[(sBqc, 1)], inc=(sBqc, 1))
    ctx.on("vector", lambda e: e.tensor_tensor(bqc[:, 0:8], bqc[:, 0:8], tmpa[:, 0:8], ALU.add),
           wait=[(sBqc, 2)], inc=(sBqc, 1))

    sW = [ctx.dsem("p0_sW0"), ctx.dsem("p0_sW1")]
    sWF = [ctx.sem("p0_sWF0"), ctx.sem("p0_sWF1")]
    sQD = [ctx.dsem("p0_sQD0"), ctx.dsem("p0_sQD1")]
    sRB = [ctx.dsem("p0_sRB0"), ctx.dsem("p0_sRB1")]
    norm_sems = []
    prev_gate = list(s_ready)
    prev_norm = None
    prev_sSum = None

    for hf in range(2):
        ctx.begin("half")
        sXC = emit_cast_xbh(ctx, mk, hf, prev_gate)
        wsrcs = ([("q", h) for h in range(32)] + [("k", h) for h in range(8)]
                 + [("v", h) for h in range(8)])
        for li, (kind, h) in enumerate(wsrcs):
            gi = hf * 48 + li
            par = gi % 2
            dram = {"q": mk.p0_wq, "k": mk.p0_wk, "v": mk.p0_wv}[kind]
            w = [(sWF[par], gi // 2)] if gi >= 2 else []
            ctx.on("gpsimd",
                   lambda e, dram=dram, h=h, par=par: e.dma_start(
                       out=wbuf[par][:].rearrange("p (a d) -> p a d", a=NP),
                       in_=dram[h].rearrange("(a p) d -> p a d", p=128)),
                   wait=w, inc=(sW[par], 16))

        sAcQ = ctx.sem(f"p0h{hf}_sAcQ")
        sEvQ = ctx.sem(f"p0h{hf}_sEvQ")
        sTrp = ctx.sem(f"p0h{hf}_sTrp")
        sTrE = ctx.sem(f"p0h{hf}_sTrE")
        sSc = ctx.sem(f"p0h{hf}_sSc")
        sU = ctx.sem(f"p0h{hf}_sU")
        sUC = ctx.sem(f"p0h{hf}_sUC")
        sVg = ctx.sem(f"p0h{hf}_sVg")
        sM = ctx.sem(f"p0h{hf}_sM")
        sZi = ctx.sem(f"p0h{hf}_sZi")
        sAsm = ctx.sem(f"p0h{hf}_sAsm")
        sMu = ctx.sem(f"p0h{hf}_sMu")
        sVin = ctx.sem(f"p0h{hf}_sVin")

        proj = ([("qc", hk) for hk in range(8)] + [("kc", hk) for hk in range(8)]
                + [("vc", hk) for hk in range(8)])
        for gidx, (kind, hk) in enumerate(proj):
            bank = gidx % 2
            if kind == "qc":
                lis = [4 * hk + r for r in range(4)]
            elif kind == "kc":
                lis = [32 + hk]
            else:
                lis = [40 + hk]
            mms = [(li, a) for li in lis for a in range(NP)]
            for mi, (li, a) in enumerate(mms):
                gi = hf * 48 + li
                par = gi % 2
                w = []
                if a == 0:
                    w.append((sW[par], 16 * (gi // 2 + 1)))
                if mi == 0:
                    w.append((sXC, NP))
                    if gidx >= 2:
                        w.append((sEvQ, gidx - 1))
                last = (mi == len(mms) - 1)
                if last:
                    inc = (sAcQ, 1)
                elif a == NP - 1:
                    inc = (sWF[par], 1)
                else:
                    inc = None
                ctx.on("tensor",
                       lambda e, a=a, par=par, bank=bank, mi=mi, last=last:
                           e.matmul(bankf(mk, bank), wbuf[par][:, a * 128:(a + 1) * 128],
                                    mk.xbh[:, a * TH:(a + 1) * TH],
                                    start=(mi == 0), stop=last),
                       wait=w, inc=inc)
            par_last = (hf * 48 + lis[-1]) % 2
            ctx.on("vector", lambda e: e.engine_nop(),
                   wait=[(sAcQ, gidx + 1)], inc=(sWF[par_last], 1))
            if kind == "vc":
                ctx.on("scalar",
                       lambda e, hk=hk, bank=bank: e.activation(
                           out=vc[:, hk * TH:(hk + 1) * TH], in_=bankf(mk, bank),
                           func=AF.Identity, bias=bv[:, hk:hk + 1]),
                       wait=[(sAcQ, gidx + 1), (sBqc, 3)], inc=(sEvQ, 1))
            else:
                bias = bqc[:, hk:hk + 1] if kind == "qc" else bk[:, hk:hk + 1]
                dstt = qc[gidx % 2]
                ctx.on("scalar",
                       lambda e, dstt=dstt, bank=bank, bias=bias: e.activation(
                           out=dstt[:], in_=bankf(mk, bank),
                           func=AF.Identity, bias=bias),
                       wait=[(sAcQ, gidx + 1), (sBqc, 3)] +
                            ([(sTrp, 4 * (gidx - 1))] if gidx >= 2 else []),
                       inc=(sEvQ, 1))
                dram = mk.p0_qd if kind == "qc" else mk.p0_kd
                for tc in range(4):
                    kk = 4 * gidx + tc
                    pslice = mk.psumb[:, (kk % 2) * 1024:(kk % 2) * 1024 + 128]
                    w = [(sEvQ, gidx + 1)] if tc == 0 else []
                    if kk >= 2:
                        w.append((sTrE, kk - 1))
                    ctx.on("tensor",
                           lambda e, dstt=dstt, tc=tc, pslice=pslice: e.transpose(
                               pslice, dstt[:, tc * 128:(tc + 1) * 128], mk.eyeb[:]),
                           wait=w, inc=(sTrp, 1))
                    w = [(sTrp, kk + 1)]
                    if kk >= 2:
                        w.append((sQD[kk % 2], 16 * (hf * 32 + kk // 2)))
                    ctx.on("vector",
                           lambda e, kk=kk, pslice=pslice: e.tensor_copy(
                               tev[kk % 2][:], pslice),
                           wait=w, inc=(sTrE, 1))
                    ctx.on("scalar",
                           lambda e, dram=dram, hk=hk, tc=tc, kk=kk: e.dma_start(
                               out=dram[hk, tc * 128:(tc + 1) * 128, :], in_=tev[kk % 2][:]),
                           wait=[(sTrE, kk + 1)], inc=(sQD[kk % 2], 16))

        for b in range(2):
            ctx.on("vector", lambda e, b=b: e.memset(vcat[b][:, 8:504:9], 1.0),
                   inc=(sVin, 1))

        for c in range(NCH):
            w = [(sQD[0], 16 * 32 * (hf + 1)), (sQD[1], 16 * 32 * (hf + 1))]
            if c == 0:
                w.append((sAcQ, 24))
            if c == 0 and prev_norm is not None:
                w.append(prev_norm)
            if c >= 2:
                w.append((sSc, 4 * (c - 1)))
            ctx.on("sync",
                   lambda e, c=c: e.dma_start(
                       out=qt[c % 2][0:8, 0:2048],
                       in_=mk.p0_qd[:, c * 16:(c + 1) * 16, :].rearrange("h t d -> h (t d)")),
                   wait=w, inc=(sRB[c % 2], 16))
            ctx.on("sync",
                   lambda e, c=c: e.dma_start(
                       out=kt[c % 2][0:8, 0:2048],
                       in_=mk.p0_kd[:, c * 16:(c + 1) * 16, :].rearrange("h t d -> h (t d)")),
                   inc=(sRB[c % 2], 16))

        for sb in range(TH // 4):
            bank = 2 + sb % 2
            mb = (sb * 4) // 56
            t0 = mb * 56
            B = min(56, TH - t0)
            bankM = 4 + mb % 2
            if sb * 4 == t0:
                for hk in range(8):
                    w = [(sVin, 2)]
                    if hk == 0:
                        w.append((sEvQ, 24))
                        w.append((sBqc, 3))
                        if mb >= 2:
                            w.append((sM, mb - 1))
                    ctx.on("vector",
                           lambda e, hk=hk, t0=t0, B=B, mb=mb: e.tensor_copy(
                               vcat[mb % 2][:, hk:9 * B:9],
                               vc[:, hk * TH + t0:hk * TH + t0 + B]),
                           wait=w, inc=(sVg, 1))
            for i in range(4):
                t = sb * 4 + i
                c = t // 16; tl = t % 16
                w = []
                if i == 0:
                    w.append((sRB[c % 2], 32 * (hf * NCH // 2 + c // 2 + 1)))
                    if sb >= 2:
                        w.append((sU, sb - 1))
                ctx.on("tensor",
                       lambda e, bank=bank, i=i, c=c, tl=tl: e.matmul(
                           bankf(mk, bank)[:, i * 128:(i + 1) * 128],
                           kt[c % 2][0:8, tl * 128:(tl + 1) * 128],
                           qt[c % 2][0:8, tl * 128:(tl + 1) * 128],
                           start=True, stop=True),
                       wait=w, inc=(sSc, 1) if i == 3 else None)
            ctx.on("scalar",
                   lambda e, sb=sb, bank=bank: e.activation(
                       out=ubuf[sb % 2][:], in_=bankf(mk, bank), func=AF.Exp, scale=SCL),
                   wait=[(sSc, sb + 1)] + ([(sUC, sb - 1)] if sb >= 2 else []),
                   inc=(sU, 1))
            for i in range(4):
                t = sb * 4 + i
                j = t - t0
                w = []
                if i == 0:
                    w.append((sU, sb + 1))
                if j == 0:
                    w.append((sVg, 8 * (mb + 1)))
                    if mb >= 2:
                        w.append((sAsm, 32 * (mb - 1)))
                    if mb == 0 and prev_norm is not None:
                        w.append(prev_norm)
                lastmb = (j == B - 1)
                ctx.on("tensor",
                       lambda e, bankM=bankM, j=j, sb=sb, i=i, mb=mb: e.matmul(
                           bankf(mk, bankM)[:, j * 9:(j + 1) * 9],
                           ubuf[sb % 2][:, i * 128:(i + 1) * 128],
                           vcat[mb % 2][:, j * 9:(j + 1) * 9],
                           start=True, stop=True),
                       wait=w,
                       inc=(sM, 1) if lastmb else ((sUC, 1) if i == 3 else None))
            if (sb * 4 + 4) == t0 + B:
                ctx.on("vector", lambda e, mb=mb: e.engine_nop(),
                       wait=[(sM, mb + 1)], inc=(sUC, 1))
                ctx.on("vector",
                       lambda e, bankM=bankM, B=B: e.reciprocal(
                           zinv[:, 0:B], bankf(mk, bankM)[:, 8:9 * B:9]),
                       wait=[(sM, mb + 1)], inc=(sZi, 1))
                for hk in range(8):
                    w = [(sZi, mb + 1)] if hk == 0 else [(sAsm, 32 * mb + 4 * hk)]
                    if hk == 0 and mb == 0:
                        w.append((sBqc, 3))
                    ctx.on("vector",
                           lambda e, bankM=bankM, hk=hk, B=B: e.tensor_tensor(
                               tmpa[:, 0:B], bankf(mk, bankM)[:, hk:9 * B:9], zinv[:, 0:B],
                               ALU.mult),
                           wait=w, inc=(sMu, 1))
                    nmul = sMu.n
                    for r in range(4):
                        h = 4 * hk + r
                        st = mk.stream[:, h * TC + hf * TH + t0:h * TC + hf * TH + t0 + B]
                        ctx.on("vector",
                               lambda e, st=st, B=B: e.tensor_tensor(
                                   st, st, tmpa[:, 0:B], ALU.add),
                               wait=[(sMu, nmul)], inc=(sAsm, 1))

        ctx.release(ln_base)
        sNorm, sSum = emit_ln_half(ctx, mk, f"p0ln{hf}", hf, sAsm, 32 * NB,
                                   extra_first_wait=[(sAcQ, 24)], flat=True,
                                   alias_off=qt_alias, alias_all=True)
        prev_gate = [(sAcQ, 24), (sAsm, 32 * NB), (sSum, 1)]
        prev_norm = (sNorm, NP)
        prev_sSum = sSum
        norm_sems.append((sNorm, NP))
    ctx.release(mark)
    return norm_sems


def declare_p1(ctx, mk):
    nc = ctx.nc
    mk.p1_baseT = nc.dram_tensor("p1_baseT", [D, D], F32, kind="ExternalInput")
    mk.p1_wrT = nc.dram_tensor("p1_wrT", [D, 8], F32, kind="ExternalInput")
    mk.p1_laA = nc.dram_tensor("p1_laA", [D, 128], F32, kind="ExternalInput")
    mk.p1_laB = nc.dram_tensor("p1_laB", [128, D], F32, kind="ExternalInput")


def emit_p1(ctx, mk, s_ready, trunc=None):
    import os
    ctx.begin("phase")
    """L0_ffn + residual + LN (both halves). s_ready: stream fully valid.
    Returns list of per-half sNorm sems [(sem, NP), ...] (all tiles normalized)."""
    mark = ctx.mark()
    wb = [ctx.sb("p1_wb0", D, BF16), ctx.sb("p1_wb1", D, BF16)]
    laAb = ctx.sb("p1_laAb", NP * 128, BF16)
    laBb = ctx.sb("p1_laBb", D, BF16)
    wrt = ctx.sb("p1_wrt", NP * 8, F32)
    logit = ctx.sb("p1_logit", 4 * 8, F32)
    wds = ctx.sb("p1_wds", 4 * 8, F32)
    ab_alias = (ctx.sbuf_off + 63) & ~63
    wdT = ctx.sb("p1_wdT", TH, F32)
    wfull = ctx.sb("p1_wfull", TH, F32)
    midw = ctx.sb("p1_midw", TH, BF16)
    srow = ctx.sb("p1_srow", 16, F32)
    ln_base = ctx.mark()

    sLdS = ctx.dsem("p1_sLdS")
    sLdG = ctx.dsem("p1_sLdG")
    ctx.on("scalar", lambda e: e.dma_start(
        out=wrt[:].rearrange("p (a e) -> p a e", a=NP),
        in_=mk.p1_wrT.rearrange("(a p) e -> p a e", p=128)),
        wait=list(s_ready), inc=(sLdS, 16))
    ctx.on("gpsimd", lambda e: e.dma_start(
        out=laAb[:].rearrange("p (a r) -> p a r", a=NP),
        in_=mk.p1_laA.rearrange("(a p) r -> p a r", p=128)),
        wait=list(s_ready), inc=(sLdG, 16))
    ctx.on("gpsimd", lambda e: e.dma_start(out=laBb[:], in_=mk.p1_laB[:, :]), inc=(sLdG, 16))

    sW = [ctx.dsem("p1_sW0"), ctx.dsem("p1_sW1")]
    sWF = [ctx.sem("p1_sWF0"), ctx.sem("p1_sWF1")]
    norm_sems = []
    prev_sAcc = None
    prev_sSum = None

    for hf in range(2):
        ctx.begin("half")
        if prev_sAcc is None:
            cast_wait = list(s_ready)
        else:
            cast_wait = [(prev_sAcc, NP), (prev_sSum, 1)]
        sXC = emit_cast_xbh(ctx, mk, hf, cast_wait)
        if trunc == "h1cast" and hf == 1:
            norm_sems.append((sXC, NP))
            break
        if trunc is None or (hf == 0 and trunc != "logits"):
            for o in range(NP):
                par = o % 2
                gi = hf * NP + o
                ctx.on("gpsimd",
                       lambda e, o=o, par=par: e.dma_start(
                           out=wb[par][:].rearrange("p (a o) -> p a o", a=NP),
                           in_=mk.p1_baseT[:, o * 128:(o + 1) * 128].rearrange(
                               "(a p) o -> p a o", p=128)),
                       wait=[(sWF[par], gi // 2)] if gi >= 2 else list(s_ready),
                       inc=(sW[par], 16))

        sLg = ctx.sem(f"p1h{hf}_sLg")
        sLgE = ctx.sem(f"p1h{hf}_sLgE")
        for tt in range(4):
            bank = 2 + tt % 2
            dst = mk.psumf[0:128, bank * 512:bank * 512 + 8]
            if trunc == "h1mm" and hf == 1 and tt > 0:
                continue
            for a in range(NP):
                w = []
                if tt == 0 and a == 0:
                    w = list(s_ready) + [(sLdS, 16)]
                elif a == 0 and tt >= 2:
                    w = [(sLgE, tt - 1)]
                ctx.on("tensor",
                       lambda e, tt=tt, a=a, dst=dst, hf=hf: e.matmul(
                           dst,
                           mk.stream[:, a * TC + hf * TH + tt * 128:
                                     a * TC + hf * TH + (tt + 1) * 128],
                           wrt[:, a * 8:(a + 1) * 8],
                           start=(a == 0), stop=(a == NP - 1)),
                       wait=w, inc=(sLg, 1) if a == NP - 1 else None)
            if not (trunc == "h1mm" and hf == 1):
                ctx.on("vector",
                       lambda e, tt=tt, dst=dst: e.tensor_copy(logit[:, tt * 8:(tt + 1) * 8], dst),
                       wait=[(sLg, tt + 1)], inc=(sLgE, 1))
        if trunc == "h1mm" and hf == 1:
            norm_sems.append((sLg, 1))
            break
        if trunc == "logits":
            norm_sems.append((sLgE, 4))
            break
        if trunc == "h1logits" and hf == 1:
            norm_sems.append((sLgE, 4))
            break
        if trunc == "h0full" and hf == 0:
            pass

        BIGC = 1.0e30
        sT2 = ctx.sem(f"p1h{hf}_sT2")
        sTop = ctx.sem(f"p1h{hf}_sTop")
        for tt in range(4):
            L = logit[:, tt * 8:(tt + 1) * 8]
            W8 = wds[:, tt * 8:(tt + 1) * 8]
            m1 = srow[:, 0:1]; m2 = srow[:, 1:2]; nm1 = srow[:, 2:3]
            e2 = srow[:, 3:4]; dinv = srow[:, 4:5]; t8 = srow[:, 8:16]
            k = sT2.n
            ctx.on("vector", lambda e, L=L, m1=m1: e.reduce_max(m1, L, axis=mybir.AxisListType.X),
                   wait=[(sLgE, tt + 1)] + ([(sT2, k)] if k else []), inc=(sT2, 1))
            ctx.on("vector", lambda e, L=L, m1=m1, t8=t8: e.tensor_scalar(
                t8, L, m1[:, 0:1], BIGC, ALU.is_equal, ALU.mult),
                   wait=[(sT2, k + 1)], inc=(sT2, 1))
            ctx.on("vector", lambda e, L=L, t8=t8: e.tensor_tensor(t8, L, t8, ALU.subtract),
                   wait=[(sT2, k + 2)], inc=(sT2, 1))
            ctx.on("vector", lambda e, m2=m2, t8=t8: e.reduce_max(m2, t8, axis=mybir.AxisListType.X),
                   wait=[(sT2, k + 3)], inc=(sT2, 1))
            ctx.on("vector", lambda e, m1=m1, nm1=nm1: e.tensor_scalar_mul(nm1, m1, -1.0),
                   wait=[(sT2, k + 4)], inc=(sT2, 1))
            ctx.on("scalar", lambda e, m2=m2, e2=e2, nm1=nm1: e.activation(
                out=e2, in_=m2, func=AF.Exp, bias=nm1[:, 0:1]),
                   wait=[(sT2, k + 5)], inc=(sT2, 1))
            ctx.on("vector", lambda e, e2=e2: e.tensor_scalar_add(e2, e2, 1.0),
                   wait=[(sT2, k + 6)], inc=(sT2, 1))
            ctx.on("vector", lambda e, e2=e2, dinv=dinv: e.reciprocal(dinv, e2),
                   wait=[(sT2, k + 7)], inc=(sT2, 1))
            ctx.on("scalar", lambda e, L=L, W8=W8, nm1=nm1: e.activation(
                out=W8, in_=L, func=AF.Exp, bias=nm1[:, 0:1]),
                   wait=[(sT2, k + 8)], inc=(sT2, 1))
            ctx.on("vector", lambda e, L=L, t8=t8, m2=m2: e.tensor_scalar(
                t8, L, m2[:, 0:1], None, ALU.is_ge),
                   wait=[(sT2, k + 9)], inc=(sT2, 1))
            ctx.on("vector", lambda e, W8=W8, t8=t8: e.tensor_tensor(W8, W8, t8, ALU.mult),
                   wait=[(sT2, k + 10)], inc=(sT2, 1))
            ctx.on("vector", lambda e, W8=W8, dinv=dinv: e.tensor_scalar_mul(W8, W8, dinv[:, 0:1]),
                   wait=[(sT2, k + 11)], inc=(sTop, 1))

        sTrp = ctx.sem(f"p1h{hf}_sTrp")
        sTrpE = ctx.sem(f"p1h{hf}_sTrpE")
        tp = mk.psumf[0:8, 2 * 512:2 * 512 + 128]
        for tt in range(4):
            w = [(sTop, tt + 1)]
            if tt == 0:
                w.append((sLgE, 4))
            else:
                w.append((sTrpE, tt))
            ctx.on("tensor",
                   lambda e, tt=tt: e.transpose(tp, wds[:, tt * 8:(tt + 1) * 8], mk.eyef[:]),
                   wait=w, inc=(sTrp, 1))
            ctx.on("vector",
                   lambda e, tt=tt: e.tensor_copy(wdT[0:8, tt * 128:(tt + 1) * 128], tp),
                   wait=[(sTrp, tt + 1)], inc=(sTrpE, 1))
        sWfl = ctx.sem(f"p1h{hf}_sWfl")
        sWflE = ctx.sem(f"p1h{hf}_sWflE")
        ctx.on("tensor",
               lambda e: e.matmul(bankf(mk, 3), mk.rbc[0:8, :], wdT[0:8, 0:TH],
                                  start=True, stop=True),
               wait=[(sTrpE, 4)], inc=(sWfl, 1))
        ctx.on("vector",
               lambda e: e.tensor_copy(wfull[:], bankf(mk, 3)),
               wait=[(sWfl, 1)], inc=(sWflE, 1))

        sMid = ctx.sem(f"p1h{hf}_sMid")
        sMidw = ctx.sem(f"p1h{hf}_sMidw")
        for a in range(NP):
            ctx.on("tensor",
                   lambda e, a=a: e.matmul(
                       bankf(mk, 4), laAb[:, a * 128:(a + 1) * 128],
                       mk.xbh[:, a * TH:(a + 1) * TH],
                       start=(a == 0), stop=(a == NP - 1)),
                   wait=[(sLdG, 32), (sXC, NP)] if a == 0 else (),
                   inc=(sMid, 1) if a == NP - 1 else None)
        ctx.on("vector",
               lambda e: e.tensor_tensor(midw[:], bankf(mk, 4), wfull[:], ALU.mult),
               wait=[(sMid, 1), (sWflE, 1)], inc=(sMidw, 1))

        sAcc = ctx.sem(f"p1h{hf}_sAcc")
        sAdd = ctx.sem(f"p1h{hf}_sAdd")
        for o in range(NP):
            par = o % 2
            bank = o % 2
            gi = hf * NP + o
            w = [(sW[par], 16 * (gi // 2 + 1))]
            if o >= 2:
                w.append((sAdd, o - 1))
            for a in range(NP):
                ctx.on("tensor",
                       lambda e, a=a, par=par, bank=bank: e.matmul(
                           bankf(mk, bank), wb[par][:, a * 128:(a + 1) * 128],
                           mk.xbh[:, a * TH:(a + 1) * TH],
                           start=(a == 0), stop=False),
                       wait=w if a == 0 else ())
            ctx.on("tensor",
                   lambda e, o=o, bank=bank: e.matmul(
                       bankf(mk, bank), laBb[:, o * 128:(o + 1) * 128], midw[:],
                       start=False, stop=True),
                   wait=[(sMidw, 1)] if o == 0 else [],
                   inc=(sAcc, 1))
            ctx.on("vector",
                   lambda e, o=o, bank=bank, hf=hf: e.tensor_tensor(
                       mk.stream[:, o * TC + hf * TH:o * TC + hf * TH + TH],
                       mk.stream[:, o * TC + hf * TH:o * TC + hf * TH + TH],
                       bankf(mk, bank), ALU.add),
                   wait=[(sAcc, o + 1)], inc=(sAdd, 1))
            ctx.on("vector", lambda e, o=o: e.engine_nop(), inc=(sWF[par], 1))

        prev_sAcc = sAcc
        ctx.release(ln_base)
        sNorm, prev_sSum = emit_ln_half(ctx, mk, f"p1ln{hf}", hf, sAdd, 1,
                                        extra_first_wait=[(sAcc, NP)],
                                        alias_off=ab_alias)
        norm_sems.append((sNorm, NP))
        if trunc == "h0full":
            break
    ctx.release(mark)
    return norm_sems


def host_inputs_p2(inp):
    def percol(a):
        return np.ascontiguousarray(np.asarray(a).reshape(4, 4, 128).transpose(2, 0, 1)
                                    .reshape(128, 16))
    winT = np.ascontiguousarray(inp["L1_attn_Win"].T[PIDX])
    w1T = np.ascontiguousarray(inp["L1_attn_W1"].transpose(0, 2, 1))
    w2T = np.ascontiguousarray(inp["L1_attn_W2"].transpose(0, 2, 1))
    woutT = np.ascontiguousarray(inp["L1_attn_Wout"][PIDX].T)
    return {
        "p2_winT": winT, "p2_w1T": w1T, "p2_w2T": w2T, "p2_woutT": woutT,
        "p2_bin": np.ascontiguousarray(inp["L1_attn_bin"].reshape(4, 128).T),
        "p2_b1": np.ascontiguousarray(inp["L1_attn_b1"].reshape(4, 16, 128)
                                      .transpose(2, 0, 1).reshape(128, 64)),
        "p2_b2": percol(inp["L1_attn_b2"]),
        "p2_ln1w": percol(inp["L1_attn_ln1_w"]), "p2_ln1b": percol(inp["L1_attn_ln1_b"]),
        "p2_ln2w": percol(inp["L1_attn_ln2_w"]), "p2_ln2b": percol(inp["L1_attn_ln2_b"]),
        "p2_phzre": percol(inp["L1_attn_phz_re"]), "p2_phzim": percol(inp["L1_attn_phz_im"]),
        "p2_lcre": percol(inp["L1_attn_lc_re"]), "p2_lcim": percol(inp["L1_attn_lc_im"]),
        "p2_scl": percol(inp["L1_attn_sc_lin"]),
        "p2_lnlw": np.ascontiguousarray(inp["L1_attn_lnl_w"].reshape(4, 128).T),
        "p2_lnlb": np.ascontiguousarray(inp["L1_attn_lnl_b"].reshape(4, 128).T),
        "p2_bout": np.ascontiguousarray(inp["L1_attn_bout"][PIDX].reshape(32, 128).T),
    }


def declare_p2(ctx, mk):
    nc = ctx.nc
    mk.p2_winT = nc.dram_tensor("p2_winT", [D, 512], F32, kind="ExternalInput")
    mk.p2_w1T = nc.dram_tensor("p2_w1T", [4, 512, 2048], F32, kind="ExternalInput")
    mk.p2_w2T = nc.dram_tensor("p2_w2T", [4, 2048, 512], F32, kind="ExternalInput")
    mk.p2_woutT = nc.dram_tensor("p2_woutT", [512, D], F32, kind="ExternalInput")
    for nm, sh in (("bin", [128, 4]), ("b1", [128, 64]), ("b2", [128, 16]),
                   ("ln1w", [128, 16]), ("ln1b", [128, 16]), ("ln2w", [128, 16]),
                   ("ln2b", [128, 16]), ("phzre", [128, 16]), ("phzim", [128, 16]),
                   ("lcre", [128, 16]), ("lcim", [128, 16]), ("scl", [128, 16]),
                   ("lnlw", [128, 4]), ("lnlb", [128, 4]), ("bout", [128, 32])):
        setattr(mk, "p2_" + nm, nc.dram_tensor("p2_" + nm, sh, F32, kind="ExternalInput"))
    mk.p2_mask = nc.dram_tensor("p2_mask", [1, 8], F32, kind="ExternalInput")
    mk.p2_xsp = nc.dram_tensor("p2_xsp", [128, NP * TC], F32)
    mk.p2_cin = nc.dram_tensor("p2_cin", [128, 8], F32)
    mk.p2_cout = nc.dram_tensor("p2_cout", [1024, 8], F32)


def emit_ln_inner(ctx, mk, S, h, hbf, out, rows, absb, wcol, bcol, sq,
                  silu=False, first_wait=()):
    sC, sMM, sEv, sSt, sBc, sN = (S["c"], S["mm"], S["ev"], S["st"], S["bc"], S["n"])
    kc, kmm, kev, kst, kbc = sC.n, sMM.n, sEv.n, sSt.n, sBc.n
    sum5 = mk.psumf[0:1, 5 * 512:5 * 512 + TH]
    ssq4 = mk.psumf[0:1, 4 * 512:4 * 512 + TH]
    row_a = rows[0:1, 0:TC]
    row_b = rows[0:1, TC:2 * TC]
    ctx.on("vector", lambda e: e.tensor_copy(hbf[:], h[:]),
           wait=list(first_wait) + [(sBc, kbc)], inc=(sC, 1))
    ctx.on("scalar", lambda e: e.activation(out=sq[:], in_=h[:], func=AF.Square),
           wait=list(first_wait), inc=(sC, 1))
    for g, (srcb, bank) in enumerate(((hbf, sum5), (sq, ssq4))):
        for hf in range(2):
            for dt in range(4):
                w = []
                if dt == 0:
                    if g == 0 and hf == 0:
                        w = [(sC, kc + 2), (sBc, kbc)]
                    else:
                        w = [(sEv, sEv.n)]
                ctx.on("tensor",
                       lambda e, srcb=srcb, bank=bank, dt=dt, hf=hf: e.matmul(
                           bank, mk.onesb[:, 0:1],
                           srcb[:, dt * TC + hf * TH:dt * TC + hf * TH + TH],
                           start=(dt == 0), stop=(dt == 3)),
                       wait=w, inc=(sMM, 1) if dt == 3 else None)
            dstrow = row_a if g == 0 else row_b
            ctx.on("vector",
                   lambda e, dstrow=dstrow, hf=hf, bank=bank: e.tensor_copy(
                       dstrow[:, hf * TH:hf * TH + TH], bank),
                   wait=[(sMM, sMM.n)], inc=(sEv, 1))
    c = 1.0 / 512.0
    ctx.on("vector", lambda e: e.tensor_scalar_mul(row_a, row_a, c),
           wait=[(sEv, kev + 4)], inc=(sSt, 1))
    ctx.on("vector", lambda e: e.tensor_scalar_mul(row_b, row_b, c),
           wait=[(sSt, sSt.n)], inc=(sSt, 1))
    ctx.on("vector", lambda e: e.tensor_tensor(absb[0:1, 0:TC],
                                               row_a, row_a, ALU.mult),
           wait=[(sSt, sSt.n)], inc=(sSt, 1))
    ctx.on("vector", lambda e: e.tensor_tensor(row_b, row_b, absb[0:1, 0:TC],
                                               ALU.subtract),
           wait=[(sSt, sSt.n)], inc=(sSt, 1))
    ctx.on("scalar", lambda e: e.activation(out=row_b, in_=row_b, func=AF.Ln,
                                            bias=mk.epsc[0:1, 0:1]),
           wait=[(sSt, sSt.n), (mk.sConst, 1)], inc=(sSt, 1))
    ctx.on("scalar", lambda e: e.activation(out=row_b, in_=row_b, func=AF.Exp, scale=-0.5),
           wait=[(sSt, sSt.n)], inc=(sSt, 1))
    ctx.on("vector", lambda e: e.tensor_tensor(row_a, row_a, row_b, ALU.mult),
           wait=[(sSt, sSt.n)], inc=(sSt, 1))
    ctx.on("vector", lambda e: e.tensor_scalar_mul(row_a, row_a, -1.0),
           wait=[(sSt, sSt.n)], inc=(sSt, 1))
    for g, (srcr, bank) in enumerate(((row_b, 5), (row_a, 4))):
        for hf in range(2):
            w = [(sSt, kst + 8)] if (g == 0 and hf == 0) else []
            if hf == 1:
                w.append((sEv, sEv.n))
            ctx.on("tensor",
                   lambda e, srcr=srcr, bank=bank, hf=hf: e.matmul(
                       mk.psumf[:, bank * 512:bank * 512 + TH], mk.ones1f[0:1, :],
                       srcr[:, hf * TH:hf * TH + TH], start=True, stop=True),
                   wait=w, inc=(sBc, 1))
            off = (0 if g == 0 else TC) + hf * TH
            ctx.on("vector",
                   lambda e, off=off, bank=bank: e.tensor_copy(
                       absb[:, off:off + TH],
                       mk.psumf[:, bank * 512:bank * 512 + TH]),
                   wait=[(sBc, sBc.n)], inc=(sEv, 1))
    for dt in range(4):
        ctx.on("vector",
               lambda e, dt=dt: e.tensor_tensor(
                   out[:, dt * TC:(dt + 1) * TC], h[:, dt * TC:(dt + 1) * TC],
                   absb[:, 0:TC], ALU.mult),
               wait=[(sEv, kev + 8)] if dt == 0 else [(sN, sN.n)], inc=(sN, 1))
        ctx.on("vector",
               lambda e, dt=dt: e.tensor_tensor(
                   out[:, dt * TC:(dt + 1) * TC], out[:, dt * TC:(dt + 1) * TC],
                   absb[:, TC:2 * TC], ALU.add),
               wait=[(sN, sN.n)], inc=(sN, 1))
        ctx.on("vector",
               lambda e, dt=dt: e.tensor_scalar(
                   out[:, dt * TC:(dt + 1) * TC], out[:, dt * TC:(dt + 1) * TC],
                   wcol[:, dt:dt + 1], bcol[:, dt:dt + 1], ALU.mult, ALU.add),
               wait=[(sN, sN.n)], inc=(sN, 1))
        if silu:
            ctx.on("scalar",
                   lambda e, dt=dt: e.activation(
                       out=out[:, dt * TC:(dt + 1) * TC],
                       in_=out[:, dt * TC:(dt + 1) * TC], func=AF.Silu),
                   wait=[(sN, sN.n)], inc=(sN, 1))
    return


def emit_p2(ctx, mk, s_ready):
    ctx.begin("phase")
    ctx.begin("half")
    RB = 16512
    Ec = ctx.sb_at("p2_Ec", 4 * TC, F32, RB + 0)
    Es = ctx.sb_at("p2_Es", 4 * TC, F32, RB + 16 * 1024)
    br = ctx.sb_at("p2_br", 4 * TC, F32, RB + 32 * 1024)
    bi = ctx.sb_at("p2_bi", 4 * TC, F32, RB + 48 * 1024)
    hso = ctx.sb_at("p2_h", 4 * TC, F32, RB + 64 * 1024)
    ybuf = ctx.sb_at("p2_y", 4 * TC, F32, RB + 80 * 1024)
    scr = ctx.sb_at("p2_scr", TC, F32, RB + 96 * 1024)
    Ein = ctx.sb_at("p2_Ein", 2 * 512, F32, RB + 100 * 1024)
    cols = ctx.sb_at("p2_cols", 256, F32, RB + 104 * 1024)
    rows = ctx.sb_at("p2_rows", 2 * TC, F32, RB + 106 * 1024)
    absb = ctx.sb_at("p2_absb", 2 * TC, F32, RB + 114 * 1024)
    gath = ctx.sb_at("p2_gath", 64, F32, RB + 122 * 1024)
    cinb = ctx.sb_at("p2_cinb", 8, F32, RB + 122 * 1024 + 512)
    ybf = ctx.sb_at("p2_ybf", 4 * TC, BF16, RB + 128 * 1024)
    h1b = ctx.sb_at("p2_h1b", 16 * TH, BF16, RB + 136 * 1024)
    sqb = ctx.sb_at("p2_sqb", 4 * TC, BF16, RB + 152 * 1024)
    W1REG = RB + 162 * 1024
    w1b = ctx.sb_at("p2_w1b", 4 * 2048, BF16, W1REG)
    w2b = ctx.sb_at("p2_w2b", 16 * 512, BF16, W1REG + 16 * 1024)
    woutb = ctx.sb_at("p2_woutb", 4 * D, BF16, W1REG)
    winb = ctx.sb_at("p2_winb", 32 * 512, BF16, W1REG)
    prm = ctx.sb_at("p2_prm", 288, F32, RB + 204 * 1024)
    maskb = ctx.sb_at("p2_maskb", 8, F32, RB + 206 * 1024)
    wsel = ctx.sb_at("p2_wsel", 8, F32, RB + 206 * 1024 + 64)
    wbuf8 = ctx.sb_at("p2_wbuf8", 64, F32, RB + 206 * 1024 + 128)
    ln_off = RB + 194 * 1024

    PRM = {}
    off = 0
    for nm, wdt in (("bin", 4), ("b1", 64), ("b2", 16), ("ln1w", 16), ("ln1b", 16),
                    ("ln2w", 16), ("ln2b", 16), ("phzre", 16), ("phzim", 16),
                    ("lcre", 16), ("lcim", 16), ("scl", 16), ("lnlw", 4), ("lnlb", 4),
                    ("bout", 32)):
        PRM[nm] = (off, wdt)
        off += wdt
    assert off <= 288

    sLdP = ctx.dsem("p2_sLdP")
    sWinL = ctx.dsem("p2_sWinL")
    sSp = ctx.dsem("p2_sSp")
    sWL = [ctx.dsem("p2_sWL0"), ctx.dsem("p2_sWL1")]
    sWoL = ctx.dsem("p2_sWoL")
    sCC = ctx.dsem("p2_sCC")
    sXC = ctx.sem("p2_sXC")
    sWinA = ctx.sem("p2_sWinA")
    sWinE = ctx.sem("p2_sWinE")
    sCol = ctx.sem("p2_sCol")
    sTab = ctx.sem("p2_sTab")
    sFFA = ctx.sem("p2_sFFA")
    sFFE = ctx.sem("p2_sFFE")
    sFFE2 = ctx.sem("p2_sFFE2")
    sWoE = ctx.sem("p2_sWoE")
    sFFH = ctx.sem("p2_sFFH")
    sWoA = ctx.sem("p2_sWoA")
    sHFB = ctx.sem("p2_sHFB")
    sRel = [ctx.dsem("p2_sRel0"), ctx.dsem("p2_sRel1")]
    sFin = ctx.sem("p2_sFin")
    SLN = {"c": ctx.sem("p2_lnc"), "mm": ctx.sem("p2_lnmm"), "ev": ctx.sem("p2_lnev"),
           "st": ctx.sem("p2_lnst"), "bc": ctx.sem("p2_lnbc"), "n": ctx.sem("p2_lnn")}

    for nm in PRM:
        o, wdt = PRM[nm]
        dram = getattr(mk, "p2_" + nm)
        ctx.on("scalar", lambda e, dram=dram, o=o, wdt=wdt: e.dma_start(
            out=prm[:, o:o + wdt], in_=dram[:, :]),
            wait=list(s_ready) if o == 0 else [], inc=(sLdP, 16))
    ctx.on("scalar", lambda e: e.dma_start(out=maskb[0:1, 0:8], in_=mk.p2_mask[:, :]),
           inc=(sLdP, 16))
    ctx.on("gpsimd", lambda e: e.dma_start(
        out=winb[:].rearrange("p (a d) -> p a d", a=NP),
        in_=mk.p2_winT.rearrange("(a p) d -> p a d", p=128)),
        wait=list(s_ready), inc=(sWinL, 16))

    prev = None
    cast_sems = []
    for hf in range(2):
        cast_wait = list(s_ready) if hf == 0 else [(prev, 4)]
        sXCh = emit_cast_xbh(ctx, mk, hf, cast_wait)
        cast_sems.append(sXCh)
        for dt in range(4):
            idx = hf * 4 + dt
            bank = idx % 4
            for a in range(NP):
                w = []
                if a == 0:
                    w = [(sXCh, NP), (sWinL, 16)]
                    if idx >= 4:
                        w.append((sWinE, idx - 3))
                ctx.on("tensor",
                       lambda e, a=a, dt=dt, hf=hf, bank=bank: e.matmul(
                           bankf(mk, bank), winb[:, a * 512 + dt * 128:a * 512 + (dt + 1) * 128],
                           mk.xbh[:, a * TH:(a + 1) * TH],
                           start=(a == 0), stop=(a == NP - 1)),
                       wait=w, inc=(sWinA, 1) if a == NP - 1 else None)
            ctx.on("scalar",
                   lambda e, dt=dt, hf=hf, bank=bank: e.activation(
                       out=hso[:, dt * TC + hf * TH:dt * TC + hf * TH + TH],
                       in_=bankf(mk, bank), func=AF.Identity,
                       bias=prm[:, PRM["bin"][0] + dt:PRM["bin"][0] + dt + 1]),
                   wait=[(sWinA, idx + 1), (sLdP, 16 * 16), (sSp, 16)],
                   inc=(sWinE, 1))
        prev = sWinA
    ctx.on("sync", lambda e: e.dma_start(out=mk.p2_xsp[:, :], in_=mk.stream[:]),
           wait=[(cast_sems[1], NP)], inc=(sSp, 16))
    ctx.on("tensor", lambda e: e.matmul(mk.psumf[:, 3 * 512:3 * 512 + 8],
                                        mk.ones1f[0:1, :], maskb[0:1, 0:8],
                                        start=True, stop=True),
           wait=[(sLdP, 16 * 16), (sWinE, 8)], inc=(sWinA, 1))
    ctx.on("vector", lambda e: e.tensor_copy(maskb[:], mk.psumf[:, 3 * 512:3 * 512 + 8]),
           wait=[(sWinA, sWinA.n)], inc=(sTab, 1))

    def emit_wload(l):
        par = 0
        w = [(sWinA, 8)] if l == 0 else [(sFFH, 8 * l)]
        ctx.on("gpsimd",
               lambda e, l=l: e.dma_start(
                   out=w1b[:].rearrange("p (a f) -> p a f", a=4),
                   in_=mk.p2_w1T[l].rearrange("(a p) f -> p a f", p=128)),
               wait=w, inc=(sWL[0], 16))
        ctx.on("gpsimd",
               lambda e, l=l: e.dma_start(
                   out=w2b[:].rearrange("p (a d) -> p a d", a=16),
                   in_=mk.p2_w2T[l].rearrange("(a p) d -> p a d", p=128)),
               inc=(sWL[1], 16))
    emit_wload(0)

    import os as _os
    NL = int(_os.environ.get("P2_NLAYERS", "4"))
    for l in range(NL):
        P = lambda nm, i: prm[:, PRM[nm][0] + l * 4 + i:PRM[nm][0] + l * 4 + i + 1]
        kt = sTab.n
        c_er = cols[:, 0:4]; c_ei = cols[:, 4:8]; c_rho = cols[:, 8:12]
        c_r2 = cols[:, 12:16]; c_ri = cols[:, 16:20]
        c_Ec = cols[:, 20:24]; c_Es = cols[:, 24:28]
        c_R = cols[:, 28:32]
        c_t1 = cols[:, 32:36]; c_t2 = cols[:, 36:40]
        c_wr = cols[:, 40:44]; c_wi = cols[:, 44:48]
        c_nEs = cols[:, 48:52]
        c_Qc = cols[:, 64:96]; c_Qs = cols[:, 96:128]
        ctx.on("vector", lambda e, l=l: e.tensor_tensor(c_r2,
                                                        prm[:, PRM["phzre"][0] + l * 4:PRM["phzre"][0] + l * 4 + 4],
                                                        prm[:, PRM["phzre"][0] + l * 4:PRM["phzre"][0] + l * 4 + 4], ALU.mult),
               wait=[(sTab, kt), (sLdP, 16 * 16)] + ([(sSp, 16)] if l == 0 else []),
               inc=(sTab, 1))
        ctx.on("vector", lambda e, l=l: e.tensor_tensor(c_t1, prm[:, PRM["phzim"][0] + l * 4:PRM["phzim"][0] + l * 4 + 4],
                                                        prm[:, PRM["phzim"][0] + l * 4:PRM["phzim"][0] + l * 4 + 4], ALU.mult),
               wait=[(sTab, kt + 1)], inc=(sTab, 1))
        ctx.on("vector", lambda e: e.tensor_tensor(c_r2, c_r2, c_t1, ALU.add),
               wait=[(sTab, kt + 2)], inc=(sTab, 1))
        ctx.on("scalar", lambda e: e.activation(out=c_ri, in_=c_r2, func=AF.Ln),
               wait=[(sTab, kt + 3)], inc=(sTab, 1))
        ctx.on("scalar", lambda e: e.activation(out=c_ri, in_=c_ri, func=AF.Exp, scale=-0.5),
               wait=[(sTab, kt + 4)], inc=(sTab, 1))
        ctx.on("vector", lambda e, l=l: e.tensor_tensor(c_er, prm[:, PRM["phzre"][0] + l * 4:PRM["phzre"][0] + l * 4 + 4], c_ri, ALU.mult),
               wait=[(sTab, kt + 5)], inc=(sTab, 1))
        ctx.on("vector", lambda e, l=l: e.tensor_tensor(c_ei, prm[:, PRM["phzim"][0] + l * 4:PRM["phzim"][0] + l * 4 + 4], c_ri, ALU.mult),
               wait=[(sTab, kt + 6)], inc=(sTab, 1))
        ctx.on("vector", lambda e: e.tensor_tensor(c_t1, c_r2, c_ri, ALU.mult),
               wait=[(sTab, kt + 7)], inc=(sTab, 1))
        ctx.on("scalar", lambda e: e.activation(out=c_rho, in_=c_t1, func=AF.Tanh),
               wait=[(sTab, kt + 8)], inc=(sTab, 1))
        Ci = Ein[:, 0:512]; Si = Ein[:, 512:1024]
        ctx.on("vector", lambda e: e.memset(Ci[:, 0:512:128], 1.0),
               wait=[(sTab, kt + 9)], inc=(sTab, 1))
        ctx.on("vector", lambda e: e.memset(Si[:, 0:512:128], 0.0),
               wait=[(sTab, sTab.n)], inc=(sTab, 1))
        ctx.on("vector", lambda e: e.tensor_scalar_mul(c_nEs, c_ei, -1.0),
               wait=[(sTab, sTab.n)], inc=(sTab, 1))
        for step_i, m in enumerate([1, 2, 4, 8, 16, 32, 64]):
            for dt in range(4):
                base = dt * 128
                pass
            break
        em_r = cols[:, 52:56]; em_i = cols[:, 56:60]; tmpc = cols[:, 60:64]
        ctx.on("vector", lambda e: e.tensor_copy(em_r, c_er), wait=[(sTab, sTab.n)], inc=(sTab, 1))
        ctx.on("vector", lambda e: e.tensor_copy(em_i, c_nEs), wait=[(sTab, sTab.n)], inc=(sTab, 1))
        Ci3 = Ci.rearrange("p (d s) -> p d s", d=4)
        Si3 = Si.rearrange("p (d s) -> p d s", d=4)
        for m in [1, 2, 4, 8, 16, 32, 64]:
            n = m
            erb = em_r.rearrange("p (d o) -> p d o", o=1).broadcast_to([128, 4, n])
            eib = em_i.rearrange("p (d o) -> p d o", o=1).broadcast_to([128, 4, n])
            s1 = scr[:, 0:4 * n].rearrange("p (d s) -> p d s", d=4)
            s2 = scr[:, 512:512 + 4 * n].rearrange("p (d s) -> p d s", d=4)
            ctx.on("vector", lambda e, s1=s1, n=n, erb=erb: e.tensor_tensor(
                s1, Ci3[:, :, 0:n], erb, ALU.mult),
                wait=[(sTab, sTab.n)], inc=(sTab, 1))
            ctx.on("vector", lambda e, s2=s2, n=n, eib=eib: e.tensor_tensor(
                s2, Si3[:, :, 0:n], eib, ALU.mult),
                wait=[(sTab, sTab.n)], inc=(sTab, 1))
            ctx.on("vector", lambda e, s1=s1, s2=s2, m=m, n=n: e.tensor_tensor(
                Ci3[:, :, m:m + n], s1, s2, ALU.subtract),
                wait=[(sTab, sTab.n)], inc=(sTab, 1))
            ctx.on("vector", lambda e, s1=s1, n=n, eib=eib: e.tensor_tensor(
                s1, Ci3[:, :, 0:n], eib, ALU.mult),
                wait=[(sTab, sTab.n)], inc=(sTab, 1))
            ctx.on("vector", lambda e, s2=s2, n=n, erb=erb: e.tensor_tensor(
                s2, Si3[:, :, 0:n], erb, ALU.mult),
                wait=[(sTab, sTab.n)], inc=(sTab, 1))
            ctx.on("vector", lambda e, s1=s1, s2=s2, m=m, n=n: e.tensor_tensor(
                Si3[:, :, m:m + n], s1, s2, ALU.add),
                wait=[(sTab, sTab.n)], inc=(sTab, 1))
            if m < 64:
                ctx.on("vector", lambda e: e.tensor_tensor(tmpc, em_r, em_r, ALU.mult),
                       wait=[(sTab, sTab.n)], inc=(sTab, 1))
                ctx.on("vector", lambda e: e.tensor_tensor(c_t2, em_i, em_i, ALU.mult),
                       wait=[(sTab, sTab.n)], inc=(sTab, 1))
                ctx.on("vector", lambda e: e.tensor_tensor(c_t1, em_r, em_i, ALU.mult),
                       wait=[(sTab, sTab.n)], inc=(sTab, 1))
                ctx.on("vector", lambda e: e.tensor_tensor(em_r, tmpc, c_t2, ALU.subtract),
                       wait=[(sTab, sTab.n)], inc=(sTab, 1))
                ctx.on("vector", lambda e: e.tensor_scalar_mul(em_i, c_t1, 2.0),
                       wait=[(sTab, sTab.n)], inc=(sTab, 1))
        ctx.on("vector", lambda e: e.tensor_tensor(tmpc, em_r, em_r, ALU.mult),
               wait=[(sTab, sTab.n)], inc=(sTab, 1))
        ctx.on("vector", lambda e: e.tensor_tensor(c_t2, em_i, em_i, ALU.mult),
               wait=[(sTab, sTab.n)], inc=(sTab, 1))
        ctx.on("vector", lambda e: e.tensor_tensor(c_t1, em_r, em_i, ALU.mult),
               wait=[(sTab, sTab.n)], inc=(sTab, 1))
        ctx.on("vector", lambda e: e.tensor_tensor(em_r, tmpc, c_t2, ALU.subtract),
               wait=[(sTab, sTab.n)], inc=(sTab, 1))
        ctx.on("vector", lambda e: e.tensor_scalar_mul(em_i, c_t1, 2.0),
               wait=[(sTab, sTab.n)], inc=(sTab, 1))
        Qc = lambda q: cols[:, 64 + q:96:8]
        Qs = lambda q: cols[:, 96 + q:128:8]
        ctx.on("vector", lambda e: e.memset(cols[:, 64:96:8], 1.0),
               wait=[(sTab, sTab.n)], inc=(sTab, 1))
        ctx.on("vector", lambda e: e.memset(cols[:, 96:128:8], 0.0),
               wait=[(sTab, sTab.n)], inc=(sTab, 1))
        for q in range(7):
            ctx.on("vector", lambda e, q=q: e.tensor_tensor(c_t1, Qc(q), em_r, ALU.mult),
                   wait=[(sTab, sTab.n)], inc=(sTab, 1))
            ctx.on("vector", lambda e, q=q: e.tensor_tensor(c_t2, Qs(q), em_i, ALU.mult),
                   wait=[(sTab, sTab.n)], inc=(sTab, 1))
            ctx.on("vector", lambda e, q=q: e.tensor_tensor(Qc(q + 1), c_t1, c_t2, ALU.subtract),
                   wait=[(sTab, sTab.n)], inc=(sTab, 1))
            ctx.on("vector", lambda e, q=q: e.tensor_tensor(c_t1, Qc(q), em_i, ALU.mult),
                   wait=[(sTab, sTab.n)], inc=(sTab, 1))
            ctx.on("vector", lambda e, q=q: e.tensor_tensor(c_t2, Qs(q), em_r, ALU.mult),
                   wait=[(sTab, sTab.n)], inc=(sTab, 1))
            ctx.on("vector", lambda e, q=q: e.tensor_tensor(Qs(q + 1), c_t1, c_t2, ALU.add),
                   wait=[(sTab, sTab.n)], inc=(sTab, 1))
        for dt in range(4):
            qc = c_Qc[:, dt * 8:dt * 8 + 8].rearrange("p (q o) -> p q o", o=1).broadcast_to([128, 8, 128])
            qs = c_Qs[:, dt * 8:dt * 8 + 8].rearrange("p (q o) -> p q o", o=1).broadcast_to([128, 8, 128])
            ci3 = Ci[:, dt * 128:(dt + 1) * 128].rearrange("p (o s) -> p o s", o=1).broadcast_to([128, 8, 128])
            si3 = Si[:, dt * 128:(dt + 1) * 128].rearrange("p (o s) -> p o s", o=1).broadcast_to([128, 8, 128])
            ecv = Ec[:, dt * TC:(dt + 1) * TC].rearrange("p (q s) -> p q s", q=8)
            esv = Es[:, dt * TC:(dt + 1) * TC].rearrange("p (q s) -> p q s", q=8)
            scv = scr[:].rearrange("p (q s) -> p q s", q=8)
            ctx.on("vector", lambda e, ecv=ecv, qc=qc, ci3=ci3: e.tensor_tensor(ecv, qc, ci3, ALU.mult),
                   wait=[(sTab, sTab.n)], inc=(sTab, 1))
            ctx.on("vector", lambda e, scv=scv, qs=qs, si3=si3: e.tensor_tensor(scv, qs, si3, ALU.mult),
                   wait=[(sTab, sTab.n)], inc=(sTab, 1))
            ctx.on("vector", lambda e, ecv=ecv, scv=scv, dt=dt: e.tensor_tensor(
                Ec[:, dt * TC:(dt + 1) * TC], Ec[:, dt * TC:(dt + 1) * TC], scr[:], ALU.subtract),
                   wait=[(sTab, sTab.n)], inc=(sTab, 1))
            ctx.on("vector", lambda e, esv=esv, qc=qc, si3=si3: e.tensor_tensor(esv, qc, si3, ALU.mult),
                   wait=[(sTab, sTab.n)], inc=(sTab, 1))
            ctx.on("vector", lambda e, scv=scv, qs=qs, ci3=ci3: e.tensor_tensor(scv, qs, ci3, ALU.mult),
                   wait=[(sTab, sTab.n)], inc=(sTab, 1))
            ctx.on("vector", lambda e, dt=dt: e.tensor_tensor(
                Es[:, dt * TC:(dt + 1) * TC], Es[:, dt * TC:(dt + 1) * TC], scr[:], ALU.add),
                   wait=[(sTab, sTab.n)], inc=(sTab, 1))
        for _ in range(3):
            ctx.on("vector", lambda e: e.tensor_tensor(c_t1, em_r, em_r, ALU.mult),
                   wait=[(sTab, sTab.n)], inc=(sTab, 1))
            ctx.on("vector", lambda e: e.tensor_tensor(c_t2, em_i, em_i, ALU.mult),
                   wait=[(sTab, sTab.n)], inc=(sTab, 1))
            ctx.on("vector", lambda e: e.tensor_tensor(tmpc, em_r, em_i, ALU.mult),
                   wait=[(sTab, sTab.n)], inc=(sTab, 1))
            ctx.on("vector", lambda e: e.tensor_tensor(em_r, c_t1, c_t2, ALU.subtract),
                   wait=[(sTab, sTab.n)], inc=(sTab, 1))
            ctx.on("vector", lambda e: e.tensor_scalar_mul(em_i, tmpc, 2.0),
                   wait=[(sTab, sTab.n)], inc=(sTab, 1))
        ctx.on("vector", lambda e: e.tensor_copy(c_Ec, em_r), wait=[(sTab, sTab.n)], inc=(sTab, 1))
        ctx.on("vector", lambda e: e.tensor_copy(c_nEs, em_i), wait=[(sTab, sTab.n)], inc=(sTab, 1))
        ctx.on("vector", lambda e: e.tensor_scalar_mul(c_Es, em_i, -1.0),
               wait=[(sTab, sTab.n)], inc=(sTab, 1))
        ctx.on("vector", lambda e: e.tensor_copy(c_R, c_rho), wait=[(sTab, sTab.n)], inc=(sTab, 1))
        for _ in range(10):
            ctx.on("vector", lambda e: e.tensor_tensor(c_R, c_R, c_R, ALU.mult),
                   wait=[(sTab, sTab.n)], inc=(sTab, 1))

        fw = [(sWinE, 8), (sSp, 16)] if l == 0 else [(sFFH, 8 * l)]
        emit_ln_inner(ctx, mk, SLN, hso, ybf, ybuf, rows, absb,
                      prm[:, PRM["ln1w"][0] + l * 4:PRM["ln1w"][0] + (l + 1) * 4],
                      prm[:, PRM["ln1b"][0] + l * 4:PRM["ln1b"][0] + (l + 1) * 4],
                      sqb, silu=True, first_wait=fw + [(sLdP, 16 * 16)])

        kb = sTab.n
        for dt in range(4):
            ctx.on("vector", lambda e, dt=dt: e.tensor_tensor(
                br[:, dt * TC:(dt + 1) * TC], Ec[:, dt * TC:(dt + 1) * TC],
                ybuf[:, dt * TC:(dt + 1) * TC], ALU.mult),
                   wait=[(sTab, sTab.n), (SLN["n"], SLN["n"].n)], inc=(sTab, 1))
            ctx.on("vector", lambda e, dt=dt: e.tensor_tensor(
                bi[:, dt * TC:(dt + 1) * TC], Es[:, dt * TC:(dt + 1) * TC],
                ybuf[:, dt * TC:(dt + 1) * TC], ALU.mult),
                   wait=[(sTab, sTab.n)], inc=(sTab, 1))
            rhob = c_rho[:, dt:dt + 1].broadcast_to([128, TC])
            ctx.on("vector", lambda e, dt=dt, rhob=rhob: e.tensor_tensor_scan(
                scr[:], rhob, br[:, dt * TC:(dt + 1) * TC], 0.0, ALU.mult, ALU.add),
                   wait=[(sTab, sTab.n)], inc=(sTab, 1))
            ctx.on("vector", lambda e, dt=dt: e.tensor_copy(
                cinb[:, dt * 2:dt * 2 + 1], scr[:, TC - 1:TC]),
                   wait=[(sTab, sTab.n)], inc=(sTab, 1))
            ctx.on("vector", lambda e, dt=dt, rhob=rhob: e.tensor_tensor_scan(
                scr[:], rhob, bi[:, dt * TC:(dt + 1) * TC], 0.0, ALU.mult, ALU.add),
                   wait=[(sTab, sTab.n)], inc=(sTab, 1))
            ctx.on("vector", lambda e, dt=dt: e.tensor_copy(
                cinb[:, dt * 2 + 1:dt * 2 + 2], scr[:, TC - 1:TC]),
                   wait=[(sTab, sTab.n)], inc=(sTab, 1))
        kcc = sCC.n
        ctx.on("gpsimd", lambda e: e.dma_start(out=mk.p2_cin[:, :], in_=cinb[:]),
               wait=[(sTab, sTab.n)], inc=(sCC, 16))
        ctx.on("gpsimd", lambda e: e.collective_compute(
            "AllGather", ALU.bypass, replica_groups=[list(range(8))],
            ins=[mk.p2_cin[:, :]], outs=[mk.p2_cout[:, :]]),
            wait=[(sCC, kcc + 16)], inc=(sCol, 1))
        ctx.on("gpsimd", lambda e: e.dma_start(
            out=gath[:].rearrange("p (r c) -> p r c", r=8),
            in_=mk.p2_cout.rearrange("(r p) c -> p r c", p=128)),
            wait=[(sCol, sCol.n)], inc=(sCC, 16))
        kw = sTab.n
        lcr = prm[:, PRM["lcre"][0] + l * 4:PRM["lcre"][0] + l * 4 + 4]
        lci = prm[:, PRM["lcim"][0] + l * 4:PRM["lcim"][0] + l * 4 + 4]
        ctx.on("vector", lambda e, lcr=lcr: e.tensor_tensor(c_wr, c_er, lcr, ALU.mult),
               wait=[(sTab, sTab.n), (sCC, kcc + 32)], inc=(sTab, 1))
        ctx.on("vector", lambda e, lci=lci: e.scalar_tensor_tensor(
            c_wr, lci, -1.0, c_wr, ALU.mult, ALU.add) if False else e.tensor_tensor(
            c_t1, c_ei, lci, ALU.mult),
               wait=[(sTab, sTab.n)], inc=(sTab, 1))
        ctx.on("vector", lambda e: e.tensor_tensor(c_wr, c_wr, c_t1, ALU.subtract),
               wait=[(sTab, sTab.n)], inc=(sTab, 1))
        ctx.on("vector", lambda e, lci=lci: e.tensor_tensor(c_wi, c_er, lci, ALU.mult),
               wait=[(sTab, sTab.n)], inc=(sTab, 1))
        ctx.on("vector", lambda e, lcr=lcr: e.tensor_tensor(c_t1, c_ei, lcr, ALU.mult),
               wait=[(sTab, sTab.n)], inc=(sTab, 1))
        ctx.on("vector", lambda e: e.tensor_tensor(c_wi, c_wi, c_t1, ALU.add),
               wait=[(sTab, sTab.n)], inc=(sTab, 1))
        ctx.on("vector", lambda e: e.tensor_copy(wbuf8[:, 0:8:2], c_wr),
               wait=[(sTab, sTab.n)], inc=(sTab, 1))
        ctx.on("vector", lambda e: e.tensor_copy(wbuf8[:, 1:8:2], c_wi),
               wait=[(sTab, sTab.n)], inc=(sTab, 1))
        for k in range(1, 8):
            gl = gath[:, (k - 1) * 8:(k - 1) * 8 + 8]
            ctx.on("vector", lambda e: e.tensor_tensor(c_t1, c_R, c_wr, ALU.mult),
                   wait=[(sTab, sTab.n)], inc=(sTab, 1))
            ctx.on("vector", lambda e, gl=gl: e.tensor_tensor(c_t1, c_t1, gl[:, 0:8:2], ALU.add),
                   wait=[(sTab, sTab.n)], inc=(sTab, 1))
            ctx.on("vector", lambda e: e.tensor_tensor(c_t2, c_R, c_wi, ALU.mult),
                   wait=[(sTab, sTab.n)], inc=(sTab, 1))
            ctx.on("vector", lambda e, gl=gl: e.tensor_tensor(c_t2, c_t2, gl[:, 1:8:2], ALU.add),
                   wait=[(sTab, sTab.n)], inc=(sTab, 1))
            ctx.on("vector", lambda e: e.tensor_tensor(c_wr, c_Ec, c_t1, ALU.mult),
                   wait=[(sTab, sTab.n)], inc=(sTab, 1))
            ctx.on("vector", lambda e: e.tensor_tensor(tmpc, c_Es, c_t2, ALU.mult),
                   wait=[(sTab, sTab.n)], inc=(sTab, 1))
            ctx.on("vector", lambda e: e.tensor_tensor(c_wr, c_wr, tmpc, ALU.subtract),
                   wait=[(sTab, sTab.n)], inc=(sTab, 1))
            ctx.on("vector", lambda e: e.tensor_tensor(c_wi, c_Ec, c_t2, ALU.mult),
                   wait=[(sTab, sTab.n)], inc=(sTab, 1))
            ctx.on("vector", lambda e: e.tensor_tensor(tmpc, c_Es, c_t1, ALU.mult),
                   wait=[(sTab, sTab.n)], inc=(sTab, 1))
            ctx.on("vector", lambda e: e.tensor_tensor(c_wi, c_wi, tmpc, ALU.add),
                   wait=[(sTab, sTab.n)], inc=(sTab, 1))
            ctx.on("vector", lambda e, k=k: e.tensor_copy(wbuf8[:, k * 8:k * 8 + 8:2], c_wr),
                   wait=[(sTab, sTab.n)], inc=(sTab, 1))
            ctx.on("vector", lambda e, k=k: e.tensor_copy(wbuf8[:, k * 8 + 1:k * 8 + 8:2], c_wi),
                   wait=[(sTab, sTab.n)], inc=(sTab, 1))
        ctx.on("vector", lambda e: e.memset(wsel[:], 0.0),
               wait=[(sTab, sTab.n)], inc=(sTab, 1))
        for k in range(8):
            ctx.on("vector", lambda e, k=k: e.scalar_tensor_tensor(
                wsel[:], wbuf8[:, k * 8:(k + 1) * 8], maskb[:, k:k + 1], wsel[:],
                ALU.mult, ALU.add),
                   wait=[(sTab, sTab.n)], inc=(sTab, 1))
        for dt in range(4):
            rhob = c_rho[:, dt:dt + 1].broadcast_to([128, TC])
            ctx.on("vector", lambda e, dt=dt, rhob=rhob: e.tensor_tensor_scan(
                br[:, dt * TC:(dt + 1) * TC], rhob, br[:, dt * TC:(dt + 1) * TC],
                wsel[:, dt * 2:dt * 2 + 1], ALU.mult, ALU.add),
                   wait=[(sTab, sTab.n)], inc=(sTab, 1))
            ctx.on("vector", lambda e, dt=dt, rhob=rhob: e.tensor_tensor_scan(
                bi[:, dt * TC:(dt + 1) * TC], rhob, bi[:, dt * TC:(dt + 1) * TC],
                wsel[:, dt * 2 + 1:dt * 2 + 2], ALU.mult, ALU.add),
                   wait=[(sTab, sTab.n)], inc=(sTab, 1))
            ctx.on("vector", lambda e, dt=dt: e.tensor_tensor(
                scr[:], Ec[:, dt * TC:(dt + 1) * TC], br[:, dt * TC:(dt + 1) * TC], ALU.mult),
                   wait=[(sTab, sTab.n)], inc=(sTab, 1))
            ctx.on("vector", lambda e, dt=dt: e.tensor_tensor(
                br[:, dt * TC:(dt + 1) * TC], Es[:, dt * TC:(dt + 1) * TC],
                bi[:, dt * TC:(dt + 1) * TC], ALU.mult),
                   wait=[(sTab, sTab.n)], inc=(sTab, 1))
            ctx.on("vector", lambda e, dt=dt: e.tensor_tensor(
                scr[:], scr[:], br[:, dt * TC:(dt + 1) * TC], ALU.add),
                   wait=[(sTab, sTab.n)], inc=(sTab, 1))
            ctx.on("vector", lambda e, dt=dt, l=l: e.scalar_tensor_tensor(
                hso[:, dt * TC:(dt + 1) * TC], scr[:],
                prm[:, PRM["scl"][0] + l * 4 + dt:PRM["scl"][0] + l * 4 + dt + 1],
                hso[:, dt * TC:(dt + 1) * TC], ALU.mult, ALU.add),
                   wait=[(sTab, sTab.n)], inc=(sTab, 1))
        emit_ln_inner(ctx, mk, SLN, hso, ybf, ybuf, rows, absb,
                      prm[:, PRM["ln2w"][0] + l * 4:PRM["ln2w"][0] + (l + 1) * 4],
                      prm[:, PRM["ln2b"][0] + l * 4:PRM["ln2b"][0] + (l + 1) * 4],
                      sqb, silu=False, first_wait=[(sTab, sTab.n)])
        ctx.on("vector", lambda e: e.tensor_copy(ybf[:], ybuf[:]),
               wait=[(SLN["n"], SLN["n"].n)], inc=(sTab, 1))
        ktc = sTab.n
        kffa = sFFA.n; kffe = sFFE.n; kffh = sFFH.n; kffe2 = sFFE2.n
        for hf in range(2):
            hbase = kffa + hf * 20
            for ft in range(16):
                gi = hf * 16 + ft
                bank = gi % 2
                for a in range(4):
                    w = []
                    if a == 0:
                        w = [(sTab, ktc), (sWL[0], 16 * (l + 1))]
                        if gi >= 2:
                            w.append((sFFE, kffe + gi - 1))
                    ctx.on("tensor",
                           lambda e, a=a, ft=ft, hf=hf, bank=bank: e.matmul(
                               bankf(mk, bank), w1b[:, a * 2048 + ft * 128:a * 2048 + (ft + 1) * 128],
                               ybf[:, a * TC + hf * TH:a * TC + hf * TH + TH],
                               start=(a == 0), stop=(a == 3)),
                           wait=w, inc=(sFFA, 1) if a == 3 else None)
                ctx.on("scalar",
                       lambda e, ft=ft, hf=hf, bank=bank, l=l: e.activation(
                           out=h1b[:, ft * TH:(ft + 1) * TH], in_=bankf(mk, bank),
                           func=AF.Silu, bias=prm[:, PRM["b1"][0] + l * 16 + ft:PRM["b1"][0] + l * 16 + ft + 1]),
                       wait=[(sFFA, hbase + ft + 1)], inc=(sFFE, 1))
            for dt in range(4):
                gi = hf * 4 + dt
                bank = 2 + gi % 2
                for f in range(16):
                    w = []
                    if f == 0:
                        w = [(sFFE, kffe + hf * 16 + 16), (sWL[1], 16 * (l + 1))]
                        if gi >= 2:
                            w.append((sFFE2, kffe2 + gi - 1))
                        if l == 0 and gi == 1:
                            w.append((sTab, 1))
                    ctx.on("tensor",
                           lambda e, f=f, dt=dt, bank=bank: e.matmul(
                               bankf(mk, bank), w2b[:, f * 512 + dt * 128:f * 512 + (dt + 1) * 128],
                               h1b[:, f * TH:(f + 1) * TH],
                               start=(f == 0), stop=(f == 15)),
                           wait=w, inc=(sFFA, 1) if f == 15 else None)
                ctx.on("scalar",
                       lambda e, dt=dt, hf=hf, bank=bank, l=l: e.activation(
                           out=ybuf[:, dt * TC + hf * TH:dt * TC + hf * TH + TH],
                           in_=bankf(mk, bank), func=AF.Identity,
                           bias=prm[:, PRM["b2"][0] + l * 4 + dt:PRM["b2"][0] + l * 4 + dt + 1]),
                       wait=[(sFFA, sFFA.n)], inc=(sFFE2, 1))
                ctx.on("gpsimd",
                       lambda e, dt=dt, hf=hf: e.tensor_tensor(
                           hso[:, dt * TC + hf * TH:dt * TC + hf * TH + TH],
                           hso[:, dt * TC + hf * TH:dt * TC + hf * TH + TH],
                           ybuf[:, dt * TC + hf * TH:dt * TC + hf * TH + TH], ALU.add),
                       wait=[(sFFE2, sFFE2.n)], inc=(sFFH, 1))
        if l < NL - 1:
            emit_wload(l + 1)

    emit_ln_inner(ctx, mk, SLN, hso, ybf, ybuf, rows, absb,
                  prm[:, PRM["lnlw"][0]:PRM["lnlw"][0] + 4],
                  prm[:, PRM["lnlb"][0]:PRM["lnlb"][0] + 4], sqb, silu=False,
                  first_wait=[(sFFH, sFFH.n)])
    ctx.on("vector", lambda e: e.tensor_copy(ybf[:], ybuf[:]),
           wait=[(SLN["n"], SLN["n"].n)], inc=(sHFB, 1))
    ctx.on("gpsimd", lambda e: e.dma_start(
        out=woutb[:].rearrange("p (a o) -> p a o", a=4),
        in_=mk.p2_woutT.rearrange("(a p) o -> p a o", p=128)),
        wait=[(sFFH, sFFH.n)], inc=(sWoL, 16))
    wo_ev = ctx.sb_at("p2_woev", 2 * TC, F32, RB + 136 * 1024)
    wo_rl = ctx.sb_at("p2_worl", 2 * TC, F32, RB + 144 * 1024)
    for o in range(NP):
        kf = o % 2
        w = [(sSp, 16), (sHFB, 1)] if o == 0 else []
        if o >= 2:
            w.append((sFin, o - 1))
        ctx.on("sync",
               lambda e, o=o, kf=kf: e.dma_start(
                   out=wo_rl[:, kf * TC:(kf + 1) * TC],
                   in_=mk.p2_xsp[:, o * TC:(o + 1) * TC]),
               wait=w, inc=(sRel[kf], 16))
        for th in range(2):
            idx = o * 2 + th
            bank = idx % 4
            for a in range(4):
                w2 = []
                if a == 0:
                    w2 = [(sHFB, 1), (sWoL, 16)]
                    if idx >= 4:
                        w2.append((sWoE, idx - 3))
                ctx.on("tensor",
                       lambda e, a=a, o=o, th=th, bank=bank: e.matmul(
                           bankf(mk, bank),
                           woutb[:, a * D + o * 128:a * D + (o + 1) * 128],
                           ybf[:, a * TC + th * TH:a * TC + th * TH + TH],
                           start=(a == 0), stop=(a == 3)),
                       wait=w2, inc=(sWoA, 1) if a == 3 else None)
            ctx.on("scalar",
                   lambda e, o=o, th=th, bank=bank: e.activation(
                       out=wo_ev[:, (o % 2) * TC + th * TH:(o % 2) * TC + th * TH + TH],
                       in_=bankf(mk, bank), func=AF.Identity,
                       bias=prm[:, PRM["bout"][0] + o:PRM["bout"][0] + o + 1]),
                   wait=[(sWoA, idx + 1), (sHFB, 1)] +
                        ([(sFin, o - 1)] if o >= 2 else []),
                   inc=(sWoE, 1))
        ctx.on("vector",
               lambda e, o=o, kf=kf: e.tensor_tensor(
                   mk.stream[:, o * TC:(o + 1) * TC],
                   wo_ev[:, kf * TC:(kf + 1) * TC],
                   wo_rl[:, kf * TC:(kf + 1) * TC], ALU.add),
               wait=[(sWoE, 2 * o + 2), (sRel[kf], 16 * (o // 2 + 1))], inc=(sFin, 1))

    norm_sems = []
    prev = None
    for hf in range(2):
        ctx.begin("half")
        extra = [] if prev is None else [(prev, 1)]
        sNorm, sSum = emit_ln_half(ctx, mk, f"p2ln{hf}", hf, sFin, NP,
                                   extra_first_wait=extra, alias_off=ln_off,
                                   alias_all=True, flat=True)
        norm_sems.append((sNorm, NP))
        prev = sSum
    return norm_sems


def host_inputs_p3(inp):
    wgT = np.ascontiguousarray(inp["L1_ffn_Wg"].T[PIDX])
    wuT = np.ascontiguousarray(inp["L1_ffn_Wu"].T[PIDX])
    wdT3 = np.ascontiguousarray(inp["L1_ffn_Wd"].T[:, PIDX])
    ag = np.ascontiguousarray(inp["L1_ffn_ag"].reshape(86, 128).T)
    au = np.ascontiguousarray(inp["L1_ffn_au"].reshape(86, 128).T)
    ad = np.ascontiguousarray(inp["L1_ffn_ad"][PIDX].reshape(32, 128).T)
    return {"p3_wgT": wgT, "p3_wuT": wuT, "p3_wdT": wdT3,
            "p3_ag": ag, "p3_au": au, "p3_ad": ad}


def declare_p3(ctx, mk):
    nc = ctx.nc
    I3 = 11008
    mk.p3_wgT = nc.dram_tensor("p3_wgT", [D, I3], F32, kind="ExternalInput")
    mk.p3_wuT = nc.dram_tensor("p3_wuT", [D, I3], F32, kind="ExternalInput")
    mk.p3_wdT = nc.dram_tensor("p3_wdT", [I3, D], F32, kind="ExternalInput")
    mk.p3_ag = nc.dram_tensor("p3_ag", [128, 86], F32, kind="ExternalInput")
    mk.p3_au = nc.dram_tensor("p3_au", [128, 86], F32, kind="ExternalInput")
    mk.p3_ad = nc.dram_tensor("p3_ad", [128, 32], F32, kind="ExternalInput")
    mk.p3_xsp = nc.dram_tensor("p3_xsp", [128, NP * TC], F32)
    mk.p3_dacc = nc.dram_tensor("p3_dacc", [NP, 128, TC], F32)


def emit_p3(ctx, mk, s_ready):
    ctx.begin("phase")
    ctx.begin("half")
    I3 = 11008
    CH_TILES = [22, 22, 21, 21]
    CH_I0 = [0, 22, 44, 65]

    RB = 16512
    xbf = ctx.sb_at("p3_xbf", NP * TC, BF16, RB + 96 * 1024)
    hbuf = ctx.sb_at("p3_h", 22 * TC, BF16, RB)
    wg = [ctx.sb_at("p3_wg0", D, BF16, RB + 45056),
          ctx.sb_at("p3_wg1", D, BF16, RB + 45056 + 8192)]
    wu = [ctx.sb_at("p3_wu0", D, BF16, RB + 61440),
          ctx.sb_at("p3_wu1", D, BF16, RB + 61440 + 8192)]
    wd = [ctx.sb_at("p3_wd0", 22 * 128, BF16, RB + 77824),
          ctx.sb_at("p3_wd1", 22 * 128, BF16, RB + 77824 + 5632)]
    gs = [ctx.sb_at("p3_gs0", TH, F32, RB + 89088),
          ctx.sb_at("p3_gs1", TH, F32, RB + 89088 + 2048)]
    us = [ctx.sb_at("p3_us0", TH, F32, RB + 93184),
          ctx.sb_at("p3_us1", TH, F32, RB + 93184 + 2048)]
    dsg = [ctx.sb_at("p3_dsg0", TH, F32, RB + 162 * 1024),
           ctx.sb_at("p3_dsg1", TH, F32, RB + 162 * 1024 + 2048)]
    XB = RB + 128 * 1024
    fin = [ctx.sb_at("p3_fin0", 2 * TC, F32, XB),
           ctx.sb_at("p3_fin1", 2 * TC, F32, XB + 8192)]
    dst_ = [ctx.sb_at("p3_dst0", TC, F32, XB + 16384),
            ctx.sb_at("p3_dst1", TC, F32, XB + 16384 + 4096)]
    agt = ctx.sb_at("p3_agt", 86, F32, RB + 166 * 1024)
    aut = ctx.sb_at("p3_aut", 86, F32, RB + 166 * 1024 + 512)
    adt = ctx.sb_at("p3_adt", 32, F32, RB + 166 * 1024 + 1024)
    ln_off = RB + 176 * 1024

    sLd = ctx.dsem("p3_sLd")
    sSp = ctx.dsem("p3_sSp")
    sXC = ctx.sem("p3_sXC")
    sWg = [ctx.dsem("p3_sWg0"), ctx.dsem("p3_sWg1")]
    sWgF = [ctx.sem("p3_sWgF0"), ctx.sem("p3_sWgF1")]
    sWu = [ctx.dsem("p3_sWu0"), ctx.dsem("p3_sWu1")]
    sWuF = [ctx.sem("p3_sWuF0"), ctx.sem("p3_sWuF1")]
    sWd = [ctx.dsem("p3_sWd0"), ctx.dsem("p3_sWd1")]
    sWdF = [ctx.sem("p3_sWdF0"), ctx.sem("p3_sWdF1")]
    sGA = ctx.sem("p3_sGA")
    sGE = ctx.sem("p3_sGE")
    sH = ctx.sem("p3_sH")
    sDA = ctx.sem("p3_sDA")
    sDE = ctx.sem("p3_sDE")
    sDD = [ctx.dsem("p3_sDD0"), ctx.dsem("p3_sDD1")]
    sFL = [ctx.dsem("p3_sFL0"), ctx.dsem("p3_sFL1")]
    sFin = ctx.sem("p3_sFin")
    sFA = ctx.sem("p3_sFA")
    sFS = ctx.sem("p3_sFS")

    sSp2 = ctx.dsem("p3_sSp2")
    ctx.on("scalar", lambda e: e.dma_start(out=agt[:, 0:86], in_=mk.p3_ag[:, :]),
           wait=list(s_ready), inc=(sLd, 16))
    ctx.on("scalar", lambda e: e.dma_start(out=aut[:, 0:86], in_=mk.p3_au[:, :]), inc=(sLd, 16))
    ctx.on("scalar", lambda e: e.dma_start(out=adt[:, 0:32], in_=mk.p3_ad[:, :]), inc=(sLd, 16))
    ctx.on("sync", lambda e: e.dma_start(out=mk.p3_xsp[:, 0:16 * TC],
                                         in_=mk.stream[:, 0:16 * TC]),
           wait=list(s_ready), inc=(sSp, 16))
    ctx.on("scalar", lambda e: e.dma_start(out=mk.p3_xsp[:, 16 * TC:NP * TC],
                                           in_=mk.stream[:, 16 * TC:NP * TC]),
           wait=list(s_ready), inc=(sSp2, 16))
    for p in range(NP - 1, -1, -1):
        ctx.on("vector",
               lambda e, p=p: e.tensor_copy(xbf[:, p * TC:(p + 1) * TC],
                                            mk.stream[:, p * TC:(p + 1) * TC]),
               wait=[(sSp, 16), (sSp2, 16)] if p == NP - 1 else [(sXC, NP - 1 - p)],
               inc=(sXC, 1))

    def emit_dn_load(ld):
        if ld >= 4 * NP:
            return
        ic2 = ld // NP
        o2 = ld % NP
        nt2 = CH_TILES[ic2]
        i0 = CH_I0[ic2] * 128
        ni = nt2 * 128
        par = ld % 2
        w = [(sWdF[par], ld // 2)] if ld >= 2 else [(sXC, NP)]
        ctx.on("gpsimd",
               lambda e, i0=i0, ni=ni, o2=o2, par=par, nt2=nt2: e.dma_start(
                   out=wd[par][:, 0:nt2 * 128].rearrange("p (a o) -> p a o", a=nt2),
                   in_=mk.p3_wdT[i0:i0 + ni, o2 * 128:(o2 + 1) * 128].rearrange(
                       "(a p) o -> p a o", p=128)),
               wait=w, inc=(sWd[par], 16))

    nlg = 0; nld = 0; ng = 0; nd = 0
    emit_dn_load(0)
    emit_dn_load(1)
    for ic in range(4):
        nt = CH_TILES[ic]
        for j in range(nt):
            it = CH_I0[ic] + j
            lg = nlg + j
            par = lg % 2
            w = [(sXC, NP)] if lg < 2 else [(sWgF[par], lg // 2)]
            ctx.on("gpsimd",
                   lambda e, it=it, par=par: e.dma_start(
                       out=wg[par][:].rearrange("p (a i) -> p a i", a=NP),
                       in_=mk.p3_wgT[:, it * 128:(it + 1) * 128].rearrange(
                           "(a p) i -> p a i", p=128)),
                   wait=w, inc=(sWg[par], 16))
            w = [(sWuF[par], lg // 2)] if lg >= 2 else [(sXC, NP)]
            ctx.on("gpsimd",
                   lambda e, it=it, par=par: e.dma_start(
                       out=wu[par][:].rearrange("p (a i) -> p a i", a=NP),
                       in_=mk.p3_wuT[:, it * 128:(it + 1) * 128].rearrange(
                           "(a p) i -> p a i", p=128)),
                   wait=w, inc=(sWu[par], 16))
        for j in range(nt):
            it = CH_I0[ic] + j
            parw = nlg % 2
            for th in range(2):
                for a in range(NP):
                    w = []
                    if a == 0:
                        w = [(sWg[parw], 16 * (nlg // 2 + 1))]
                        if ng >= 2:
                            w.append((sGE, 2 * (ng - 1)))
                    ctx.on("tensor",
                           lambda e, a=a, th=th, parw=parw: e.matmul(
                               bankf(mk, th), wg[parw][:, a * 128:(a + 1) * 128],
                               xbf[:, a * TC + th * TH:a * TC + th * TH + TH],
                               start=(a == 0), stop=(a == NP - 1)),
                           wait=w, inc=(sGA, 1) if a == NP - 1 else None)
                for a in range(NP):
                    w = []
                    if a == 0:
                        w = [(sWu[parw], 16 * (nlg // 2 + 1))]
                        if ng >= 2:
                            w.append((sGE, 2 * (ng - 1)))
                    ctx.on("tensor",
                           lambda e, a=a, th=th, parw=parw: e.matmul(
                               bankf(mk, 2 + th), wu[parw][:, a * 128:(a + 1) * 128],
                               xbf[:, a * TC + th * TH:a * TC + th * TH + TH],
                               start=(a == 0), stop=(a == NP - 1)),
                           wait=w, inc=(sGA, 1) if a == NP - 1 else None)
                kge = ng % 2
                ctx.on("scalar",
                       lambda e, th=th, it=it, kge=kge: e.activation(
                           out=gs[kge][:], in_=bankf(mk, th), func=AF.Silu,
                           scale=agt[:, it:it + 1]),
                       wait=[(sGA, 2 * ng + 2), (sLd, 48)] +
                            ([(sH, ng - 1)] if ng >= 2 else []),
                       inc=(sGE, 1))
                ctx.on("scalar",
                       lambda e, th=th, it=it, kge=kge: e.activation(
                           out=us[kge][:], in_=bankf(mk, 2 + th), func=AF.Identity,
                           scale=aut[:, it:it + 1]),
                       inc=(sGE, 1))
                wh = [(sGE, 2 * ng + 2)]
                if j == 0 and th == 0 and ic >= 1:
                    wh.append((sDA, 64 * ic))
                ctx.on("vector",
                       lambda e, j=j, th=th, kge=kge: e.tensor_tensor(
                           hbuf[:, j * TC + th * TH:j * TC + th * TH + TH],
                           gs[kge][:], us[kge][:], ALU.mult),
                       wait=wh, inc=(sH, 1))
                ng += 1
            ctx.on("vector", lambda e: e.engine_nop(), wait=[(sGA, 2 * ng)],
                   inc=(sWgF[parw], 1))
            ctx.on("vector", lambda e: e.engine_nop(), inc=(sWuF[parw], 1))
            nlg += 1
        for o in range(NP):
            parw = nld % 2
            for th in range(2):
                bankd = 4 + nd % 2
                for j in range(nt):
                    w = []
                    if j == 0:
                        w = [(sWd[parw], 16 * (nld // 2 + 1)), (sH, 2 * (CH_I0[ic] + nt))]
                        if nd >= 2:
                            w.append((sDE, nd - 1))
                    ctx.on("tensor",
                           lambda e, j=j, th=th, parw=parw, bankd=bankd, nt=nt: e.matmul(
                               bankf(mk, bankd), wd[parw][:, j * 128:(j + 1) * 128],
                               hbuf[:, j * TC + th * TH:j * TC + th * TH + TH],
                               start=(j == 0), stop=(j == nt - 1)),
                           wait=w, inc=(sDA, 1) if j == nt - 1 else None)
                kd = nd % 2
                ctx.on("vector",
                       lambda e, bankd=bankd, kd=kd: e.tensor_copy(dsg[kd][:], bankf(mk, bankd)),
                       wait=[(sDA, nd + 1)] + ([(sDD[kd], 16 * (nd // 2))] if nd >= 2 else []),
                       inc=(sDE, 1))
                op = ALU.bypass if ic == 0 else ALU.add
                ctx.on("gpsimd",
                       lambda e, o=o, th=th, kd=kd, op=op: e.dma_start(
                           out=mk.p3_dacc[o, :, th * TH:th * TH + TH], in_=dsg[kd][:],
                           accum_op=op),
                       wait=[(sDE, nd + 1)] + ([(sDD[kd], 16 * (nd // 2))] if nd >= 2 else []),
                       inc=(sDD[kd], 16))
                nd += 1
            ctx.on("vector", lambda e, parw=parw: e.engine_nop(),
                   wait=[(sDA, nd)], inc=(sWdF[parw], 1))
            emit_dn_load(nld + 2)
            nld += 1

    sFLx = [ctx.dsem("p3_sFLx0"), ctx.dsem("p3_sFLx1")]
    for o in range(NP):
        kf = o % 2
        w = [(sDD[0], 16 * (97 + o)), (sDD[1], 16 * (97 + o))]
        if o >= 2:
            w.append((sFin, o - 1))
        ctx.on("sync",
               lambda e, o=o, kf=kf: e.dma_start(
                   out=fin[kf][:, 0:TC], in_=mk.p3_dacc[o, :, :]),
               wait=w, inc=(sFL[kf], 16))
        ctx.on("gpsimd",
               lambda e, o=o, kf=kf: e.dma_start(
                   out=fin[kf][:, TC:2 * TC], in_=mk.p3_xsp[:, o * TC:(o + 1) * TC]),
               wait=w, inc=(sFLx[kf], 16))
        ctx.on("scalar",
               lambda e, o=o, kf=kf: e.activation(
                   out=dst_[kf][:], in_=fin[kf][:, 0:TC], func=AF.Identity,
                   scale=adt[:, o:o + 1]),
               wait=[(sFL[kf], 16 * (o // 2 + 1))] + ([(sFin, o - 1)] if o >= 2 else []),
               inc=(sFS, 1))
        ctx.on("vector",
               lambda e, o=o, kf=kf: e.tensor_tensor(
                   mk.stream[:, o * TC:(o + 1) * TC], dst_[kf][:],
                   fin[kf][:, TC:2 * TC], ALU.add),
               wait=[(sFS, o + 1), (sFLx[kf], 16 * (o // 2 + 1))], inc=(sFin, 1))

    norm_sems = []
    prev = None
    for hf in range(2):
        ctx.begin("half")
        extra = [] if prev is None else [(prev, 1)]
        sNorm, sSum = emit_ln_half(ctx, mk, f"p3ln{hf}", hf, sFin, NP,
                                   extra_first_wait=extra, alias_off=ln_off, alias_all=True,
                                   flat=True)
        norm_sems.append((sNorm, NP))
        prev = sSum
    return norm_sems


def build_kernel(phases=("p1",)):
    nc = bass.Bass("TRN2", target_bir_lowering=False, debug=False)
    ctx = Ctx(nc)
    mk = MK()
    declare_common(ctx, mk)
    if "p0" in phases:
        declare_p0(ctx, mk)
    if "p1" in phases:
        declare_p1(ctx, mk)
    if "p2" in phases:
        declare_p2(ctx, mk)
    if "p3" in phases:
        declare_p3(ctx, mk)
    emit_init_loads(ctx, mk)
    ready = mk.init_ready
    if "p0" in phases:
        norm = emit_p0(ctx, mk, ready)
        ready = norm
    if "p1" in phases:
        import os as _os
        norm = emit_p1(ctx, mk, ready, trunc=_os.environ.get("P1_TRUNC"))
        ready = norm
    if "p2" in phases:
        norm = emit_p2(ctx, mk, ready)
        ready = norm
    if "p3" in phases:
        norm = emit_p3(ctx, mk, ready)
        ready = norm
    emit_store_out(ctx, mk, list(ready))
    ctx.finalize()
    return nc


def kernel(**inputs):
    inp = {k: np.asarray(v) for k, v in inputs.items()}
    x = np.asarray(inp["L0_attn_x"], np.float32)
    shards = shard_x(x)
    ins = dict(host_consts(), **host_inputs_p0(inp), **host_inputs_p1(inp),
               **host_inputs_p2(inp), **host_inputs_p3(inp))
    nc = build_kernel(("p0", "p1", "p2", "p3"))
    in_maps = []
    for c in range(NCORE):
        m = np.zeros((1, 8), np.float32)
        m[0, c] = 1.0
        in_maps.append(dict(ins, x_in=shards[c], p2_mask=m))
    res = run_bass_kernel_spmd(nc, in_maps, core_ids=list(range(NCORE)))
    parts = [np.asarray(res.results[c]["y_out"]) for c in range(NCORE)]
    return unshard_out(parts).astype(np.float32)


# version 08 (final)
# speedup vs baseline: 1.0379x; 1.0028x over previous
import numpy as np
import ml_dtypes
import concourse.bass as bass
import concourse.mybir as mybir
from concourse.bass_utils import run_bass_kernel_spmd

F32 = mybir.dt.float32
BF16 = mybir.dt.bfloat16
AF = mybir.ActivationFunctionType
ALU = mybir.AluOpType

NCORE = 8
S = 8192
D = 4096
TC = S // NCORE
NP = D // 128
TH = 512
EPS = 1e-5

PIDX = ((np.arange(D) % 128) * 32 + (np.arange(D) // 128)).astype(np.int64)
BF = ml_dtypes.bfloat16


class Sem:
    __slots__ = ("h", "_n", "name")
    def __init__(self, h, name):
        self.h = h; self._n = 0; self.name = name
    @property
    def n(self):
        return self._n
    def bump(self, k):
        self._n += k
    def absval(self, v):
        return v


class SemView:
    __slots__ = ("s", "base", "name")
    def __init__(self, s, name):
        self.s = s; self.base = s.n; self.name = name
    @property
    def h(self):
        return self.s.h
    @property
    def n(self):
        return self.s.n - self.base
    def bump(self, k):
        self.s.bump(k)
    def absval(self, v):
        return self.base + v


class Ctx:
    ENGINES = ("sync", "tensor", "vector", "scalar", "gpsimd")

    def __init__(self, nc):
        self.nc = nc
        self.q = {e: [] for e in self.ENGINES}
        self.sbuf_off = 16512
        self.sbuf_limit = nc.SBUF_PARTITION_SIZE_BYTES
        self.peak = 0

    def sem_global(self, name):
        return Sem(self.nc.alloc_semaphore(name), name)

    def begin(self, level):
        if not hasattr(self, "_pools"):
            self._pools = {"phase": [[], []], "half": [[], []]}
            self._dpools = {"phase": [[], []], "half": [[], []]}
            self._gen = {"phase": 0, "half": 0}
        g = self._gen[level] % 2
        self._cur = self._pools[level][g]
        self._dcur = self._dpools[level][g]
        self._gen[level] += 1
        self._pos = 0
        self._dpos = 0

    def _draw(self, pool, pos, name):
        if pos < len(pool):
            s = pool[pos]
        else:
            s = Sem(self.nc.alloc_semaphore(name), name)
            pool.append(s)
        return SemView(s, name)

    def sem(self, name):
        if not hasattr(self, "_cur") or self._cur is None:
            return self.sem_global(name)
        v = self._draw(self._cur, self._pos, name)
        self._pos += 1
        return v

    def dsem(self, name):
        if not hasattr(self, "_dcur") or self._dcur is None:
            return self.sem_global(name)
        v = self._draw(self._dcur, self._dpos, name)
        self._dpos += 1
        return v

    def sb(self, name, cols, dtype):
        nbytes = cols * (4 if dtype == F32 else 2)
        off = (self.sbuf_off + 63) & ~63
        assert off + nbytes <= self.sbuf_limit, (name, off, nbytes)
        t = self.nc.alloc_sbuf_tensor_at(name, [128, cols], dtype, offset=off)
        self.sbuf_off = off + nbytes
        self.peak = max(self.peak, self.sbuf_off)
        return t

    def sb_at(self, name, cols, dtype, off):
        nbytes = cols * (4 if dtype == F32 else 2)
        assert off + nbytes <= self.sbuf_limit, (name, off, nbytes)
        return self.nc.alloc_sbuf_tensor_at(name, [128, cols], dtype, offset=off)

    def mark(self):
        return self.sbuf_off

    def release(self, mark):
        self.sbuf_off = mark

    def on(self, eng, fn, wait=(), inc=None, incs=()):
        allincs = list(incs) + ([inc] if inc is not None else [])
        for s, k in allincs:
            s.bump(k)
        w = [(s.h, s.absval(v)) for (s, v) in wait]
        def run(e, w=w, fn=fn, allincs=allincs):
            for h, v in w:
                e.wait_ge(h, v)
            r = fn(e)
            for s, k in allincs:
                r = r.then_inc(s.h, k)
        self.q[eng].append(run)

    def finalize(self):
        nc = self.nc
        with nc.Block() as block:
            @block.sync
            def _(e, q=self.q["sync"]):
                for f in q: f(e)
            @block.tensor
            def _(e, q=self.q["tensor"]):
                for f in q: f(e)
            @block.vector
            def _(e, q=self.q["vector"]):
                for f in q: f(e)
            @block.scalar
            def _(e, q=self.q["scalar"]):
                for f in q: f(e)
            @block.gpsimd
            def _(e, q=self.q["gpsimd"]):
                for f in q: f(e)


def w_only(ctx, eng, *pairs):
    w = [(s.h, s.absval(v)) for (s, v) in pairs]
    def run(e, w=w):
        for h, v in w:
            e.wait_ge(h, v)
    ctx.q[eng].append(run)


def host_consts():
    rbc = np.zeros((8, 128), dtype=np.float32)
    for e in range(8):
        rbc[e, e * 16:(e + 1) * 16] = 2.0
    return {
        "c_eyef": np.eye(128, dtype=np.float32),
        "c_eyeb": np.eye(128, dtype=BF),
        "c_onesb": np.ones((128, 1), dtype=BF),
        "c_ones1f": np.ones((1, 128), dtype=np.float32),
        "c_rbc": rbc,
    }


def host_inputs_p0(inp):
    wq = np.ascontiguousarray(inp["L0_attn_Wq"].transpose(2, 0, 1)[:, PIDX, :])
    wk = np.ascontiguousarray(inp["L0_attn_Wk"].transpose(2, 0, 1)[:, PIDX, :])
    wv = np.ascontiguousarray(inp["L0_attn_Wv"].transpose(2, 0, 1)[:, PIDX, :])
    return {"p0_wq": wq, "p0_wk": wk, "p0_wv": wv,
            "p0_bq": np.ascontiguousarray(inp["L0_attn_bq"]),
            "p0_bk": np.ascontiguousarray(inp["L0_attn_bk"]),
            "p0_bv": np.ascontiguousarray(inp["L0_attn_bv"])}


def host_inputs_p1(inp):
    baseT = np.ascontiguousarray(inp["L0_ffn_base_W"].T[PIDX][:, PIDX])
    wrT = np.ascontiguousarray(inp["L0_ffn_router_W"].T[PIDX])
    laA = np.ascontiguousarray(inp["L0_ffn_lora_A"].transpose(1, 0, 2).reshape(D, 128)[PIDX])
    laB = np.ascontiguousarray(inp["L0_ffn_lora_B"].reshape(128, D)[:, PIDX])
    return {"p1_baseT": baseT, "p1_wrT": wrT, "p1_laA": laA, "p1_laB": laB}


def shard_x(x):
    x2 = np.ascontiguousarray(x.reshape(S, D)[:, PIDX].T)
    return [np.ascontiguousarray(x2[:, c * TC:(c + 1) * TC]) for c in range(NCORE)]


def unshard_out(parts):
    full = np.concatenate(parts, axis=1)
    out = np.empty((S, D), dtype=np.float32)
    out[:, PIDX] = full.T
    return out.reshape(1, S, D)


class MK:
    pass


def bankf(mk, b, n=512):
    return mk.psumf[:, b * 512:b * 512 + n]


def declare_common(ctx, mk):
    nc = ctx.nc
    mk.x_in = nc.dram_tensor("x_in", [D, TC], F32, kind="ExternalInput")
    mk.y_out = nc.dram_tensor("y_out", [D, TC], F32, kind="ExternalOutput")
    mk.c_eyef = nc.dram_tensor("c_eyef", [128, 128], F32, kind="ExternalInput")
    mk.c_eyeb = nc.dram_tensor("c_eyeb", [128, 128], BF16, kind="ExternalInput")
    mk.c_onesb = nc.dram_tensor("c_onesb", [128, 1], BF16, kind="ExternalInput")
    mk.c_ones1f = nc.dram_tensor("c_ones1f", [1, 128], F32, kind="ExternalInput")
    mk.c_rbc = nc.dram_tensor("c_rbc", [8, 128], F32, kind="ExternalInput")

    mk.stream = ctx.sb("stream", NP * TC, F32)
    mk.xbh = ctx.sb("xbh", NP * TH, BF16)
    mk.eyef = ctx.sb("eyef", 128, F32)
    mk.eyeb = ctx.sb("eyeb", 128, BF16)
    mk.onesb = ctx.sb("onesb", 1, BF16)
    mk.ones1f = ctx.sb("ones1f", 128, F32)
    mk.rbc = ctx.sb("rbc", 128, F32)
    mk.epsc = ctx.sb("epsc", 1, F32)
    mk.psumf = nc.alloc_psum_tensor("psumf", [128, 6 * 512], F32)
    mk.psumb = nc.alloc_psum_tensor("psumb", [128, 2048], BF16)
    mk.sInit = ctx.sem("sInit")


def emit_init_loads(ctx, mk):
    s = mk.sInit
    for src, dst in ((mk.c_eyef, mk.eyef[:]), (mk.c_eyeb, mk.eyeb[:]),
                     (mk.c_onesb, mk.onesb[:]), (mk.c_ones1f, mk.ones1f[0:1, :]),
                     (mk.c_rbc, mk.rbc[0:8, :])):
        ctx.on("sync", lambda e, src=src, dst=dst: e.dma_start(out=dst, in_=src[:, :]),
               inc=(s, 16))
    sX2 = ctx.sem("sInitX")
    xrA = mk.x_in[0:16 * 128, :].rearrange("(a p) t -> p a t", p=128)
    xrB = mk.x_in[16 * 128:D, :].rearrange("(a p) t -> p a t", p=128)
    soA = mk.stream[:, 0:16 * TC].rearrange("p (a t) -> p a t", a=16)
    soB = mk.stream[:, 16 * TC:NP * TC].rearrange("p (a t) -> p a t", a=16)
    ctx.on("sync", lambda e: e.dma_start(out=soA, in_=xrA), inc=(s, 16))
    ctx.on("scalar", lambda e: e.dma_start(out=soB, in_=xrB), inc=(sX2, 16))
    mk.sConst = ctx.sem("sConst")
    ctx.on("vector", lambda e: e.memset(mk.epsc[:], EPS), inc=(mk.sConst, 1))
    mk.init_ready = [(s, 96), (sX2, 16)]


def emit_cast_xbh(ctx, mk, hf, waits):
    sC = ctx.sem(f"cast_h{hf}_{len(ctx.q['vector'])}")
    for p in range(NP):
        ctx.on("vector",
               lambda e, p=p: e.tensor_copy(
                   mk.xbh[:, p * TH:(p + 1) * TH],
                   mk.stream[:, p * TC + hf * TH:p * TC + hf * TH + TH]),
               wait=waits if p == 0 else (), inc=(sC, 1))
    return sC


def emit_store_out(ctx, mk, waits):
    if len(waits) != 2:
        sSt = ctx.sem("sSt")
        yr = mk.y_out.rearrange("(a p) t -> p a t", p=128)
        so = mk.stream[:].rearrange("p (a t) -> p a t", a=NP)
        ctx.on("sync", lambda e, yr=yr, so=so: e.dma_start(out=yr, in_=so),
               wait=waits, inc=(sSt, 16))
        w_only(ctx, "sync", (sSt, 16))
        return
    sStA = ctx.sem_global("sStA")
    sStB = ctx.sem_global("sStB")
    so3 = mk.stream[:].rearrange("p (a t) -> p a t", a=NP)
    for hf in range(2):
        yr3 = mk.y_out[:, hf * TH:hf * TH + TH].rearrange("(a p) t -> p a t", p=128)
        w = [waits[hf]]
        ctx.on("sync",
               lambda e, yr3=yr3, hf=hf: e.dma_start(
                   out=yr3[:, 0:16, :], in_=so3[:, 0:16, hf * TH:hf * TH + TH]),
               wait=w, inc=(sStA, 16))
        ctx.on("gpsimd",
               lambda e, yr3=yr3, hf=hf: e.dma_start(
                   out=yr3[:, 16:NP, :], in_=so3[:, 16:NP, hf * TH:hf * TH + TH]),
               wait=w, inc=(sStB, 16))
    w_only(ctx, "sync", (sStA, 32), (sStB, 32))


def emit_ln_half(ctx, mk, tag, hf, s_add, add_target, extra_first_wait=(), alias_off=None,
                 flat=False, alias_all=False):
    mark = ctx.mark()
    ybf = mk.xbh
    if alias_all and alias_off is not None:
        sq = ctx.sb_at(f"{tag}_sq", 2 * TH, BF16, alias_off)
        rab = ctx.sb_at(f"{tag}_rab", 2 * TH, F32, alias_off + 2 * TH * 2)
        absb = ctx.sb_at(f"{tag}_ab", 2 * TH, F32, alias_off + 2 * TH * 2 + 2 * TH * 4)
    elif alias_off is not None:
        sq = ctx.sb(f"{tag}_sq", 2 * TH, BF16)
        rab = ctx.sb(f"{tag}_rab", 2 * TH, F32)
        absb = ctx.sb_at(f"{tag}_ab", 2 * TH, F32, alias_off)
    else:
        sq = ctx.sb(f"{tag}_sq", 2 * TH, BF16)
        rab = ctx.sb(f"{tag}_rab", 2 * TH, F32)
        absb = ctx.sb(f"{tag}_ab", 2 * TH, F32)

    sCast = ctx.sem(f"{tag}_cast")
    sSq = ctx.sem(f"{tag}_sq")
    sMM = ctx.sem(f"{tag}_mm")
    sSum = ctx.sem(f"{tag}_sum")
    sStat = ctx.sem(f"{tag}_stat")
    sBc = ctx.sem(f"{tag}_bc")
    sBcE = ctx.sem(f"{tag}_bce")
    sNorm = ctx.sem(f"{tag}_norm")

    row_a = rab[0:1, 0:TH]
    row_b = rab[0:1, TH:2 * TH]
    psum6 = mk.psumf[0:1, 5 * 512:5 * 512 + TH]
    psum5 = mk.psumf[0:1, 4 * 512:4 * 512 + TH]

    for p in range(NP):
        st = mk.stream[:, p * TC + hf * TH:p * TC + hf * TH + TH]
        tw = [(s_add, add_target)] if flat else [(s_add, add_target + p)]
        ctx.on("vector",
               lambda e, p=p, st=st: e.tensor_copy(ybf[:, p * TH:(p + 1) * TH], st),
               wait=(tw + (list(extra_first_wait) if p == 0 else [])),
               inc=(sCast, 1))
        ctx.on("scalar",
               lambda e, p=p, st=st: e.activation(
                   out=sq[:, (p % 2) * TH:(p % 2) * TH + TH], in_=st, func=AF.Square),
               wait=tw + ([(sMM, p - 1)] if p >= 2 else []) +
                    (list(extra_first_wait) if p == 0 else []),
               inc=(sSq, 1))
        ctx.on("tensor",
               lambda e, p=p: e.matmul(
                   psum6, mk.onesb[:, 0:1], ybf[:, p * TH:(p + 1) * TH],
                   start=(p == 0), stop=(p == NP - 1)),
               wait=[(sCast, p + 1)])
        ctx.on("tensor",
               lambda e, p=p: e.matmul(
                   psum5, mk.onesb[:, 0:1], sq[:, (p % 2) * TH:(p % 2) * TH + TH],
                   start=(p == 0), stop=(p == NP - 1)),
               wait=[(sSq, p + 1)],
               inc=(sMM, 1) if p < NP - 1 else (sSum, 1))

    c = 1.0 / D
    ctx.on("vector", lambda e: e.tensor_scalar_mul(row_b, psum6, c),
           wait=[(sSum, 1)], inc=(sStat, 1))
    ctx.on("vector", lambda e: e.tensor_scalar_mul(row_a, psum5, c),
           wait=[(sStat, 1)], inc=(sStat, 1))
    ctx.on("vector", lambda e: e.tensor_tensor(psum6, row_b, row_b, ALU.mult),
           wait=[(sStat, 2)], inc=(sStat, 1))
    ctx.on("vector", lambda e: e.tensor_tensor(row_a, row_a, psum6, ALU.subtract),
           wait=[(sStat, 3)], inc=(sStat, 1))
    ctx.on("scalar", lambda e: e.activation(out=row_a, in_=row_a, func=AF.Ln,
                                            bias=mk.epsc[0:1, 0:1]),
           wait=[(sStat, 4), (mk.sConst, 1)], inc=(sStat, 1))
    ctx.on("scalar", lambda e: e.activation(out=row_a, in_=row_a, func=AF.Exp, scale=-0.5),
           wait=[(sStat, 5)], inc=(sStat, 1))
    ctx.on("vector", lambda e: e.tensor_tensor(row_b, row_b, row_a, ALU.mult),
           wait=[(sStat, 6)], inc=(sStat, 1))
    ctx.on("vector", lambda e: e.tensor_scalar_mul(row_b, row_b, -1.0),
           wait=[(sStat, 7)], inc=(sStat, 1))

    ctx.on("tensor", lambda e: e.matmul(bankf(mk, 5), mk.ones1f[0:1, :], row_a,
                                        start=True, stop=True),
           wait=[(sStat, 8)], inc=(sBc, 1))
    ctx.on("tensor", lambda e: e.matmul(bankf(mk, 4), mk.ones1f[0:1, :], row_b,
                                        start=True, stop=True),
           wait=[(sStat, 8)], inc=(sBc, 1))
    ctx.on("vector", lambda e: e.tensor_copy(absb[:, 0:TH], bankf(mk, 5)),
           wait=[(sBc, 2)], inc=(sBcE, 1))
    ctx.on("vector", lambda e: e.tensor_copy(absb[:, TH:2 * TH], bankf(mk, 4)),
           wait=[(sBcE, 1)], inc=(sBcE, 1))

    sNm = ctx.sem(f"{tag}_nm")
    for p in range(NP):
        st = mk.stream[:, p * TC + hf * TH:p * TC + hf * TH + TH]
        ctx.on("vector",
               lambda e, st=st: e.tensor_tensor(st, st, absb[:, 0:TH], ALU.mult),
               wait=[(sBcE, 2)] if p == 0 else (), inc=(sNm, 1))
        ctx.on("vector",
               lambda e, st=st: e.tensor_tensor(st, st, absb[:, TH:2 * TH], ALU.add),
               wait=[(sNm, p + 1)], inc=(sNorm, 1))
    ctx.release(mark)
    return sNorm, sSum


def declare_p0(ctx, mk):
    nc = ctx.nc
    mk.p0_wq = nc.dram_tensor("p0_wq", [32, D, 128], F32, kind="ExternalInput")
    mk.p0_wk = nc.dram_tensor("p0_wk", [8, D, 128], F32, kind="ExternalInput")
    mk.p0_wv = nc.dram_tensor("p0_wv", [8, D, 128], F32, kind="ExternalInput")
    mk.p0_bq = nc.dram_tensor("p0_bq", [128, 32], F32, kind="ExternalInput")
    mk.p0_bk = nc.dram_tensor("p0_bk", [128, 8], F32, kind="ExternalInput")
    mk.p0_bv = nc.dram_tensor("p0_bv", [128, 8], F32, kind="ExternalInput")
    mk.p0_qd = nc.dram_tensor("p0_qd", [8, TH, 128], BF16)
    mk.p0_kd = nc.dram_tensor("p0_kd", [8, TH, 128], BF16)


def emit_p0(ctx, mk, s_ready):
    ctx.begin("phase")
    mark = ctx.mark()
    wbuf = [ctx.sb("p0_wb0", D, BF16), ctx.sb("p0_wb1", D, BF16)]
    qc_base = (ctx.sbuf_off + 63) & ~63
    qc = [ctx.sb("p0_qc0", TH, BF16), ctx.sb("p0_qc1", TH, BF16)]
    vc = ctx.sb("p0_vc", 8 * TH, BF16)
    tev = [ctx.sb("p0_tev0", 128, BF16), ctx.sb("p0_tev1", 128, BF16)]
    qt_alias = (ctx.sbuf_off + 63) & ~63
    qt = [ctx.sb("p0_qt0", 16 * 128, BF16), ctx.sb("p0_qt1", 16 * 128, BF16)]
    kt = [ctx.sb("p0_kt0", 16 * 128, BF16), ctx.sb("p0_kt1", 16 * 128, BF16)]
    vcat = [ctx.sb("p0_vcat0", 9 * 56, BF16), ctx.sb("p0_vcat1", 9 * 56, BF16)]
    ubuf = [ctx.sb_at("p0_u0", 512, BF16, qc_base),
            ctx.sb_at("p0_u1", 512, BF16, qc_base + 1024)]
    zinv = ctx.sb("p0_zinv", 56, F32)
    tmpa = ctx.sb("p0_tmpa", 56, F32)
    bq = ctx.sb("p0_bq", 32, F32)
    bk = ctx.sb("p0_bk", 8, F32)
    bv = ctx.sb("p0_bv", 8, F32)
    bqc = ctx.sb("p0_bqc", 8, F32)
    ln_base = ctx.mark()

    SCL = 1.0 / float(np.sqrt(128.0))
    NB = (TH + 55) // 56
    NCH = TH // 16

    sLdB = ctx.dsem("p0_sLdB")
    ctx.on("scalar", lambda e: e.dma_start(out=bq[:, 0:32], in_=mk.p0_bq[:, :]), inc=(sLdB, 16))
    ctx.on("scalar", lambda e: e.dma_start(out=bk[:, 0:8], in_=mk.p0_bk[:, :]), inc=(sLdB, 16))
    ctx.on("scalar", lambda e: e.dma_start(out=bv[:, 0:8], in_=mk.p0_bv[:, :]), inc=(sLdB, 16))
    sBqc = ctx.sem("p0_sBqc")
    ctx.on("vector", lambda e: e.tensor_tensor(bqc[:, 0:8], bq[:, 0:32:4], bq[:, 1:32:4], ALU.add),
           wait=[(sLdB, 48)], inc=(sBqc, 1))
    ctx.on("vector", lambda e: e.tensor_tensor(tmpa[:, 0:8], bq[:, 2:32:4], bq[:, 3:32:4], ALU.add),
           wait=[(sBqc, 1)], inc=(sBqc, 1))
    ctx.on("vector", lambda e: e.tensor_tensor(bqc[:, 0:8], bqc[:, 0:8], tmpa[:, 0:8], ALU.add),
           wait=[(sBqc, 2)], inc=(sBqc, 1))

    sW = [ctx.dsem("p0_sW0"), ctx.dsem("p0_sW1")]
    sWF = [ctx.sem("p0_sWF0"), ctx.sem("p0_sWF1")]
    sQD = [ctx.dsem("p0_sQD0"), ctx.dsem("p0_sQD1")]
    sRB = [ctx.dsem("p0_sRB0"), ctx.dsem("p0_sRB1")]
    norm_sems = []
    prev_gate = list(s_ready)
    prev_norm = None
    prev_sSum = None

    for hf in range(2):
        ctx.begin("half")
        sXC = emit_cast_xbh(ctx, mk, hf, prev_gate)
        wsrcs = ([("q", h) for h in range(32)] + [("k", h) for h in range(8)]
                 + [("v", h) for h in range(8)])
        for li, (kind, h) in enumerate(wsrcs):
            gi = hf * 48 + li
            par = gi % 2
            dram = {"q": mk.p0_wq, "k": mk.p0_wk, "v": mk.p0_wv}[kind]
            w = [(sWF[par], gi // 2)] if gi >= 2 else []
            ctx.on("gpsimd",
                   lambda e, dram=dram, h=h, par=par: e.dma_start(
                       out=wbuf[par][:].rearrange("p (a d) -> p a d", a=NP),
                       in_=dram[h].rearrange("(a p) d -> p a d", p=128)),
                   wait=w, inc=(sW[par], 16))

        sAcQ = ctx.sem(f"p0h{hf}_sAcQ")
        sEvQ = ctx.sem(f"p0h{hf}_sEvQ")
        sTrp = ctx.sem(f"p0h{hf}_sTrp")
        sTrE = ctx.sem(f"p0h{hf}_sTrE")
        sSc = ctx.sem(f"p0h{hf}_sSc")
        sU = ctx.sem(f"p0h{hf}_sU")
        sUC = ctx.sem(f"p0h{hf}_sUC")
        sVg = ctx.sem(f"p0h{hf}_sVg")
        sM = ctx.sem(f"p0h{hf}_sM")
        sZi = ctx.sem(f"p0h{hf}_sZi")
        sAsm = ctx.sem(f"p0h{hf}_sAsm")
        sMu = ctx.sem(f"p0h{hf}_sMu")
        sVin = ctx.sem(f"p0h{hf}_sVin")

        proj = ([("qc", hk) for hk in range(8)] + [("kc", hk) for hk in range(8)]
                + [("vc", hk) for hk in range(8)])
        for gidx, (kind, hk) in enumerate(proj):
            bank = gidx % 2
            if kind == "qc":
                lis = [4 * hk + r for r in range(4)]
            elif kind == "kc":
                lis = [32 + hk]
            else:
                lis = [40 + hk]
            mms = [(li, a) for li in lis for a in range(NP)]
            for mi, (li, a) in enumerate(mms):
                gi = hf * 48 + li
                par = gi % 2
                w = []
                if a == 0:
                    w.append((sW[par], 16 * (gi // 2 + 1)))
                if mi == 0:
                    w.append((sXC, NP))
                    if gidx >= 2:
                        w.append((sEvQ, gidx - 1))
                last = (mi == len(mms) - 1)
                if last:
                    inc = (sAcQ, 1)
                elif a == NP - 1:
                    inc = (sWF[par], 1)
                else:
                    inc = None
                ctx.on("tensor",
                       lambda e, a=a, par=par, bank=bank, mi=mi, last=last:
                           e.matmul(bankf(mk, bank), wbuf[par][:, a * 128:(a + 1) * 128],
                                    mk.xbh[:, a * TH:(a + 1) * TH],
                                    start=(mi == 0), stop=last),
                       wait=w, inc=inc)
            par_last = (hf * 48 + lis[-1]) % 2
            ctx.on("vector", lambda e: e.engine_nop(),
                   wait=[(sAcQ, gidx + 1)], inc=(sWF[par_last], 1))
            if kind == "vc":
                ctx.on("scalar",
                       lambda e, hk=hk, bank=bank: e.activation(
                           out=vc[:, hk * TH:(hk + 1) * TH], in_=bankf(mk, bank),
                           func=AF.Identity, bias=bv[:, hk:hk + 1]),
                       wait=[(sAcQ, gidx + 1), (sBqc, 3)], inc=(sEvQ, 1))
            else:
                bias = bqc[:, hk:hk + 1] if kind == "qc" else bk[:, hk:hk + 1]
                dstt = qc[gidx % 2]
                ctx.on("scalar",
                       lambda e, dstt=dstt, bank=bank, bias=bias: e.activation(
                           out=dstt[:], in_=bankf(mk, bank),
                           func=AF.Identity, bias=bias),
                       wait=[(sAcQ, gidx + 1), (sBqc, 3)] +
                            ([(sTrp, 4 * (gidx - 1))] if gidx >= 2 else []),
                       inc=(sEvQ, 1))
                dram = mk.p0_qd if kind == "qc" else mk.p0_kd
                for tc in range(4):
                    kk = 4 * gidx + tc
                    pslice = mk.psumb[:, (kk % 2) * 1024:(kk % 2) * 1024 + 128]
                    w = [(sEvQ, gidx + 1)] if tc == 0 else []
                    if kk >= 2:
                        w.append((sTrE, kk - 1))
                    ctx.on("tensor",
                           lambda e, dstt=dstt, tc=tc, pslice=pslice: e.transpose(
                               pslice, dstt[:, tc * 128:(tc + 1) * 128], mk.eyeb[:]),
                           wait=w, inc=(sTrp, 1))
                    w = [(sTrp, kk + 1)]
                    if kk >= 2:
                        w.append((sQD[kk % 2], 16 * (hf * 32 + kk // 2)))
                    ctx.on("vector",
                           lambda e, kk=kk, pslice=pslice: e.tensor_copy(
                               tev[kk % 2][:], pslice),
                           wait=w, inc=(sTrE, 1))
                    ctx.on("scalar",
                           lambda e, dram=dram, hk=hk, tc=tc, kk=kk: e.dma_start(
                               out=dram[hk, tc * 128:(tc + 1) * 128, :], in_=tev[kk % 2][:]),
                           wait=[(sTrE, kk + 1)], inc=(sQD[kk % 2], 16))

        for b in range(2):
            ctx.on("vector", lambda e, b=b: e.memset(vcat[b][:, 8:504:9], 1.0),
                   inc=(sVin, 1))

        for c in range(NCH):
            w = [(sQD[0], 16 * 32 * (hf + 1)), (sQD[1], 16 * 32 * (hf + 1))]
            if c == 0:
                w.append((sAcQ, 24))
            if c == 0 and prev_norm is not None:
                w.append(prev_norm)
            if c >= 2:
                w.append((sSc, 4 * (c - 1)))
            ctx.on("sync",
                   lambda e, c=c: e.dma_start(
                       out=qt[c % 2][0:8, 0:2048],
                       in_=mk.p0_qd[:, c * 16:(c + 1) * 16, :].rearrange("h t d -> h (t d)")),
                   wait=w, inc=(sRB[c % 2], 16))
            ctx.on("sync",
                   lambda e, c=c: e.dma_start(
                       out=kt[c % 2][0:8, 0:2048],
                       in_=mk.p0_kd[:, c * 16:(c + 1) * 16, :].rearrange("h t d -> h (t d)")),
                   inc=(sRB[c % 2], 16))

        for sb in range(TH // 4):
            bank = 2 + sb % 2
            mb = (sb * 4) // 56
            t0 = mb * 56
            B = min(56, TH - t0)
            bankM = 4 + mb % 2
            if sb * 4 == t0:
                for hk in range(8):
                    w = [(sVin, 2)]
                    if hk == 0:
                        w.append((sEvQ, 24))
                        w.append((sBqc, 3))
                        if mb >= 2:
                            w.append((sM, mb - 1))
                    ctx.on("vector",
                           lambda e, hk=hk, t0=t0, B=B, mb=mb: e.tensor_copy(
                               vcat[mb % 2][:, hk:9 * B:9],
                               vc[:, hk * TH + t0:hk * TH + t0 + B]),
                           wait=w, inc=(sVg, 1))
            for i in range(4):
                t = sb * 4 + i
                c = t // 16; tl = t % 16
                w = []
                if i == 0:
                    w.append((sRB[c % 2], 32 * (hf * NCH // 2 + c // 2 + 1)))
                    if sb >= 2:
                        w.append((sU, sb - 1))
                ctx.on("tensor",
                       lambda e, bank=bank, i=i, c=c, tl=tl: e.matmul(
                           bankf(mk, bank)[:, i * 128:(i + 1) * 128],
                           kt[c % 2][0:8, tl * 128:(tl + 1) * 128],
                           qt[c % 2][0:8, tl * 128:(tl + 1) * 128],
                           start=True, stop=True),
                       wait=w, inc=(sSc, 1) if i == 3 else None)
            ctx.on("scalar",
                   lambda e, sb=sb, bank=bank: e.activation(
                       out=ubuf[sb % 2][:], in_=bankf(mk, bank), func=AF.Exp, scale=SCL),
                   wait=[(sSc, sb + 1)] + ([(sUC, sb - 1)] if sb >= 2 else []),
                   inc=(sU, 1))
            for i in range(4):
                t = sb * 4 + i
                j = t - t0
                w = []
                if i == 0:
                    w.append((sU, sb + 1))
                if j == 0:
                    w.append((sVg, 8 * (mb + 1)))
                    if mb >= 2:
                        w.append((sAsm, 32 * (mb - 1)))
                    if mb == 0 and prev_norm is not None:
                        w.append(prev_norm)
                lastmb = (j == B - 1)
                ctx.on("tensor",
                       lambda e, bankM=bankM, j=j, sb=sb, i=i, mb=mb: e.matmul(
                           bankf(mk, bankM)[:, j * 9:(j + 1) * 9],
                           ubuf[sb % 2][:, i * 128:(i + 1) * 128],
                           vcat[mb % 2][:, j * 9:(j + 1) * 9],
                           start=True, stop=True),
                       wait=w,
                       inc=(sM, 1) if lastmb else ((sUC, 1) if i == 3 else None))
            if (sb * 4 + 4) == t0 + B:
                ctx.on("vector", lambda e, mb=mb: e.engine_nop(),
                       wait=[(sM, mb + 1)], inc=(sUC, 1))
                ctx.on("vector",
                       lambda e, bankM=bankM, B=B: e.reciprocal(
                           zinv[:, 0:B], bankf(mk, bankM)[:, 8:9 * B:9]),
                       wait=[(sM, mb + 1)], inc=(sZi, 1))
                for hk in range(8):
                    w = [(sZi, mb + 1)] if hk == 0 else [(sAsm, 32 * mb + 4 * hk)]
                    if hk == 0 and mb == 0:
                        w.append((sBqc, 3))
                    ctx.on("vector",
                           lambda e, bankM=bankM, hk=hk, B=B: e.tensor_tensor(
                               tmpa[:, 0:B], bankf(mk, bankM)[:, hk:9 * B:9], zinv[:, 0:B],
                               ALU.mult),
                           wait=w, inc=(sMu, 1))
                    nmul = sMu.n
                    for r in range(4):
                        h = 4 * hk + r
                        st = mk.stream[:, h * TC + hf * TH + t0:h * TC + hf * TH + t0 + B]
                        ctx.on("vector",
                               lambda e, st=st, B=B: e.tensor_tensor(
                                   st, st, tmpa[:, 0:B], ALU.add),
                               wait=[(sMu, nmul)], inc=(sAsm, 1))

        ctx.release(ln_base)
        sNorm, sSum = emit_ln_half(ctx, mk, f"p0ln{hf}", hf, sAsm, 32 * NB,
                                   extra_first_wait=[(sAcQ, 24)], flat=True,
                                   alias_off=qt_alias, alias_all=True)
        prev_gate = [(sAcQ, 24), (sAsm, 32 * NB), (sSum, 1)]
        prev_norm = (sNorm, NP)
        prev_sSum = sSum
        norm_sems.append((sNorm, NP))
    ctx.release(mark)
    return norm_sems


def declare_p1(ctx, mk):
    nc = ctx.nc
    mk.p1_baseT = nc.dram_tensor("p1_baseT", [D, D], F32, kind="ExternalInput")
    mk.p1_wrT = nc.dram_tensor("p1_wrT", [D, 8], F32, kind="ExternalInput")
    mk.p1_laA = nc.dram_tensor("p1_laA", [D, 128], F32, kind="ExternalInput")
    mk.p1_laB = nc.dram_tensor("p1_laB", [128, D], F32, kind="ExternalInput")


def emit_p1(ctx, mk, s_ready, trunc=None):
    import os
    ctx.begin("phase")
    """L0_ffn + residual + LN (both halves). s_ready: stream fully valid.
    Returns list of per-half sNorm sems [(sem, NP), ...] (all tiles normalized)."""
    mark = ctx.mark()
    wb = [ctx.sb("p1_wb0", D, BF16), ctx.sb("p1_wb1", D, BF16)]
    laAb = ctx.sb("p1_laAb", NP * 128, BF16)
    laBb = ctx.sb("p1_laBb", D, BF16)
    wrt = ctx.sb("p1_wrt", NP * 8, F32)
    logit = ctx.sb("p1_logit", 4 * 8, F32)
    wds = ctx.sb("p1_wds", 4 * 8, F32)
    ab_alias = (ctx.sbuf_off + 63) & ~63
    wdT = ctx.sb("p1_wdT", TH, F32)
    wfull = ctx.sb("p1_wfull", TH, F32)
    midw = ctx.sb("p1_midw", TH, BF16)
    srow = ctx.sb("p1_srow", 16, F32)
    ln_base = ctx.mark()

    sLdS = ctx.dsem("p1_sLdS")
    sLdG = ctx.dsem("p1_sLdG")
    ctx.on("scalar", lambda e: e.dma_start(
        out=wrt[:].rearrange("p (a e) -> p a e", a=NP),
        in_=mk.p1_wrT.rearrange("(a p) e -> p a e", p=128)),
        wait=list(s_ready), inc=(sLdS, 16))
    ctx.on("gpsimd", lambda e: e.dma_start(
        out=laAb[:].rearrange("p (a r) -> p a r", a=NP),
        in_=mk.p1_laA.rearrange("(a p) r -> p a r", p=128)),
        wait=list(s_ready), inc=(sLdG, 16))
    ctx.on("gpsimd", lambda e: e.dma_start(out=laBb[:], in_=mk.p1_laB[:, :]), inc=(sLdG, 16))

    sW = [ctx.dsem("p1_sW0"), ctx.dsem("p1_sW1")]
    sWF = [ctx.sem("p1_sWF0"), ctx.sem("p1_sWF1")]
    norm_sems = []
    prev_sAcc = None
    prev_sSum = None

    for hf in range(2):
        ctx.begin("half")
        if prev_sAcc is None:
            cast_wait = list(s_ready)
        else:
            cast_wait = [(prev_sAcc, NP), (prev_sSum, 1)]
        sXC = emit_cast_xbh(ctx, mk, hf, cast_wait)
        if trunc == "h1cast" and hf == 1:
            norm_sems.append((sXC, NP))
            break
        if trunc is None or (hf == 0 and trunc != "logits"):
            for o in range(NP):
                par = o % 2
                gi = hf * NP + o
                ctx.on("gpsimd",
                       lambda e, o=o, par=par: e.dma_start(
                           out=wb[par][:].rearrange("p (a o) -> p a o", a=NP),
                           in_=mk.p1_baseT[:, o * 128:(o + 1) * 128].rearrange(
                               "(a p) o -> p a o", p=128)),
                       wait=[(sWF[par], gi // 2)] if gi >= 2 else list(s_ready),
                       inc=(sW[par], 16))

        sLg = ctx.sem(f"p1h{hf}_sLg")
        sLgE = ctx.sem(f"p1h{hf}_sLgE")
        for tt in range(4):
            bank = 2 + tt % 2
            dst = mk.psumf[0:128, bank * 512:bank * 512 + 8]
            if trunc == "h1mm" and hf == 1 and tt > 0:
                continue
            for a in range(NP):
                w = []
                if tt == 0 and a == 0:
                    w = list(s_ready) + [(sLdS, 16)]
                elif a == 0 and tt >= 2:
                    w = [(sLgE, tt - 1)]
                ctx.on("tensor",
                       lambda e, tt=tt, a=a, dst=dst, hf=hf: e.matmul(
                           dst,
                           mk.stream[:, a * TC + hf * TH + tt * 128:
                                     a * TC + hf * TH + (tt + 1) * 128],
                           wrt[:, a * 8:(a + 1) * 8],
                           start=(a == 0), stop=(a == NP - 1)),
                       wait=w, inc=(sLg, 1) if a == NP - 1 else None)
            if not (trunc == "h1mm" and hf == 1):
                ctx.on("vector",
                       lambda e, tt=tt, dst=dst: e.tensor_copy(logit[:, tt * 8:(tt + 1) * 8], dst),
                       wait=[(sLg, tt + 1)], inc=(sLgE, 1))
        if trunc == "h1mm" and hf == 1:
            norm_sems.append((sLg, 1))
            break
        if trunc == "logits":
            norm_sems.append((sLgE, 4))
            break
        if trunc == "h1logits" and hf == 1:
            norm_sems.append((sLgE, 4))
            break
        if trunc == "h0full" and hf == 0:
            pass

        BIGC = 1.0e30
        sT2 = ctx.sem(f"p1h{hf}_sT2")
        sTop = ctx.sem(f"p1h{hf}_sTop")
        for tt in range(4):
            L = logit[:, tt * 8:(tt + 1) * 8]
            W8 = wds[:, tt * 8:(tt + 1) * 8]
            m1 = srow[:, 0:1]; m2 = srow[:, 1:2]; nm1 = srow[:, 2:3]
            e2 = srow[:, 3:4]; dinv = srow[:, 4:5]; t8 = srow[:, 8:16]
            k = sT2.n
            ctx.on("vector", lambda e, L=L, m1=m1: e.reduce_max(m1, L, axis=mybir.AxisListType.X),
                   wait=[(sLgE, tt + 1)] + ([(sT2, k)] if k else []), inc=(sT2, 1))
            ctx.on("vector", lambda e, L=L, m1=m1, t8=t8: e.tensor_scalar(
                t8, L, m1[:, 0:1], BIGC, ALU.is_equal, ALU.mult),
                   wait=[(sT2, k + 1)], inc=(sT2, 1))
            ctx.on("vector", lambda e, L=L, t8=t8: e.tensor_tensor(t8, L, t8, ALU.subtract),
                   wait=[(sT2, k + 2)], inc=(sT2, 1))
            ctx.on("vector", lambda e, m2=m2, t8=t8: e.reduce_max(m2, t8, axis=mybir.AxisListType.X),
                   wait=[(sT2, k + 3)], inc=(sT2, 1))
            ctx.on("vector", lambda e, m1=m1, nm1=nm1: e.tensor_scalar_mul(nm1, m1, -1.0),
                   wait=[(sT2, k + 4)], inc=(sT2, 1))
            ctx.on("scalar", lambda e, m2=m2, e2=e2, nm1=nm1: e.activation(
                out=e2, in_=m2, func=AF.Exp, bias=nm1[:, 0:1]),
                   wait=[(sT2, k + 5)], inc=(sT2, 1))
            ctx.on("vector", lambda e, e2=e2: e.tensor_scalar_add(e2, e2, 1.0),
                   wait=[(sT2, k + 6)], inc=(sT2, 1))
            ctx.on("vector", lambda e, e2=e2, dinv=dinv: e.reciprocal(dinv, e2),
                   wait=[(sT2, k + 7)], inc=(sT2, 1))
            ctx.on("scalar", lambda e, L=L, W8=W8, nm1=nm1: e.activation(
                out=W8, in_=L, func=AF.Exp, bias=nm1[:, 0:1]),
                   wait=[(sT2, k + 8)], inc=(sT2, 1))
            ctx.on("vector", lambda e, L=L, t8=t8, m2=m2: e.tensor_scalar(
                t8, L, m2[:, 0:1], None, ALU.is_ge),
                   wait=[(sT2, k + 9)], inc=(sT2, 1))
            ctx.on("vector", lambda e, W8=W8, t8=t8: e.tensor_tensor(W8, W8, t8, ALU.mult),
                   wait=[(sT2, k + 10)], inc=(sT2, 1))
            ctx.on("vector", lambda e, W8=W8, dinv=dinv: e.tensor_scalar_mul(W8, W8, dinv[:, 0:1]),
                   wait=[(sT2, k + 11)], inc=(sTop, 1))

        sTrp = ctx.sem(f"p1h{hf}_sTrp")
        sTrpE = ctx.sem(f"p1h{hf}_sTrpE")
        tp = mk.psumf[0:8, 2 * 512:2 * 512 + 128]
        for tt in range(4):
            w = [(sTop, tt + 1)]
            if tt == 0:
                w.append((sLgE, 4))
            else:
                w.append((sTrpE, tt))
            ctx.on("tensor",
                   lambda e, tt=tt: e.transpose(tp, wds[:, tt * 8:(tt + 1) * 8], mk.eyef[:]),
                   wait=w, inc=(sTrp, 1))
            ctx.on("vector",
                   lambda e, tt=tt: e.tensor_copy(wdT[0:8, tt * 128:(tt + 1) * 128], tp),
                   wait=[(sTrp, tt + 1)], inc=(sTrpE, 1))
        sWfl = ctx.sem(f"p1h{hf}_sWfl")
        sWflE = ctx.sem(f"p1h{hf}_sWflE")
        ctx.on("tensor",
               lambda e: e.matmul(bankf(mk, 3), mk.rbc[0:8, :], wdT[0:8, 0:TH],
                                  start=True, stop=True),
               wait=[(sTrpE, 4)], inc=(sWfl, 1))
        ctx.on("vector",
               lambda e: e.tensor_copy(wfull[:], bankf(mk, 3)),
               wait=[(sWfl, 1)], inc=(sWflE, 1))

        sMid = ctx.sem(f"p1h{hf}_sMid")
        sMidw = ctx.sem(f"p1h{hf}_sMidw")
        for a in range(NP):
            ctx.on("tensor",
                   lambda e, a=a: e.matmul(
                       bankf(mk, 4), laAb[:, a * 128:(a + 1) * 128],
                       mk.xbh[:, a * TH:(a + 1) * TH],
                       start=(a == 0), stop=(a == NP - 1)),
                   wait=[(sLdG, 32), (sXC, NP)] if a == 0 else (),
                   inc=(sMid, 1) if a == NP - 1 else None)
        ctx.on("vector",
               lambda e: e.tensor_tensor(midw[:], bankf(mk, 4), wfull[:], ALU.mult),
               wait=[(sMid, 1), (sWflE, 1)], inc=(sMidw, 1))

        sAcc = ctx.sem(f"p1h{hf}_sAcc")
        sAdd = ctx.sem(f"p1h{hf}_sAdd")
        for o in range(NP):
            par = o % 2
            bank = o % 2
            gi = hf * NP + o
            w = [(sW[par], 16 * (gi // 2 + 1))]
            if o >= 2:
                w.append((sAdd, o - 1))
            for a in range(NP):
                ctx.on("tensor",
                       lambda e, a=a, par=par, bank=bank: e.matmul(
                           bankf(mk, bank), wb[par][:, a * 128:(a + 1) * 128],
                           mk.xbh[:, a * TH:(a + 1) * TH],
                           start=(a == 0), stop=False),
                       wait=w if a == 0 else ())
            ctx.on("tensor",
                   lambda e, o=o, bank=bank: e.matmul(
                       bankf(mk, bank), laBb[:, o * 128:(o + 1) * 128], midw[:],
                       start=False, stop=True),
                   wait=[(sMidw, 1)] if o == 0 else [],
                   inc=(sAcc, 1))
            ctx.on("vector",
                   lambda e, o=o, bank=bank, hf=hf: e.tensor_tensor(
                       mk.stream[:, o * TC + hf * TH:o * TC + hf * TH + TH],
                       mk.stream[:, o * TC + hf * TH:o * TC + hf * TH + TH],
                       bankf(mk, bank), ALU.add),
                   wait=[(sAcc, o + 1)], inc=(sAdd, 1))
            ctx.on("vector", lambda e, o=o: e.engine_nop(), inc=(sWF[par], 1))

        prev_sAcc = sAcc
        ctx.release(ln_base)
        sNorm, prev_sSum = emit_ln_half(ctx, mk, f"p1ln{hf}", hf, sAdd, 1,
                                        extra_first_wait=[(sAcc, NP)],
                                        alias_off=ab_alias)
        norm_sems.append((sNorm, NP))
        if trunc == "h0full":
            break
    ctx.release(mark)
    return norm_sems


def host_inputs_p2(inp):
    def percol(a):
        return np.ascontiguousarray(np.asarray(a).reshape(4, 4, 128).transpose(2, 0, 1)
                                    .reshape(128, 16))
    winT = np.ascontiguousarray(inp["L1_attn_Win"].T[PIDX])
    w1T = np.ascontiguousarray(inp["L1_attn_W1"].transpose(0, 2, 1))
    w2T = np.ascontiguousarray(inp["L1_attn_W2"].transpose(0, 2, 1))
    woutT = np.ascontiguousarray(inp["L1_attn_Wout"][PIDX].T)
    return {
        "p2_winT": winT, "p2_w1T": w1T, "p2_w2T": w2T, "p2_woutT": woutT,
        "p2_bin": np.ascontiguousarray(inp["L1_attn_bin"].reshape(4, 128).T),
        "p2_b1": np.ascontiguousarray(inp["L1_attn_b1"].reshape(4, 16, 128)
                                      .transpose(2, 0, 1).reshape(128, 64)),
        "p2_b2": percol(inp["L1_attn_b2"]),
        "p2_ln1w": percol(inp["L1_attn_ln1_w"]), "p2_ln1b": percol(inp["L1_attn_ln1_b"]),
        "p2_ln2w": percol(inp["L1_attn_ln2_w"]), "p2_ln2b": percol(inp["L1_attn_ln2_b"]),
        "p2_phzre": percol(inp["L1_attn_phz_re"]), "p2_phzim": percol(inp["L1_attn_phz_im"]),
        "p2_lcre": percol(inp["L1_attn_lc_re"]), "p2_lcim": percol(inp["L1_attn_lc_im"]),
        "p2_scl": percol(inp["L1_attn_sc_lin"]),
        "p2_lnlw": np.ascontiguousarray(inp["L1_attn_lnl_w"].reshape(4, 128).T),
        "p2_lnlb": np.ascontiguousarray(inp["L1_attn_lnl_b"].reshape(4, 128).T),
        "p2_bout": np.ascontiguousarray(inp["L1_attn_bout"][PIDX].reshape(32, 128).T),
    }


def declare_p2(ctx, mk):
    nc = ctx.nc
    mk.p2_winT = nc.dram_tensor("p2_winT", [D, 512], F32, kind="ExternalInput")
    mk.p2_w1T = nc.dram_tensor("p2_w1T", [4, 512, 2048], F32, kind="ExternalInput")
    mk.p2_w2T = nc.dram_tensor("p2_w2T", [4, 2048, 512], F32, kind="ExternalInput")
    mk.p2_woutT = nc.dram_tensor("p2_woutT", [512, D], F32, kind="ExternalInput")
    for nm, sh in (("bin", [128, 4]), ("b1", [128, 64]), ("b2", [128, 16]),
                   ("ln1w", [128, 16]), ("ln1b", [128, 16]), ("ln2w", [128, 16]),
                   ("ln2b", [128, 16]), ("phzre", [128, 16]), ("phzim", [128, 16]),
                   ("lcre", [128, 16]), ("lcim", [128, 16]), ("scl", [128, 16]),
                   ("lnlw", [128, 4]), ("lnlb", [128, 4]), ("bout", [128, 32])):
        setattr(mk, "p2_" + nm, nc.dram_tensor("p2_" + nm, sh, F32, kind="ExternalInput"))
    mk.p2_mask = nc.dram_tensor("p2_mask", [1, 8], F32, kind="ExternalInput")
    mk.p2_xsp = nc.dram_tensor("p2_xsp", [128, NP * TC], F32)
    mk.p2_cin = nc.dram_tensor("p2_cin", [128, 8], F32)
    mk.p2_cout = nc.dram_tensor("p2_cout", [1024, 8], F32)


def emit_ln_inner(ctx, mk, S, h, hbf, out, rows, absb, wcol, bcol, sq,
                  silu=False, first_wait=()):
    sC, sMM, sEv, sSt, sBc, sN = (S["c"], S["mm"], S["ev"], S["st"], S["bc"], S["n"])
    kc, kmm, kev, kst, kbc = sC.n, sMM.n, sEv.n, sSt.n, sBc.n
    sum5 = mk.psumf[0:1, 5 * 512:5 * 512 + TH]
    ssq4 = mk.psumf[0:1, 4 * 512:4 * 512 + TH]
    row_a = rows[0:1, 0:TC]
    row_b = rows[0:1, TC:2 * TC]
    ctx.on("vector", lambda e: e.tensor_copy(hbf[:], h[:]),
           wait=list(first_wait) + [(sBc, kbc)], inc=(sC, 1))
    ctx.on("scalar", lambda e: e.activation(out=sq[:], in_=h[:], func=AF.Square),
           wait=list(first_wait), inc=(sC, 1))
    for g, (srcb, bank) in enumerate(((hbf, sum5), (sq, ssq4))):
        for hf in range(2):
            for dt in range(4):
                w = []
                if dt == 0:
                    if g == 0 and hf == 0:
                        w = [(sC, kc + 2), (sBc, kbc)]
                    else:
                        w = [(sEv, sEv.n)]
                ctx.on("tensor",
                       lambda e, srcb=srcb, bank=bank, dt=dt, hf=hf: e.matmul(
                           bank, mk.onesb[:, 0:1],
                           srcb[:, dt * TC + hf * TH:dt * TC + hf * TH + TH],
                           start=(dt == 0), stop=(dt == 3)),
                       wait=w, inc=(sMM, 1) if dt == 3 else None)
            dstrow = row_a if g == 0 else row_b
            ctx.on("vector",
                   lambda e, dstrow=dstrow, hf=hf, bank=bank: e.tensor_copy(
                       dstrow[:, hf * TH:hf * TH + TH], bank),
                   wait=[(sMM, sMM.n)], inc=(sEv, 1))
    c = 1.0 / 512.0
    ctx.on("vector", lambda e: e.tensor_scalar_mul(row_a, row_a, c),
           wait=[(sEv, kev + 4)], inc=(sSt, 1))
    ctx.on("vector", lambda e: e.tensor_scalar_mul(row_b, row_b, c),
           wait=[(sSt, sSt.n)], inc=(sSt, 1))
    ctx.on("vector", lambda e: e.tensor_tensor(absb[0:1, 0:TC],
                                               row_a, row_a, ALU.mult),
           wait=[(sSt, sSt.n)], inc=(sSt, 1))
    ctx.on("vector", lambda e: e.tensor_tensor(row_b, row_b, absb[0:1, 0:TC],
                                               ALU.subtract),
           wait=[(sSt, sSt.n)], inc=(sSt, 1))
    ctx.on("scalar", lambda e: e.activation(out=row_b, in_=row_b, func=AF.Ln,
                                            bias=mk.epsc[0:1, 0:1]),
           wait=[(sSt, sSt.n), (mk.sConst, 1)], inc=(sSt, 1))
    ctx.on("scalar", lambda e: e.activation(out=row_b, in_=row_b, func=AF.Exp, scale=-0.5),
           wait=[(sSt, sSt.n)], inc=(sSt, 1))
    ctx.on("vector", lambda e: e.tensor_tensor(row_a, row_a, row_b, ALU.mult),
           wait=[(sSt, sSt.n)], inc=(sSt, 1))
    ctx.on("vector", lambda e: e.tensor_scalar_mul(row_a, row_a, -1.0),
           wait=[(sSt, sSt.n)], inc=(sSt, 1))
    for g, (srcr, bank) in enumerate(((row_b, 5), (row_a, 4))):
        for hf in range(2):
            w = [(sSt, kst + 8)] if (g == 0 and hf == 0) else []
            if hf == 1:
                w.append((sEv, sEv.n))
            ctx.on("tensor",
                   lambda e, srcr=srcr, bank=bank, hf=hf: e.matmul(
                       mk.psumf[:, bank * 512:bank * 512 + TH], mk.ones1f[0:1, :],
                       srcr[:, hf * TH:hf * TH + TH], start=True, stop=True),
                   wait=w, inc=(sBc, 1))
            off = (0 if g == 0 else TC) + hf * TH
            ctx.on("vector",
                   lambda e, off=off, bank=bank: e.tensor_copy(
                       absb[:, off:off + TH],
                       mk.psumf[:, bank * 512:bank * 512 + TH]),
                   wait=[(sBc, sBc.n)], inc=(sEv, 1))
    for dt in range(4):
        ctx.on("vector",
               lambda e, dt=dt: e.tensor_tensor(
                   out[:, dt * TC:(dt + 1) * TC], h[:, dt * TC:(dt + 1) * TC],
                   absb[:, 0:TC], ALU.mult),
               wait=[(sEv, kev + 8)] if dt == 0 else [(sN, sN.n)], inc=(sN, 1))
        ctx.on("vector",
               lambda e, dt=dt: e.tensor_tensor(
                   out[:, dt * TC:(dt + 1) * TC], out[:, dt * TC:(dt + 1) * TC],
                   absb[:, TC:2 * TC], ALU.add),
               wait=[(sN, sN.n)], inc=(sN, 1))
        ctx.on("vector",
               lambda e, dt=dt: e.tensor_scalar(
                   out[:, dt * TC:(dt + 1) * TC], out[:, dt * TC:(dt + 1) * TC],
                   wcol[:, dt:dt + 1], bcol[:, dt:dt + 1], ALU.mult, ALU.add),
               wait=[(sN, sN.n)], inc=(sN, 1))
        if silu:
            ctx.on("scalar",
                   lambda e, dt=dt: e.activation(
                       out=out[:, dt * TC:(dt + 1) * TC],
                       in_=out[:, dt * TC:(dt + 1) * TC], func=AF.Silu),
                   wait=[(sN, sN.n)], inc=(sN, 1))
    return


def emit_p2(ctx, mk, s_ready):
    ctx.begin("phase")
    ctx.begin("half")
    RB = 16512
    Ec = ctx.sb_at("p2_Ec", 4 * TC, F32, RB + 0)
    Es = ctx.sb_at("p2_Es", 4 * TC, F32, RB + 16 * 1024)
    br = ctx.sb_at("p2_br", 4 * TC, F32, RB + 32 * 1024)
    bi = ctx.sb_at("p2_bi", 4 * TC, F32, RB + 48 * 1024)
    hso = ctx.sb_at("p2_h", 4 * TC, F32, RB + 64 * 1024)
    ybuf = ctx.sb_at("p2_y", 4 * TC, F32, RB + 80 * 1024)
    scr = ctx.sb_at("p2_scr", TC, F32, RB + 96 * 1024)
    Ein = ctx.sb_at("p2_Ein", 2 * 512, F32, RB + 100 * 1024)
    cols = ctx.sb_at("p2_cols", 256, F32, RB + 104 * 1024)
    rows = ctx.sb_at("p2_rows", 2 * TC, F32, RB + 106 * 1024)
    absb = ctx.sb_at("p2_absb", 2 * TC, F32, RB + 114 * 1024)
    gath = ctx.sb_at("p2_gath", 64, F32, RB + 122 * 1024)
    cinb = ctx.sb_at("p2_cinb", 8, F32, RB + 122 * 1024 + 512)
    ybf = ctx.sb_at("p2_ybf", 4 * TC, BF16, RB + 128 * 1024)
    h1b = ctx.sb_at("p2_h1b", 16 * TH, BF16, RB + 136 * 1024)
    sqb = ctx.sb_at("p2_sqb", 4 * TC, BF16, RB + 152 * 1024)
    W1REG = RB + 162 * 1024
    w1b = ctx.sb_at("p2_w1b", 4 * 2048, BF16, W1REG)
    w2b = ctx.sb_at("p2_w2b", 16 * 512, BF16, W1REG + 16 * 1024)
    woutb = ctx.sb_at("p2_woutb", 4 * D, BF16, W1REG)
    winb = ctx.sb_at("p2_winb", 32 * 512, BF16, W1REG)
    prm = ctx.sb_at("p2_prm", 288, F32, RB + 204 * 1024)
    maskb = ctx.sb_at("p2_maskb", 8, F32, RB + 206 * 1024)
    wsel = ctx.sb_at("p2_wsel", 8, F32, RB + 206 * 1024 + 64)
    wbuf8 = ctx.sb_at("p2_wbuf8", 64, F32, RB + 206 * 1024 + 128)
    ln_off = RB + 194 * 1024

    PRM = {}
    off = 0
    for nm, wdt in (("bin", 4), ("b1", 64), ("b2", 16), ("ln1w", 16), ("ln1b", 16),
                    ("ln2w", 16), ("ln2b", 16), ("phzre", 16), ("phzim", 16),
                    ("lcre", 16), ("lcim", 16), ("scl", 16), ("lnlw", 4), ("lnlb", 4),
                    ("bout", 32)):
        PRM[nm] = (off, wdt)
        off += wdt
    assert off <= 288

    sLdP = ctx.dsem("p2_sLdP")
    sWinL = ctx.dsem("p2_sWinL")
    sSp = ctx.dsem("p2_sSp")
    sWL = [ctx.dsem("p2_sWL0"), ctx.dsem("p2_sWL1")]
    sWoL = ctx.dsem("p2_sWoL")
    sCC = ctx.dsem("p2_sCC")
    sXC = ctx.sem("p2_sXC")
    sWinA = ctx.sem("p2_sWinA")
    sWinE = ctx.sem("p2_sWinE")
    sCol = ctx.sem("p2_sCol")
    sTab = ctx.sem("p2_sTab")
    sFFA = ctx.sem("p2_sFFA")
    sFFE = ctx.sem("p2_sFFE")
    sFFE2 = ctx.sem("p2_sFFE2")
    sWoE = ctx.sem("p2_sWoE")
    sFFH = ctx.sem("p2_sFFH")
    sWoA = ctx.sem("p2_sWoA")
    sHFB = ctx.sem("p2_sHFB")
    sRel = [ctx.dsem("p2_sRel0"), ctx.dsem("p2_sRel1")]
    sFin = ctx.sem("p2_sFin")
    SLN = {"c": ctx.sem("p2_lnc"), "mm": ctx.sem("p2_lnmm"), "ev": ctx.sem("p2_lnev"),
           "st": ctx.sem("p2_lnst"), "bc": ctx.sem("p2_lnbc"), "n": ctx.sem("p2_lnn")}

    for nm in PRM:
        o, wdt = PRM[nm]
        dram = getattr(mk, "p2_" + nm)
        ctx.on("scalar", lambda e, dram=dram, o=o, wdt=wdt: e.dma_start(
            out=prm[:, o:o + wdt], in_=dram[:, :]),
            wait=list(s_ready) if o == 0 else [], inc=(sLdP, 16))
    ctx.on("scalar", lambda e: e.dma_start(out=maskb[0:1, 0:8], in_=mk.p2_mask[:, :]),
           inc=(sLdP, 16))
    ctx.on("gpsimd", lambda e: e.dma_start(
        out=winb[:].rearrange("p (a d) -> p a d", a=NP),
        in_=mk.p2_winT.rearrange("(a p) d -> p a d", p=128)),
        wait=list(s_ready), inc=(sWinL, 16))

    prev = None
    cast_sems = []
    for hf in range(2):
        cast_wait = list(s_ready) if hf == 0 else [(prev, 4)]
        sXCh = emit_cast_xbh(ctx, mk, hf, cast_wait)
        cast_sems.append(sXCh)
        for dt in range(4):
            idx = hf * 4 + dt
            bank = idx % 4
            for a in range(NP):
                w = []
                if a == 0:
                    w = [(sXCh, NP), (sWinL, 16)]
                    if idx >= 4:
                        w.append((sWinE, idx - 3))
                ctx.on("tensor",
                       lambda e, a=a, dt=dt, hf=hf, bank=bank: e.matmul(
                           bankf(mk, bank), winb[:, a * 512 + dt * 128:a * 512 + (dt + 1) * 128],
                           mk.xbh[:, a * TH:(a + 1) * TH],
                           start=(a == 0), stop=(a == NP - 1)),
                       wait=w, inc=(sWinA, 1) if a == NP - 1 else None)
            ctx.on("scalar",
                   lambda e, dt=dt, hf=hf, bank=bank: e.activation(
                       out=hso[:, dt * TC + hf * TH:dt * TC + hf * TH + TH],
                       in_=bankf(mk, bank), func=AF.Identity,
                       bias=prm[:, PRM["bin"][0] + dt:PRM["bin"][0] + dt + 1]),
                   wait=[(sWinA, idx + 1), (sLdP, 16 * 16), (sSp, 16)],
                   inc=(sWinE, 1))
        prev = sWinA
    ctx.on("sync", lambda e: e.dma_start(out=mk.p2_xsp[:, :], in_=mk.stream[:]),
           wait=[(cast_sems[1], NP)], inc=(sSp, 16))
    ctx.on("tensor", lambda e: e.matmul(mk.psumf[:, 3 * 512:3 * 512 + 8],
                                        mk.ones1f[0:1, :], maskb[0:1, 0:8],
                                        start=True, stop=True),
           wait=[(sLdP, 16 * 16), (sWinE, 8)], inc=(sWinA, 1))
    ctx.on("vector", lambda e: e.tensor_copy(maskb[:], mk.psumf[:, 3 * 512:3 * 512 + 8]),
           wait=[(sWinA, sWinA.n)], inc=(sTab, 1))

    def emit_wload(l):
        par = 0
        w = [(sWinA, 8)] if l == 0 else [(sFFH, 8 * l)]
        ctx.on("gpsimd",
               lambda e, l=l: e.dma_start(
                   out=w1b[:].rearrange("p (a f) -> p a f", a=4),
                   in_=mk.p2_w1T[l].rearrange("(a p) f -> p a f", p=128)),
               wait=w, inc=(sWL[0], 16))
        ctx.on("gpsimd",
               lambda e, l=l: e.dma_start(
                   out=w2b[:].rearrange("p (a d) -> p a d", a=16),
                   in_=mk.p2_w2T[l].rearrange("(a p) d -> p a d", p=128)),
               inc=(sWL[1], 16))
    emit_wload(0)

    import os as _os
    NL = int(_os.environ.get("P2_NLAYERS", "4"))
    for l in range(NL):
        P = lambda nm, i: prm[:, PRM[nm][0] + l * 4 + i:PRM[nm][0] + l * 4 + i + 1]
        kt = sTab.n
        c_er = cols[:, 0:4]; c_ei = cols[:, 4:8]; c_rho = cols[:, 8:12]
        c_r2 = cols[:, 12:16]; c_ri = cols[:, 16:20]
        c_Ec = cols[:, 20:24]; c_Es = cols[:, 24:28]
        c_R = cols[:, 28:32]
        c_t1 = cols[:, 32:36]; c_t2 = cols[:, 36:40]
        c_wr = cols[:, 40:44]; c_wi = cols[:, 44:48]
        c_nEs = cols[:, 48:52]
        c_Qc = cols[:, 64:96]; c_Qs = cols[:, 96:128]
        ctx.on("vector", lambda e, l=l: e.tensor_tensor(c_r2,
                                                        prm[:, PRM["phzre"][0] + l * 4:PRM["phzre"][0] + l * 4 + 4],
                                                        prm[:, PRM["phzre"][0] + l * 4:PRM["phzre"][0] + l * 4 + 4], ALU.mult),
               wait=[(sTab, kt), (sLdP, 16 * 16)] + ([(sSp, 16)] if l == 0 else []),
               inc=(sTab, 1))
        ctx.on("vector", lambda e, l=l: e.tensor_tensor(c_t1, prm[:, PRM["phzim"][0] + l * 4:PRM["phzim"][0] + l * 4 + 4],
                                                        prm[:, PRM["phzim"][0] + l * 4:PRM["phzim"][0] + l * 4 + 4], ALU.mult),
               wait=[(sTab, kt + 1)], inc=(sTab, 1))
        ctx.on("vector", lambda e: e.tensor_tensor(c_r2, c_r2, c_t1, ALU.add),
               wait=[(sTab, kt + 2)], inc=(sTab, 1))
        ctx.on("scalar", lambda e: e.activation(out=c_ri, in_=c_r2, func=AF.Ln),
               wait=[(sTab, kt + 3)], inc=(sTab, 1))
        ctx.on("scalar", lambda e: e.activation(out=c_ri, in_=c_ri, func=AF.Exp, scale=-0.5),
               wait=[(sTab, kt + 4)], inc=(sTab, 1))
        ctx.on("vector", lambda e, l=l: e.tensor_tensor(c_er, prm[:, PRM["phzre"][0] + l * 4:PRM["phzre"][0] + l * 4 + 4], c_ri, ALU.mult),
               wait=[(sTab, kt + 5)], inc=(sTab, 1))
        ctx.on("vector", lambda e, l=l: e.tensor_tensor(c_ei, prm[:, PRM["phzim"][0] + l * 4:PRM["phzim"][0] + l * 4 + 4], c_ri, ALU.mult),
               wait=[(sTab, kt + 6)], inc=(sTab, 1))
        ctx.on("vector", lambda e: e.tensor_tensor(c_t1, c_r2, c_ri, ALU.mult),
               wait=[(sTab, kt + 7)], inc=(sTab, 1))
        ctx.on("scalar", lambda e: e.activation(out=c_rho, in_=c_t1, func=AF.Tanh),
               wait=[(sTab, kt + 8)], inc=(sTab, 1))
        Ci = Ein[:, 0:512]; Si = Ein[:, 512:1024]
        ctx.on("vector", lambda e: e.memset(Ci[:, 0:512:128], 1.0),
               wait=[(sTab, kt + 9)], inc=(sTab, 1))
        ctx.on("vector", lambda e: e.memset(Si[:, 0:512:128], 0.0),
               wait=[(sTab, sTab.n)], inc=(sTab, 1))
        ctx.on("vector", lambda e: e.tensor_scalar_mul(c_nEs, c_ei, -1.0),
               wait=[(sTab, sTab.n)], inc=(sTab, 1))
        for step_i, m in enumerate([1, 2, 4, 8, 16, 32, 64]):
            for dt in range(4):
                base = dt * 128
                pass
            break
        em_r = cols[:, 52:56]; em_i = cols[:, 56:60]; tmpc = cols[:, 60:64]
        ctx.on("vector", lambda e: e.tensor_copy(em_r, c_er), wait=[(sTab, sTab.n)], inc=(sTab, 1))
        ctx.on("vector", lambda e: e.tensor_copy(em_i, c_nEs), wait=[(sTab, sTab.n)], inc=(sTab, 1))
        Ci3 = Ci.rearrange("p (d s) -> p d s", d=4)
        Si3 = Si.rearrange("p (d s) -> p d s", d=4)
        for m in [1, 2, 4, 8, 16, 32, 64]:
            n = m
            erb = em_r.rearrange("p (d o) -> p d o", o=1).broadcast_to([128, 4, n])
            eib = em_i.rearrange("p (d o) -> p d o", o=1).broadcast_to([128, 4, n])
            s1 = scr[:, 0:4 * n].rearrange("p (d s) -> p d s", d=4)
            s2 = scr[:, 512:512 + 4 * n].rearrange("p (d s) -> p d s", d=4)
            ctx.on("vector", lambda e, s1=s1, n=n, erb=erb: e.tensor_tensor(
                s1, Ci3[:, :, 0:n], erb, ALU.mult),
                wait=[(sTab, sTab.n)], inc=(sTab, 1))
            ctx.on("vector", lambda e, s2=s2, n=n, eib=eib: e.tensor_tensor(
                s2, Si3[:, :, 0:n], eib, ALU.mult),
                wait=[(sTab, sTab.n)], inc=(sTab, 1))
            ctx.on("vector", lambda e, s1=s1, s2=s2, m=m, n=n: e.tensor_tensor(
                Ci3[:, :, m:m + n], s1, s2, ALU.subtract),
                wait=[(sTab, sTab.n)], inc=(sTab, 1))
            ctx.on("vector", lambda e, s1=s1, n=n, eib=eib: e.tensor_tensor(
                s1, Ci3[:, :, 0:n], eib, ALU.mult),
                wait=[(sTab, sTab.n)], inc=(sTab, 1))
            ctx.on("vector", lambda e, s2=s2, n=n, erb=erb: e.tensor_tensor(
                s2, Si3[:, :, 0:n], erb, ALU.mult),
                wait=[(sTab, sTab.n)], inc=(sTab, 1))
            ctx.on("vector", lambda e, s1=s1, s2=s2, m=m, n=n: e.tensor_tensor(
                Si3[:, :, m:m + n], s1, s2, ALU.add),
                wait=[(sTab, sTab.n)], inc=(sTab, 1))
            if m < 64:
                ctx.on("vector", lambda e: e.tensor_tensor(tmpc, em_r, em_r, ALU.mult),
                       wait=[(sTab, sTab.n)], inc=(sTab, 1))
                ctx.on("vector", lambda e: e.tensor_tensor(c_t2, em_i, em_i, ALU.mult),
                       wait=[(sTab, sTab.n)], inc=(sTab, 1))
                ctx.on("vector", lambda e: e.tensor_tensor(c_t1, em_r, em_i, ALU.mult),
                       wait=[(sTab, sTab.n)], inc=(sTab, 1))
                ctx.on("vector", lambda e: e.tensor_tensor(em_r, tmpc, c_t2, ALU.subtract),
                       wait=[(sTab, sTab.n)], inc=(sTab, 1))
                ctx.on("vector", lambda e: e.tensor_scalar_mul(em_i, c_t1, 2.0),
                       wait=[(sTab, sTab.n)], inc=(sTab, 1))
        ctx.on("vector", lambda e: e.tensor_tensor(tmpc, em_r, em_r, ALU.mult),
               wait=[(sTab, sTab.n)], inc=(sTab, 1))
        ctx.on("vector", lambda e: e.tensor_tensor(c_t2, em_i, em_i, ALU.mult),
               wait=[(sTab, sTab.n)], inc=(sTab, 1))
        ctx.on("vector", lambda e: e.tensor_tensor(c_t1, em_r, em_i, ALU.mult),
               wait=[(sTab, sTab.n)], inc=(sTab, 1))
        ctx.on("vector", lambda e: e.tensor_tensor(em_r, tmpc, c_t2, ALU.subtract),
               wait=[(sTab, sTab.n)], inc=(sTab, 1))
        ctx.on("vector", lambda e: e.tensor_scalar_mul(em_i, c_t1, 2.0),
               wait=[(sTab, sTab.n)], inc=(sTab, 1))
        Qc = lambda q: cols[:, 64 + q:96:8]
        Qs = lambda q: cols[:, 96 + q:128:8]
        ctx.on("vector", lambda e: e.memset(cols[:, 64:96:8], 1.0),
               wait=[(sTab, sTab.n)], inc=(sTab, 1))
        ctx.on("vector", lambda e: e.memset(cols[:, 96:128:8], 0.0),
               wait=[(sTab, sTab.n)], inc=(sTab, 1))
        for q in range(7):
            ctx.on("vector", lambda e, q=q: e.tensor_tensor(c_t1, Qc(q), em_r, ALU.mult),
                   wait=[(sTab, sTab.n)], inc=(sTab, 1))
            ctx.on("vector", lambda e, q=q: e.tensor_tensor(c_t2, Qs(q), em_i, ALU.mult),
                   wait=[(sTab, sTab.n)], inc=(sTab, 1))
            ctx.on("vector", lambda e, q=q: e.tensor_tensor(Qc(q + 1), c_t1, c_t2, ALU.subtract),
                   wait=[(sTab, sTab.n)], inc=(sTab, 1))
            ctx.on("vector", lambda e, q=q: e.tensor_tensor(c_t1, Qc(q), em_i, ALU.mult),
                   wait=[(sTab, sTab.n)], inc=(sTab, 1))
            ctx.on("vector", lambda e, q=q: e.tensor_tensor(c_t2, Qs(q), em_r, ALU.mult),
                   wait=[(sTab, sTab.n)], inc=(sTab, 1))
            ctx.on("vector", lambda e, q=q: e.tensor_tensor(Qs(q + 1), c_t1, c_t2, ALU.add),
                   wait=[(sTab, sTab.n)], inc=(sTab, 1))
        for dt in range(4):
            qc = c_Qc[:, dt * 8:dt * 8 + 8].rearrange("p (q o) -> p q o", o=1).broadcast_to([128, 8, 128])
            qs = c_Qs[:, dt * 8:dt * 8 + 8].rearrange("p (q o) -> p q o", o=1).broadcast_to([128, 8, 128])
            ci3 = Ci[:, dt * 128:(dt + 1) * 128].rearrange("p (o s) -> p o s", o=1).broadcast_to([128, 8, 128])
            si3 = Si[:, dt * 128:(dt + 1) * 128].rearrange("p (o s) -> p o s", o=1).broadcast_to([128, 8, 128])
            ecv = Ec[:, dt * TC:(dt + 1) * TC].rearrange("p (q s) -> p q s", q=8)
            esv = Es[:, dt * TC:(dt + 1) * TC].rearrange("p (q s) -> p q s", q=8)
            scv = scr[:].rearrange("p (q s) -> p q s", q=8)
            ctx.on("vector", lambda e, ecv=ecv, qc=qc, ci3=ci3: e.tensor_tensor(ecv, qc, ci3, ALU.mult),
                   wait=[(sTab, sTab.n)], inc=(sTab, 1))
            ctx.on("vector", lambda e, scv=scv, qs=qs, si3=si3: e.tensor_tensor(scv, qs, si3, ALU.mult),
                   wait=[(sTab, sTab.n)], inc=(sTab, 1))
            ctx.on("vector", lambda e, ecv=ecv, scv=scv, dt=dt: e.tensor_tensor(
                Ec[:, dt * TC:(dt + 1) * TC], Ec[:, dt * TC:(dt + 1) * TC], scr[:], ALU.subtract),
                   wait=[(sTab, sTab.n)], inc=(sTab, 1))
            ctx.on("vector", lambda e, esv=esv, qc=qc, si3=si3: e.tensor_tensor(esv, qc, si3, ALU.mult),
                   wait=[(sTab, sTab.n)], inc=(sTab, 1))
            ctx.on("vector", lambda e, scv=scv, qs=qs, ci3=ci3: e.tensor_tensor(scv, qs, ci3, ALU.mult),
                   wait=[(sTab, sTab.n)], inc=(sTab, 1))
            ctx.on("vector", lambda e, dt=dt: e.tensor_tensor(
                Es[:, dt * TC:(dt + 1) * TC], Es[:, dt * TC:(dt + 1) * TC], scr[:], ALU.add),
                   wait=[(sTab, sTab.n)], inc=(sTab, 1))
        for _ in range(3):
            ctx.on("vector", lambda e: e.tensor_tensor(c_t1, em_r, em_r, ALU.mult),
                   wait=[(sTab, sTab.n)], inc=(sTab, 1))
            ctx.on("vector", lambda e: e.tensor_tensor(c_t2, em_i, em_i, ALU.mult),
                   wait=[(sTab, sTab.n)], inc=(sTab, 1))
            ctx.on("vector", lambda e: e.tensor_tensor(tmpc, em_r, em_i, ALU.mult),
                   wait=[(sTab, sTab.n)], inc=(sTab, 1))
            ctx.on("vector", lambda e: e.tensor_tensor(em_r, c_t1, c_t2, ALU.subtract),
                   wait=[(sTab, sTab.n)], inc=(sTab, 1))
            ctx.on("vector", lambda e: e.tensor_scalar_mul(em_i, tmpc, 2.0),
                   wait=[(sTab, sTab.n)], inc=(sTab, 1))
        ctx.on("vector", lambda e: e.tensor_copy(c_Ec, em_r), wait=[(sTab, sTab.n)], inc=(sTab, 1))
        ctx.on("vector", lambda e: e.tensor_copy(c_nEs, em_i), wait=[(sTab, sTab.n)], inc=(sTab, 1))
        ctx.on("vector", lambda e: e.tensor_scalar_mul(c_Es, em_i, -1.0),
               wait=[(sTab, sTab.n)], inc=(sTab, 1))
        ctx.on("vector", lambda e: e.tensor_copy(c_R, c_rho), wait=[(sTab, sTab.n)], inc=(sTab, 1))
        for _ in range(10):
            ctx.on("vector", lambda e: e.tensor_tensor(c_R, c_R, c_R, ALU.mult),
                   wait=[(sTab, sTab.n)], inc=(sTab, 1))

        fw = [(sWinE, 8), (sSp, 16)] if l == 0 else [(sFFH, 8 * l)]
        emit_ln_inner(ctx, mk, SLN, hso, ybf, ybuf, rows, absb,
                      prm[:, PRM["ln1w"][0] + l * 4:PRM["ln1w"][0] + (l + 1) * 4],
                      prm[:, PRM["ln1b"][0] + l * 4:PRM["ln1b"][0] + (l + 1) * 4],
                      sqb, silu=True, first_wait=fw + [(sLdP, 16 * 16)])

        kb = sTab.n
        for dt in range(4):
            ctx.on("vector", lambda e, dt=dt: e.tensor_tensor(
                br[:, dt * TC:(dt + 1) * TC], Ec[:, dt * TC:(dt + 1) * TC],
                ybuf[:, dt * TC:(dt + 1) * TC], ALU.mult),
                   wait=[(sTab, sTab.n), (SLN["n"], SLN["n"].n)], inc=(sTab, 1))
            ctx.on("vector", lambda e, dt=dt: e.tensor_tensor(
                bi[:, dt * TC:(dt + 1) * TC], Es[:, dt * TC:(dt + 1) * TC],
                ybuf[:, dt * TC:(dt + 1) * TC], ALU.mult),
                   wait=[(sTab, sTab.n)], inc=(sTab, 1))
            rhob = c_rho[:, dt:dt + 1].broadcast_to([128, TC])
            ctx.on("vector", lambda e, dt=dt, rhob=rhob: e.tensor_tensor_scan(
                scr[:], rhob, br[:, dt * TC:(dt + 1) * TC], 0.0, ALU.mult, ALU.add),
                   wait=[(sTab, sTab.n)], inc=(sTab, 1))
            ctx.on("vector", lambda e, dt=dt: e.tensor_copy(
                cinb[:, dt * 2:dt * 2 + 1], scr[:, TC - 1:TC]),
                   wait=[(sTab, sTab.n)], inc=(sTab, 1))
            ctx.on("vector", lambda e, dt=dt, rhob=rhob: e.tensor_tensor_scan(
                scr[:], rhob, bi[:, dt * TC:(dt + 1) * TC], 0.0, ALU.mult, ALU.add),
                   wait=[(sTab, sTab.n)], inc=(sTab, 1))
            ctx.on("vector", lambda e, dt=dt: e.tensor_copy(
                cinb[:, dt * 2 + 1:dt * 2 + 2], scr[:, TC - 1:TC]),
                   wait=[(sTab, sTab.n)], inc=(sTab, 1))
        kcc = sCC.n
        ctx.on("gpsimd", lambda e: e.dma_start(out=mk.p2_cin[:, :], in_=cinb[:]),
               wait=[(sTab, sTab.n)], inc=(sCC, 16))
        ctx.on("gpsimd", lambda e: e.collective_compute(
            "AllGather", ALU.bypass, replica_groups=[list(range(8))],
            ins=[mk.p2_cin[:, :]], outs=[mk.p2_cout[:, :]]),
            wait=[(sCC, kcc + 16)], inc=(sCol, 1))
        ctx.on("gpsimd", lambda e: e.dma_start(
            out=gath[:].rearrange("p (r c) -> p r c", r=8),
            in_=mk.p2_cout.rearrange("(r p) c -> p r c", p=128)),
            wait=[(sCol, sCol.n)], inc=(sCC, 16))
        kw = sTab.n
        lcr = prm[:, PRM["lcre"][0] + l * 4:PRM["lcre"][0] + l * 4 + 4]
        lci = prm[:, PRM["lcim"][0] + l * 4:PRM["lcim"][0] + l * 4 + 4]
        ctx.on("vector", lambda e, lcr=lcr: e.tensor_tensor(c_wr, c_er, lcr, ALU.mult),
               wait=[(sTab, sTab.n), (sCC, kcc + 32)], inc=(sTab, 1))
        ctx.on("vector", lambda e, lci=lci: e.scalar_tensor_tensor(
            c_wr, lci, -1.0, c_wr, ALU.mult, ALU.add) if False else e.tensor_tensor(
            c_t1, c_ei, lci, ALU.mult),
               wait=[(sTab, sTab.n)], inc=(sTab, 1))
        ctx.on("vector", lambda e: e.tensor_tensor(c_wr, c_wr, c_t1, ALU.subtract),
               wait=[(sTab, sTab.n)], inc=(sTab, 1))
        ctx.on("vector", lambda e, lci=lci: e.tensor_tensor(c_wi, c_er, lci, ALU.mult),
               wait=[(sTab, sTab.n)], inc=(sTab, 1))
        ctx.on("vector", lambda e, lcr=lcr: e.tensor_tensor(c_t1, c_ei, lcr, ALU.mult),
               wait=[(sTab, sTab.n)], inc=(sTab, 1))
        ctx.on("vector", lambda e: e.tensor_tensor(c_wi, c_wi, c_t1, ALU.add),
               wait=[(sTab, sTab.n)], inc=(sTab, 1))
        ctx.on("vector", lambda e: e.tensor_copy(wbuf8[:, 0:8:2], c_wr),
               wait=[(sTab, sTab.n)], inc=(sTab, 1))
        ctx.on("vector", lambda e: e.tensor_copy(wbuf8[:, 1:8:2], c_wi),
               wait=[(sTab, sTab.n)], inc=(sTab, 1))
        for k in range(1, 8):
            gl = gath[:, (k - 1) * 8:(k - 1) * 8 + 8]
            ctx.on("vector", lambda e: e.tensor_tensor(c_t1, c_R, c_wr, ALU.mult),
                   wait=[(sTab, sTab.n)], inc=(sTab, 1))
            ctx.on("vector", lambda e, gl=gl: e.tensor_tensor(c_t1, c_t1, gl[:, 0:8:2], ALU.add),
                   wait=[(sTab, sTab.n)], inc=(sTab, 1))
            ctx.on("vector", lambda e: e.tensor_tensor(c_t2, c_R, c_wi, ALU.mult),
                   wait=[(sTab, sTab.n)], inc=(sTab, 1))
            ctx.on("vector", lambda e, gl=gl: e.tensor_tensor(c_t2, c_t2, gl[:, 1:8:2], ALU.add),
                   wait=[(sTab, sTab.n)], inc=(sTab, 1))
            ctx.on("vector", lambda e: e.tensor_tensor(c_wr, c_Ec, c_t1, ALU.mult),
                   wait=[(sTab, sTab.n)], inc=(sTab, 1))
            ctx.on("vector", lambda e: e.tensor_tensor(tmpc, c_Es, c_t2, ALU.mult),
                   wait=[(sTab, sTab.n)], inc=(sTab, 1))
            ctx.on("vector", lambda e: e.tensor_tensor(c_wr, c_wr, tmpc, ALU.subtract),
                   wait=[(sTab, sTab.n)], inc=(sTab, 1))
            ctx.on("vector", lambda e: e.tensor_tensor(c_wi, c_Ec, c_t2, ALU.mult),
                   wait=[(sTab, sTab.n)], inc=(sTab, 1))
            ctx.on("vector", lambda e: e.tensor_tensor(tmpc, c_Es, c_t1, ALU.mult),
                   wait=[(sTab, sTab.n)], inc=(sTab, 1))
            ctx.on("vector", lambda e: e.tensor_tensor(c_wi, c_wi, tmpc, ALU.add),
                   wait=[(sTab, sTab.n)], inc=(sTab, 1))
            ctx.on("vector", lambda e, k=k: e.tensor_copy(wbuf8[:, k * 8:k * 8 + 8:2], c_wr),
                   wait=[(sTab, sTab.n)], inc=(sTab, 1))
            ctx.on("vector", lambda e, k=k: e.tensor_copy(wbuf8[:, k * 8 + 1:k * 8 + 8:2], c_wi),
                   wait=[(sTab, sTab.n)], inc=(sTab, 1))
        ctx.on("vector", lambda e: e.memset(wsel[:], 0.0),
               wait=[(sTab, sTab.n)], inc=(sTab, 1))
        for k in range(8):
            ctx.on("vector", lambda e, k=k: e.scalar_tensor_tensor(
                wsel[:], wbuf8[:, k * 8:(k + 1) * 8], maskb[:, k:k + 1], wsel[:],
                ALU.mult, ALU.add),
                   wait=[(sTab, sTab.n)], inc=(sTab, 1))
        for dt in range(4):
            rhob = c_rho[:, dt:dt + 1].broadcast_to([128, TC])
            ctx.on("vector", lambda e, dt=dt, rhob=rhob: e.tensor_tensor_scan(
                br[:, dt * TC:(dt + 1) * TC], rhob, br[:, dt * TC:(dt + 1) * TC],
                wsel[:, dt * 2:dt * 2 + 1], ALU.mult, ALU.add),
                   wait=[(sTab, sTab.n)], inc=(sTab, 1))
            ctx.on("vector", lambda e, dt=dt, rhob=rhob: e.tensor_tensor_scan(
                bi[:, dt * TC:(dt + 1) * TC], rhob, bi[:, dt * TC:(dt + 1) * TC],
                wsel[:, dt * 2 + 1:dt * 2 + 2], ALU.mult, ALU.add),
                   wait=[(sTab, sTab.n)], inc=(sTab, 1))
            ctx.on("vector", lambda e, dt=dt: e.tensor_tensor(
                scr[:], Ec[:, dt * TC:(dt + 1) * TC], br[:, dt * TC:(dt + 1) * TC], ALU.mult),
                   wait=[(sTab, sTab.n)], inc=(sTab, 1))
            ctx.on("vector", lambda e, dt=dt: e.tensor_tensor(
                br[:, dt * TC:(dt + 1) * TC], Es[:, dt * TC:(dt + 1) * TC],
                bi[:, dt * TC:(dt + 1) * TC], ALU.mult),
                   wait=[(sTab, sTab.n)], inc=(sTab, 1))
            ctx.on("vector", lambda e, dt=dt: e.tensor_tensor(
                scr[:], scr[:], br[:, dt * TC:(dt + 1) * TC], ALU.add),
                   wait=[(sTab, sTab.n)], inc=(sTab, 1))
            ctx.on("vector", lambda e, dt=dt, l=l: e.scalar_tensor_tensor(
                hso[:, dt * TC:(dt + 1) * TC], scr[:],
                prm[:, PRM["scl"][0] + l * 4 + dt:PRM["scl"][0] + l * 4 + dt + 1],
                hso[:, dt * TC:(dt + 1) * TC], ALU.mult, ALU.add),
                   wait=[(sTab, sTab.n)], inc=(sTab, 1))
        emit_ln_inner(ctx, mk, SLN, hso, ybf, ybuf, rows, absb,
                      prm[:, PRM["ln2w"][0] + l * 4:PRM["ln2w"][0] + (l + 1) * 4],
                      prm[:, PRM["ln2b"][0] + l * 4:PRM["ln2b"][0] + (l + 1) * 4],
                      sqb, silu=False, first_wait=[(sTab, sTab.n)])
        ctx.on("vector", lambda e: e.tensor_copy(ybf[:], ybuf[:]),
               wait=[(SLN["n"], SLN["n"].n)], inc=(sTab, 1))
        ktc = sTab.n
        kffa = sFFA.n; kffe = sFFE.n; kffh = sFFH.n; kffe2 = sFFE2.n
        for hf in range(2):
            hbase = kffa + hf * 20
            for ft in range(16):
                gi = hf * 16 + ft
                bank = gi % 2
                for a in range(4):
                    w = []
                    if a == 0:
                        w = [(sTab, ktc), (sWL[0], 16 * (l + 1))]
                        if gi >= 2:
                            w.append((sFFE, kffe + gi - 1))
                    ctx.on("tensor",
                           lambda e, a=a, ft=ft, hf=hf, bank=bank: e.matmul(
                               bankf(mk, bank), w1b[:, a * 2048 + ft * 128:a * 2048 + (ft + 1) * 128],
                               ybf[:, a * TC + hf * TH:a * TC + hf * TH + TH],
                               start=(a == 0), stop=(a == 3)),
                           wait=w, inc=(sFFA, 1) if a == 3 else None)
                ctx.on("scalar",
                       lambda e, ft=ft, hf=hf, bank=bank, l=l: e.activation(
                           out=h1b[:, ft * TH:(ft + 1) * TH], in_=bankf(mk, bank),
                           func=AF.Silu, bias=prm[:, PRM["b1"][0] + l * 16 + ft:PRM["b1"][0] + l * 16 + ft + 1]),
                       wait=[(sFFA, hbase + ft + 1)], inc=(sFFE, 1))
            for dt in range(4):
                gi = hf * 4 + dt
                bank = 2 + gi % 2
                for f in range(16):
                    w = []
                    if f == 0:
                        w = [(sFFE, kffe + hf * 16 + 16), (sWL[1], 16 * (l + 1))]
                        if gi >= 2:
                            w.append((sFFE2, kffe2 + gi - 1))
                        if l == 0 and gi == 1:
                            w.append((sTab, 1))
                    ctx.on("tensor",
                           lambda e, f=f, dt=dt, bank=bank: e.matmul(
                               bankf(mk, bank), w2b[:, f * 512 + dt * 128:f * 512 + (dt + 1) * 128],
                               h1b[:, f * TH:(f + 1) * TH],
                               start=(f == 0), stop=(f == 15)),
                           wait=w, inc=(sFFA, 1) if f == 15 else None)
                ctx.on("scalar",
                       lambda e, dt=dt, hf=hf, bank=bank, l=l: e.activation(
                           out=ybuf[:, dt * TC + hf * TH:dt * TC + hf * TH + TH],
                           in_=bankf(mk, bank), func=AF.Identity,
                           bias=prm[:, PRM["b2"][0] + l * 4 + dt:PRM["b2"][0] + l * 4 + dt + 1]),
                       wait=[(sFFA, sFFA.n)], inc=(sFFE2, 1))
                ctx.on("gpsimd",
                       lambda e, dt=dt, hf=hf: e.tensor_tensor(
                           hso[:, dt * TC + hf * TH:dt * TC + hf * TH + TH],
                           hso[:, dt * TC + hf * TH:dt * TC + hf * TH + TH],
                           ybuf[:, dt * TC + hf * TH:dt * TC + hf * TH + TH], ALU.add),
                       wait=[(sFFE2, sFFE2.n)], inc=(sFFH, 1))
        if l < NL - 1:
            emit_wload(l + 1)

    emit_ln_inner(ctx, mk, SLN, hso, ybf, ybuf, rows, absb,
                  prm[:, PRM["lnlw"][0]:PRM["lnlw"][0] + 4],
                  prm[:, PRM["lnlb"][0]:PRM["lnlb"][0] + 4], sqb, silu=False,
                  first_wait=[(sFFH, sFFH.n)])
    ctx.on("vector", lambda e: e.tensor_copy(ybf[:], ybuf[:]),
           wait=[(SLN["n"], SLN["n"].n)], inc=(sHFB, 1))
    ctx.on("gpsimd", lambda e: e.dma_start(
        out=woutb[:].rearrange("p (a o) -> p a o", a=4),
        in_=mk.p2_woutT.rearrange("(a p) o -> p a o", p=128)),
        wait=[(sFFH, sFFH.n)], inc=(sWoL, 16))
    wo_ev = ctx.sb_at("p2_woev", 2 * TC, F32, RB + 136 * 1024)
    wo_rl = ctx.sb_at("p2_worl", 2 * TC, F32, RB + 144 * 1024)
    for o in range(NP):
        kf = o % 2
        w = [(sSp, 16), (sHFB, 1)] if o == 0 else []
        if o >= 2:
            w.append((sFin, o - 1))
        ctx.on("sync",
               lambda e, o=o, kf=kf: e.dma_start(
                   out=wo_rl[:, kf * TC:(kf + 1) * TC],
                   in_=mk.p2_xsp[:, o * TC:(o + 1) * TC]),
               wait=w, inc=(sRel[kf], 16))
        for th in range(2):
            idx = o * 2 + th
            bank = idx % 4
            for a in range(4):
                w2 = []
                if a == 0:
                    w2 = [(sHFB, 1), (sWoL, 16)]
                    if idx >= 4:
                        w2.append((sWoE, idx - 3))
                ctx.on("tensor",
                       lambda e, a=a, o=o, th=th, bank=bank: e.matmul(
                           bankf(mk, bank),
                           woutb[:, a * D + o * 128:a * D + (o + 1) * 128],
                           ybf[:, a * TC + th * TH:a * TC + th * TH + TH],
                           start=(a == 0), stop=(a == 3)),
                       wait=w2, inc=(sWoA, 1) if a == 3 else None)
            ctx.on("scalar",
                   lambda e, o=o, th=th, bank=bank: e.activation(
                       out=wo_ev[:, (o % 2) * TC + th * TH:(o % 2) * TC + th * TH + TH],
                       in_=bankf(mk, bank), func=AF.Identity,
                       bias=prm[:, PRM["bout"][0] + o:PRM["bout"][0] + o + 1]),
                   wait=[(sWoA, idx + 1), (sHFB, 1)] +
                        ([(sFin, o - 1)] if o >= 2 else []),
                   inc=(sWoE, 1))
        ctx.on("vector",
               lambda e, o=o, kf=kf: e.tensor_tensor(
                   mk.stream[:, o * TC:(o + 1) * TC],
                   wo_ev[:, kf * TC:(kf + 1) * TC],
                   wo_rl[:, kf * TC:(kf + 1) * TC], ALU.add),
               wait=[(sWoE, 2 * o + 2), (sRel[kf], 16 * (o // 2 + 1))], inc=(sFin, 1))

    norm_sems = []
    prev = None
    for hf in range(2):
        ctx.begin("half")
        extra = [] if prev is None else [(prev, 1)]
        sNorm, sSum = emit_ln_half(ctx, mk, f"p2ln{hf}", hf, sFin, NP,
                                   extra_first_wait=extra, alias_off=ln_off,
                                   alias_all=True, flat=True)
        norm_sems.append((sNorm, NP))
        prev = sSum
    return norm_sems


def host_inputs_p3(inp):
    wgT = np.ascontiguousarray(inp["L1_ffn_Wg"].T[PIDX])
    wuT = np.ascontiguousarray(inp["L1_ffn_Wu"].T[PIDX])
    wdT3 = np.ascontiguousarray(inp["L1_ffn_Wd"].T[:, PIDX])
    ag = np.ascontiguousarray(inp["L1_ffn_ag"].reshape(86, 128).T)
    au = np.ascontiguousarray(inp["L1_ffn_au"].reshape(86, 128).T)
    ad = np.ascontiguousarray(inp["L1_ffn_ad"][PIDX].reshape(32, 128).T)
    return {"p3_wgT": wgT, "p3_wuT": wuT, "p3_wdT": wdT3,
            "p3_ag": ag, "p3_au": au, "p3_ad": ad}


def declare_p3(ctx, mk):
    nc = ctx.nc
    I3 = 11008
    mk.p3_wgT = nc.dram_tensor("p3_wgT", [D, I3], F32, kind="ExternalInput")
    mk.p3_wuT = nc.dram_tensor("p3_wuT", [D, I3], F32, kind="ExternalInput")
    mk.p3_wdT = nc.dram_tensor("p3_wdT", [I3, D], F32, kind="ExternalInput")
    mk.p3_ag = nc.dram_tensor("p3_ag", [128, 86], F32, kind="ExternalInput")
    mk.p3_au = nc.dram_tensor("p3_au", [128, 86], F32, kind="ExternalInput")
    mk.p3_ad = nc.dram_tensor("p3_ad", [128, 32], F32, kind="ExternalInput")
    mk.p3_xsp = nc.dram_tensor("p3_xsp", [128, NP * TC], F32)
    mk.p3_dacc = nc.dram_tensor("p3_dacc", [NP, 128, TC], F32)


def emit_p3(ctx, mk, s_ready):
    ctx.begin("phase")
    ctx.begin("half")
    I3 = 11008
    CH_TILES = [22, 22, 21, 21]
    CH_I0 = [0, 22, 44, 65]

    RB = 16512
    xbf = ctx.sb_at("p3_xbf", NP * TC, BF16, RB + 96 * 1024)
    hbuf = ctx.sb_at("p3_h", 22 * TC, BF16, RB)
    wg = [ctx.sb_at("p3_wg0", D, BF16, RB + 45056),
          ctx.sb_at("p3_wg1", D, BF16, RB + 45056 + 8192)]
    wu = [ctx.sb_at("p3_wu0", D, BF16, RB + 61440),
          ctx.sb_at("p3_wu1", D, BF16, RB + 61440 + 8192)]
    wd = [ctx.sb_at("p3_wd0", 22 * 128, BF16, RB + 77824),
          ctx.sb_at("p3_wd1", 22 * 128, BF16, RB + 77824 + 5632)]
    gs = [ctx.sb_at("p3_gs0", TH, F32, RB + 89088),
          ctx.sb_at("p3_gs1", TH, F32, RB + 89088 + 2048)]
    us = [ctx.sb_at("p3_us0", TH, F32, RB + 93184),
          ctx.sb_at("p3_us1", TH, F32, RB + 93184 + 2048)]
    dsg = [ctx.sb_at("p3_dsg0", TH, F32, RB + 162 * 1024),
           ctx.sb_at("p3_dsg1", TH, F32, RB + 162 * 1024 + 2048)]
    XB = RB + 128 * 1024
    fin = [ctx.sb_at("p3_fin0", 2 * TC, F32, XB),
           ctx.sb_at("p3_fin1", 2 * TC, F32, XB + 8192)]
    dst_ = [ctx.sb_at("p3_dst0", TC, F32, XB + 16384),
            ctx.sb_at("p3_dst1", TC, F32, XB + 16384 + 4096)]
    agt = ctx.sb_at("p3_agt", 86, F32, RB + 166 * 1024)
    aut = ctx.sb_at("p3_aut", 86, F32, RB + 166 * 1024 + 512)
    adt = ctx.sb_at("p3_adt", 32, F32, RB + 166 * 1024 + 1024)
    ln_off = RB + 176 * 1024

    sLd = ctx.dsem("p3_sLd")
    sSp = ctx.dsem("p3_sSp")
    sXC = ctx.sem("p3_sXC")
    sWg = [ctx.dsem("p3_sWg0"), ctx.dsem("p3_sWg1")]
    sWgF = [ctx.sem("p3_sWgF0"), ctx.sem("p3_sWgF1")]
    sWu = [ctx.dsem("p3_sWu0"), ctx.dsem("p3_sWu1")]
    sWuF = [ctx.sem("p3_sWuF0"), ctx.sem("p3_sWuF1")]
    sWd = [ctx.dsem("p3_sWd0"), ctx.dsem("p3_sWd1")]
    sWdF = [ctx.sem("p3_sWdF0"), ctx.sem("p3_sWdF1")]
    sGA = ctx.sem("p3_sGA")
    sGE = ctx.sem("p3_sGE")
    sH = ctx.sem("p3_sH")
    sDA = ctx.sem("p3_sDA")
    sDE = ctx.sem("p3_sDE")
    sDD = [ctx.dsem("p3_sDD0"), ctx.dsem("p3_sDD1")]
    sFL = [ctx.dsem("p3_sFL0"), ctx.dsem("p3_sFL1")]
    sFin = ctx.sem("p3_sFin")
    sFA = ctx.sem("p3_sFA")
    sFS = ctx.sem("p3_sFS")

    sSp2 = ctx.dsem("p3_sSp2")
    ctx.on("scalar", lambda e: e.dma_start(out=agt[:, 0:86], in_=mk.p3_ag[:, :]),
           wait=list(s_ready), inc=(sLd, 16))
    ctx.on("scalar", lambda e: e.dma_start(out=aut[:, 0:86], in_=mk.p3_au[:, :]), inc=(sLd, 16))
    ctx.on("scalar", lambda e: e.dma_start(out=adt[:, 0:32], in_=mk.p3_ad[:, :]), inc=(sLd, 16))
    ctx.on("sync", lambda e: e.dma_start(out=mk.p3_xsp[:, 0:16 * TC],
                                         in_=mk.stream[:, 0:16 * TC]),
           wait=list(s_ready), inc=(sSp, 16))
    ctx.on("scalar", lambda e: e.dma_start(out=mk.p3_xsp[:, 16 * TC:NP * TC],
                                           in_=mk.stream[:, 16 * TC:NP * TC]),
           wait=list(s_ready), inc=(sSp2, 16))
    for p in range(NP - 1, -1, -1):
        ctx.on("vector",
               lambda e, p=p: e.tensor_copy(xbf[:, p * TC:(p + 1) * TC],
                                            mk.stream[:, p * TC:(p + 1) * TC]),
               wait=[(sSp, 16), (sSp2, 16)] if p == NP - 1 else [(sXC, NP - 1 - p)],
               inc=(sXC, 1))

    def emit_dn_load(ld):
        if ld >= 4 * NP:
            return
        ic2 = ld // NP
        o2 = ld % NP
        nt2 = CH_TILES[ic2]
        i0 = CH_I0[ic2] * 128
        ni = nt2 * 128
        par = ld % 2
        w = [(sWdF[par], ld // 2)] if ld >= 2 else [(sXC, NP)]
        ctx.on("gpsimd",
               lambda e, i0=i0, ni=ni, o2=o2, par=par, nt2=nt2: e.dma_start(
                   out=wd[par][:, 0:nt2 * 128].rearrange("p (a o) -> p a o", a=nt2),
                   in_=mk.p3_wdT[i0:i0 + ni, o2 * 128:(o2 + 1) * 128].rearrange(
                       "(a p) o -> p a o", p=128)),
               wait=w, inc=(sWd[par], 16))

    nlg = 0; nld = 0; ng = 0; nd = 0
    emit_dn_load(0)
    emit_dn_load(1)
    for ic in range(4):
        nt = CH_TILES[ic]
        for j in range(nt):
            it = CH_I0[ic] + j
            lg = nlg + j
            par = lg % 2
            w = [(sXC, NP)] if lg < 2 else [(sWgF[par], lg // 2)]
            ctx.on("gpsimd",
                   lambda e, it=it, par=par: e.dma_start(
                       out=wg[par][:].rearrange("p (a i) -> p a i", a=NP),
                       in_=mk.p3_wgT[:, it * 128:(it + 1) * 128].rearrange(
                           "(a p) i -> p a i", p=128)),
                   wait=w, inc=(sWg[par], 16))
            w = [(sWuF[par], lg // 2)] if lg >= 2 else [(sXC, NP)]
            ctx.on("gpsimd",
                   lambda e, it=it, par=par: e.dma_start(
                       out=wu[par][:].rearrange("p (a i) -> p a i", a=NP),
                       in_=mk.p3_wuT[:, it * 128:(it + 1) * 128].rearrange(
                           "(a p) i -> p a i", p=128)),
                   wait=w, inc=(sWu[par], 16))
        for j in range(nt):
            it = CH_I0[ic] + j
            parw = nlg % 2
            for th in range(2):
                for a in range(NP):
                    w = []
                    if a == 0:
                        w = [(sWg[parw], 16 * (nlg // 2 + 1))]
                        if ng >= 2:
                            w.append((sGE, 2 * (ng - 1)))
                    ctx.on("tensor",
                           lambda e, a=a, th=th, parw=parw: e.matmul(
                               bankf(mk, th), wg[parw][:, a * 128:(a + 1) * 128],
                               xbf[:, a * TC + th * TH:a * TC + th * TH + TH],
                               start=(a == 0), stop=(a == NP - 1)),
                           wait=w, inc=(sGA, 1) if a == NP - 1 else None)
                for a in range(NP):
                    w = []
                    if a == 0:
                        w = [(sWu[parw], 16 * (nlg // 2 + 1))]
                        if ng >= 2:
                            w.append((sGE, 2 * (ng - 1)))
                    ctx.on("tensor",
                           lambda e, a=a, th=th, parw=parw: e.matmul(
                               bankf(mk, 2 + th), wu[parw][:, a * 128:(a + 1) * 128],
                               xbf[:, a * TC + th * TH:a * TC + th * TH + TH],
                               start=(a == 0), stop=(a == NP - 1)),
                           wait=w, inc=(sGA, 1) if a == NP - 1 else None)
                kge = ng % 2
                ctx.on("scalar",
                       lambda e, th=th, it=it, kge=kge: e.activation(
                           out=gs[kge][:], in_=bankf(mk, th), func=AF.Silu,
                           scale=agt[:, it:it + 1]),
                       wait=[(sGA, 2 * ng + 2), (sLd, 48)] +
                            ([(sH, ng - 1)] if ng >= 2 else []),
                       inc=(sGE, 1))
                ctx.on("scalar",
                       lambda e, th=th, it=it, kge=kge: e.activation(
                           out=us[kge][:], in_=bankf(mk, 2 + th), func=AF.Identity,
                           scale=aut[:, it:it + 1]),
                       inc=(sGE, 1))
                wh = [(sGE, 2 * ng + 2)]
                if j == 0 and th == 0 and ic >= 1:
                    wh.append((sDA, 64 * ic))
                ctx.on("vector",
                       lambda e, j=j, th=th, kge=kge: e.tensor_tensor(
                           hbuf[:, j * TC + th * TH:j * TC + th * TH + TH],
                           gs[kge][:], us[kge][:], ALU.mult),
                       wait=wh, inc=(sH, 1))
                ng += 1
            ctx.on("vector", lambda e: e.engine_nop(), wait=[(sGA, 2 * ng)],
                   inc=(sWgF[parw], 1))
            ctx.on("vector", lambda e: e.engine_nop(), inc=(sWuF[parw], 1))
            nlg += 1
        for o in range(NP):
            parw = nld % 2
            for th in range(2):
                bankd = 4 + nd % 2
                for j in range(nt):
                    w = []
                    if j == 0:
                        w = [(sWd[parw], 16 * (nld // 2 + 1)), (sH, 2 * (CH_I0[ic] + nt))]
                        if nd >= 2:
                            w.append((sDE, nd - 1))
                    ctx.on("tensor",
                           lambda e, j=j, th=th, parw=parw, bankd=bankd, nt=nt: e.matmul(
                               bankf(mk, bankd), wd[parw][:, j * 128:(j + 1) * 128],
                               hbuf[:, j * TC + th * TH:j * TC + th * TH + TH],
                               start=(j == 0), stop=(j == nt - 1)),
                           wait=w, inc=(sDA, 1) if j == nt - 1 else None)
                kd = nd % 2
                ctx.on("vector",
                       lambda e, bankd=bankd, kd=kd: e.tensor_copy(dsg[kd][:], bankf(mk, bankd)),
                       wait=[(sDA, nd + 1)] + ([(sDD[kd], 16 * (nd // 2))] if nd >= 2 else []),
                       inc=(sDE, 1))
                op = ALU.bypass if ic == 0 else ALU.add
                ctx.on("gpsimd",
                       lambda e, o=o, th=th, kd=kd, op=op: e.dma_start(
                           out=mk.p3_dacc[o, :, th * TH:th * TH + TH], in_=dsg[kd][:],
                           accum_op=op),
                       wait=[(sDE, nd + 1)] + ([(sDD[kd], 16 * (nd // 2))] if nd >= 2 else []),
                       inc=(sDD[kd], 16))
                nd += 1
            ctx.on("vector", lambda e, parw=parw: e.engine_nop(),
                   wait=[(sDA, nd)], inc=(sWdF[parw], 1))
            emit_dn_load(nld + 2)
            nld += 1

    sFLx = [ctx.dsem("p3_sFLx0"), ctx.dsem("p3_sFLx1")]
    for o in range(NP):
        kf = o % 2
        w = [(sDD[0], 16 * (97 + o)), (sDD[1], 16 * (97 + o))]
        if o >= 2:
            w.append((sFin, o - 1))
        ctx.on("sync",
               lambda e, o=o, kf=kf: e.dma_start(
                   out=fin[kf][:, 0:TC], in_=mk.p3_dacc[o, :, :]),
               wait=w, inc=(sFL[kf], 16))
        ctx.on("gpsimd",
               lambda e, o=o, kf=kf: e.dma_start(
                   out=fin[kf][:, TC:2 * TC], in_=mk.p3_xsp[:, o * TC:(o + 1) * TC]),
               wait=w, inc=(sFLx[kf], 16))
        ctx.on("scalar",
               lambda e, o=o, kf=kf: e.activation(
                   out=dst_[kf][:], in_=fin[kf][:, 0:TC], func=AF.Identity,
                   scale=adt[:, o:o + 1]),
               wait=[(sFL[kf], 16 * (o // 2 + 1))] + ([(sFin, o - 1)] if o >= 2 else []),
               inc=(sFS, 1))
        ctx.on("vector",
               lambda e, o=o, kf=kf: e.tensor_tensor(
                   mk.stream[:, o * TC:(o + 1) * TC], dst_[kf][:],
                   fin[kf][:, TC:2 * TC], ALU.add),
               wait=[(sFS, o + 1), (sFLx[kf], 16 * (o // 2 + 1))], inc=(sFin, 1))

    norm_sems = []
    prev = None
    for hf in range(2):
        ctx.begin("half")
        extra = [] if prev is None else [(prev, 1)]
        sNorm, sSum = emit_ln_half(ctx, mk, f"p3ln{hf}", hf, sFin, NP,
                                   extra_first_wait=extra, alias_off=ln_off, alias_all=True,
                                   flat=True)
        norm_sems.append((sNorm, NP))
        prev = sSum
    return norm_sems


def build_kernel(phases=("p1",)):
    nc = bass.Bass("TRN2", target_bir_lowering=False, debug=False)
    ctx = Ctx(nc)
    mk = MK()
    declare_common(ctx, mk)
    if "p0" in phases:
        declare_p0(ctx, mk)
    if "p1" in phases:
        declare_p1(ctx, mk)
    if "p2" in phases:
        declare_p2(ctx, mk)
    if "p3" in phases:
        declare_p3(ctx, mk)
    emit_init_loads(ctx, mk)
    ready = mk.init_ready
    if "p0" in phases:
        norm = emit_p0(ctx, mk, ready)
        ready = norm
    if "p1" in phases:
        import os as _os
        norm = emit_p1(ctx, mk, ready, trunc=_os.environ.get("P1_TRUNC"))
        ready = norm
    if "p2" in phases:
        norm = emit_p2(ctx, mk, ready)
        ready = norm
    if "p3" in phases:
        norm = emit_p3(ctx, mk, ready)
        ready = norm
    emit_store_out(ctx, mk, list(ready))
    ctx.finalize()
    return nc


def kernel(**inputs):
    inp = {k: np.asarray(v) for k, v in inputs.items()}
    x = np.asarray(inp["L0_attn_x"], np.float32)
    shards = shard_x(x)
    ins = dict(host_consts(), **host_inputs_p0(inp), **host_inputs_p1(inp),
               **host_inputs_p2(inp), **host_inputs_p3(inp))
    nc = build_kernel(("p0", "p1", "p2", "p3"))
    in_maps = []
    for c in range(NCORE):
        m = np.zeros((1, 8), np.float32)
        m[0, c] = 1.0
        in_maps.append(dict(ins, x_in=shards[c], p2_mask=m))
    res = run_bass_kernel_spmd(nc, in_maps, core_ids=list(range(NCORE)))
    parts = [np.asarray(res.results[c]["y_out"]) for c in range(NCORE)]
    return unshard_out(parts).astype(np.float32)
